# version 01
# baseline (speedup 1.0000x reference)
import math
from contextlib import ExitStack

import numpy as np
import ml_dtypes

import concourse.bass as bass
import concourse.mybir as mybir
from concourse.bass_utils import run_bass_kernel_spmd

F32 = mybir.dt.float32
BF16 = mybir.dt.bfloat16
AF = mybir.ActivationFunctionType
OP = mybir.AluOpType
AX = mybir.AxisListType

NCORES = 8
T = 2048
D = 2048
NT = 16
SLAB = 512
NSLAB = 4
LN_EPS = 1e-5
GLA_NORM = 16.0
GLA_SCALE_LOG = math.log(128.0 ** (-0.5))
MLA_SCALE = 1.0 / math.sqrt(192.0)
INTER = 8192

M23 = 1.5 * 2.0 ** 23
M52 = 1.5 * 2.0 ** 52
M32 = 1.5 * 2.0 ** 32
C52_32 = M52 - M32

_UPTO = 4



class Eng:

    def __init__(self, e):
        self.e = e
        self.dirty = set()
        self.rd = set()
        self.waited = {}

    @staticmethod
    def _names(aps):
        out = []
        for a in aps:
            if a is None or isinstance(a, (int, float)):
                continue
            t = getattr(a, "tensor", None)
            if t is not None:
                out.append(t.name)
        return out

    def _pre(self, outs, ins):
        o = set(self._names(outs))
        i = set(self._names(ins))
        if (i & self.dirty) or (o & self.dirty) or (o & self.rd):
            self.e.drain()
            self.dirty.clear()
            self.rd.clear()
        self.dirty |= o
        self.rd |= i

    def wait(self, sem, val):
        val = int(val)
        if val <= 0 or self.waited.get(sem.name, -1) >= val:
            return
        self.e.wait_ge(sem.h, val)
        self.waited[sem.name] = val

    def _em(self, inst):
        self.last = inst
        return inst

    def dma(self, out, in_):
        self._pre([out], [in_])
        return self._em(self.e.dma_start(out=out, in_=in_))

    def copy(self, out, in_):
        self._pre([out], [in_])
        return self._em(self.e.tensor_copy(out, in_))

    def ts(self, out, in0, s1, s2, op0, op1=None):
        self._pre([out], [in0, s1, s2])
        if op1 is None:
            return self._em(self.e.tensor_scalar(out, in0, s1, None, op0))
        return self._em(self.e.tensor_scalar(out, in0, s1, s2, op0, op1))

    def tt(self, out, in0, in1, op):
        self._pre([out], [in0, in1])
        return self._em(self.e.tensor_tensor(out, in0, in1, op))

    def stt(self, out, in0, s, in1, op0, op1):
        self._pre([out], [in0, s, in1])
        return self._em(self.e.scalar_tensor_tensor(out, in0, s, in1, op0, op1))

    def scan(self, out, d0, d1, init, op0, op1):
        self._pre([out], [d0, d1])
        return self._em(self.e.tensor_tensor_scan(out, d0, d1, init, op0, op1))

    def red(self, out, in_, op, absval=False):
        self._pre([out], [in_])
        return self._em(self.e.tensor_reduce(out, in_, AX.X, op, apply_absolute_value=absval))

    def recip(self, out, in_):
        self._pre([out], [in_])
        return self._em(self.e.reciprocal(out, in_))

    def act(self, out, in_, func, bias=0.0, scale=1.0, accum_out=None):
        outs = [out] + ([accum_out] if accum_out is not None else [])
        self._pre(outs, [in_, bias, scale])
        return self._em(self.e.activation(out, in_, func, bias=bias, scale=scale, accum_out=accum_out))

    def mm(self, out, lhsT, rhs, start, stop):
        self._pre([out], [lhsT, rhs])
        return self._em(self.e.matmul(out, lhsT, rhs, start=start, stop=stop))

    def trans(self, out, in_, ident):
        self._pre([out], [in_, ident])
        return self._em(self.e.transpose(out, in_, ident))

    def memset(self, ap, v):
        self._pre([ap], [])
        return self._em(self.e.memset(ap, v))

    def nop(self):
        return self.e.nop()


class Sem:
    def __init__(self, h, name):
        self.h = h
        self.name = name
        self.n = 0

    def inc(self, inst, by=1):
        inst.then_inc(self.h, by)
        self.n += by
        return self.n


class Ctx:
    pass



def build_program(upto=_UPTO):
    nc = bass.Bass("TRN2", target_bir_lowering=False)
    c = Ctx()
    c.nc = nc

    def din(name, shape, dt=F32):
        return nc.dram_tensor(name, shape, dt, kind="ExternalInput").ap()

    c.x_in = din("x_in", [T, D])
    c.out_hbm = nc.dram_tensor("out", [T, D], F32, kind="ExternalOutput").ap()

    c.ident = din("c_ident", [128, 128], BF16)
    c.maskUT = din("c_mask", [128, 128], BF16)
    c.h128 = din("c_h128", [128, 128], BF16)
    c.onesrow = din("c_onesrow", [1, 128], BF16)
    c.onescol = din("c_onescol", [128, 1], BF16)
    c.onesrowf = din("c_onesrowf", [1, 128])

    c.gla_qk_t = din("gla_qk_t", [2, 2, 4, 128, 2048])
    c.gla_vgo_t = din("gla_vgo_t", [3, 4, 4, 128, 2048])
    c.wkg1_t = din("wkg1_t", [128, 256])
    c.wkg2T = din("wkg2T", [16, 1024])
    c.bkg2_cols = din("bkg2_cols", [128, 8])
    c.bg = din("bg", [1, D])

    c.ug_t = din("ug_t", [2, 4, 4, 4, 128, 2048])
    c.dn_t = din("dn_t", [4, 16, 128, 2048])
    c.su_ug_cols = din("su_ug_cols", [128, 16])
    c.wsc_up_cols = din("wsc_up_cols", [128, 64])
    c.wsc_gate_cols = din("wsc_gate_cols", [128, 64])
    c.su_dp_cols = din("su_dp_cols", [128, 64])
    c.wsc_dp_row = din("wsc_dp_row", [1, D])

    c.kvd_t = din("kvd_t", [4, 128, 2048])
    c.qd_t = din("qd_t", [2, 4, 128, 2048])
    c.kup_t = din("kup_t", [16, 128, 512])
    c.vup_t = din("vup_t", [16, 128, 512])
    c.kr_t = din("kr_t", [16, 128, 256])
    c.qup_t = din("qup_t", [16, 128, 1024])
    c.qr_t = din("qr_t", [16, 128, 512])
    c.outw_t = din("outw_t", [4, 4, 128, 2048])
    c.kvd_b = din("kvd_b", [128, 4])
    c.kup_b = din("kup_b", [128, 16])
    c.vup_b = din("vup_b", [128, 16])
    c.kr_b = din("kr_b", [64, 16])
    c.qd_b = din("qd_b", [128, 8])
    c.qup_b = din("qup_b", [128, 16])
    c.qr_b = din("qr_b", [64, 16])
    c.outw_b = din("outw_b", [1, D])

    c.w1_t = din("w1_t", [16, 16, 128, 512])
    c.w2_t = din("w2_t", [4, 64, 128, 512])
    c.w1q_hbm = nc.dram_tensor("w1q_s", [16, 16, 128, 512], BF16, kind="Internal").ap()
    c.w2q_hbm = nc.dram_tensor("w2q_s", [4, 64, 128, 512], BF16, kind="Internal").ap()

    with ExitStack() as stack:
        e = stack.enter_context
        c.block = e(nc.Block())
        c.T = Eng(nc.tensor); c.V = Eng(nc.vector); c.S = Eng(nc.scalar)
        c.G = Eng(nc.gpsimd); c.Y = Eng(nc.sync)
        for name in ["ld", "wd", "tp", "vp", "sp", "gp", "st", "ini"]:
            setattr(c, name, Sem(e(nc.semaphore(name)), name))
        c.wsem = [Sem(e(nc.semaphore(f"w{i}")), f"w{i}") for i in range(5)]

        c.res = [e(nc.sbuf_tensor(f"res{i}", [128, D], BF16)) for i in range(NT)]
        c.fmA = [e(nc.sbuf_tensor(f"fmA{i}", [128, D], BF16)) for i in range(16)]
        c.fmB = [e(nc.sbuf_tensor(f"fmB{i}", [128, D], BF16)) for i in range(4)]
        c.f32a = [e(nc.sbuf_tensor(f"f32a{i}", [128, D], F32)) for i in range(2)]
        c.wb = [e(nc.sbuf_tensor(f"wb{i}", [128, D], BF16)) for i in range(6)]
        c.Sst = e(nc.sbuf_tensor("Sst", [128, 1024], F32))
        c.Stmp = e(nc.sbuf_tensor("Stmp", [128, 256], F32))
        c.Sbf = e(nc.sbuf_tensor("Sbf", [128, 1024], BF16))
        c.stat = e(nc.sbuf_tensor("stat", [128, 64], F32))
        c.cols = e(nc.sbuf_tensor("cols", [128, 160], F32))
        c.rows = e(nc.sbuf_tensor("rows", [1, 512], F32))
        c.rbf = e(nc.sbuf_tensor("rbf", [1, 2048], BF16))
        c.bcols = e(nc.sbuf_tensor("bcols", [128, 96], F32))
        c.sid = e(nc.sbuf_tensor("sid", [128, 128], BF16))
        c.smask = e(nc.sbuf_tensor("smask", [128, 128], BF16))
        c.sh128 = e(nc.sbuf_tensor("sh128", [128, 128], BF16))
        c.soner = e(nc.sbuf_tensor("soner", [1, 128], BF16))
        c.sonec = e(nc.sbuf_tensor("sonec", [128, 1], BF16))
        c.sonerf = e(nc.sbuf_tensor("sonerf", [1, 128], F32))
        c.skg2 = e(nc.sbuf_tensor("skg2", [16, 1024], BF16))
        c.outf32 = c.f32a[1]

        c.PS = [e(nc.psum_tensor(f"PS{i}", [128, 512], F32)) for i in range(6)]
        c.PT = [e(nc.psum_tensor(f"PT{i}", [128, 1024], BF16)) for i in range(2)]
        c.dumb = c.PS[5]

        for i, cv in enumerate([0.0, 1.0, -1.0, 0.5, LN_EPS, GLA_SCALE_LOG]):
            nc.const_aps.register(F32, cv, c.cols[:, 150 + i:151 + i])

        _init(c)
        if upto >= 1:
            _gla(c)
        if upto >= 2:
            _ffn0(c)
        if upto >= 3:
            _mla(c)
        if upto >= 4:
            _ffn1(c)
        _finish(c)

    return nc


def _init(c):
    Tn, V, S, G, Y = c.T, c.V, c.S, c.G, c.Y
    ld, ini = c.ld, c.ini
    ld.inc(Y.dma(c.sid[:], c.ident), 16)
    ld.inc(Y.dma(c.smask[:], c.maskUT), 16)
    ld.inc(Y.dma(c.sh128[:], c.h128), 16)
    ld.inc(Y.dma(c.soner[:], c.onesrow), 16)
    ld.inc(Y.dma(c.sonerf[:], c.onesrowf), 16)
    ld.inc(Y.dma(c.sonec[:], c.onescol), 16)
    for i in range(NT):
        c.wd.inc(G.dma(c.res[i][:], c.x_in[i * 128:(i + 1) * 128, :]), 16)
    for i, cv in enumerate([0.0, 1.0, -1.0, 0.5, LN_EPS, GLA_SCALE_LOG]):
        G.memset(c.cols[:, 150 + i:151 + i], cv)
    c.wd.inc(G.dma(c.skg2[:], c.wkg2T), 16)
    ld.inc(Y.dma(c.bcols[:, 8:16], c.bkg2_cols), 16)
    c.wd.inc(G.dma(c.rbf[:, 0:D], c.bg), 16)
    G.memset(c.Sst[:], 0.0)
    G.memset(c.Sbf[:], 0.0)
    ini.inc(G.last, 1)
    for E in (Tn, V, S, Y):
        E.wait(ini, 1)
        E.wait(ld, ld.n)
        E.wait(c.wd, c.wd.n)
    V.ts(c.bcols[:, 0:8], c.bcols[:, 8:16], -1.0, None, OP.mult)
    c.vp.inc(V.last)
    c.negbkg2 = c.bcols[:, 0:8]
    c.bg_row = c.rbf[:, 0:D]


def _finish(c):
    V, Y = c.V, c.Y
    vp, st = c.vp, c.st
    for tc in range(NT):
        V.wait(st, st.n)
        V.copy(c.outf32[:], c.res[tc][:])
        iv = vp.inc(V.last)
        Y.wait(vp, iv)
        st.inc(Y.dma(c.out_hbm[tc * 128:(tc + 1) * 128, :], c.outf32[:]), 16)
    Y.wait(st, st.n)



def _transpose_in(c, pairs, su_cols=None):
    Tn, V = c.T, c.V
    tp, vp = c.tp, c.vp
    hist = {}
    vp0 = vp.n
    for n, (src, dst) in enumerate(pairs):
        pt = c.PT[n % 2][:, ((n // 2) % 4) * 256:((n // 2) % 4) * 256 + 128]
        Tn.wait(vp, hist[n - 2] if n >= 2 else vp0)
        Tn.trans(pt, src, c.sid[:])
        it = tp.inc(Tn.last)
        V.wait(tp, it)
        if su_cols is not None and su_cols[n] is not None:
            V.ts(dst, pt, su_cols[n], None, OP.mult)
        else:
            V.copy(dst, pt)
        hist[n] = vp.inc(V.last)


def _ln_write(c, tc, src_tm_bf, final=False):
    V, S = c.V, c.S
    vp, sp = c.vp, c.sp
    s = c.f32a[0]
    V.tt(s[:], src_tm_bf, c.res[tc][:], OP.add)
    V.red(c.stat[:, 0:1], s[:], OP.add)
    iv = vp.inc(V.last)
    S.wait(vp, iv)
    for q in range(4):
        S.act(c.dumb[:, 0:512], s[:, q * 512:(q + 1) * 512], AF.Square, accum_out=c.stat[:, 4 + q:5 + q])
    isp = sp.inc(S.last)
    V.wait(sp, isp)
    V.ts(c.stat[:, 1:2], c.stat[:, 0:1], 1.0 / 2048.0, None, OP.mult)
    V.tt(c.stat[:, 8:9], c.stat[:, 4:5], c.stat[:, 5:6], OP.add)
    V.tt(c.stat[:, 9:10], c.stat[:, 6:7], c.stat[:, 7:8], OP.add)
    V.tt(c.stat[:, 10:11], c.stat[:, 8:9], c.stat[:, 9:10], OP.add)
    V.ts(c.stat[:, 11:12], c.stat[:, 10:11], 1.0 / 2048.0, None, OP.mult)
    V.tt(c.stat[:, 12:13], c.stat[:, 1:2], c.stat[:, 1:2], OP.mult)
    V.tt(c.stat[:, 13:14], c.stat[:, 11:12], c.stat[:, 12:13], OP.subtract)
    iv = vp.inc(V.last)
    S.wait(vp, iv)
    S.act(c.stat[:, 14:15], c.stat[:, 13:14], AF.Ln, bias=LN_EPS)
    S.act(c.stat[:, 2:3], c.stat[:, 14:15], AF.Exp, scale=-0.5)
    isp = sp.inc(S.last)
    V.wait(sp, isp)
    dst = c.outf32[:] if final else c.res[tc][:]
    V.ts(dst, s[:], c.stat[:, 1:2], c.stat[:, 2:3], OP.subtract, OP.mult)
    return vp.inc(V.last)



def _gla(c):
    Tn, V, S, G = c.T, c.V, c.S, c.G
    tp, vp, sp, wd = c.tp, c.vp, c.sp, c.wd
    PS = c.PS
    fmA, fmB, f32a, wb = c.fmA, c.fmB, c.f32a, c.wb
    ring_rel = {i: tp.n for i in range(5)}

    def xfm(f):
        return fmB[f // 4][:, (f % 4) * 512:(f % 4) * 512 + 512]

    def gpt(m):
        return f32a[m // 4][:, (m % 4) * 512:(m % 4) * 512 + 512]

    for sb in range(NSLAB):
        tcs = [sb * 4 + i for i in range(4)]

        pairs = []
        for ci in range(4):
            for f in range(16):
                pairs.append((c.res[tcs[ci]][:, f * 128:(f + 1) * 128],
                              fmB[f // 4][:, (f % 4) * 512 + ci * 128:(f % 4) * 512 + ci * 128 + 128]))
        _transpose_in(c, pairs)
        xfm_ready = vp.n

        G.wait(tp, ring_rel.get(4, 0))
        iw4 = c.wsem[4].inc(G.dma(wb[4][:, 0:256], c.wkg1_t), 16)
        Tn.wait(c.wsem[4], iw4)
        Tn.wait(vp, xfm_ready)
        Tn.wait(sp, c.sp.n)
        pk = PS[0][0:16, :]
        for f in range(16):
            Tn.mm(pk, wb[4][:, f * 16:(f + 1) * 16], xfm(f), start=(f == 0), stop=(f == 15))
        it = tp.inc(Tn.last)
        ring_rel[4] = it
        V.wait(tp, it)
        kg1 = wb[5][0:16, 0:512]
        V.copy(kg1, pk)
        ivk = vp.inc(V.last)
        Tn.wait(vp, ivk)
        sp_hist = {}
        for m in range(8):
            p = PS[m % 2][:]
            if m >= 2:
                Tn.wait(sp, sp_hist[m - 2])
            Tn.mm(p, c.skg2[:, m * 128:(m + 1) * 128], kg1, start=True, stop=True)
            it = tp.inc(Tn.last)
            S.wait(tp, it)
            S.wait(c.vp, 1)
            S.act(gpt(m), p, AF.Exp, scale=-1.0, bias=c.negbkg2[:, m:m + 1])
            S.act(gpt(m), gpt(m), AF.Ln, bias=1.0)
            sp_hist[m] = sp.inc(S.last)
        V.wait(sp, sp.n)
        for m in range(8):
            for ch in range(4):
                seg = gpt(m)[:, ch * 128:(ch + 1) * 128]
                V.scan(seg, seg, seg, 0.0, OP.add, OP.bypass)
        ivG = vp.inc(V.last)
        S.wait(vp, ivG)
        for m in range(8):
            S.act(c.cols[:, 16 + m * 4:16 + m * 4 + 4], gpt(m)[:, 127::128], AF.Exp, scale=-1.0 / GLA_NORM)
        sp_a = sp.inc(S.last)

        def qk(qi, base, escale, ebias):
            for mg in range(2):
                for fg in range(4):
                    slot = wb[fg % 2]
                    G.wait(tp, ring_rel.get(fg % 2, 0))
                    ws = c.wsem[fg % 2]
                    iw = ws.inc(G.dma(slot[:], c.gla_qk_t[qi, mg, fg]), 16)
                    Tn.wait(ws, iw)
                    if fg == 0:
                        Tn.wait(vp, vp.n)
                        Tn.wait(sp, sp.n)
                    for fi in range(4):
                        f = fg * 4 + fi
                        for m4 in range(4):
                            Tn.mm(PS[2 + m4][:], slot[:, fi * 512 + m4 * 128:fi * 512 + m4 * 128 + 128],
                                  xfm(f), start=(f == 0), stop=(f == 15))
                    ring_rel[fg % 2] = tp.n + 1
                    tp.inc(Tn.last)
                itg = tp.n
                for m4 in range(4):
                    m = mg * 4 + m4
                    et = wb[4][:, 256 + (m4 % 2) * 512:256 + (m4 % 2) * 512 + 512]
                    S.wait(vp, ivG)
                    S.wait(vp, c._et_vp.get(m4 % 2, 0))
                    S.act(et, gpt(m), AF.Exp, bias=ebias, scale=escale)
                    isp = sp.inc(S.last)
                    V.wait(tp, itg)
                    V.wait(sp, isp)
                    dst = fmA[base + m // 4][:, (m % 4) * 512:(m % 4) * 512 + 512]
                    V.tt(dst, PS[2 + m4][:], et, OP.mult)
                    c._et_vp[m4 % 2] = vp.inc(V.last)

        c._et_vp = {}
        qk(0, 0, -1.0 / GLA_NORM, 0.0)
        qk(1, 2, +1.0 / GLA_NORM, GLA_SCALE_LOG)

        def tmproj(wi, base, bias_row):
            for oc in range(4):
                for fg in range(4):
                    slot = wb[2 + fg % 2]
                    G.wait(tp, ring_rel.get(2 + fg % 2, 0))
                    ws = c.wsem[2 + fg % 2]
                    iw = ws.inc(G.dma(slot[:], c.gla_vgo_t[wi, oc, fg]), 16)
                    Tn.wait(ws, iw)
                    if fg == 0:
                        Tn.wait(vp, vp.n)
                        Tn.wait(sp, sp.n)
                    for fi in range(4):
                        f = fg * 4 + fi
                        for tci in range(4):
                            Tn.mm(PS[2 + tci][:], xfm(f)[:, tci * 128:(tci + 1) * 128],
                                  slot[:, fi * 512:(fi + 1) * 512],
                                  start=(f == 0), stop=(f == 15 and bias_row is None))
                    ring_rel[2 + fg % 2] = tp.n + 1
                    tp.inc(Tn.last)
                if bias_row is not None:
                    for tci in range(4):
                        Tn.mm(PS[2 + tci][:], c.soner[0:1, 0:128], bias_row[:, oc * 512:(oc + 1) * 512],
                              start=False, stop=True)
                    itg = tp.inc(Tn.last)
                else:
                    itg = tp.n
                V.wait(tp, itg)
                for tci in range(4):
                    V.copy(fmA[base + tci][:, oc * 512:(oc + 1) * 512], PS[2 + tci][:])
                vp.inc(V.last)

        tmproj(0, 4, None)
        tmproj(1, 8, c.bg_row)

        v_ready = vp.n
        prev_state_vp = getattr(c, "_gla_state_vp", {})
        kv_read_vp = getattr(c, "_gla_kv_vp", 0)
        sq_hist = getattr(c, "_gla_sq", {})
        for ch in range(4):
            o_evac = {}
            for h in range(16):
                m = h // 2
                r0 = (h % 2) * 64
                qs = fmA[m // 4][r0:r0 + 64, (m % 4) * 512 + ch * 128:(m % 4) * 512 + ch * 128 + 128]
                ks = fmA[2 + m // 4][r0:r0 + 64, (m % 4) * 512 + ch * 128:(m % 4) * 512 + ch * 128 + 128]
                vs = fmA[4 + ch][:, h * 128:(h + 1) * 128]
                Sst_h = c.Sst[r0:r0 + 64, (h // 2) * 128:(h // 2) * 128 + 128]
                Stmp_h = c.Stmp[r0:r0 + 64, 0:128]
                Sbf_h = c.Sbf[r0:r0 + 64, (h // 2) * 128:(h // 2) * 128 + 128]
                a_col = c.cols[r0:r0 + 64, 16 + m * 4 + ch:16 + m * 4 + ch + 1]
                STp = PS[h % 2][:, 0:128]
                Op = PS[2 + h % 2][:, 0:128]
                KVp = PS[4][r0:r0 + 64, 0:128]
                kTp = c.PT[0][:, (h % 2) * 256:(h % 2) * 256 + 64]
                stm = wb[5][:, 1024 + (h % 2) * 128:1024 + (h % 2) * 128 + 128]
                kT_sb = wb[5][:, 1536 + (h % 2) * 64:1536 + (h % 2) * 64 + 64]

                Tn.wait(vp, v_ready)
                if h >= 2:
                    Tn.wait(vp, o_evac.get(h - 2, 0))
                Tn.wait(sp, sq_hist.get((ch, h - 2), sq_hist.get((ch - 1, h + 14), 0)))
                Tn.mm(STp, ks, qs, start=True, stop=True)
                it1 = tp.inc(Tn.last)
                V.wait(tp, it1)
                V.tt(stm, STp, c.smask[:], OP.mult)
                iv1 = vp.inc(V.last)
                Tn.wait(vp, iv1)
                Tn.wait(vp, prev_state_vp.get(h, 0))
                Tn.mm(Op, stm, vs, start=True, stop=False)
                Tn.mm(Op, qs, Sbf_h, start=False, stop=True)
                it2 = tp.inc(Tn.last)
                Tn.trans(kTp, ks, c.sid[r0:r0 + 64, r0:r0 + 64])
                it3 = tp.inc(Tn.last)
                V.wait(tp, it3)
                V.copy(kT_sb, kTp)
                iv2 = vp.inc(V.last)
                Tn.wait(vp, iv2)
                Tn.wait(vp, kv_read_vp)
                Tn.mm(KVp, kT_sb, vs, start=True, stop=True)
                it4 = tp.inc(Tn.last)
                V.wait(tp, it4)
                V.wait(sp, sp_a)
                V.tt(Stmp_h, Sst_h, KVp, OP.add)
                V.ts(Sst_h, Stmp_h, a_col, None, OP.mult)
                V.copy(Sbf_h, Sst_h)
                prev_state_vp[h] = kv_read_vp = vp.inc(V.last)
                V.wait(tp, it2)
                V.copy(fmA[12 + ch][:, h * 128:(h + 1) * 128], Op)
                V.red(c.stat[:, 32 + h:33 + h], Op, OP.add)
                o_evac[h] = vp.inc(V.last)
                S.wait(tp, it2)
                S.act(c.dumb[:, 0:128], Op, AF.Square, accum_out=c.stat[:, 48 + h:49 + h])
                sq_hist[(ch, h)] = sp.inc(S.last)
            V.wait(sp, sp.n)
            V.ts(c.stat[:, 0:16], c.stat[:, 32:48], 1.0 / 128.0, None, OP.mult)
            V.ts(c.cols[:, 48:64], c.stat[:, 48:64], 1.0 / 128.0, None, OP.mult)
            V.tt(c.cols[:, 64:80], c.stat[:, 0:16], c.stat[:, 0:16], OP.mult)
            V.tt(c.cols[:, 80:96], c.cols[:, 48:64], c.cols[:, 64:80], OP.subtract)
            ivs = vp.inc(V.last)
            S.wait(vp, ivs)
            S.act(c.cols[:, 96:112], c.cols[:, 80:96], AF.Ln, bias=LN_EPS)
            S.act(c.stat[:, 16:32], c.cols[:, 96:112], AF.Exp, scale=-0.5)
            S.act(f32a[0][:], fmA[8 + ch][:], AF.Silu)
            isp = sp.inc(S.last)
            V.wait(sp, isp)
            for h in range(16):
                V.ts(fmA[8 + ch][:, h * 128:(h + 1) * 128], fmA[12 + ch][:, h * 128:(h + 1) * 128],
                     c.stat[:, 0 + h:1 + h], c.stat[:, 16 + h:17 + h], OP.subtract, OP.mult)
            for h in range(16):
                V.tt(fmA[12 + ch][:, h * 128:(h + 1) * 128], fmA[8 + ch][:, h * 128:(h + 1) * 128],
                     f32a[0][:, h * 128:(h + 1) * 128], OP.mult)
            vp.inc(V.last)
        c._gla_state_vp = prev_state_vp
        c._gla_kv_vp = kv_read_vp
        c._gla_sq = sq_hist

        z_pairs = []
        for ci in range(4):
            for f in range(16):
                z_pairs.append((fmA[12 + ci][:, f * 128:(f + 1) * 128],
                                fmB[f // 4][:, (f % 4) * 512 + ci * 128:(f % 4) * 512 + ci * 128 + 128]))
        Tn.wait(vp, vp.n)
        _transpose_in(c, z_pairs)
        zfm_ready = vp.n

        for oc in range(4):
            for fg in range(4):
                slot = wb[fg % 2]
                G.wait(tp, ring_rel.get(fg % 2, 0))
                ws = c.wsem[fg % 2]
                iw = ws.inc(G.dma(slot[:], c.gla_vgo_t[2, oc, fg]), 16)
                Tn.wait(ws, iw)
                if fg == 0:
                    Tn.wait(vp, max(zfm_ready, getattr(c, "_wo_evac", 0)))
                    Tn.wait(sp, sp.n)
                for fi in range(4):
                    f = fg * 4 + fi
                    for tci in range(4):
                        Tn.mm(PS[2 + tci][:], xfm(f)[:, tci * 128:(tci + 1) * 128],
                              slot[:, fi * 512:(fi + 1) * 512], start=(f == 0), stop=(f == 15))
                ring_rel[fg % 2] = tp.n + 1
                tp.inc(Tn.last)
            itg = tp.n
            V.wait(tp, itg)
            for tci in range(4):
                V.copy(fmA[4 + tci][:, oc * 512:(oc + 1) * 512], PS[2 + tci][:])
            c._wo_evac = vp.inc(V.last)
        for tci in range(4):
            _ln_write(c, tcs[tci], fmA[4 + tci][:])



FFN0_BLOCKS = None


def _ffn0(c):
    Tn, V, S, G, Y = c.T, c.V, c.S, c.G, c.Y
    tp, vp, sp, ld = c.tp, c.vp, c.sp, c.ld
    PS = c.PS
    fmA, fmB, f32a, wb = c.fmA, c.fmB, c.f32a, c.wb
    ring_rel = {i: tp.n for i in range(5)}

    def xfm(f):
        return fmB[f // 4][:, (f % 4) * 512:(f % 4) * 512 + 512]

    def zsl(jj):
        return fmA[jj // 4][:, (jj % 4) * 512:(jj % 4) * 512 + 512]

    SC1 = 2.0 ** -5.5
    SC2 = 2.0 ** -6.5

    Y.wait(vp, vp.n)
    Y.wait(sp, sp.n)
    Y.wait(tp, tp.n)
    ld.inc(Y.dma(c.cols[:, 0:16], c.su_ug_cols), 16)
    ld.inc(Y.dma(c.cols[:, 16:80], c.wsc_up_cols), 16)
    ld.inc(Y.dma(c.cols[:, 80:144], c.wsc_gate_cols), 16)
    ld.inc(Y.dma(c.stat[:, 0:64], c.su_dp_cols), 16)
    ld.inc(Y.dma(f32a[0][0:1, 0:2048], c.wsc_dp_row), 16)
    V.wait(ld, ld.n)
    V.tt(c.cols[:, 16:80], c.cols[:, 16:80], c.stat[:, 0:64], OP.mult)
    V.ts(c.cols[:, 16:80], c.cols[:, 16:80], SC1, None, OP.mult)
    V.ts(c.cols[:, 80:144], c.cols[:, 80:144], SC1, None, OP.mult)
    V.ts(c.rbf[0:1, 0:2048], f32a[0][0:1, 0:2048], SC2, None, OP.mult)
    ivc = vp.inc(V.last)
    Tn.wait(vp, ivc)
    for oc4 in range(4):
        Tn.mm(PS[2][:], c.soner[0:1, 0:128], c.rbf[0:1, oc4 * 512:(oc4 + 1) * 512], start=True, stop=True)
        itb = tp.inc(Tn.last)
        V.wait(tp, itb)
        V.copy(wb[4][:, oc4 * 512:(oc4 + 1) * 512], PS[2][:])
        ivb = vp.inc(V.last)
        Tn.wait(vp, ivb)

    for sb in range(NSLAB):
        tcs = [sb * 4 + i for i in range(4)]
        pairs, sus = [], []
        for ci in range(4):
            for f in range(16):
                pairs.append((c.res[tcs[ci]][:, f * 128:(f + 1) * 128],
                              fmB[f // 4][:, (f % 4) * 512 + ci * 128:(f % 4) * 512 + ci * 128 + 128]))
                sus.append(c.cols[:, f:f + 1])
        _transpose_in(c, pairs, su_cols=sus)
        Tn.wait(vp, vp.n)
        Tn.wait(sp, sp.n)
        ev = {}
        for f in range(16):
            p = PS[2 + f % 4][:]
            Tn.wait(vp, ev.get(f - 4, 0))
            Tn.mm(p, c.sh128[:], xfm(f), start=True, stop=True)
            it = tp.inc(Tn.last)
            V.wait(tp, it)
            V.copy(xfm(f), p)
            ev[f] = vp.inc(V.last)
        V.wait(vp, vp.n)
        for s in (1, 2, 4, 8):
            for t0 in range(16):
                if t0 & s:
                    continue
                a, b = xfm(t0), xfm(t0 ^ s)
                V.tt(a, a, b, OP.add)
                V.stt(b, b, -2.0, a, OP.mult, OP.add)
        xh_ready = vp.inc(V.last)

        for jc in range(4):
            for jg in range(4):
                for pas in range(2):
                    for fg in range(4):
                        slot = wb[fg % 2]
                        G.wait(tp, ring_rel.get(fg % 2, 0))
                        ws = c.wsem[fg % 2]
                        iw = ws.inc(G.dma(slot[:], c.ug_t[pas, jc, jg, fg]), 16)
                        Tn.wait(ws, iw)
                        if fg == 0:
                            Tn.wait(vp, max(xh_ready, vp.n))
                            Tn.wait(sp, sp.n)
                        for fi in range(4):
                            f = fg * 4 + fi
                            for j2 in range(4):
                                Tn.mm(PS[2 + j2][:], slot[:, fi * 512 + j2 * 128:fi * 512 + j2 * 128 + 128],
                                      xfm(f), start=(f == 0), stop=(f == 15))
                        ring_rel[fg % 2] = tp.n + 1
                        tp.inc(Tn.last)
                    itg = tp.n
                    for j2 in range(4):
                        jj = jc * 16 + jg * 4 + j2
                        if pas == 0:
                            V.wait(tp, itg)
                            V.copy(zsl(jj), PS[2 + j2][:])
                        else:
                            S.wait(tp, itg)
                            S.act(f32a[0][:, j2 * 512:(j2 + 1) * 512], PS[2 + j2][:], AF.Silu,
                                  scale=c.cols[:, 80 + jj:81 + jj])
                            isp = sp.inc(S.last)
                            V.wait(sp, isp)
                            V.stt(zsl(jj), zsl(jj), c.cols[:, 16 + jj:17 + jj],
                                  f32a[0][:, j2 * 512:(j2 + 1) * 512], OP.mult, OP.mult)
                    vp.inc(V.last)
            Tn.wait(vp, vp.n)
            ev2 = {}
            for jt in range(16):
                jj = jc * 16 + jt
                p = PS[2 + jt % 4][:]
                Tn.wait(vp, ev2.get(jt - 4, 0))
                Tn.mm(p, c.sh128[:], zsl(jj), start=True, stop=True)
                it = tp.inc(Tn.last)
                V.wait(tp, it)
                V.copy(zsl(jj), p)
                ev2[jt] = vp.inc(V.last)
            V.wait(vp, vp.n)
            for s in (1, 2, 4, 8):
                for t0 in range(16):
                    if t0 & s:
                        continue
                    a, b = zsl(jc * 16 + t0), zsl(jc * 16 + (t0 ^ s))
                    V.tt(a, a, b, OP.add)
                    V.stt(b, b, -2.0, a, OP.mult, OP.add)
            vp.inc(V.last)
        for s in (16, 32):
            for t0 in range(64):
                if t0 & s:
                    continue
                a, b = zsl(t0), zsl(t0 ^ s)
                V.tt(a, a, b, OP.add)
                V.stt(b, b, -2.0, a, OP.mult, OP.add)
        z_ready = vp.inc(V.last)

        for oc in range(4):
            for jg in range(16):
                slot = wb[2 + jg % 2]
                G.wait(tp, ring_rel.get(2 + jg % 2, 0))
                ws = c.wsem[2 + jg % 2]
                iw = ws.inc(G.dma(slot[:], c.dn_t[oc, jg]), 16)
                Tn.wait(ws, iw)
                if jg == 0:
                    Tn.wait(vp, max(z_ready, vp.n))
                    Tn.wait(sp, sp.n)
                for ji in range(4):
                    j = jg * 4 + ji
                    for tci in range(4):
                        Tn.mm(PS[2 + tci][:], zsl(j)[:, tci * 128:(tci + 1) * 128],
                              slot[:, ji * 512:(ji + 1) * 512], start=(j == 0), stop=(j == 63))
                ring_rel[2 + jg % 2] = tp.n + 1
                tp.inc(Tn.last)
            itg = tp.n
            V.wait(tp, itg)
            for tci in range(4):
                V.tt(fmB[tci][:, oc * 512:(oc + 1) * 512], PS[2 + tci][:],
                     wb[4][:, oc * 512:(oc + 1) * 512], OP.mult)
            vp.inc(V.last)
        for tci in range(4):
            _ln_write(c, tcs[tci], fmB[tci][:])


def _mla(c):
    Tn, V, S, G, Y = c.T, c.V, c.S, c.G, c.Y
    tp, vp, sp, ld = c.tp, c.vp, c.sp, c.ld
    PS, PT = c.PS, c.PT
    fmA, fmB, f32a, wb = c.fmA, c.fmB, c.f32a, c.wb
    ring_rel = {i: tp.n for i in range(5)}

    Y.wait(vp, vp.n)
    Y.wait(sp, sp.n)
    Y.wait(tp, tp.n)
    ld.inc(Y.dma(c.bcols[:, 16:20], c.kvd_b), 16)
    ld.inc(Y.dma(c.bcols[:, 20:28], c.qd_b), 16)
    ld.inc(Y.dma(c.bcols[:, 28:44], c.kup_b), 16)
    ld.inc(Y.dma(c.bcols[:, 44:60], c.vup_b), 16)
    ld.inc(Y.dma(c.bcols[0:64, 60:76], c.kr_b), 16)
    ld.inc(Y.dma(c.bcols[:, 76:92], c.qup_b), 16)
    ld.inc(Y.dma(c.cols[0:64, 128:144], c.qr_b), 16)
    G.wait(tp, tp.n)
    G.wait(vp, vp.n)
    c.wd.inc(G.dma(c.rbf[0:1, 0:2048], c.outw_b), 16)
    bias_ld, bias_wd = ld.n, c.wd.n

    def xfm(f):
        return fmB[f // 4][:, (f % 4) * 512:(f % 4) * 512 + 512]

    def kvc(m):
        return fmA[m]

    def tin(xs):
        pairs = []
        for ci in range(4):
            tc = xs * 4 + ci
            for f in range(16):
                pairs.append((c.res[tc][:, f * 128:(f + 1) * 128],
                              fmB[f // 4][:, (f % 4) * 512 + ci * 128:(f % 4) * 512 + ci * 128 + 128]))
        _transpose_in(c, pairs)
        Tn.wait(vp, vp.n)
        Tn.wait(sp, sp.n)

    Tn.wait(ld, bias_ld)
    V.wait(ld, bias_ld)
    V.wait(c.wd, bias_wd)
    for xs in range(4):
        tin(xs)
        for fg in range(4):
            slot = wb[fg % 2]
            G.wait(tp, ring_rel.get(fg % 2, 0))
            ws = c.wsem[fg % 2]
            iw = ws.inc(G.dma(slot[:], c.kvd_t[fg]), 16)
            Tn.wait(ws, iw)
            if fg == 0:
                Tn.wait(vp, vp.n)
            for fi in range(4):
                f = fg * 4 + fi
                for m in range(4):
                    Tn.mm(PS[2 + m][:], slot[:, fi * 512 + m * 128:fi * 512 + m * 128 + 128],
                          xfm(f), start=(f == 0), stop=(f == 15))
            ring_rel[fg % 2] = tp.n + 1
            tp.inc(Tn.last)
        itg = tp.n
        V.wait(tp, itg)
        for m in range(4):
            V.ts(kvc(m)[:, xs * 512:(xs + 1) * 512], PS[2 + m][:], c.bcols[:, 16 + m:17 + m], None, OP.add)
        vp.inc(V.last)

    for qsl in range(2):
        for xs2 in range(2):
            xs = qsl * 2 + xs2
            tin(xs)
            for mg in range(2):
                for fg in range(4):
                    slot = wb[fg % 2]
                    G.wait(tp, ring_rel.get(fg % 2, 0))
                    ws = c.wsem[fg % 2]
                    iw = ws.inc(G.dma(slot[:], c.qd_t[mg, fg]), 16)
                    Tn.wait(ws, iw)
                    if fg == 0:
                        Tn.wait(vp, vp.n)
                    for fi in range(4):
                        f = fg * 4 + fi
                        for m4 in range(4):
                            Tn.mm(PS[2 + m4][:], slot[:, fi * 512 + m4 * 128:fi * 512 + m4 * 128 + 128],
                                  xfm(f), start=(f == 0), stop=(f == 15))
                    ring_rel[fg % 2] = tp.n + 1
                    tp.inc(Tn.last)
                itg = tp.n
                V.wait(tp, itg)
                for m4 in range(4):
                    m = mg * 4 + m4
                    dst = fmA[4 + m // 2][:, (m % 2) * 1024 + xs2 * 512:(m % 2) * 1024 + xs2 * 512 + 512]
                    V.ts(dst, PS[2 + m4][:], c.bcols[:, 20 + m:21 + m], None, OP.add)
                vp.inc(V.last)

        for h in range(16):
            ws = c.wsem[4]
            G.wait(tp, ring_rel.get(4, 0))
            ws.inc(G.dma(wb[4][:, 0:512], c.kup_t[h]), 16)
            ws.inc(G.dma(wb[4][:, 512:1024], c.vup_t[h]), 16)
            ws.inc(G.dma(wb[4][:, 1024:1280], c.kr_t[h]), 16)
            ws.inc(G.dma(wb[5][:, 0:1024], c.qup_t[h]), 16)
            iw = ws.inc(G.dma(wb[5][:, 1024:1536], c.qr_t[h]), 16)
            Tn.wait(ws, iw)
            Tn.wait(vp, vp.n)
            Tn.wait(sp, sp.n)
            ev = {}
            for tkc in range(4):
                p = PS[2 + tkc % 2][:]
                if tkc >= 2:
                    Tn.wait(vp, ev[tkc - 2])
                for kf in range(4):
                    Tn.mm(p, wb[4][:, kf * 128:(kf + 1) * 128], kvc(kf)[:, tkc * 512:(tkc + 1) * 512],
                          start=(kf == 0), stop=(kf == 3))
                it = tp.inc(Tn.last)
                V.wait(tp, it)
                V.ts(fmB[0][:, tkc * 512:(tkc + 1) * 512], p, c.bcols[:, 28 + h:29 + h], None, OP.add)
                ev[tkc] = vp.inc(V.last)
            for tkc in range(4):
                p = PS[2 + tkc % 2][0:64, :]
                if tkc >= 2:
                    Tn.wait(vp, ev[100 + tkc - 2])
                else:
                    Tn.wait(vp, ev[2 + tkc])
                for kf in range(4):
                    Tn.mm(p, wb[4][:, 1024 + kf * 64:1024 + (kf + 1) * 64],
                          kvc(kf)[:, tkc * 512:(tkc + 1) * 512], start=(kf == 0), stop=(kf == 3))
                it = tp.inc(Tn.last)
                V.wait(tp, it)
                V.ts(fmB[1][0:64, tkc * 512:(tkc + 1) * 512], p, c.bcols[0:64, 60 + h:61 + h], None, OP.add)
                ev[100 + tkc] = vp.inc(V.last)
            for tk16 in range(16):
                p = PS[4][:, 0:128]
                if tk16 >= 1:
                    Tn.wait(vp, ev[200 + tk16 - 1])
                for kf in range(4):
                    Tn.mm(p, kvc(kf)[:, tk16 * 128:(tk16 + 1) * 128],
                          wb[4][:, 512 + kf * 128:512 + (kf + 1) * 128], start=(kf == 0), stop=(kf == 3))
                it = tp.inc(Tn.last)
                V.wait(tp, it)
                V.ts(fmB[2][:, tk16 * 128:(tk16 + 1) * 128], p, c.bcols[:, 44 + h:45 + h], None, OP.add)
                ev[200 + tk16] = vp.inc(V.last)
            for qn in range(2):
                p = PS[2 + qn][:]
                Tn.wait(vp, ev[100 + 2 + qn])
                for qf in range(8):
                    rhs = fmA[4 + qf // 2][:, (qf % 2) * 1024 + qn * 512:(qf % 2) * 1024 + qn * 512 + 512]
                    Tn.mm(p, wb[5][:, qf * 128:(qf + 1) * 128], rhs, start=(qf == 0), stop=(qf == 7))
                it = tp.inc(Tn.last)
                V.wait(tp, it)
                V.ts(fmB[3][:, qn * 512:(qn + 1) * 512], p, c.bcols[:, 76 + h:77 + h], None, OP.add)
                ev[300 + qn] = vp.inc(V.last)
            for qn in range(2):
                p = PS[2 + qn][0:64, :]
                Tn.wait(vp, ev[300 + qn])
                for qf in range(8):
                    rhs = fmA[4 + qf // 2][:, (qf % 2) * 1024 + qn * 512:(qf % 2) * 1024 + qn * 512 + 512]
                    Tn.mm(p, wb[5][:, 1024 + qf * 64:1024 + (qf + 1) * 64], rhs, start=(qf == 0), stop=(qf == 7))
                it = tp.inc(Tn.last)
                V.wait(tp, it)
                V.ts(fmB[3][0:64, 1024 + qn * 512:1024 + (qn + 1) * 512], p,
                     c.cols[0:64, 128 + h:129 + h], None, OP.add)
                ev[400 + qn] = vp.inc(V.last)
            ring_rel[4] = tp.n

            Tn.wait(vp, vp.n)
            for kc4 in range(4):
                p = PS[2 + kc4][:]
                Tn.mm(p, fmB[3][:, 0:128], fmB[0][:, kc4 * 512:(kc4 + 1) * 512], start=True, stop=False)
                Tn.mm(p, fmB[3][0:64, 1024:1152], fmB[1][0:64, kc4 * 512:(kc4 + 1) * 512],
                      start=False, stop=True)
            it = tp.inc(Tn.last)
            V.wait(tp, it)
            for kc4 in range(4):
                V.red(c.stat[:, 20 + kc4:21 + kc4], PS[2 + kc4][:], OP.max)
            V.tt(c.stat[:, 20:21], c.stat[:, 20:21], c.stat[:, 21:22], OP.max)
            V.tt(c.stat[:, 22:23], c.stat[:, 22:23], c.stat[:, 23:24], OP.max)
            V.tt(c.stat[:, 20:21], c.stat[:, 20:21], c.stat[:, 22:23], OP.max)
            V.copy(wb[5][:, 1536:1537], c.stat[:, 20:21])
            ivm = vp.inc(V.last)
            Tn.wait(vp, ivm)
            Tn.trans(PT[0][0:1, 0:128], wb[5][:, 1536:1537], c.sid[:])
            it = tp.inc(Tn.last)
            V.wait(tp, it)
            V.red(c.stat[0:1, 24:25], PT[0][0:1, 0:128], OP.max)
            ivm = vp.inc(V.last)
            Tn.wait(vp, ivm)
            Tn.mm(PS[5][:, 0:1], c.sonerf[:], c.stat[0:1, 24:25], start=True, stop=True)
            it = tp.inc(Tn.last)
            V.wait(tp, it)
            V.ts(c.cols[:, 144:145], PS[5][:, 0:1], -MLA_SCALE, None, OP.mult)
            ebias_vp = vp.inc(V.last)
            S.wait(vp, ebias_vp)

            for qn in range(2):
                pexp = {}
                Tn.wait(vp, vp.n)
                for kc in range(16):
                    STp = PS[0 + kc % 2][:]
                    if kc >= 2:
                        Tn.wait(sp, pexp[kc - 2])
                    Tn.mm(STp, fmB[0][:, kc * 128:(kc + 1) * 128], fmB[3][:, qn * 512:(qn + 1) * 512],
                          start=True, stop=False)
                    Tn.mm(STp, fmB[1][0:64, kc * 128:(kc + 1) * 128],
                          fmB[3][0:64, 1024 + qn * 512:1024 + (qn + 1) * 512], start=False, stop=True)
                    it = tp.inc(Tn.last)
                    S.wait(tp, it)
                    pslot = wb[0][:, (kc % 2) * 512:(kc % 2) * 512 + 512]
                    S.act(pslot, STp, AF.Exp, scale=MLA_SCALE, bias=c.cols[:, 144:145])
                    pexp[kc] = sp.inc(S.last)
                    Tn.wait(sp, pexp[kc])
                    Tn.mm(PS[2][:], fmB[2][:, kc * 128:(kc + 1) * 128], pslot, start=(kc == 0), stop=(kc == 15))
                    Tn.mm(PS[3][0:1, :], c.sonec[:], pslot, start=(kc == 0), stop=(kc == 15))
                    tp.inc(Tn.last)
                it = tp.n
                V.wait(tp, it)
                V.recip(c.rows[0:1, 0:512], PS[3][0:1, :])
                ivr = vp.inc(V.last)
                Tn.wait(vp, ivr)
                Tn.mm(PS[4][:], c.sonerf[:], c.rows[0:1, 0:512], start=True, stop=True)
                it = tp.inc(Tn.last)
                V.wait(tp, it)
                V.copy(f32a[0][:, 0:512], PS[4][:])
                V.tt(fmA[8 + h // 2][:, (h % 2) * 1024 + qn * 512:(h % 2) * 1024 + qn * 512 + 512],
                     PS[2][:], f32a[0][:, 0:512], OP.mult)
                vp.inc(V.last)

        for tcg in range(2):
            for oc in range(4):
                for hg in range(4):
                    slot = wb[2 + hg % 2]
                    G.wait(tp, ring_rel.get(2 + hg % 2, 0))
                    ws = c.wsem[2 + hg % 2]
                    iw = ws.inc(G.dma(slot[:], c.outw_t[oc, hg]), 16)
                    Tn.wait(ws, iw)
                    if hg == 0:
                        Tn.wait(vp, vp.n)
                        Tn.wait(sp, sp.n)
                    for hi in range(4):
                        h = hg * 4 + hi
                        for tci in range(4):
                            tq = tcg * 4 + tci
                            lhs = fmA[8 + h // 2][:, (h % 2) * 1024 + tq * 128:(h % 2) * 1024 + tq * 128 + 128]
                            Tn.mm(PS[2 + tci][:], lhs, slot[:, hi * 512:(hi + 1) * 512],
                                  start=(h == 0), stop=False)
                    ring_rel[2 + hg % 2] = tp.n + 1
                    tp.inc(Tn.last)
                for tci in range(4):
                    Tn.mm(PS[2 + tci][:], c.soner[0:1, 0:128], c.rbf[0:1, oc * 512:(oc + 1) * 512],
                          start=False, stop=True)
                itg = tp.inc(Tn.last)
                V.wait(tp, itg)
                for tci in range(4):
                    V.copy(fmB[tci][:, oc * 512:(oc + 1) * 512], PS[2 + tci][:])
                vp.inc(V.last)
            for tci in range(4):
                tc = qsl * 8 + tcg * 4 + tci
                _ln_write(c, tc, fmB[tci][:])


def _ffn1(c):
    Tn, V, S, G, Y = c.T, c.V, c.S, c.G, c.Y
    tp, vp, sp, gp, ld = c.tp, c.vp, c.sp, c.gp, c.ld
    PS, PT = c.PS, c.PT
    fmA, fmB, f32a, wb = c.fmA, c.fmB, c.f32a, c.wb
    ring_rel = {i: tp.n for i in range(5)}
    G.wait(tp, tp.n)
    G.wait(vp, vp.n)
    G.wait(sp, sp.n)

    def fsl(i, j, w=512):
        return fmA[i][:, j * w:(j + 1) * w]

    for jc in range(16):
        ws = c.wsem[4]
        G.wait(gp, ring_rel.get("q1", 0))
        for f in range(16):
            ws.inc(G.dma(fsl(f // 4, f % 4), c.w1_t[jc, f]), 16)
        iw = ws.n
        S.wait(c.wsem[4], iw)
        Tn.wait(c.wsem[4], iw)
        mmv = {}
        for f in range(16):
            if f >= 4:
                S.wait(tp, mmv[f - 4])
            S.act(wb[4][:, (f % 4) * 512:(f % 4) * 512 + 512], fsl(f // 4, f % 4), AF.Abs, scale=2.0)
            isp = sp.inc(S.last)
            Tn.wait(sp, isp)
            Tn.mm(PS[0][0:1, :], c.sonec[:], wb[4][:, (f % 4) * 512:(f % 4) * 512 + 512],
                  start=(f == 0), stop=(f == 15))
            mmv[f] = tp.inc(Tn.last)
        it = tp.n
        V.wait(tp, it)
        V.ts(c.rows[0:1, 0:512], PS[0][0:1, :], 1.0 / 4096.0, 1e-5, OP.mult, OP.max)
        ivr = vp.inc(V.last)
        Tn.wait(vp, ivr)
        Tn.mm(PS[1][:], c.sonerf[:], c.rows[0:1, 0:512], start=True, stop=True)
        it = tp.inc(Tn.last)
        V.wait(tp, it)
        V.copy(f32a[0][:, 0:512], PS[1][:])
        vp.inc(V.last)
        for f in range(16):
            ab = wb[4][:, (f % 4) * 512:(f % 4) * 512 + 512]
            S.wait(vp, vp.n)
            S.act(ab, fsl(f // 4, f % 4), AF.Abs, scale=2.0)
            S.act(wb[5][:, (f % 4) * 512:(f % 4) * 512 + 512], fsl(f // 4, f % 4), AF.Sign)
            isp = sp.inc(S.last)
            V.wait(sp, isp)
            V.tt(ab, ab, f32a[0][:, 0:512], OP.is_ge)
            V.tt(ab, ab, wb[5][:, (f % 4) * 512:(f % 4) * 512 + 512], OP.mult)
            V.tt(fsl(f // 4, f % 4), ab, f32a[0][:, 0:512], OP.mult)
            ivq = vp.inc(V.last)
            Y.wait(vp, ivq)
            gp.inc(Y.dma(c.w1q_hbm[jc, f], fsl(f // 4, f % 4)), 16)
        ring_rel["q1"] = gp.n
    for osr in range(4):
        ws = c.wsem[4]
        G.wait(gp, gp.n)
        for j in range(64):
            ws.inc(G.dma(fsl(j // 4, j % 4), c.w2_t[osr, j]), 16)
        iw = ws.n
        S.wait(c.wsem[4], iw)
        Tn.wait(c.wsem[4], iw)
        mmv = {}
        for j in range(64):
            if j >= 4:
                S.wait(tp, mmv[j - 4])
            S.act(wb[4][:, (j % 4) * 512:(j % 4) * 512 + 512], fsl(j // 4, j % 4), AF.Abs, scale=2.0)
            isp = sp.inc(S.last)
            Tn.wait(sp, isp)
            Tn.mm(PS[0][0:1, :], c.sonec[:], wb[4][:, (j % 4) * 512:(j % 4) * 512 + 512],
                  start=(j == 0), stop=(j == 63))
            mmv[j] = tp.inc(Tn.last)
        it = tp.n
        V.wait(tp, it)
        V.ts(c.rows[0:1, 0:512], PS[0][0:1, :], 1.0 / 16384.0, 1e-5, OP.mult, OP.max)
        ivr = vp.inc(V.last)
        Tn.wait(vp, ivr)
        Tn.mm(PS[1][:], c.sonerf[:], c.rows[0:1, 0:512], start=True, stop=True)
        it = tp.inc(Tn.last)
        V.wait(tp, it)
        V.copy(f32a[0][:, 0:512], PS[1][:])
        vp.inc(V.last)
        for j in range(64):
            ab = wb[4][:, (j % 4) * 512:(j % 4) * 512 + 512]
            S.wait(vp, vp.n)
            S.act(ab, fsl(j // 4, j % 4), AF.Abs, scale=2.0)
            S.act(wb[5][:, (j % 4) * 512:(j % 4) * 512 + 512], fsl(j // 4, j % 4), AF.Sign)
            isp = sp.inc(S.last)
            V.wait(sp, isp)
            V.tt(ab, ab, f32a[0][:, 0:512], OP.is_ge)
            V.tt(ab, ab, wb[5][:, (j % 4) * 512:(j % 4) * 512 + 512], OP.mult)
            V.tt(fsl(j // 4, j % 4), ab, f32a[0][:, 0:512], OP.mult)
            ivq = vp.inc(V.last)
            Y.wait(vp, ivq)
            gp.inc(Y.dma(c.w2q_hbm[osr, j], fsl(j // 4, j % 4)), 16)
    stash_done = gp.n
    G.wait(gp, stash_done)

    perm = {}
    for tc in range(4):
        for jb in range(64):
            s_id = (tc * 4 + jb // 16) * 16 + (jb % 16)
            d_id = (jb // 4) * 16 + (jb % 4) * 4 + tc
            perm[s_id] = d_id
    cycles = []
    seen = set()
    for s0 in range(256):
        if s0 in seen:
            continue
        cyc = [s0]
        seen.add(s0)
        nxt = perm[s0]
        while nxt != s0:
            cyc.append(nxt)
            seen.add(nxt)
            nxt = perm[nxt]
        cycles.append(cyc)

    def slot_ap(sid):
        return fmA[sid // 16][:, (sid % 16) * 128:(sid % 16) * 128 + 128]

    V.wait(gp, stash_done)
    for sb in range(NSLAB):
        tcs = [sb * 4 + i for i in range(4)]
        pairs = []
        for ci, tc in enumerate(tcs):
            V.wait(tp, tp.n)
            V.red(c.stat[:, 30:31], c.res[tc][:], OP.max, absval=True)
            V.ts(c.stat[:, 31:32], c.stat[:, 30:31], 1e-5, None, OP.max)
            V.recip(c.stat[:, 32:33], c.stat[:, 31:32])
            V.ts(c.cols[:, 32 + ci:33 + ci], c.stat[:, 32:33], 127.0, None, OP.mult)
            V.ts(c.cols[:, 36 + ci:37 + ci], c.stat[:, 31:32], 1.0 / 127.0, None, OP.mult)
            sx = c.cols[:, 32 + ci:33 + ci]
            for half in range(2):
                hs = slice(half * 1024, half * 1024 + 1024)
                v_ = f32a[0][:, 0:1024]
                V.ts(v_, c.res[tc][:, hs], sx, None, OP.mult)
                V.ts(f32a[0][:, 1024:2048], v_, M52, C52_32, OP.add, OP.subtract)
                V.ts(f32a[0][:, 1024:2048], f32a[0][:, 1024:2048], M32, None, OP.subtract)
                V.ts(f32a[1][:, 0:1024], v_, M23, M23, OP.add, OP.subtract)
                V.ts(f32a[1][:, 1024:2048], v_, M32, M32, OP.add, OP.subtract)
                V.tt(f32a[1][:, 0:1024], f32a[1][:, 0:1024], f32a[1][:, 1024:2048], OP.subtract)
                V.tt(fmA[15][:, hs], f32a[0][:, 1024:2048], f32a[1][:, 0:1024], OP.add)
            iv = vp.inc(V.last)
            for f in range(16):
                pairs.append((fmA[15][:, f * 128:(f + 1) * 128],
                              fmB[f // 4][:, (f % 4) * 512 + ci * 128:(f % 4) * 512 + ci * 128 + 128]))
            _transpose_in(c, pairs)
            pairs = []
        xq_ready = vp.n

        for jc in range(16):
            for g2 in range(4):
                slot = wb[g2 % 2]
                G.wait(tp, ring_rel.get(g2 % 2, 0))
                ws = c.wsem[g2 % 2]
                for fi4 in range(4):
                    ws.inc(G.dma(slot[:, fi4 * 512:(fi4 + 1) * 512], c.w1q_hbm[jc, g2 * 4 + fi4]), 16)
                iw = ws.n
                Tn.wait(ws, iw)
                if g2 == 0:
                    Tn.wait(vp, max(xq_ready, vp.n))
                    Tn.wait(sp, sp.n)
                for fi in range(4):
                    f = g2 * 4 + fi
                    for tci in range(4):
                        Tn.mm(PS[2 + tci][:], fmB[f // 4][:, (f % 4) * 512 + tci * 128:(f % 4) * 512 + tci * 128 + 128],
                              slot[:, fi * 512:(fi + 1) * 512], start=(f == 0), stop=(f == 15))
                ring_rel[g2 % 2] = tp.n + 1
                tp.inc(Tn.last)
            itg = tp.n
            S.wait(tp, itg)
            for tci in range(4):
                S.act(fmA[tci * 4 + jc // 4][:, (jc % 4) * 512:(jc % 4) * 512 + 512], PS[2 + tci][:],
                      AF.Gelu_apprx_tanh, scale=c.cols[:, 36 + tci:37 + tci])
            sp.inc(S.last)
        h_done = sp.n

        V.wait(sp, h_done)
        for tci in range(4):
            for q in range(4):
                V.red(c.stat[:, 40 + q:41 + q], fmA[tci * 4 + q][:], OP.max, absval=True)
            V.tt(c.stat[:, 40:41], c.stat[:, 40:41], c.stat[:, 41:42], OP.max)
            V.tt(c.stat[:, 42:43], c.stat[:, 42:43], c.stat[:, 43:44], OP.max)
            V.tt(c.stat[:, 40:41], c.stat[:, 40:41], c.stat[:, 42:43], OP.max)
            V.ts(c.stat[:, 44:45], c.stat[:, 40:41], 1e-5, None, OP.max)
            V.recip(c.stat[:, 45:46], c.stat[:, 44:45])
            V.ts(c.cols[:, 40 + tci:41 + tci], c.stat[:, 45:46], 127.0, None, OP.mult)
            V.ts(c.cols[:, 44 + tci:45 + tci], c.stat[:, 44:45], 1.0 / 127.0, None, OP.mult)
            sh = c.cols[:, 40 + tci:41 + tci]
            for q in range(4):
                for half in range(2):
                    hh = slice(half * 1024, half * 1024 + 1024)
                    v_ = f32a[0][:, 0:1024]
                    V.ts(v_, fmA[tci * 4 + q][:, hh], sh, None, OP.mult)
                    V.ts(f32a[0][:, 1024:2048], v_, M52, C52_32, OP.add, OP.subtract)
                    V.ts(f32a[0][:, 1024:2048], f32a[0][:, 1024:2048], M32, None, OP.subtract)
                    V.ts(f32a[1][:, 0:1024], v_, M23, M23, OP.add, OP.subtract)
                    V.ts(f32a[1][:, 1024:2048], v_, M32, M32, OP.add, OP.subtract)
                    V.tt(f32a[1][:, 0:1024], f32a[1][:, 0:1024], f32a[1][:, 1024:2048], OP.subtract)
                    V.tt(fmA[tci * 4 + q][:, hh], f32a[0][:, 1024:2048], f32a[1][:, 0:1024], OP.add)
        nh_done = vp.inc(V.last)

        Tn.wait(vp, nh_done)
        Tn.wait(sp, sp.n)
        lagged = []
        hist = {}
        n = 0
        vp0 = vp.n
        for cyc in cycles:
            for i in range(len(cyc)):
                s_id = cyc[i]
                pt = PT[n % 2][:, ((n // 2) % 4) * 256:((n // 2) % 4) * 256 + 128]
                Tn.wait(vp, hist[n - 2] if n >= 2 else vp0)
                Tn.trans(pt, slot_ap(s_id), c.sid[:])
                it = tp.inc(Tn.last)
                lagged.append((it, pt, perm[s_id]))
                if len(lagged) > 1:
                    it2, pt2, d2 = lagged.pop(0)
                    V.wait(tp, max(it2, it))
                    V.copy(slot_ap(d2), pt2)
                    hist[n - 1] = vp.inc(V.last)
                n += 1
            it2, pt2, d2 = lagged.pop(0)
            V.wait(tp, it2)
            V.copy(slot_ap(d2), pt2)
            hist[n - 1] = vp.inc(V.last)
        hq_ready = vp.n

        for oc in range(4):
            for jg in range(16):
                slot = wb[2 + jg % 2]
                G.wait(tp, ring_rel.get(2 + jg % 2, 0))
                ws = c.wsem[2 + jg % 2]
                for ji4 in range(4):
                    ws.inc(G.dma(slot[:, ji4 * 512:(ji4 + 1) * 512], c.w2q_hbm[oc, jg * 4 + ji4]), 16)
                iw = ws.n
                Tn.wait(ws, iw)
                if jg == 0:
                    Tn.wait(vp, max(hq_ready, vp.n))
                    Tn.wait(sp, sp.n)
                for ji in range(4):
                    j = jg * 4 + ji
                    for tci in range(4):
                        lhs = fmA[j // 4][:, (j % 4) * 512 + tci * 128:(j % 4) * 512 + tci * 128 + 128]
                        Tn.mm(PS[2 + tci][:], lhs, slot[:, ji * 512:(ji + 1) * 512],
                              start=(j == 0), stop=(j == 63))
                ring_rel[2 + jg % 2] = tp.n + 1
                tp.inc(Tn.last)
            itg = tp.n
            V.wait(tp, itg)
            for tci in range(4):
                V.ts(fmB[tci][:, oc * 512:(oc + 1) * 512], PS[2 + tci][:],
                     c.cols[:, 44 + tci:45 + tci], None, OP.mult)
            vp.inc(V.last)
        for tci in range(4):
            _ln_write(c, tcs[tci], fmB[tci][:])



def _bf(x):
    return np.ascontiguousarray(x, dtype=ml_dtypes.bfloat16)


def _f32(x):
    return np.ascontiguousarray(x, dtype=np.float32)


def _tile_fm(wT, groups):
    raise NotImplementedError


def _prep_gla_qk(wT):
    out = np.empty((2, 4, 128, 2048), np.float32)
    for mg in range(2):
        for fg in range(4):
            for fi in range(4):
                f = fg * 4 + fi
                blk = wT[f * 128:(f + 1) * 128, mg * 512:(mg + 1) * 512]
                out[mg, fg, :, fi * 512:(fi + 1) * 512] = blk
    return out


def _prep_gla_vgo(wT):
    out = np.empty((4, 4, 128, 2048), np.float32)
    for oc in range(4):
        for fg in range(4):
            for fi in range(4):
                f = fg * 4 + fi
                out[oc, fg, :, fi * 512:(fi + 1) * 512] = wT[f * 128:(f + 1) * 128, oc * 512:(oc + 1) * 512]
    return out


def prep_inputs(inputs):
    g = {}
    g["c_ident"] = np.eye(128, dtype=ml_dtypes.bfloat16)
    g["c_mask"] = _bf(np.tril(np.ones((128, 128), np.float32)).T)
    H = np.array([[1.0]], np.float32)
    while H.shape[0] < 128:
        H = np.block([[H, H], [H, -H]])
    g["c_h128"] = _bf(H)
    g["c_onesrow"] = np.ones((1, 128), ml_dtypes.bfloat16)
    g["c_onescol"] = np.ones((128, 1), ml_dtypes.bfloat16)
    g["c_onesrowf"] = np.ones((1, 128), np.float32)

    wqT = _f32(inputs["L0_attn_Wq"].T)
    wkT = _f32(inputs["L0_attn_Wk"].T)
    g["gla_qk_t"] = np.stack([_prep_gla_qk(wqT), _prep_gla_qk(wkT)])
    wvT = _f32(inputs["L0_attn_Wv"].T)
    wgT = _f32(inputs["L0_attn_Wg"].T)
    woT = _f32(inputs["L0_attn_Wo"].T)
    g["gla_vgo_t"] = np.stack([_prep_gla_vgo(wvT), _prep_gla_vgo(wgT), _prep_gla_vgo(woT)])
    wkg1T = _f32(inputs["L0_attn_Wkg1"].T)
    g["wkg1_t"] = _f32(wkg1T.reshape(16, 128, 16).transpose(1, 0, 2).reshape(128, 256))
    g["wkg2T"] = _f32(inputs["L0_attn_Wkg2"].T)
    g["bkg2_cols"] = _f32(inputs["L0_attn_bkg2"].reshape(8, 128).T)
    g["bg"] = _f32(inputs["L0_attn_bg"].reshape(1, -1))

    wupT = _f32(inputs["L0_ffn_W_up"].T)
    wgateT = _f32(inputs["L0_ffn_W_gate"].T)
    wdownT = _f32(inputs["L0_ffn_W_down"].T)
    ug = np.empty((2, 4, 4, 4, 128, 2048), np.float32)
    for wi, w in enumerate([wupT, wgateT]):
        for jc in range(4):
            for jg in range(4):
                for fg in range(4):
                    for fi in range(4):
                        f = fg * 4 + fi
                        ug[wi, jc, jg, fg, :, fi * 512:(fi + 1) * 512] = \
                            w[f * 128:(f + 1) * 128, jc * 2048 + jg * 512:jc * 2048 + (jg + 1) * 512]
    g["ug_t"] = ug
    dn = np.empty((4, 16, 128, 2048), np.float32)
    for oc in range(4):
        for jg in range(16):
            for ji in range(4):
                j = jg * 4 + ji
                dn[oc, jg, :, ji * 512:(ji + 1) * 512] = \
                    wdownT[j * 128:(j + 1) * 128, oc * 512:(oc + 1) * 512]
    g["dn_t"] = dn
    g["su_ug_cols"] = _f32(inputs["L0_ffn_SU_ug"].reshape(16, 128).T)
    g["su_dp_cols"] = _f32(inputs["L0_ffn_SU_dp"].reshape(64, 128).T)
    wsc = _f32(inputs["L0_ffn_Wscale_ug"])
    g["wsc_up_cols"] = _f32(wsc[:INTER].reshape(64, 128).T)
    g["wsc_gate_cols"] = _f32(wsc[INTER:].reshape(64, 128).T)
    g["wsc_dp_row"] = _f32(inputs["L0_ffn_Wscale_dp"].reshape(1, -1))

    kvdT = _f32(inputs["L1_attn_kv_down_w"].T)
    kvd = np.empty((4, 128, 2048), np.float32)
    for fg in range(4):
        for fi in range(4):
            f = fg * 4 + fi
            kvd[fg, :, fi * 512:(fi + 1) * 512] = kvdT[f * 128:(f + 1) * 128, :]
    g["kvd_t"] = kvd
    qdT = _f32(inputs["L1_attn_query_down_w"].T)
    g["qd_t"] = _prep_gla_qk(qdT)
    kupT = _f32(inputs["L1_attn_key_up_w"].T)
    vupT = _f32(inputs["L1_attn_value_up_w"].T)
    krT = _f32(inputs["L1_attn_key_rope_w"].T)
    kup = np.empty((16, 128, 512), np.float32)
    vup = np.empty((16, 128, 512), np.float32)
    krt = np.empty((16, 128, 256), np.float32)
    for h in range(16):
        for kf in range(4):
            kup[h, :, kf * 128:(kf + 1) * 128] = kupT[kf * 128:(kf + 1) * 128, h * 128:(h + 1) * 128]
            vup[h, :, kf * 128:(kf + 1) * 128] = vupT[kf * 128:(kf + 1) * 128, h * 128:(h + 1) * 128]
            krt[h, :, kf * 64:(kf + 1) * 64] = krT[kf * 128:(kf + 1) * 128, h * 64:(h + 1) * 64]
    g["kup_t"] = kup
    g["vup_t"] = vup
    g["kr_t"] = krt
    qupT = _f32(inputs["L1_attn_query_up_w"].T)
    qup = np.empty((16, 128, 1024), np.float32)
    for h in range(16):
        for qf in range(8):
            qup[h, :, qf * 128:(qf + 1) * 128] = qupT[qf * 128:(qf + 1) * 128, h * 128:(h + 1) * 128]
    g["qup_t"] = qup
    qrT = _f32(inputs["L1_attn_query_rope_w"].T)
    qr = np.empty((16, 128, 512), np.float32)
    for h in range(16):
        for qf in range(8):
            qr[h, :, qf * 64:(qf + 1) * 64] = qrT[qf * 128:(qf + 1) * 128, h * 64:(h + 1) * 64]
    g["qr_t"] = qr
    outwT = _f32(inputs["L1_attn_out_w"].T)
    ow = np.empty((4, 4, 128, 2048), np.float32)
    for oc in range(4):
        for hg in range(4):
            for hi in range(4):
                h = hg * 4 + hi
                ow[oc, hg, :, hi * 512:(hi + 1) * 512] = \
                    outwT[h * 128:(h + 1) * 128, oc * 512:(oc + 1) * 512]
    g["outw_t"] = ow
    for nm, key in [("kvd_b", "L1_attn_kv_down_b"), ("kup_b", "L1_attn_key_up_b"),
                    ("vup_b", "L1_attn_value_up_b"),
                    ("qd_b", "L1_attn_query_down_b"), ("qup_b", "L1_attn_query_up_b")]:
        g[nm] = _f32(inputs[key].reshape(-1, 128).T)
    g["kr_b"] = _f32(inputs["L1_attn_key_rope_b"].reshape(16, 64).T)
    g["qr_b"] = _f32(inputs["L1_attn_query_rope_b"].reshape(16, 64).T)
    g["outw_b"] = _f32(inputs["L1_attn_out_b"].reshape(1, -1))

    w1T = _f32(inputs["L1_ffn_w1"].T)
    w1t = np.empty((16, 16, 128, 512), np.float32)
    for jc in range(16):
        for f in range(16):
            w1t[jc, f] = w1T[f * 128:(f + 1) * 128, jc * 512:(jc + 1) * 512]
    g["w1_t"] = w1t
    w2T = _f32(inputs["L1_ffn_w2"].T)
    w2t = np.empty((4, 64, 128, 512), np.float32)
    for os_ in range(4):
        for j in range(64):
            w2t[os_, j] = w2T[j * 128:(j + 1) * 128, os_ * 512:(os_ + 1) * 512]
    g["w2_t"] = w2t
    return g


def kernel(**inputs):
    g = prep_inputs(inputs)
    x = np.ascontiguousarray(inputs["L0_attn_x"], dtype=np.float32)
    nc = build_program(_UPTO)
    in_maps = []
    for core in range(NCORES):
        m = dict(g)
        m["x_in"] = x[core]
        in_maps.append(m)
    res = run_bass_kernel_spmd(nc, in_maps, core_ids=list(range(NCORES)))
    out = np.stack([np.asarray(res.results[i]["out"], np.float32) for i in range(NCORES)])
    return out


# version 02
# speedup vs baseline: 1.0099x; 1.0099x over previous
import math
from contextlib import ExitStack

import numpy as np
import ml_dtypes

import concourse.bass as bass
import concourse.mybir as mybir
from concourse.bass_utils import run_bass_kernel_spmd

F32 = mybir.dt.float32
BF16 = mybir.dt.bfloat16
AF = mybir.ActivationFunctionType
OP = mybir.AluOpType
AX = mybir.AxisListType

NCORES = 8
T = 2048
D = 2048
NT = 16
SLAB = 512
NSLAB = 4
LN_EPS = 1e-5
GLA_NORM = 16.0
GLA_SCALE_LOG = math.log(128.0 ** (-0.5))
MLA_SCALE = 1.0 / math.sqrt(192.0)
INTER = 8192

M23 = 1.5 * 2.0 ** 23
M52 = 1.5 * 2.0 ** 52
M32 = 1.5 * 2.0 ** 32
C52_32 = M52 - M32

_UPTO = 4



class Eng:

    def __init__(self, e):
        self.e = e
        self.dirty = set()
        self.rd = set()
        self.waited = {}

    @staticmethod
    def _names(aps):
        out = []
        for a in aps:
            if a is None or isinstance(a, (int, float)):
                continue
            t = getattr(a, "tensor", None)
            if t is not None:
                out.append(t.name)
        return out

    def _pre(self, outs, ins):
        o = set(self._names(outs))
        i = set(self._names(ins))
        if (i & self.dirty) or (o & self.dirty) or (o & self.rd):
            self.e.drain()
            self.dirty.clear()
            self.rd.clear()
        self.dirty |= o
        self.rd |= i

    def wait(self, sem, val):
        val = int(val)
        if val <= 0 or self.waited.get(sem.name, -1) >= val:
            return
        self.e.wait_ge(sem.h, val)
        self.waited[sem.name] = val

    def _em(self, inst):
        self.last = inst
        return inst

    def dma(self, out, in_):
        self._pre([out], [in_])
        return self._em(self.e.dma_start(out=out, in_=in_))

    def copy(self, out, in_):
        self._pre([out], [in_])
        return self._em(self.e.tensor_copy(out, in_))

    def ts(self, out, in0, s1, s2, op0, op1=None):
        self._pre([out], [in0, s1, s2])
        if op1 is None:
            return self._em(self.e.tensor_scalar(out, in0, s1, None, op0))
        return self._em(self.e.tensor_scalar(out, in0, s1, s2, op0, op1))

    def tt(self, out, in0, in1, op):
        self._pre([out], [in0, in1])
        return self._em(self.e.tensor_tensor(out, in0, in1, op))

    def stt(self, out, in0, s, in1, op0, op1):
        self._pre([out], [in0, s, in1])
        return self._em(self.e.scalar_tensor_tensor(out, in0, s, in1, op0, op1))

    def scan(self, out, d0, d1, init, op0, op1):
        self._pre([out], [d0, d1])
        return self._em(self.e.tensor_tensor_scan(out, d0, d1, init, op0, op1))

    def red(self, out, in_, op, absval=False):
        self._pre([out], [in_])
        return self._em(self.e.tensor_reduce(out, in_, AX.X, op, apply_absolute_value=absval))

    def recip(self, out, in_):
        self._pre([out], [in_])
        return self._em(self.e.reciprocal(out, in_))

    def act(self, out, in_, func, bias=0.0, scale=1.0, accum_out=None):
        outs = [out] + ([accum_out] if accum_out is not None else [])
        self._pre(outs, [in_, bias, scale])
        return self._em(self.e.activation(out, in_, func, bias=bias, scale=scale, accum_out=accum_out))

    def mm(self, out, lhsT, rhs, start, stop):
        self._pre([out], [lhsT, rhs])
        return self._em(self.e.matmul(out, lhsT, rhs, start=start, stop=stop))

    def trans(self, out, in_, ident):
        self._pre([out], [in_, ident])
        return self._em(self.e.transpose(out, in_, ident))

    def memset(self, ap, v):
        self._pre([ap], [])
        return self._em(self.e.memset(ap, v))

    def nop(self):
        return self.e.nop()


class Sem:
    def __init__(self, h, name):
        self.h = h
        self.name = name
        self.n = 0

    def inc(self, inst, by=1):
        inst.then_inc(self.h, by)
        self.n += by
        return self.n


class Ctx:
    pass



def build_program(upto=_UPTO):
    nc = bass.Bass("TRN2", target_bir_lowering=False)
    c = Ctx()
    c.nc = nc

    def din(name, shape, dt=F32):
        return nc.dram_tensor(name, shape, dt, kind="ExternalInput").ap()

    c.x_in = din("x_in", [T, D])
    c.out_hbm = nc.dram_tensor("out", [T, D], F32, kind="ExternalOutput").ap()

    c.ident = din("c_ident", [128, 128], BF16)
    c.maskUT = din("c_mask", [128, 128], BF16)
    c.h128 = din("c_h128", [128, 128], BF16)
    c.onesrow = din("c_onesrow", [1, 128], BF16)
    c.onescol = din("c_onescol", [128, 1], BF16)
    c.onesrowf = din("c_onesrowf", [1, 128])

    c.gla_qk_t = din("gla_qk_t", [2, 2, 4, 128, 2048])
    c.gla_vgo_t = din("gla_vgo_t", [3, 4, 4, 128, 2048])
    c.wkg1_t = din("wkg1_t", [128, 256])
    c.wkg2T = din("wkg2T", [16, 1024])
    c.bkg2_cols = din("bkg2_cols", [128, 8])
    c.bg = din("bg", [1, D])

    c.ug_t = din("ug_t", [2, 4, 4, 4, 128, 2048])
    c.dn_t = din("dn_t", [4, 16, 128, 2048])
    c.su_ug_cols = din("su_ug_cols", [128, 16])
    c.wsc_up_cols = din("wsc_up_cols", [128, 64])
    c.wsc_gate_cols = din("wsc_gate_cols", [128, 64])
    c.su_dp_cols = din("su_dp_cols", [128, 64])
    c.wsc_dp_row = din("wsc_dp_row", [1, D])

    c.kvd_t = din("kvd_t", [4, 128, 2048])
    c.qd_t = din("qd_t", [2, 4, 128, 2048])
    c.kup_t = din("kup_t", [16, 128, 512])
    c.vup_t = din("vup_t", [16, 128, 512])
    c.kr_t = din("kr_t", [16, 128, 256])
    c.qup_t = din("qup_t", [16, 128, 1024])
    c.qr_t = din("qr_t", [16, 128, 512])
    c.outw_t = din("outw_t", [4, 4, 128, 2048])
    c.kvd_b = din("kvd_b", [128, 4])
    c.kup_b = din("kup_b", [128, 16])
    c.vup_b = din("vup_b", [128, 16])
    c.kr_b = din("kr_b", [64, 16])
    c.qd_b = din("qd_b", [128, 8])
    c.qup_b = din("qup_b", [128, 16])
    c.qr_b = din("qr_b", [64, 16])
    c.outw_b = din("outw_b", [1, D])

    c.w1_t = din("w1_t", [16, 16, 128, 512])
    c.w2_t = din("w2_t", [4, 64, 128, 512])
    c.w1q_hbm = nc.dram_tensor("w1q_s", [16, 16, 128, 512], BF16, kind="Internal").ap()
    c.w2q_hbm = nc.dram_tensor("w2q_s", [4, 64, 128, 512], BF16, kind="Internal").ap()

    with ExitStack() as stack:
        e = stack.enter_context
        c.block = e(nc.Block())
        c.T = Eng(nc.tensor); c.V = Eng(nc.vector); c.S = Eng(nc.scalar)
        c.G = Eng(nc.gpsimd); c.Y = Eng(nc.sync)
        for name in ["ld", "wd", "tp", "vp", "sp", "gp", "st", "ini"]:
            setattr(c, name, Sem(e(nc.semaphore(name)), name))
        c.wsem = [Sem(e(nc.semaphore(f"w{i}")), f"w{i}") for i in range(5)]

        c.res = [e(nc.sbuf_tensor(f"res{i}", [128, D], BF16)) for i in range(NT)]
        c.fmA = [e(nc.sbuf_tensor(f"fmA{i}", [128, D], BF16)) for i in range(16)]
        c.fmB = [e(nc.sbuf_tensor(f"fmB{i}", [128, D], BF16)) for i in range(4)]
        c.f32a = [e(nc.sbuf_tensor(f"f32a{i}", [128, D], F32)) for i in range(2)]
        c.wb = [e(nc.sbuf_tensor(f"wb{i}", [128, D], BF16)) for i in range(6)]
        c.Sst = e(nc.sbuf_tensor("Sst", [128, 1024], F32))
        c.Stmp = e(nc.sbuf_tensor("Stmp", [128, 256], F32))
        c.Sbf = e(nc.sbuf_tensor("Sbf", [128, 1024], BF16))
        c.stat = e(nc.sbuf_tensor("stat", [128, 64], F32))
        c.cols = e(nc.sbuf_tensor("cols", [128, 160], F32))
        c.rows = e(nc.sbuf_tensor("rows", [1, 512], F32))
        c.rbf = e(nc.sbuf_tensor("rbf", [1, 2048], BF16))
        c.bcols = e(nc.sbuf_tensor("bcols", [128, 96], F32))
        c.sid = e(nc.sbuf_tensor("sid", [128, 128], BF16))
        c.smask = e(nc.sbuf_tensor("smask", [128, 128], BF16))
        c.sh128 = e(nc.sbuf_tensor("sh128", [128, 128], BF16))
        c.soner = e(nc.sbuf_tensor("soner", [1, 128], BF16))
        c.sonec = e(nc.sbuf_tensor("sonec", [128, 1], BF16))
        c.sonerf = e(nc.sbuf_tensor("sonerf", [1, 128], F32))
        c.skg2 = e(nc.sbuf_tensor("skg2", [16, 1024], BF16))
        c.outf32 = c.f32a[1]

        c.PS = [e(nc.psum_tensor(f"PS{i}", [128, 512], F32)) for i in range(6)]
        c.PT = [e(nc.psum_tensor(f"PT{i}", [128, 1024], BF16)) for i in range(2)]
        c.dumb = c.f32a[1]
        c.bank_rel = {}

        def _relb(b, sem, val):
            c.bank_rel[b] = (sem, val)
        c.relb = _relb

        def _waitb(E, banks):
            for b in banks:
                if b in c.bank_rel:
                    sem, val = c.bank_rel[b]
                    E.wait(sem, val)
        c.waitb = _waitb

        for i, cv in enumerate([0.0, 1.0, -1.0, 0.5, LN_EPS, GLA_SCALE_LOG]):
            nc.const_aps.register(F32, cv, c.cols[:, 150 + i:151 + i])

        _init(c)
        if upto >= 1:
            _gla(c)
        if upto >= 2:
            _ffn0(c)
        if upto >= 3:
            _mla(c)
        if upto >= 4:
            _ffn1(c)
        _finish(c)

    return nc


def _init(c):
    Tn, V, S, G, Y = c.T, c.V, c.S, c.G, c.Y
    ld, ini = c.ld, c.ini
    ld.inc(Y.dma(c.sid[:], c.ident), 16)
    ld.inc(Y.dma(c.smask[:], c.maskUT), 16)
    ld.inc(Y.dma(c.sh128[:], c.h128), 16)
    ld.inc(Y.dma(c.soner[:], c.onesrow), 16)
    ld.inc(Y.dma(c.sonerf[:], c.onesrowf), 16)
    ld.inc(Y.dma(c.sonec[:], c.onescol), 16)
    for i in range(NT):
        c.wd.inc(G.dma(c.res[i][:], c.x_in[i * 128:(i + 1) * 128, :]), 16)
    for i, cv in enumerate([0.0, 1.0, -1.0, 0.5, LN_EPS, GLA_SCALE_LOG]):
        G.memset(c.cols[:, 150 + i:151 + i], cv)
    c.wd.inc(G.dma(c.skg2[:], c.wkg2T), 16)
    ld.inc(Y.dma(c.bcols[:, 8:16], c.bkg2_cols), 16)
    c.wd.inc(G.dma(c.rbf[:, 0:D], c.bg), 16)
    G.memset(c.Sst[:], 0.0)
    G.memset(c.Sbf[:], 0.0)
    ini.inc(G.last, 1)
    for E in (Tn, V, S, Y):
        E.wait(ini, 1)
        E.wait(ld, ld.n)
        E.wait(c.wd, c.wd.n)
    V.ts(c.bcols[:, 0:8], c.bcols[:, 8:16], -1.0, None, OP.mult)
    c.vp.inc(V.last)
    c.negbkg2 = c.bcols[:, 0:8]
    c.bg_row = c.rbf[:, 0:D]


def _finish(c):
    V, Y = c.V, c.Y
    vp, st = c.vp, c.st
    for tc in range(NT):
        V.wait(st, st.n)
        V.copy(c.outf32[:], c.res[tc][:])
        iv = vp.inc(V.last)
        Y.wait(vp, iv)
        st.inc(Y.dma(c.out_hbm[tc * 128:(tc + 1) * 128, :], c.outf32[:]), 16)
    Y.wait(st, st.n)



def _transpose_in(c, pairs, su_cols=None):
    Tn, V = c.T, c.V
    tp, vp = c.tp, c.vp
    hist = {}
    vp0 = vp.n
    for n, (src, dst) in enumerate(pairs):
        pt = c.PT[n % 2][:, ((n // 2) % 4) * 256:((n // 2) % 4) * 256 + 128]
        Tn.wait(vp, hist[n - 2] if n >= 2 else vp0)
        Tn.trans(pt, src, c.sid[:])
        it = tp.inc(Tn.last)
        V.wait(tp, it)
        if su_cols is not None and su_cols[n] is not None:
            V.ts(dst, pt, su_cols[n], None, OP.mult)
        else:
            V.copy(dst, pt)
        hist[n] = vp.inc(V.last)


def _ln_write(c, tc, src_tm_bf, final=False):
    V, S = c.V, c.S
    vp, sp = c.vp, c.sp
    s = c.f32a[0]
    V.tt(s[:], src_tm_bf, c.res[tc][:], OP.add)
    V.red(c.stat[:, 0:1], s[:], OP.add)
    iv = vp.inc(V.last)
    S.wait(vp, iv)
    for q in range(4):
        S.act(c.dumb[:, 0:512], s[:, q * 512:(q + 1) * 512], AF.Square, accum_out=c.stat[:, 4 + q:5 + q])
    isp = sp.inc(S.last)
    V.wait(sp, isp)
    V.ts(c.stat[:, 1:2], c.stat[:, 0:1], 1.0 / 2048.0, None, OP.mult)
    V.tt(c.stat[:, 8:9], c.stat[:, 4:5], c.stat[:, 5:6], OP.add)
    V.tt(c.stat[:, 9:10], c.stat[:, 6:7], c.stat[:, 7:8], OP.add)
    V.tt(c.stat[:, 10:11], c.stat[:, 8:9], c.stat[:, 9:10], OP.add)
    V.ts(c.stat[:, 11:12], c.stat[:, 10:11], 1.0 / 2048.0, None, OP.mult)
    V.tt(c.stat[:, 12:13], c.stat[:, 1:2], c.stat[:, 1:2], OP.mult)
    V.tt(c.stat[:, 13:14], c.stat[:, 11:12], c.stat[:, 12:13], OP.subtract)
    iv = vp.inc(V.last)
    S.wait(vp, iv)
    S.act(c.stat[:, 14:15], c.stat[:, 13:14], AF.Ln, bias=LN_EPS)
    S.act(c.stat[:, 2:3], c.stat[:, 14:15], AF.Exp, scale=-0.5)
    isp = sp.inc(S.last)
    V.wait(sp, isp)
    dst = c.outf32[:] if final else c.res[tc][:]
    V.ts(dst, s[:], c.stat[:, 1:2], c.stat[:, 2:3], OP.subtract, OP.mult)
    return vp.inc(V.last)



def _gla(c):
    Tn, V, S, G = c.T, c.V, c.S, c.G
    tp, vp, sp, wd = c.tp, c.vp, c.sp, c.wd
    PS = c.PS
    fmA, fmB, f32a, wb = c.fmA, c.fmB, c.f32a, c.wb
    ring_rel = {i: tp.n for i in range(5)}

    def xfm(f):
        return fmB[f // 4][:, (f % 4) * 512:(f % 4) * 512 + 512]

    def gpt(m):
        return f32a[m // 4][:, (m % 4) * 512:(m % 4) * 512 + 512]

    for sb in range(NSLAB):
        tcs = [sb * 4 + i for i in range(4)]

        pairs = []
        for ci in range(4):
            for f in range(16):
                pairs.append((c.res[tcs[ci]][:, f * 128:(f + 1) * 128],
                              fmB[f // 4][:, (f % 4) * 512 + ci * 128:(f % 4) * 512 + ci * 128 + 128]))
        _transpose_in(c, pairs)
        xfm_ready = vp.n

        G.wait(tp, ring_rel.get(4, 0))
        iw4 = c.wsem[4].inc(G.dma(wb[4][:, 0:256], c.wkg1_t), 16)
        Tn.wait(c.wsem[4], iw4)
        Tn.wait(vp, xfm_ready)
        c.waitb(Tn, [0])
        pk = PS[0][0:16, :]
        for f in range(16):
            Tn.mm(pk, wb[4][:, f * 16:(f + 1) * 16], xfm(f), start=(f == 0), stop=(f == 15))
        it = tp.inc(Tn.last)
        ring_rel[4] = it
        V.wait(tp, it)
        kg1 = wb[5][0:16, 0:512]
        V.copy(kg1, pk)
        ivk = vp.inc(V.last)
        Tn.wait(vp, ivk)
        sp_hist = {}
        for m in range(8):
            p = PS[m % 2][:]
            if m >= 2:
                Tn.wait(sp, sp_hist[m - 2])
            Tn.mm(p, c.skg2[:, m * 128:(m + 1) * 128], kg1, start=True, stop=True)
            it = tp.inc(Tn.last)
            S.wait(tp, it)
            S.wait(c.vp, 1)
            S.act(gpt(m), p, AF.Exp, scale=-1.0, bias=c.negbkg2[:, m:m + 1])
            S.act(gpt(m), gpt(m), AF.Ln, bias=1.0)
            sp_hist[m] = sp.inc(S.last)
            c.relb(m % 2, sp, sp_hist[m])
        V.wait(sp, sp.n)
        for m in range(8):
            for ch in range(4):
                seg = gpt(m)[:, ch * 128:(ch + 1) * 128]
                V.scan(seg, seg, seg, 0.0, OP.add, OP.bypass)
        ivG = vp.inc(V.last)
        S.wait(vp, ivG)
        for m in range(8):
            S.act(c.cols[:, 16 + m * 4:16 + m * 4 + 4], gpt(m)[:, 127::128], AF.Exp, scale=-1.0 / GLA_NORM)
        sp_a = sp.inc(S.last)

        def qk(qi, base, escale, ebias):
            for mg in range(2):
                for fg in range(4):
                    slot = wb[fg % 2]
                    G.wait(tp, ring_rel.get(fg % 2, 0))
                    ws = c.wsem[fg % 2]
                    iw = ws.inc(G.dma(slot[:], c.gla_qk_t[qi, mg, fg]), 16)
                    Tn.wait(ws, iw)
                    if fg == 0:
                        Tn.wait(vp, xfm_ready)
                        c.waitb(Tn, [2, 3, 4, 5])
                    for fi in range(4):
                        f = fg * 4 + fi
                        for m4 in range(4):
                            Tn.mm(PS[2 + m4][:], slot[:, fi * 512 + m4 * 128:fi * 512 + m4 * 128 + 128],
                                  xfm(f), start=(f == 0), stop=(f == 15))
                    ring_rel[fg % 2] = tp.n + 1
                    tp.inc(Tn.last)
                itg = tp.n
                for m4 in range(4):
                    m = mg * 4 + m4
                    et = wb[4][:, 256 + (m4 % 2) * 512:256 + (m4 % 2) * 512 + 512]
                    S.wait(vp, ivG)
                    S.wait(vp, c._et_vp.get(m4 % 2, 0))
                    S.act(et, gpt(m), AF.Exp, bias=ebias, scale=escale)
                    isp = sp.inc(S.last)
                    V.wait(tp, itg)
                    V.wait(sp, isp)
                    dst = fmA[base + m // 4][:, (m % 4) * 512:(m % 4) * 512 + 512]
                    V.tt(dst, PS[2 + m4][:], et, OP.mult)
                    c._et_vp[m4 % 2] = vp.inc(V.last)
                    c.relb(2 + m4, vp, c._et_vp[m4 % 2])

        c._et_vp = {}
        qk(0, 0, -1.0 / GLA_NORM, 0.0)
        qk(1, 2, +1.0 / GLA_NORM, GLA_SCALE_LOG)

        def tmproj(wi, base, bias_row):
            for oc in range(4):
                for fg in range(4):
                    slot = wb[2 + fg % 2]
                    G.wait(tp, ring_rel.get(2 + fg % 2, 0))
                    ws = c.wsem[2 + fg % 2]
                    iw = ws.inc(G.dma(slot[:], c.gla_vgo_t[wi, oc, fg]), 16)
                    Tn.wait(ws, iw)
                    if fg == 0:
                        Tn.wait(vp, xfm_ready)
                        c.waitb(Tn, [2, 3, 4, 5])
                    for fi in range(4):
                        f = fg * 4 + fi
                        for tci in range(4):
                            Tn.mm(PS[2 + tci][:], xfm(f)[:, tci * 128:(tci + 1) * 128],
                                  slot[:, fi * 512:(fi + 1) * 512],
                                  start=(f == 0), stop=(f == 15 and bias_row is None))
                    ring_rel[2 + fg % 2] = tp.n + 1
                    tp.inc(Tn.last)
                if bias_row is not None:
                    for tci in range(4):
                        Tn.mm(PS[2 + tci][:], c.soner[0:1, 0:128], bias_row[:, oc * 512:(oc + 1) * 512],
                              start=False, stop=True)
                    itg = tp.inc(Tn.last)
                else:
                    itg = tp.n
                V.wait(tp, itg)
                for tci in range(4):
                    V.copy(fmA[base + tci][:, oc * 512:(oc + 1) * 512], PS[2 + tci][:])
                iv = vp.inc(V.last)
                for tci in range(4):
                    c.relb(2 + tci, vp, iv)

        tmproj(0, 4, None)
        tmproj(1, 8, c.bg_row)

        v_ready = vp.n
        prev_state_vp = getattr(c, "_gla_state_vp", {})
        kv_read_vp = getattr(c, "_gla_kv_vp", 0)
        sq_hist = getattr(c, "_gla_sq", {})
        for ch in range(4):
            o_evac = {}
            for h in range(16):
                m = h // 2
                r0 = (h % 2) * 64
                qs = fmA[m // 4][r0:r0 + 64, (m % 4) * 512 + ch * 128:(m % 4) * 512 + ch * 128 + 128]
                ks = fmA[2 + m // 4][r0:r0 + 64, (m % 4) * 512 + ch * 128:(m % 4) * 512 + ch * 128 + 128]
                vs = fmA[4 + ch][:, h * 128:(h + 1) * 128]
                Sst_h = c.Sst[r0:r0 + 64, (h // 2) * 128:(h // 2) * 128 + 128]
                Stmp_h = c.Stmp[r0:r0 + 64, 0:128]
                Sbf_h = c.Sbf[r0:r0 + 64, (h // 2) * 128:(h // 2) * 128 + 128]
                a_col = c.cols[r0:r0 + 64, 16 + m * 4 + ch:16 + m * 4 + ch + 1]
                STp = PS[h % 2][:, 0:128]
                Op = PS[2 + h % 2][:, 0:128]
                KVp = PS[4][r0:r0 + 64, 0:128]
                kTp = c.PT[0][:, (h % 2) * 256:(h % 2) * 256 + 64]
                stm = wb[5][:, 1024 + (h % 2) * 128:1024 + (h % 2) * 128 + 128]
                kT_sb = wb[5][:, 1536 + (h % 2) * 64:1536 + (h % 2) * 64 + 64]

                Tn.wait(vp, v_ready)
                if h >= 2:
                    Tn.wait(vp, o_evac.get(h - 2, 0))
                Tn.wait(sp, sq_hist.get((ch, h - 2), sq_hist.get((ch - 1, h + 14), 0)))
                Tn.mm(STp, ks, qs, start=True, stop=True)
                it1 = tp.inc(Tn.last)
                V.wait(tp, it1)
                V.tt(stm, STp, c.smask[:], OP.mult)
                iv1 = vp.inc(V.last)
                Tn.wait(vp, iv1)
                Tn.wait(vp, prev_state_vp.get(h, 0))
                Tn.mm(Op, stm, vs, start=True, stop=False)
                Tn.mm(Op, qs, Sbf_h, start=False, stop=True)
                it2 = tp.inc(Tn.last)
                Tn.trans(kTp, ks, c.sid[r0:r0 + 64, r0:r0 + 64])
                it3 = tp.inc(Tn.last)
                V.wait(tp, it3)
                V.copy(kT_sb, kTp)
                iv2 = vp.inc(V.last)
                Tn.wait(vp, iv2)
                Tn.wait(vp, kv_read_vp)
                Tn.mm(KVp, kT_sb, vs, start=True, stop=True)
                it4 = tp.inc(Tn.last)
                V.wait(tp, it4)
                V.wait(sp, sp_a)
                V.tt(Stmp_h, Sst_h, KVp, OP.add)
                V.ts(Sst_h, Stmp_h, a_col, None, OP.mult)
                V.copy(Sbf_h, Sst_h)
                prev_state_vp[h] = kv_read_vp = vp.inc(V.last)
                V.wait(tp, it2)
                V.copy(fmA[12 + ch][:, h * 128:(h + 1) * 128], Op)
                V.red(c.stat[:, 32 + h:33 + h], Op, OP.add)
                o_evac[h] = vp.inc(V.last)
                S.wait(tp, it2)
                S.act(c.dumb[:, 0:128], Op, AF.Square, accum_out=c.stat[:, 48 + h:49 + h])
                sq_hist[(ch, h)] = sp.inc(S.last)
            V.wait(sp, sp.n)
            V.ts(c.stat[:, 0:16], c.stat[:, 32:48], 1.0 / 128.0, None, OP.mult)
            V.ts(c.cols[:, 48:64], c.stat[:, 48:64], 1.0 / 128.0, None, OP.mult)
            V.tt(c.cols[:, 64:80], c.stat[:, 0:16], c.stat[:, 0:16], OP.mult)
            V.tt(c.cols[:, 80:96], c.cols[:, 48:64], c.cols[:, 64:80], OP.subtract)
            ivs = vp.inc(V.last)
            S.wait(vp, ivs)
            S.act(c.cols[:, 96:112], c.cols[:, 80:96], AF.Ln, bias=LN_EPS)
            S.act(c.stat[:, 16:32], c.cols[:, 96:112], AF.Exp, scale=-0.5)
            S.act(f32a[0][:], fmA[8 + ch][:], AF.Silu)
            isp = sp.inc(S.last)
            V.wait(sp, isp)
            for h in range(16):
                V.ts(fmA[8 + ch][:, h * 128:(h + 1) * 128], fmA[12 + ch][:, h * 128:(h + 1) * 128],
                     c.stat[:, 0 + h:1 + h], c.stat[:, 16 + h:17 + h], OP.subtract, OP.mult)
            for h in range(16):
                V.tt(fmA[12 + ch][:, h * 128:(h + 1) * 128], fmA[8 + ch][:, h * 128:(h + 1) * 128],
                     f32a[0][:, h * 128:(h + 1) * 128], OP.mult)
            vp.inc(V.last)
        c._gla_state_vp = prev_state_vp
        c._gla_kv_vp = kv_read_vp
        c._gla_sq = sq_hist

        z_pairs = []
        for ci in range(4):
            for f in range(16):
                z_pairs.append((fmA[12 + ci][:, f * 128:(f + 1) * 128],
                                fmB[f // 4][:, (f % 4) * 512 + ci * 128:(f % 4) * 512 + ci * 128 + 128]))
        Tn.wait(vp, vp.n)
        _transpose_in(c, z_pairs)
        zfm_ready = vp.n

        for oc in range(4):
            for fg in range(4):
                slot = wb[fg % 2]
                G.wait(tp, ring_rel.get(fg % 2, 0))
                ws = c.wsem[fg % 2]
                iw = ws.inc(G.dma(slot[:], c.gla_vgo_t[2, oc, fg]), 16)
                Tn.wait(ws, iw)
                if fg == 0:
                    Tn.wait(vp, zfm_ready)
                    c.waitb(Tn, [2, 3, 4, 5])
                for fi in range(4):
                    f = fg * 4 + fi
                    for tci in range(4):
                        Tn.mm(PS[2 + tci][:], xfm(f)[:, tci * 128:(tci + 1) * 128],
                              slot[:, fi * 512:(fi + 1) * 512], start=(f == 0), stop=(f == 15))
                ring_rel[fg % 2] = tp.n + 1
                tp.inc(Tn.last)
            itg = tp.n
            V.wait(tp, itg)
            for tci in range(4):
                V.copy(fmA[4 + tci][:, oc * 512:(oc + 1) * 512], PS[2 + tci][:])
            c._wo_evac = vp.inc(V.last)
            for tci in range(4):
                c.relb(2 + tci, vp, c._wo_evac)
        for tci in range(4):
            _ln_write(c, tcs[tci], fmA[4 + tci][:])



FFN0_BLOCKS = None


def _ffn0(c):
    Tn, V, S, G, Y = c.T, c.V, c.S, c.G, c.Y
    tp, vp, sp, ld = c.tp, c.vp, c.sp, c.ld
    PS = c.PS
    fmA, fmB, f32a, wb = c.fmA, c.fmB, c.f32a, c.wb
    ring_rel = {i: tp.n for i in range(5)}

    def xfm(f):
        return fmB[f // 4][:, (f % 4) * 512:(f % 4) * 512 + 512]

    def zsl(jj):
        return fmA[jj // 4][:, (jj % 4) * 512:(jj % 4) * 512 + 512]

    SC1 = 2.0 ** -5.5
    SC2 = 2.0 ** -6.5

    Y.wait(vp, vp.n)
    Y.wait(sp, sp.n)
    Y.wait(tp, tp.n)
    ld.inc(Y.dma(c.cols[:, 0:16], c.su_ug_cols), 16)
    ld.inc(Y.dma(c.cols[:, 16:80], c.wsc_up_cols), 16)
    ld.inc(Y.dma(c.cols[:, 80:144], c.wsc_gate_cols), 16)
    ld.inc(Y.dma(c.stat[:, 0:64], c.su_dp_cols), 16)
    ld.inc(Y.dma(f32a[0][0:1, 0:2048], c.wsc_dp_row), 16)
    V.wait(ld, ld.n)
    V.tt(c.cols[:, 16:80], c.cols[:, 16:80], c.stat[:, 0:64], OP.mult)
    V.ts(c.cols[:, 16:80], c.cols[:, 16:80], SC1, None, OP.mult)
    V.ts(c.cols[:, 80:144], c.cols[:, 80:144], SC1, None, OP.mult)
    V.ts(c.rbf[0:1, 0:2048], f32a[0][0:1, 0:2048], SC2, None, OP.mult)
    ivc = vp.inc(V.last)
    Tn.wait(vp, ivc)
    for oc4 in range(4):
        Tn.mm(PS[2][:], c.soner[0:1, 0:128], c.rbf[0:1, oc4 * 512:(oc4 + 1) * 512], start=True, stop=True)
        itb = tp.inc(Tn.last)
        V.wait(tp, itb)
        V.copy(wb[4][:, oc4 * 512:(oc4 + 1) * 512], PS[2][:])
        ivb = vp.inc(V.last)
        Tn.wait(vp, ivb)

    for sb in range(NSLAB):
        tcs = [sb * 4 + i for i in range(4)]
        pairs, sus = [], []
        for ci in range(4):
            for f in range(16):
                pairs.append((c.res[tcs[ci]][:, f * 128:(f + 1) * 128],
                              fmB[f // 4][:, (f % 4) * 512 + ci * 128:(f % 4) * 512 + ci * 128 + 128]))
                sus.append(c.cols[:, f:f + 1])
        _transpose_in(c, pairs, su_cols=sus)
        Tn.wait(vp, vp.n)
        Tn.wait(sp, sp.n)
        ev = {}
        for f in range(16):
            p = PS[2 + f % 4][:]
            Tn.wait(vp, ev.get(f - 4, 0))
            if f < 4:
                c.waitb(Tn, [2 + f % 4])
            Tn.mm(p, c.sh128[:], xfm(f), start=True, stop=True)
            it = tp.inc(Tn.last)
            V.wait(tp, it)
            V.copy(xfm(f), p)
            ev[f] = vp.inc(V.last)
            c.relb(2 + f % 4, vp, ev[f])
        V.wait(vp, vp.n)
        for s in (1, 2, 4, 8):
            prs = [t0 for t0 in range(16) if not (t0 & s)]
            for t0 in prs:
                V.tt(xfm(t0), xfm(t0), xfm(t0 ^ s), OP.add)
            for t0 in prs:
                V.stt(xfm(t0 ^ s), xfm(t0 ^ s), -2.0, xfm(t0), OP.mult, OP.add)
        xh_ready = vp.inc(V.last)

        for jc in range(4):
            for jg in range(4):
                for pas in range(2):
                    for fg in range(4):
                        slot = wb[fg % 2]
                        G.wait(tp, ring_rel.get(fg % 2, 0))
                        ws = c.wsem[fg % 2]
                        iw = ws.inc(G.dma(slot[:], c.ug_t[pas, jc, jg, fg]), 16)
                        Tn.wait(ws, iw)
                        if fg == 0:
                            Tn.wait(vp, xh_ready)
                            c.waitb(Tn, [2, 3, 4, 5])
                        for fi in range(4):
                            f = fg * 4 + fi
                            for j2 in range(4):
                                Tn.mm(PS[2 + j2][:], slot[:, fi * 512 + j2 * 128:fi * 512 + j2 * 128 + 128],
                                      xfm(f), start=(f == 0), stop=(f == 15))
                        ring_rel[fg % 2] = tp.n + 1
                        tp.inc(Tn.last)
                    itg = tp.n
                    for j2 in range(4):
                        jj = jc * 16 + jg * 4 + j2
                        if pas == 0:
                            V.wait(tp, itg)
                            V.copy(zsl(jj), PS[2 + j2][:])
                        else:
                            S.wait(tp, itg)
                            S.act(f32a[0][:, j2 * 512:(j2 + 1) * 512], PS[2 + j2][:], AF.Silu,
                                  scale=c.cols[:, 80 + jj:81 + jj])
                            isp = sp.inc(S.last)
                            c.relb(2 + j2, sp, isp)
                            V.wait(sp, isp)
                            V.stt(zsl(jj), zsl(jj), c.cols[:, 16 + jj:17 + jj],
                                  f32a[0][:, j2 * 512:(j2 + 1) * 512], OP.mult, OP.mult)
                    iv = vp.inc(V.last)
                    if pas == 0:
                        for j2 in range(4):
                            c.relb(2 + j2, vp, iv)
            Tn.wait(vp, vp.n)
            ev2 = {}
            for jt in range(16):
                jj = jc * 16 + jt
                p = PS[2 + jt % 4][:]
                Tn.wait(vp, ev2.get(jt - 4, 0))
                if jt < 4:
                    c.waitb(Tn, [2 + jt % 4])
                Tn.mm(p, c.sh128[:], zsl(jj), start=True, stop=True)
                it = tp.inc(Tn.last)
                V.wait(tp, it)
                V.copy(zsl(jj), p)
                ev2[jt] = vp.inc(V.last)
                c.relb(2 + jt % 4, vp, ev2[jt])
            V.wait(vp, vp.n)
            for s in (1, 2, 4, 8):
                prs = [t0 for t0 in range(16) if not (t0 & s)]
                for t0 in prs:
                    V.tt(zsl(jc * 16 + t0), zsl(jc * 16 + t0), zsl(jc * 16 + (t0 ^ s)), OP.add)
                for t0 in prs:
                    V.stt(zsl(jc * 16 + (t0 ^ s)), zsl(jc * 16 + (t0 ^ s)), -2.0,
                          zsl(jc * 16 + t0), OP.mult, OP.add)
            vp.inc(V.last)
        for s in (16, 32):
            prs = [t0 for t0 in range(64) if not (t0 & s)]
            for t0 in prs:
                V.tt(zsl(t0), zsl(t0), zsl(t0 ^ s), OP.add)
            for t0 in prs:
                V.stt(zsl(t0 ^ s), zsl(t0 ^ s), -2.0, zsl(t0), OP.mult, OP.add)
        z_ready = vp.inc(V.last)

        for oc in range(4):
            for jg in range(16):
                slot = wb[2 + jg % 2]
                G.wait(tp, ring_rel.get(2 + jg % 2, 0))
                ws = c.wsem[2 + jg % 2]
                iw = ws.inc(G.dma(slot[:], c.dn_t[oc, jg]), 16)
                Tn.wait(ws, iw)
                if jg == 0:
                    Tn.wait(vp, z_ready)
                    c.waitb(Tn, [2, 3, 4, 5])
                for ji in range(4):
                    j = jg * 4 + ji
                    for tci in range(4):
                        Tn.mm(PS[2 + tci][:], zsl(j)[:, tci * 128:(tci + 1) * 128],
                              slot[:, ji * 512:(ji + 1) * 512], start=(j == 0), stop=(j == 63))
                ring_rel[2 + jg % 2] = tp.n + 1
                tp.inc(Tn.last)
            itg = tp.n
            V.wait(tp, itg)
            for tci in range(4):
                V.tt(fmB[tci][:, oc * 512:(oc + 1) * 512], PS[2 + tci][:],
                     wb[4][:, oc * 512:(oc + 1) * 512], OP.mult)
            iv = vp.inc(V.last)
            for tci in range(4):
                c.relb(2 + tci, vp, iv)
        for tci in range(4):
            _ln_write(c, tcs[tci], fmB[tci][:])


def _mla(c):
    Tn, V, S, G, Y = c.T, c.V, c.S, c.G, c.Y
    tp, vp, sp, ld = c.tp, c.vp, c.sp, c.ld
    PS, PT = c.PS, c.PT
    fmA, fmB, f32a, wb = c.fmA, c.fmB, c.f32a, c.wb
    ring_rel = {i: tp.n for i in range(5)}

    Y.wait(vp, vp.n)
    Y.wait(sp, sp.n)
    Y.wait(tp, tp.n)
    ld.inc(Y.dma(c.bcols[:, 16:20], c.kvd_b), 16)
    ld.inc(Y.dma(c.bcols[:, 20:28], c.qd_b), 16)
    ld.inc(Y.dma(c.bcols[:, 28:44], c.kup_b), 16)
    ld.inc(Y.dma(c.bcols[:, 44:60], c.vup_b), 16)
    ld.inc(Y.dma(c.bcols[0:64, 60:76], c.kr_b), 16)
    ld.inc(Y.dma(c.bcols[:, 76:92], c.qup_b), 16)
    ld.inc(Y.dma(c.cols[0:64, 128:144], c.qr_b), 16)
    G.wait(tp, tp.n)
    G.wait(vp, vp.n)
    c.wd.inc(G.dma(c.rbf[0:1, 0:2048], c.outw_b), 16)
    bias_ld, bias_wd = ld.n, c.wd.n

    def xfm(f):
        return fmB[f // 4][:, (f % 4) * 512:(f % 4) * 512 + 512]

    def kvc(m):
        return fmA[m]

    def tin(xs):
        pairs = []
        for ci in range(4):
            tc = xs * 4 + ci
            for f in range(16):
                pairs.append((c.res[tc][:, f * 128:(f + 1) * 128],
                              fmB[f // 4][:, (f % 4) * 512 + ci * 128:(f % 4) * 512 + ci * 128 + 128]))
        _transpose_in(c, pairs)
        Tn.wait(vp, vp.n)
        Tn.wait(sp, sp.n)

    Tn.wait(ld, bias_ld)
    V.wait(ld, bias_ld)
    V.wait(c.wd, bias_wd)
    for xs in range(4):
        tin(xs)
        for fg in range(4):
            slot = wb[fg % 2]
            G.wait(tp, ring_rel.get(fg % 2, 0))
            ws = c.wsem[fg % 2]
            iw = ws.inc(G.dma(slot[:], c.kvd_t[fg]), 16)
            Tn.wait(ws, iw)
            if fg == 0:
                c.waitb(Tn, [2, 3, 4, 5])
            for fi in range(4):
                f = fg * 4 + fi
                for m in range(4):
                    Tn.mm(PS[2 + m][:], slot[:, fi * 512 + m * 128:fi * 512 + m * 128 + 128],
                          xfm(f), start=(f == 0), stop=(f == 15))
            ring_rel[fg % 2] = tp.n + 1
            tp.inc(Tn.last)
        itg = tp.n
        V.wait(tp, itg)
        for m in range(4):
            V.ts(kvc(m)[:, xs * 512:(xs + 1) * 512], PS[2 + m][:], c.bcols[:, 16 + m:17 + m], None, OP.add)
        iv = vp.inc(V.last)
        for m in range(4):
            c.relb(2 + m, vp, iv)

    c._head_prev = {}
    for qsl in range(2):
        for xs2 in range(2):
            xs = qsl * 2 + xs2
            tin(xs)
            for mg in range(2):
                for fg in range(4):
                    slot = wb[fg % 2]
                    G.wait(tp, ring_rel.get(fg % 2, 0))
                    ws = c.wsem[fg % 2]
                    iw = ws.inc(G.dma(slot[:], c.qd_t[mg, fg]), 16)
                    Tn.wait(ws, iw)
                    if fg == 0:
                        c.waitb(Tn, [2, 3, 4, 5])
                    for fi in range(4):
                        f = fg * 4 + fi
                        for m4 in range(4):
                            Tn.mm(PS[2 + m4][:], slot[:, fi * 512 + m4 * 128:fi * 512 + m4 * 128 + 128],
                                  xfm(f), start=(f == 0), stop=(f == 15))
                    ring_rel[fg % 2] = tp.n + 1
                    tp.inc(Tn.last)
                itg = tp.n
                V.wait(tp, itg)
                for m4 in range(4):
                    m = mg * 4 + m4
                    dst = fmA[4 + m // 2][:, (m % 2) * 1024 + xs2 * 512:(m % 2) * 1024 + xs2 * 512 + 512]
                    V.ts(dst, PS[2 + m4][:], c.bcols[:, 20 + m:21 + m], None, OP.add)
                iv = vp.inc(V.last)
                for m4 in range(4):
                    c.relb(2 + m4, vp, iv)

        for h in range(16):
            ws = c.wsem[4]
            G.wait(tp, ring_rel.get(4, 0))
            ws.inc(G.dma(wb[4][:, 0:512], c.kup_t[h]), 16)
            ws.inc(G.dma(wb[4][:, 512:1024], c.vup_t[h]), 16)
            ws.inc(G.dma(wb[4][:, 1024:1280], c.kr_t[h]), 16)
            ws.inc(G.dma(wb[5][:, 0:1024], c.qup_t[h]), 16)
            iw = ws.inc(G.dma(wb[5][:, 1024:1536], c.qr_t[h]), 16)
            Tn.wait(ws, iw)
            ev = {}
            for tkc in range(4):
                p = PS[2 + tkc % 2][:]
                if tkc >= 2:
                    Tn.wait(vp, ev[tkc - 2])
                else:
                    c.waitb(Tn, [2 + tkc % 2])
                    Tn.wait(vp, c._head_prev.get(("kq", tkc % 2), 0))
                for kf in range(4):
                    Tn.mm(p, wb[4][:, kf * 128:(kf + 1) * 128], kvc(kf)[:, tkc * 512:(tkc + 1) * 512],
                          start=(kf == 0), stop=(kf == 3))
                it = tp.inc(Tn.last)
                V.wait(tp, it)
                V.ts(fmB[0][:, tkc * 512:(tkc + 1) * 512], p, c.bcols[:, 28 + h:29 + h], None, OP.add)
                ev[tkc] = vp.inc(V.last)
            for tkc in range(4):
                p = PS[2 + tkc % 2][0:64, :]
                if tkc >= 2:
                    Tn.wait(vp, ev[100 + tkc - 2])
                else:
                    Tn.wait(vp, ev[2 + tkc])
                for kf in range(4):
                    Tn.mm(p, wb[4][:, 1024 + kf * 64:1024 + (kf + 1) * 64],
                          kvc(kf)[:, tkc * 512:(tkc + 1) * 512], start=(kf == 0), stop=(kf == 3))
                it = tp.inc(Tn.last)
                V.wait(tp, it)
                V.ts(fmB[1][0:64, tkc * 512:(tkc + 1) * 512], p, c.bcols[0:64, 60 + h:61 + h], None, OP.add)
                ev[100 + tkc] = vp.inc(V.last)
            for tk16 in range(16):
                p = PS[4][:, 0:128]
                if tk16 >= 1:
                    Tn.wait(vp, ev[200 + tk16 - 1])
                else:
                    c.waitb(Tn, [4])
                    Tn.wait(vp, c._head_prev.get("v", 0))
                for kf in range(4):
                    Tn.mm(p, kvc(kf)[:, tk16 * 128:(tk16 + 1) * 128],
                          wb[4][:, 512 + kf * 128:512 + (kf + 1) * 128], start=(kf == 0), stop=(kf == 3))
                it = tp.inc(Tn.last)
                V.wait(tp, it)
                V.ts(fmB[2][:, tk16 * 128:(tk16 + 1) * 128], p, c.bcols[:, 44 + h:45 + h], None, OP.add)
                ev[200 + tk16] = vp.inc(V.last)
            for qn in range(2):
                p = PS[2 + qn][:]
                Tn.wait(vp, ev[100 + 2 + qn])
                for qf in range(8):
                    rhs = fmA[4 + qf // 2][:, (qf % 2) * 1024 + qn * 512:(qf % 2) * 1024 + qn * 512 + 512]
                    Tn.mm(p, wb[5][:, qf * 128:(qf + 1) * 128], rhs, start=(qf == 0), stop=(qf == 7))
                it = tp.inc(Tn.last)
                V.wait(tp, it)
                V.ts(fmB[3][:, qn * 512:(qn + 1) * 512], p, c.bcols[:, 76 + h:77 + h], None, OP.add)
                ev[300 + qn] = vp.inc(V.last)
            for qn in range(2):
                p = PS[2 + qn][0:64, :]
                Tn.wait(vp, ev[300 + qn])
                for qf in range(8):
                    rhs = fmA[4 + qf // 2][:, (qf % 2) * 1024 + qn * 512:(qf % 2) * 1024 + qn * 512 + 512]
                    Tn.mm(p, wb[5][:, 1024 + qf * 64:1024 + (qf + 1) * 64], rhs, start=(qf == 0), stop=(qf == 7))
                it = tp.inc(Tn.last)
                V.wait(tp, it)
                V.ts(fmB[3][0:64, 1024 + qn * 512:1024 + (qn + 1) * 512], p,
                     c.cols[0:64, 128 + h:129 + h], None, OP.add)
                ev[400 + qn] = vp.inc(V.last)
            ring_rel[4] = tp.n
            c._head_prev = {("kq", 0): ev[400 + 0], ("kq", 1): ev[400 + 1], "v": ev[200 + 15]}

            Tn.wait(vp, vp.n)
            for kc4 in range(4):
                p = PS[2 + kc4][:]
                Tn.mm(p, fmB[3][:, 0:128], fmB[0][:, kc4 * 512:(kc4 + 1) * 512], start=True, stop=False)
                Tn.mm(p, fmB[3][0:64, 1024:1152], fmB[1][0:64, kc4 * 512:(kc4 + 1) * 512],
                      start=False, stop=True)
            it = tp.inc(Tn.last)
            V.wait(tp, it)
            for kc4 in range(4):
                V.red(c.stat[:, 20 + kc4:21 + kc4], PS[2 + kc4][:], OP.max)
            V.tt(c.stat[:, 20:21], c.stat[:, 20:21], c.stat[:, 21:22], OP.max)
            V.tt(c.stat[:, 22:23], c.stat[:, 22:23], c.stat[:, 23:24], OP.max)
            V.tt(c.stat[:, 20:21], c.stat[:, 20:21], c.stat[:, 22:23], OP.max)
            V.copy(wb[5][:, 1536:1537], c.stat[:, 20:21])
            ivm = vp.inc(V.last)
            Tn.wait(vp, ivm)
            Tn.trans(PT[0][0:1, 0:128], wb[5][:, 1536:1537], c.sid[:])
            it = tp.inc(Tn.last)
            V.wait(tp, it)
            V.red(c.stat[0:1, 24:25], PT[0][0:1, 0:128], OP.max)
            ivm = vp.inc(V.last)
            Tn.wait(vp, ivm)
            Tn.mm(PS[5][:, 0:1], c.sonerf[:], c.stat[0:1, 24:25], start=True, stop=True)
            it = tp.inc(Tn.last)
            V.wait(tp, it)
            V.ts(c.cols[:, 144:145], PS[5][:, 0:1], -MLA_SCALE, None, OP.mult)
            ebias_vp = vp.inc(V.last)
            S.wait(vp, ebias_vp)

            for qn in range(2):
                pexp = {}
                Tn.wait(vp, vp.n)
                for kc in range(16):
                    STp = PS[0 + kc % 2][:]
                    if kc >= 2:
                        Tn.wait(sp, pexp[kc - 2])
                    Tn.mm(STp, fmB[0][:, kc * 128:(kc + 1) * 128], fmB[3][:, qn * 512:(qn + 1) * 512],
                          start=True, stop=False)
                    Tn.mm(STp, fmB[1][0:64, kc * 128:(kc + 1) * 128],
                          fmB[3][0:64, 1024 + qn * 512:1024 + (qn + 1) * 512], start=False, stop=True)
                    it = tp.inc(Tn.last)
                    S.wait(tp, it)
                    pslot = wb[0][:, (kc % 2) * 512:(kc % 2) * 512 + 512]
                    S.act(pslot, STp, AF.Exp, scale=MLA_SCALE, bias=c.cols[:, 144:145])
                    pexp[kc] = sp.inc(S.last)
                    Tn.wait(sp, pexp[kc])
                    Tn.mm(PS[2][:], fmB[2][:, kc * 128:(kc + 1) * 128], pslot, start=(kc == 0), stop=(kc == 15))
                    Tn.mm(PS[3][0:1, :], c.sonec[:], pslot, start=(kc == 0), stop=(kc == 15))
                    tp.inc(Tn.last)
                it = tp.n
                V.wait(tp, it)
                V.recip(c.rows[0:1, 0:512], PS[3][0:1, :])
                ivr = vp.inc(V.last)
                Tn.wait(vp, ivr)
                Tn.mm(PS[4][:], c.sonerf[:], c.rows[0:1, 0:512], start=True, stop=True)
                it = tp.inc(Tn.last)
                V.wait(tp, it)
                V.copy(f32a[0][:, 0:512], PS[4][:])
                V.tt(fmA[8 + h // 2][:, (h % 2) * 1024 + qn * 512:(h % 2) * 1024 + qn * 512 + 512],
                     PS[2][:], f32a[0][:, 0:512], OP.mult)
                iv = vp.inc(V.last)
                c.relb(2, vp, iv)
                c.relb(4, vp, iv)

        c._ctx_ready = vp.n
        ring_rel[0] = ring_rel[1] = tp.n
        for tcg in range(2):
            for oc in range(4):
                for hg in range(4):
                    slot = wb[2 + hg % 2]
                    G.wait(tp, ring_rel.get(2 + hg % 2, 0))
                    ws = c.wsem[2 + hg % 2]
                    iw = ws.inc(G.dma(slot[:], c.outw_t[oc, hg]), 16)
                    Tn.wait(ws, iw)
                    if hg == 0:
                        Tn.wait(vp, c._ctx_ready)
                        c.waitb(Tn, [2, 3, 4, 5])
                    for hi in range(4):
                        h = hg * 4 + hi
                        for tci in range(4):
                            tq = tcg * 4 + tci
                            lhs = fmA[8 + h // 2][:, (h % 2) * 1024 + tq * 128:(h % 2) * 1024 + tq * 128 + 128]
                            Tn.mm(PS[2 + tci][:], lhs, slot[:, hi * 512:(hi + 1) * 512],
                                  start=(h == 0), stop=False)
                    ring_rel[2 + hg % 2] = tp.n + 1
                    tp.inc(Tn.last)
                for tci in range(4):
                    Tn.mm(PS[2 + tci][:], c.soner[0:1, 0:128], c.rbf[0:1, oc * 512:(oc + 1) * 512],
                          start=False, stop=True)
                itg = tp.inc(Tn.last)
                V.wait(tp, itg)
                for tci in range(4):
                    V.copy(fmB[tci][:, oc * 512:(oc + 1) * 512], PS[2 + tci][:])
                iv = vp.inc(V.last)
                for tci in range(4):
                    c.relb(2 + tci, vp, iv)
            for tci in range(4):
                tc = qsl * 8 + tcg * 4 + tci
                _ln_write(c, tc, fmB[tci][:])


def _ffn1(c):
    Tn, V, S, G, Y = c.T, c.V, c.S, c.G, c.Y
    tp, vp, sp, gp, ld = c.tp, c.vp, c.sp, c.gp, c.ld
    PS, PT = c.PS, c.PT
    fmA, fmB, f32a, wb = c.fmA, c.fmB, c.f32a, c.wb
    ring_rel = {i: tp.n for i in range(5)}
    G.wait(tp, tp.n)
    G.wait(vp, vp.n)
    G.wait(sp, sp.n)

    def fsl(i, j, w=512):
        return fmA[i][:, j * w:(j + 1) * w]

    for jc in range(16):
        ws = c.wsem[4]
        G.wait(gp, ring_rel.get("q1", 0))
        for f in range(16):
            ws.inc(G.dma(fsl(f // 4, f % 4), c.w1_t[jc, f]), 16)
        iw = ws.n
        S.wait(c.wsem[4], iw)
        Tn.wait(c.wsem[4], iw)
        mmv = {}
        c.waitb(Tn, [0, 1])
        for f in range(16):
            if f >= 4:
                S.wait(tp, mmv[f - 4])
            S.act(wb[4][:, (f % 4) * 512:(f % 4) * 512 + 512], fsl(f // 4, f % 4), AF.Abs, scale=2.0)
            isp = sp.inc(S.last)
            Tn.wait(sp, isp)
            Tn.mm(PS[0][0:1, :], c.sonec[:], wb[4][:, (f % 4) * 512:(f % 4) * 512 + 512],
                  start=(f == 0), stop=(f == 15))
            mmv[f] = tp.inc(Tn.last)
        it = tp.n
        V.wait(tp, it)
        V.ts(c.rows[0:1, 0:512], PS[0][0:1, :], 1.0 / 4096.0, 1e-5, OP.mult, OP.max)
        ivr = vp.inc(V.last)
        c.relb(0, vp, ivr)
        Tn.wait(vp, ivr)
        Tn.mm(PS[1][:], c.sonerf[:], c.rows[0:1, 0:512], start=True, stop=True)
        it = tp.inc(Tn.last)
        V.wait(tp, it)
        V.copy(f32a[0][:, 0:512], PS[1][:])
        iv = vp.inc(V.last)
        c.relb(1, vp, iv)
        for f in range(16):
            ab = wb[4][:, (f % 4) * 512:(f % 4) * 512 + 512]
            S.wait(vp, vp.n)
            S.act(ab, fsl(f // 4, f % 4), AF.Abs, scale=2.0)
            S.act(wb[5][:, (f % 4) * 512:(f % 4) * 512 + 512], fsl(f // 4, f % 4), AF.Sign)
            isp = sp.inc(S.last)
            V.wait(sp, isp)
            V.tt(ab, ab, f32a[0][:, 0:512], OP.is_ge)
            V.tt(ab, ab, wb[5][:, (f % 4) * 512:(f % 4) * 512 + 512], OP.mult)
            V.tt(fsl(f // 4, f % 4), ab, f32a[0][:, 0:512], OP.mult)
            ivq = vp.inc(V.last)
            Y.wait(vp, ivq)
            gp.inc(Y.dma(c.w1q_hbm[jc, f], fsl(f // 4, f % 4)), 16)
        ring_rel["q1"] = gp.n
    for osr in range(4):
        ws = c.wsem[4]
        G.wait(gp, gp.n)
        for j in range(64):
            ws.inc(G.dma(fsl(j // 4, j % 4), c.w2_t[osr, j]), 16)
        iw = ws.n
        S.wait(c.wsem[4], iw)
        Tn.wait(c.wsem[4], iw)
        mmv = {}
        c.waitb(Tn, [0, 1])
        for j in range(64):
            if j >= 4:
                S.wait(tp, mmv[j - 4])
            S.act(wb[4][:, (j % 4) * 512:(j % 4) * 512 + 512], fsl(j // 4, j % 4), AF.Abs, scale=2.0)
            isp = sp.inc(S.last)
            Tn.wait(sp, isp)
            Tn.mm(PS[0][0:1, :], c.sonec[:], wb[4][:, (j % 4) * 512:(j % 4) * 512 + 512],
                  start=(j == 0), stop=(j == 63))
            mmv[j] = tp.inc(Tn.last)
        it = tp.n
        V.wait(tp, it)
        V.ts(c.rows[0:1, 0:512], PS[0][0:1, :], 1.0 / 16384.0, 1e-5, OP.mult, OP.max)
        ivr = vp.inc(V.last)
        c.relb(0, vp, ivr)
        Tn.wait(vp, ivr)
        Tn.mm(PS[1][:], c.sonerf[:], c.rows[0:1, 0:512], start=True, stop=True)
        it = tp.inc(Tn.last)
        V.wait(tp, it)
        V.copy(f32a[0][:, 0:512], PS[1][:])
        iv = vp.inc(V.last)
        c.relb(1, vp, iv)
        for j in range(64):
            ab = wb[4][:, (j % 4) * 512:(j % 4) * 512 + 512]
            S.wait(vp, vp.n)
            S.act(ab, fsl(j // 4, j % 4), AF.Abs, scale=2.0)
            S.act(wb[5][:, (j % 4) * 512:(j % 4) * 512 + 512], fsl(j // 4, j % 4), AF.Sign)
            isp = sp.inc(S.last)
            V.wait(sp, isp)
            V.tt(ab, ab, f32a[0][:, 0:512], OP.is_ge)
            V.tt(ab, ab, wb[5][:, (j % 4) * 512:(j % 4) * 512 + 512], OP.mult)
            V.tt(fsl(j // 4, j % 4), ab, f32a[0][:, 0:512], OP.mult)
            ivq = vp.inc(V.last)
            Y.wait(vp, ivq)
            gp.inc(Y.dma(c.w2q_hbm[osr, j], fsl(j // 4, j % 4)), 16)
    stash_done = gp.n
    G.wait(gp, stash_done)

    perm = {}
    for tc in range(4):
        for jb in range(64):
            s_id = (tc * 4 + jb // 16) * 16 + (jb % 16)
            d_id = (jb // 4) * 16 + (jb % 4) * 4 + tc
            perm[s_id] = d_id
    cycles = []
    seen = set()
    for s0 in range(256):
        if s0 in seen:
            continue
        cyc = [s0]
        seen.add(s0)
        nxt = perm[s0]
        while nxt != s0:
            cyc.append(nxt)
            seen.add(nxt)
            nxt = perm[nxt]
        cycles.append(cyc)

    def slot_ap(sid):
        return fmA[sid // 16][:, (sid % 16) * 128:(sid % 16) * 128 + 128]

    V.wait(gp, stash_done)
    for sb in range(NSLAB):
        tcs = [sb * 4 + i for i in range(4)]
        pairs = []
        for ci, tc in enumerate(tcs):
            V.wait(tp, tp.n)
            V.red(c.stat[:, 30:31], c.res[tc][:], OP.max, absval=True)
            V.ts(c.stat[:, 31:32], c.stat[:, 30:31], 1e-5, None, OP.max)
            V.recip(c.stat[:, 32:33], c.stat[:, 31:32])
            V.ts(c.cols[:, 32 + ci:33 + ci], c.stat[:, 32:33], 127.0, None, OP.mult)
            V.ts(c.cols[:, 36 + ci:37 + ci], c.stat[:, 31:32], 1.0 / 127.0, None, OP.mult)
            sx = c.cols[:, 32 + ci:33 + ci]
            for half in range(2):
                hs = slice(half * 1024, half * 1024 + 1024)
                v_ = f32a[0][:, 0:1024]
                V.ts(v_, c.res[tc][:, hs], sx, None, OP.mult)
                V.ts(f32a[0][:, 1024:2048], v_, M52, C52_32, OP.add, OP.subtract)
                V.ts(f32a[0][:, 1024:2048], f32a[0][:, 1024:2048], M32, None, OP.subtract)
                V.ts(f32a[1][:, 0:1024], v_, M23, M23, OP.add, OP.subtract)
                V.ts(f32a[1][:, 1024:2048], v_, M32, M32, OP.add, OP.subtract)
                V.tt(f32a[1][:, 0:1024], f32a[1][:, 0:1024], f32a[1][:, 1024:2048], OP.subtract)
                V.tt(fmA[15][:, hs], f32a[0][:, 1024:2048], f32a[1][:, 0:1024], OP.add)
            iv = vp.inc(V.last)
            for f in range(16):
                pairs.append((fmA[15][:, f * 128:(f + 1) * 128],
                              fmB[f // 4][:, (f % 4) * 512 + ci * 128:(f % 4) * 512 + ci * 128 + 128]))
            _transpose_in(c, pairs)
            pairs = []
        xq_ready = vp.n

        for jc in range(16):
            for g2 in range(4):
                slot = wb[g2 % 2]
                G.wait(tp, ring_rel.get(g2 % 2, 0))
                ws = c.wsem[g2 % 2]
                for fi4 in range(4):
                    ws.inc(G.dma(slot[:, fi4 * 512:(fi4 + 1) * 512], c.w1q_hbm[jc, g2 * 4 + fi4]), 16)
                iw = ws.n
                Tn.wait(ws, iw)
                if g2 == 0:
                    Tn.wait(vp, xq_ready)
                    c.waitb(Tn, [2, 3, 4, 5])
                for fi in range(4):
                    f = g2 * 4 + fi
                    for tci in range(4):
                        Tn.mm(PS[2 + tci][:], fmB[f // 4][:, (f % 4) * 512 + tci * 128:(f % 4) * 512 + tci * 128 + 128],
                              slot[:, fi * 512:(fi + 1) * 512], start=(f == 0), stop=(f == 15))
                ring_rel[g2 % 2] = tp.n + 1
                tp.inc(Tn.last)
            itg = tp.n
            S.wait(tp, itg)
            for tci in range(4):
                S.act(fmA[tci * 4 + jc // 4][:, (jc % 4) * 512:(jc % 4) * 512 + 512], PS[2 + tci][:],
                      AF.Gelu_apprx_tanh, scale=c.cols[:, 36 + tci:37 + tci])
            isp = sp.inc(S.last)
            for tci in range(4):
                c.relb(2 + tci, sp, isp)
        h_done = sp.n

        V.wait(sp, h_done)
        for tci in range(4):
            for q in range(4):
                V.red(c.stat[:, 40 + q:41 + q], fmA[tci * 4 + q][:], OP.max, absval=True)
            V.tt(c.stat[:, 40:41], c.stat[:, 40:41], c.stat[:, 41:42], OP.max)
            V.tt(c.stat[:, 42:43], c.stat[:, 42:43], c.stat[:, 43:44], OP.max)
            V.tt(c.stat[:, 40:41], c.stat[:, 40:41], c.stat[:, 42:43], OP.max)
            V.ts(c.stat[:, 44:45], c.stat[:, 40:41], 1e-5, None, OP.max)
            V.recip(c.stat[:, 45:46], c.stat[:, 44:45])
            V.ts(c.cols[:, 40 + tci:41 + tci], c.stat[:, 45:46], 127.0, None, OP.mult)
            V.ts(c.cols[:, 44 + tci:45 + tci], c.stat[:, 44:45], 1.0 / 127.0, None, OP.mult)
            sh = c.cols[:, 40 + tci:41 + tci]
            for q in range(4):
                for half in range(2):
                    hh = slice(half * 1024, half * 1024 + 1024)
                    v_ = f32a[0][:, 0:1024]
                    V.ts(v_, fmA[tci * 4 + q][:, hh], sh, None, OP.mult)
                    V.ts(f32a[0][:, 1024:2048], v_, M52, C52_32, OP.add, OP.subtract)
                    V.ts(f32a[0][:, 1024:2048], f32a[0][:, 1024:2048], M32, None, OP.subtract)
                    V.ts(f32a[1][:, 0:1024], v_, M23, M23, OP.add, OP.subtract)
                    V.ts(f32a[1][:, 1024:2048], v_, M32, M32, OP.add, OP.subtract)
                    V.tt(f32a[1][:, 0:1024], f32a[1][:, 0:1024], f32a[1][:, 1024:2048], OP.subtract)
                    V.tt(fmA[tci * 4 + q][:, hh], f32a[0][:, 1024:2048], f32a[1][:, 0:1024], OP.add)
        nh_done = vp.inc(V.last)

        Tn.wait(vp, nh_done)
        Tn.wait(sp, sp.n)
        lagged = []
        hist = {}
        n = 0
        vp0 = vp.n
        for cyc in cycles:
            for i in range(len(cyc)):
                s_id = cyc[i]
                pt = PT[n % 2][:, ((n // 2) % 4) * 256:((n // 2) % 4) * 256 + 128]
                Tn.wait(vp, hist[n - 2] if n >= 2 else vp0)
                Tn.trans(pt, slot_ap(s_id), c.sid[:])
                it = tp.inc(Tn.last)
                lagged.append((it, pt, perm[s_id]))
                if len(lagged) > 1:
                    it2, pt2, d2 = lagged.pop(0)
                    V.wait(tp, max(it2, it))
                    V.copy(slot_ap(d2), pt2)
                    hist[n - 1] = vp.inc(V.last)
                n += 1
            it2, pt2, d2 = lagged.pop(0)
            V.wait(tp, it2)
            V.copy(slot_ap(d2), pt2)
            hist[n - 1] = vp.inc(V.last)
        hq_ready = vp.n

        for oc in range(4):
            for jg in range(16):
                slot = wb[2 + jg % 2]
                G.wait(tp, ring_rel.get(2 + jg % 2, 0))
                ws = c.wsem[2 + jg % 2]
                for ji4 in range(4):
                    ws.inc(G.dma(slot[:, ji4 * 512:(ji4 + 1) * 512], c.w2q_hbm[oc, jg * 4 + ji4]), 16)
                iw = ws.n
                Tn.wait(ws, iw)
                if jg == 0:
                    Tn.wait(vp, hq_ready)
                    c.waitb(Tn, [2, 3, 4, 5])
                for ji in range(4):
                    j = jg * 4 + ji
                    for tci in range(4):
                        lhs = fmA[j // 4][:, (j % 4) * 512 + tci * 128:(j % 4) * 512 + tci * 128 + 128]
                        Tn.mm(PS[2 + tci][:], lhs, slot[:, ji * 512:(ji + 1) * 512],
                              start=(j == 0), stop=(j == 63))
                ring_rel[2 + jg % 2] = tp.n + 1
                tp.inc(Tn.last)
            itg = tp.n
            V.wait(tp, itg)
            for tci in range(4):
                V.ts(fmB[tci][:, oc * 512:(oc + 1) * 512], PS[2 + tci][:],
                     c.cols[:, 44 + tci:45 + tci], None, OP.mult)
            iv = vp.inc(V.last)
            for tci in range(4):
                c.relb(2 + tci, vp, iv)
        for tci in range(4):
            _ln_write(c, tcs[tci], fmB[tci][:])



def _bf(x):
    return np.ascontiguousarray(x, dtype=ml_dtypes.bfloat16)


def _f32(x):
    return np.ascontiguousarray(x, dtype=np.float32)


def _tile_fm(wT, groups):
    raise NotImplementedError


def _prep_gla_qk(wT):
    out = np.empty((2, 4, 128, 2048), np.float32)
    for mg in range(2):
        for fg in range(4):
            for fi in range(4):
                f = fg * 4 + fi
                blk = wT[f * 128:(f + 1) * 128, mg * 512:(mg + 1) * 512]
                out[mg, fg, :, fi * 512:(fi + 1) * 512] = blk
    return out


def _prep_gla_vgo(wT):
    out = np.empty((4, 4, 128, 2048), np.float32)
    for oc in range(4):
        for fg in range(4):
            for fi in range(4):
                f = fg * 4 + fi
                out[oc, fg, :, fi * 512:(fi + 1) * 512] = wT[f * 128:(f + 1) * 128, oc * 512:(oc + 1) * 512]
    return out


def prep_inputs(inputs):
    g = {}
    g["c_ident"] = np.eye(128, dtype=ml_dtypes.bfloat16)
    g["c_mask"] = _bf(np.tril(np.ones((128, 128), np.float32)).T)
    H = np.array([[1.0]], np.float32)
    while H.shape[0] < 128:
        H = np.block([[H, H], [H, -H]])
    g["c_h128"] = _bf(H)
    g["c_onesrow"] = np.ones((1, 128), ml_dtypes.bfloat16)
    g["c_onescol"] = np.ones((128, 1), ml_dtypes.bfloat16)
    g["c_onesrowf"] = np.ones((1, 128), np.float32)

    wqT = _f32(inputs["L0_attn_Wq"].T)
    wkT = _f32(inputs["L0_attn_Wk"].T)
    g["gla_qk_t"] = np.stack([_prep_gla_qk(wqT), _prep_gla_qk(wkT)])
    wvT = _f32(inputs["L0_attn_Wv"].T)
    wgT = _f32(inputs["L0_attn_Wg"].T)
    woT = _f32(inputs["L0_attn_Wo"].T)
    g["gla_vgo_t"] = np.stack([_prep_gla_vgo(wvT), _prep_gla_vgo(wgT), _prep_gla_vgo(woT)])
    wkg1T = _f32(inputs["L0_attn_Wkg1"].T)
    g["wkg1_t"] = _f32(wkg1T.reshape(16, 128, 16).transpose(1, 0, 2).reshape(128, 256))
    g["wkg2T"] = _f32(inputs["L0_attn_Wkg2"].T)
    g["bkg2_cols"] = _f32(inputs["L0_attn_bkg2"].reshape(8, 128).T)
    g["bg"] = _f32(inputs["L0_attn_bg"].reshape(1, -1))

    wupT = _f32(inputs["L0_ffn_W_up"].T)
    wgateT = _f32(inputs["L0_ffn_W_gate"].T)
    wdownT = _f32(inputs["L0_ffn_W_down"].T)
    ug = np.empty((2, 4, 4, 4, 128, 2048), np.float32)
    for wi, w in enumerate([wupT, wgateT]):
        for jc in range(4):
            for jg in range(4):
                for fg in range(4):
                    for fi in range(4):
                        f = fg * 4 + fi
                        ug[wi, jc, jg, fg, :, fi * 512:(fi + 1) * 512] = \
                            w[f * 128:(f + 1) * 128, jc * 2048 + jg * 512:jc * 2048 + (jg + 1) * 512]
    g["ug_t"] = ug
    dn = np.empty((4, 16, 128, 2048), np.float32)
    for oc in range(4):
        for jg in range(16):
            for ji in range(4):
                j = jg * 4 + ji
                dn[oc, jg, :, ji * 512:(ji + 1) * 512] = \
                    wdownT[j * 128:(j + 1) * 128, oc * 512:(oc + 1) * 512]
    g["dn_t"] = dn
    g["su_ug_cols"] = _f32(inputs["L0_ffn_SU_ug"].reshape(16, 128).T)
    g["su_dp_cols"] = _f32(inputs["L0_ffn_SU_dp"].reshape(64, 128).T)
    wsc = _f32(inputs["L0_ffn_Wscale_ug"])
    g["wsc_up_cols"] = _f32(wsc[:INTER].reshape(64, 128).T)
    g["wsc_gate_cols"] = _f32(wsc[INTER:].reshape(64, 128).T)
    g["wsc_dp_row"] = _f32(inputs["L0_ffn_Wscale_dp"].reshape(1, -1))

    kvdT = _f32(inputs["L1_attn_kv_down_w"].T)
    kvd = np.empty((4, 128, 2048), np.float32)
    for fg in range(4):
        for fi in range(4):
            f = fg * 4 + fi
            kvd[fg, :, fi * 512:(fi + 1) * 512] = kvdT[f * 128:(f + 1) * 128, :]
    g["kvd_t"] = kvd
    qdT = _f32(inputs["L1_attn_query_down_w"].T)
    g["qd_t"] = _prep_gla_qk(qdT)
    kupT = _f32(inputs["L1_attn_key_up_w"].T)
    vupT = _f32(inputs["L1_attn_value_up_w"].T)
    krT = _f32(inputs["L1_attn_key_rope_w"].T)
    kup = np.empty((16, 128, 512), np.float32)
    vup = np.empty((16, 128, 512), np.float32)
    krt = np.empty((16, 128, 256), np.float32)
    for h in range(16):
        for kf in range(4):
            kup[h, :, kf * 128:(kf + 1) * 128] = kupT[kf * 128:(kf + 1) * 128, h * 128:(h + 1) * 128]
            vup[h, :, kf * 128:(kf + 1) * 128] = vupT[kf * 128:(kf + 1) * 128, h * 128:(h + 1) * 128]
            krt[h, :, kf * 64:(kf + 1) * 64] = krT[kf * 128:(kf + 1) * 128, h * 64:(h + 1) * 64]
    g["kup_t"] = kup
    g["vup_t"] = vup
    g["kr_t"] = krt
    qupT = _f32(inputs["L1_attn_query_up_w"].T)
    qup = np.empty((16, 128, 1024), np.float32)
    for h in range(16):
        for qf in range(8):
            qup[h, :, qf * 128:(qf + 1) * 128] = qupT[qf * 128:(qf + 1) * 128, h * 128:(h + 1) * 128]
    g["qup_t"] = qup
    qrT = _f32(inputs["L1_attn_query_rope_w"].T)
    qr = np.empty((16, 128, 512), np.float32)
    for h in range(16):
        for qf in range(8):
            qr[h, :, qf * 64:(qf + 1) * 64] = qrT[qf * 128:(qf + 1) * 128, h * 64:(h + 1) * 64]
    g["qr_t"] = qr
    outwT = _f32(inputs["L1_attn_out_w"].T)
    ow = np.empty((4, 4, 128, 2048), np.float32)
    for oc in range(4):
        for hg in range(4):
            for hi in range(4):
                h = hg * 4 + hi
                ow[oc, hg, :, hi * 512:(hi + 1) * 512] = \
                    outwT[h * 128:(h + 1) * 128, oc * 512:(oc + 1) * 512]
    g["outw_t"] = ow
    for nm, key in [("kvd_b", "L1_attn_kv_down_b"), ("kup_b", "L1_attn_key_up_b"),
                    ("vup_b", "L1_attn_value_up_b"),
                    ("qd_b", "L1_attn_query_down_b"), ("qup_b", "L1_attn_query_up_b")]:
        g[nm] = _f32(inputs[key].reshape(-1, 128).T)
    g["kr_b"] = _f32(inputs["L1_attn_key_rope_b"].reshape(16, 64).T)
    g["qr_b"] = _f32(inputs["L1_attn_query_rope_b"].reshape(16, 64).T)
    g["outw_b"] = _f32(inputs["L1_attn_out_b"].reshape(1, -1))

    w1T = _f32(inputs["L1_ffn_w1"].T)
    w1t = np.empty((16, 16, 128, 512), np.float32)
    for jc in range(16):
        for f in range(16):
            w1t[jc, f] = w1T[f * 128:(f + 1) * 128, jc * 512:(jc + 1) * 512]
    g["w1_t"] = w1t
    w2T = _f32(inputs["L1_ffn_w2"].T)
    w2t = np.empty((4, 64, 128, 512), np.float32)
    for os_ in range(4):
        for j in range(64):
            w2t[os_, j] = w2T[j * 128:(j + 1) * 128, os_ * 512:(os_ + 1) * 512]
    g["w2_t"] = w2t
    return g


def kernel(**inputs):
    g = prep_inputs(inputs)
    x = np.ascontiguousarray(inputs["L0_attn_x"], dtype=np.float32)
    nc = build_program(_UPTO)
    in_maps = []
    for core in range(NCORES):
        m = dict(g)
        m["x_in"] = x[core]
        in_maps.append(m)
    res = run_bass_kernel_spmd(nc, in_maps, core_ids=list(range(NCORES)))
    out = np.stack([np.asarray(res.results[i]["out"], np.float32) for i in range(NCORES)])
    return out


# version 03
# speedup vs baseline: 1.3125x; 1.2996x over previous
import math
from contextlib import ExitStack

import numpy as np
import ml_dtypes

import concourse.bass as bass
import concourse.mybir as mybir
from concourse.bass_utils import run_bass_kernel_spmd

F32 = mybir.dt.float32
BF16 = mybir.dt.bfloat16
AF = mybir.ActivationFunctionType
OP = mybir.AluOpType
AX = mybir.AxisListType

NCORES = 8
T = 2048
D = 2048
NT = 16
SLAB = 512
NSLAB = 4
LN_EPS = 1e-5
GLA_NORM = 16.0
GLA_SCALE_LOG = math.log(128.0 ** (-0.5))
MLA_SCALE = 1.0 / math.sqrt(192.0)
INTER = 8192

M23 = 1.5 * 2.0 ** 23
M52 = 1.5 * 2.0 ** 52
M32 = 1.5 * 2.0 ** 32
C52_32 = M52 - M32

_UPTO = 4



class Eng:

    def __init__(self, e, auto_drain=True):
        self.e = e
        self.auto_drain = auto_drain
        self.dirty = set()
        self.rd = set()
        self.waited = {}

    @staticmethod
    def _names(aps):
        out = []
        for a in aps:
            if a is None or isinstance(a, (int, float)):
                continue
            t = getattr(a, "tensor", None)
            if t is not None:
                out.append(t.name)
        return out

    def _pre(self, outs, ins):
        if not self.auto_drain:
            return
        o = set(self._names(outs))
        i = set(self._names(ins))
        if (i & self.dirty) or (o & self.dirty) or (o & self.rd):
            self.e.drain()
            self.dirty.clear()
            self.rd.clear()
        self.dirty |= o
        self.rd |= i

    def wait(self, sem, val):
        val = int(val)
        if val <= 0 or self.waited.get(sem.name, -1) >= val:
            return
        self.e.wait_ge(sem.h, val)
        self.waited[sem.name] = val

    def _em(self, inst):
        self.last = inst
        return inst

    def dma(self, out, in_):
        self._pre([out], [in_])
        return self._em(self.e.dma_start(out=out, in_=in_))

    def copy(self, out, in_):
        self._pre([out], [in_])
        return self._em(self.e.tensor_copy(out, in_))

    def ts(self, out, in0, s1, s2, op0, op1=None):
        self._pre([out], [in0, s1, s2])
        if op1 is None:
            return self._em(self.e.tensor_scalar(out, in0, s1, None, op0))
        return self._em(self.e.tensor_scalar(out, in0, s1, s2, op0, op1))

    def tt(self, out, in0, in1, op):
        self._pre([out], [in0, in1])
        return self._em(self.e.tensor_tensor(out, in0, in1, op))

    def stt(self, out, in0, s, in1, op0, op1):
        self._pre([out], [in0, s, in1])
        return self._em(self.e.scalar_tensor_tensor(out, in0, s, in1, op0, op1))

    def scan(self, out, d0, d1, init, op0, op1):
        self._pre([out], [d0, d1])
        return self._em(self.e.tensor_tensor_scan(out, d0, d1, init, op0, op1))

    def red(self, out, in_, op, absval=False):
        self._pre([out], [in_])
        return self._em(self.e.tensor_reduce(out, in_, AX.X, op, apply_absolute_value=absval))

    def recip(self, out, in_):
        self._pre([out], [in_])
        return self._em(self.e.reciprocal(out, in_))

    def act(self, out, in_, func, bias=0.0, scale=1.0, accum_out=None):
        outs = [out] + ([accum_out] if accum_out is not None else [])
        self._pre(outs, [in_, bias, scale])
        return self._em(self.e.activation(out, in_, func, bias=bias, scale=scale, accum_out=accum_out))

    def mm(self, out, lhsT, rhs, start, stop):
        self._pre([out], [lhsT, rhs])
        return self._em(self.e.matmul(out, lhsT, rhs, start=start, stop=stop))

    def trans(self, out, in_, ident):
        self._pre([out], [in_, ident])
        return self._em(self.e.transpose(out, in_, ident))

    def memset(self, ap, v):
        self._pre([ap], [])
        return self._em(self.e.memset(ap, v))

    def nop(self):
        return self.e.nop()


class Sem:
    def __init__(self, h, name):
        self.h = h
        self.name = name
        self.n = 0

    def inc(self, inst, by=1):
        inst.then_inc(self.h, by)
        self.n += by
        return self.n


class Ctx:
    pass



def build_program(upto=_UPTO):
    nc = bass.Bass("TRN2", target_bir_lowering=False)
    c = Ctx()
    c.nc = nc

    def din(name, shape, dt=F32):
        return nc.dram_tensor(name, shape, dt, kind="ExternalInput").ap()

    c.x_in = din("x_in", [T, D])
    c.out_hbm = nc.dram_tensor("out", [T, D], F32, kind="ExternalOutput").ap()

    c.ident = din("c_ident", [128, 128], BF16)
    c.maskUT = din("c_mask", [128, 128], BF16)
    c.h128 = din("c_h128", [128, 128], BF16)
    c.onesrow = din("c_onesrow", [1, 128], BF16)
    c.onescol = din("c_onescol", [128, 1], BF16)
    c.onesrowf = din("c_onesrowf", [1, 128])

    c.gla_qk_t = din("gla_qk_t", [2, 2, 4, 128, 2048])
    c.gla_vgo_t = din("gla_vgo_t", [3, 4, 4, 128, 2048])
    c.wkg1_t = din("wkg1_t", [128, 256])
    c.wkg2T = din("wkg2T", [16, 1024])
    c.bkg2_cols = din("bkg2_cols", [128, 8])
    c.bg = din("bg", [1, D])

    c.ug_t = din("ug_t", [2, 4, 4, 4, 128, 2048])
    c.dn_t = din("dn_t", [4, 16, 128, 2048])
    c.su_ug_cols = din("su_ug_cols", [128, 16])
    c.wsc_up_cols = din("wsc_up_cols", [128, 64])
    c.wsc_gate_cols = din("wsc_gate_cols", [128, 64])
    c.su_dp_cols = din("su_dp_cols", [128, 64])
    c.wsc_dp_row = din("wsc_dp_row", [1, D])

    c.kvd_t = din("kvd_t", [4, 128, 2048])
    c.qd_t = din("qd_t", [2, 4, 128, 2048])
    c.kup_t = din("kup_t", [16, 128, 512])
    c.vup_t = din("vup_t", [16, 128, 512])
    c.kr_t = din("kr_t", [16, 128, 256])
    c.qup_t = din("qup_t", [16, 128, 1024])
    c.qr_t = din("qr_t", [16, 128, 512])
    c.outw_t = din("outw_t", [4, 4, 128, 2048])
    c.kvd_b = din("kvd_b", [128, 4])
    c.kup_b = din("kup_b", [128, 16])
    c.vup_b = din("vup_b", [128, 16])
    c.kr_b = din("kr_b", [64, 16])
    c.qd_b = din("qd_b", [128, 8])
    c.qup_b = din("qup_b", [128, 16])
    c.qr_b = din("qr_b", [64, 16])
    c.outw_b = din("outw_b", [1, D])

    c.w1_t = din("w1_t", [16, 16, 128, 512])
    c.w2_t = din("w2_t", [4, 64, 128, 512])
    c.w1q_hbm = nc.dram_tensor("w1q_s", [16, 16, 128, 512], BF16, kind="Internal").ap()
    c.w2q_hbm = nc.dram_tensor("w2q_s", [4, 64, 128, 512], BF16, kind="Internal").ap()

    with ExitStack() as stack:
        e = stack.enter_context
        c.block = e(nc.Block())
        c.T = Eng(nc.tensor, auto_drain=False); c.V = Eng(nc.vector); c.S = Eng(nc.scalar)
        c.G = Eng(nc.gpsimd, auto_drain=False); c.Y = Eng(nc.sync, auto_drain=False)
        for name in ["ld", "wd", "tp", "vp", "sp", "gp", "st", "ini"]:
            setattr(c, name, Sem(e(nc.semaphore(name)), name))
        c.wsem = [Sem(e(nc.semaphore(f"w{i}")), f"w{i}") for i in range(5)]

        c.res = [e(nc.sbuf_tensor(f"res{i}", [128, D], BF16)) for i in range(NT)]
        c.fmA = [e(nc.sbuf_tensor(f"fmA{i}", [128, D], BF16)) for i in range(16)]
        c.fmB = [e(nc.sbuf_tensor(f"fmB{i}", [128, D], BF16)) for i in range(4)]
        c.f32a = [e(nc.sbuf_tensor(f"f32a{i}", [128, D], F32)) for i in range(2)]
        c.wb = [e(nc.sbuf_tensor(f"wb{i}", [128, D], BF16)) for i in range(6)]
        c.Sst = e(nc.sbuf_tensor("Sst", [128, 1024], F32))
        c.Stmp = e(nc.sbuf_tensor("Stmp", [128, 256], F32))
        c.Sbf = e(nc.sbuf_tensor("Sbf", [128, 1024], BF16))
        c.stat = e(nc.sbuf_tensor("stat", [128, 64], F32))
        c.cols = e(nc.sbuf_tensor("cols", [128, 160], F32))
        c.rows = e(nc.sbuf_tensor("rows", [1, 512], F32))
        c.rbf = e(nc.sbuf_tensor("rbf", [1, 2048], BF16))
        c.bcols = e(nc.sbuf_tensor("bcols", [128, 96], F32))
        c.sid = e(nc.sbuf_tensor("sid", [128, 128], BF16))
        c.smask = e(nc.sbuf_tensor("smask", [128, 128], BF16))
        c.sh128 = e(nc.sbuf_tensor("sh128", [128, 128], BF16))
        c.soner = e(nc.sbuf_tensor("soner", [1, 128], BF16))
        c.sonec = e(nc.sbuf_tensor("sonec", [128, 1], BF16))
        c.sonerf = e(nc.sbuf_tensor("sonerf", [1, 128], F32))
        c.skg2 = e(nc.sbuf_tensor("skg2", [16, 1024], BF16))
        c.outf32 = c.f32a[1]

        c.PS = [e(nc.psum_tensor(f"PS{i}", [128, 512], F32)) for i in range(6)]
        c.PT = [e(nc.psum_tensor(f"PT{i}", [128, 1024], BF16)) for i in range(2)]
        c.dumb = c.f32a[1]
        c.bank_rel = {}

        def _relb(b, sem, val):
            c.bank_rel[b] = (sem, val)
        c.relb = _relb

        def _waitb(E, banks):
            for b in banks:
                if b in c.bank_rel:
                    sem, val = c.bank_rel[b]
                    E.wait(sem, val)
        c.waitb = _waitb

        for i, cv in enumerate([0.0, 1.0, -1.0, 0.5, LN_EPS, GLA_SCALE_LOG]):
            nc.const_aps.register(F32, cv, c.cols[:, 150 + i:151 + i])

        _init(c)
        if upto >= 1:
            _gla(c)
        if upto >= 2:
            _ffn0(c)
        if upto >= 3:
            _mla(c)
        if upto >= 4:
            _ffn1(c)
        _finish(c)

    return nc


def _init(c):
    Tn, V, S, G, Y = c.T, c.V, c.S, c.G, c.Y
    ld, ini = c.ld, c.ini
    ld.inc(Y.dma(c.sid[:], c.ident), 16)
    ld.inc(Y.dma(c.smask[:], c.maskUT), 16)
    ld.inc(Y.dma(c.sh128[:], c.h128), 16)
    ld.inc(Y.dma(c.soner[:], c.onesrow), 16)
    ld.inc(Y.dma(c.sonerf[:], c.onesrowf), 16)
    ld.inc(Y.dma(c.sonec[:], c.onescol), 16)
    for i in range(NT):
        c.wd.inc(G.dma(c.res[i][:], c.x_in[i * 128:(i + 1) * 128, :]), 16)
    for i, cv in enumerate([0.0, 1.0, -1.0, 0.5, LN_EPS, GLA_SCALE_LOG]):
        G.memset(c.cols[:, 150 + i:151 + i], cv)
    c.wd.inc(G.dma(c.skg2[:], c.wkg2T), 16)
    ld.inc(Y.dma(c.bcols[:, 8:16], c.bkg2_cols), 16)
    c.wd.inc(G.dma(c.rbf[:, 0:D], c.bg), 16)
    G.memset(c.Sst[:], 0.0)
    G.memset(c.Sbf[:], 0.0)
    ini.inc(G.last, 1)
    for E in (Tn, V, S, Y):
        E.wait(ini, 1)
        E.wait(ld, ld.n)
        E.wait(c.wd, c.wd.n)
    V.ts(c.bcols[:, 0:8], c.bcols[:, 8:16], -1.0, None, OP.mult)
    c.vp.inc(V.last)
    c.negbkg2 = c.bcols[:, 0:8]
    c.bg_row = c.rbf[:, 0:D]


def _finish(c):
    V, Y = c.V, c.Y
    vp, st = c.vp, c.st
    for tc in range(NT):
        V.wait(st, st.n)
        V.copy(c.outf32[:], c.res[tc][:])
        iv = vp.inc(V.last)
        Y.wait(vp, iv)
        st.inc(Y.dma(c.out_hbm[tc * 128:(tc + 1) * 128, :], c.outf32[:]), 16)
    Y.wait(st, st.n)



def _transpose_in(c, pairs, su_cols=None):
    Tn, V = c.T, c.V
    tp, vp = c.tp, c.vp
    hist = {}
    vp0 = vp.n
    for n, (src, dst) in enumerate(pairs):
        pt = c.PT[n % 2][:, ((n // 2) % 4) * 256:((n // 2) % 4) * 256 + 128]
        Tn.wait(vp, hist[n - 2] if n >= 2 else vp0)
        Tn.trans(pt, src, c.sid[:])
        it = tp.inc(Tn.last)
        V.wait(tp, it)
        if su_cols is not None and su_cols[n] is not None:
            V.ts(dst, pt, su_cols[n], None, OP.mult)
        else:
            V.copy(dst, pt)
        hist[n] = vp.inc(V.last)


def _ln_write(c, tc, src_tm_bf, final=False):
    V, S = c.V, c.S
    vp, sp = c.vp, c.sp
    s = c.f32a[0]
    V.tt(s[:], src_tm_bf, c.res[tc][:], OP.add)
    V.red(c.stat[:, 0:1], s[:], OP.add)
    iv = vp.inc(V.last)
    S.wait(vp, iv)
    for q in range(4):
        S.act(c.dumb[:, 0:512], s[:, q * 512:(q + 1) * 512], AF.Square, accum_out=c.stat[:, 4 + q:5 + q])
    isp = sp.inc(S.last)
    V.wait(sp, isp)
    V.ts(c.stat[:, 1:2], c.stat[:, 0:1], 1.0 / 2048.0, None, OP.mult)
    V.tt(c.stat[:, 8:9], c.stat[:, 4:5], c.stat[:, 5:6], OP.add)
    V.tt(c.stat[:, 9:10], c.stat[:, 6:7], c.stat[:, 7:8], OP.add)
    V.tt(c.stat[:, 10:11], c.stat[:, 8:9], c.stat[:, 9:10], OP.add)
    V.ts(c.stat[:, 11:12], c.stat[:, 10:11], 1.0 / 2048.0, None, OP.mult)
    V.tt(c.stat[:, 12:13], c.stat[:, 1:2], c.stat[:, 1:2], OP.mult)
    V.tt(c.stat[:, 13:14], c.stat[:, 11:12], c.stat[:, 12:13], OP.subtract)
    iv = vp.inc(V.last)
    S.wait(vp, iv)
    S.act(c.stat[:, 14:15], c.stat[:, 13:14], AF.Ln, bias=LN_EPS)
    S.act(c.stat[:, 2:3], c.stat[:, 14:15], AF.Exp, scale=-0.5)
    isp = sp.inc(S.last)
    V.wait(sp, isp)
    dst = c.outf32[:] if final else c.res[tc][:]
    V.ts(dst, s[:], c.stat[:, 1:2], c.stat[:, 2:3], OP.subtract, OP.mult)
    return vp.inc(V.last)



def _gla(c):
    Tn, V, S, G = c.T, c.V, c.S, c.G
    tp, vp, sp, wd = c.tp, c.vp, c.sp, c.wd
    PS = c.PS
    fmA, fmB, f32a, wb = c.fmA, c.fmB, c.f32a, c.wb
    ring_rel = {i: tp.n for i in range(5)}

    def xfm(f):
        return fmB[f // 4][:, (f % 4) * 512:(f % 4) * 512 + 512]

    def gpt(m):
        return f32a[m // 4][:, (m % 4) * 512:(m % 4) * 512 + 512]

    for sb in range(NSLAB):
        tcs = [sb * 4 + i for i in range(4)]

        pairs = []
        for ci in range(4):
            for f in range(16):
                pairs.append((c.res[tcs[ci]][:, f * 128:(f + 1) * 128],
                              fmB[f // 4][:, (f % 4) * 512 + ci * 128:(f % 4) * 512 + ci * 128 + 128]))
        _transpose_in(c, pairs)
        xfm_ready = vp.n

        G.wait(tp, ring_rel.get(4, 0))
        iw4 = c.wsem[4].inc(G.dma(wb[4][:, 0:256], c.wkg1_t), 16)
        Tn.wait(c.wsem[4], iw4)
        Tn.wait(vp, xfm_ready)
        c.waitb(Tn, [0])
        pk = PS[0][0:16, :]
        for f in range(16):
            Tn.mm(pk, wb[4][:, f * 16:(f + 1) * 16], xfm(f), start=(f == 0), stop=(f == 15))
        it = tp.inc(Tn.last)
        ring_rel[4] = it
        V.wait(tp, it)
        kg1 = wb[5][0:16, 0:512]
        V.copy(kg1, pk)
        ivk = vp.inc(V.last)
        Tn.wait(vp, ivk)
        sp_hist = {}
        for m in range(8):
            p = PS[m % 2][:]
            if m >= 2:
                Tn.wait(sp, sp_hist[m - 2])
            Tn.mm(p, c.skg2[:, m * 128:(m + 1) * 128], kg1, start=True, stop=True)
            it = tp.inc(Tn.last)
            S.wait(tp, it)
            S.wait(c.vp, 1)
            S.act(gpt(m), p, AF.Exp, scale=-1.0, bias=c.negbkg2[:, m:m + 1])
            S.act(gpt(m), gpt(m), AF.Ln, bias=1.0)
            sp_hist[m] = sp.inc(S.last)
            c.relb(m % 2, sp, sp_hist[m])
        V.wait(sp, sp.n)
        for m in range(8):
            for ch in range(4):
                seg = gpt(m)[:, ch * 128:(ch + 1) * 128]
                V.scan(seg, seg, seg, 0.0, OP.add, OP.bypass)
        ivG = vp.inc(V.last)
        S.wait(vp, ivG)
        for m in range(8):
            S.act(c.cols[:, 16 + m * 4:16 + m * 4 + 4], gpt(m)[:, 127::128], AF.Exp, scale=-1.0 / GLA_NORM)
        sp_a = sp.inc(S.last)

        def qk(qi, base, escale, ebias):
            for mg in range(2):
                for fg in range(4):
                    slot = wb[fg % 2]
                    G.wait(tp, ring_rel.get(fg % 2, 0))
                    ws = c.wsem[fg % 2]
                    iw = ws.inc(G.dma(slot[:], c.gla_qk_t[qi, mg, fg]), 16)
                    Tn.wait(ws, iw)
                    if fg == 0:
                        Tn.wait(vp, xfm_ready)
                        c.waitb(Tn, [2, 3, 4, 5])
                    for fi in range(4):
                        f = fg * 4 + fi
                        for m4 in range(4):
                            Tn.mm(PS[2 + m4][:], slot[:, fi * 512 + m4 * 128:fi * 512 + m4 * 128 + 128],
                                  xfm(f), start=(f == 0), stop=(f == 15))
                    ring_rel[fg % 2] = tp.n + 1
                    tp.inc(Tn.last)
                itg = tp.n
                for m4 in range(4):
                    m = mg * 4 + m4
                    et = wb[4][:, 256 + (m4 % 2) * 512:256 + (m4 % 2) * 512 + 512]
                    S.wait(vp, ivG)
                    S.wait(vp, c._et_vp.get(m4 % 2, 0))
                    S.act(et, gpt(m), AF.Exp, bias=ebias, scale=escale)
                    isp = sp.inc(S.last)
                    V.wait(tp, itg)
                    V.wait(sp, isp)
                    dst = fmA[base + m // 4][:, (m % 4) * 512:(m % 4) * 512 + 512]
                    V.tt(dst, PS[2 + m4][:], et, OP.mult)
                    c._et_vp[m4 % 2] = vp.inc(V.last)
                    c.relb(2 + m4, vp, c._et_vp[m4 % 2])

        c._et_vp = {}
        qk(0, 0, -1.0 / GLA_NORM, 0.0)
        qk(1, 2, +1.0 / GLA_NORM, GLA_SCALE_LOG)

        def tmproj(wi, base, bias_row):
            for oc in range(4):
                for fg in range(4):
                    slot = wb[2 + fg % 2]
                    G.wait(tp, ring_rel.get(2 + fg % 2, 0))
                    ws = c.wsem[2 + fg % 2]
                    iw = ws.inc(G.dma(slot[:], c.gla_vgo_t[wi, oc, fg]), 16)
                    Tn.wait(ws, iw)
                    if fg == 0:
                        Tn.wait(vp, xfm_ready)
                        c.waitb(Tn, [2, 3, 4, 5])
                    for fi in range(4):
                        f = fg * 4 + fi
                        for tci in range(4):
                            Tn.mm(PS[2 + tci][:], xfm(f)[:, tci * 128:(tci + 1) * 128],
                                  slot[:, fi * 512:(fi + 1) * 512],
                                  start=(f == 0), stop=(f == 15 and bias_row is None))
                    ring_rel[2 + fg % 2] = tp.n + 1
                    tp.inc(Tn.last)
                if bias_row is not None:
                    for tci in range(4):
                        Tn.mm(PS[2 + tci][:], c.soner[0:1, 0:128], bias_row[:, oc * 512:(oc + 1) * 512],
                              start=False, stop=True)
                    itg = tp.inc(Tn.last)
                else:
                    itg = tp.n
                V.wait(tp, itg)
                for tci in range(4):
                    V.copy(fmA[base + tci][:, oc * 512:(oc + 1) * 512], PS[2 + tci][:])
                iv = vp.inc(V.last)
                for tci in range(4):
                    c.relb(2 + tci, vp, iv)

        tmproj(0, 4, None)
        tmproj(1, 8, c.bg_row)

        v_ready = vp.n
        prev_state_vp = getattr(c, "_gla_state_vp", {})
        kv_read_vp = getattr(c, "_gla_kv_vp", 0)
        sq_hist = getattr(c, "_gla_sq", {})
        for ch in range(4):
            o_evac = {}
            for h in range(16):
                m = h // 2
                r0 = (h % 2) * 64
                qs = fmA[m // 4][r0:r0 + 64, (m % 4) * 512 + ch * 128:(m % 4) * 512 + ch * 128 + 128]
                ks = fmA[2 + m // 4][r0:r0 + 64, (m % 4) * 512 + ch * 128:(m % 4) * 512 + ch * 128 + 128]
                vs = fmA[4 + ch][:, h * 128:(h + 1) * 128]
                Sst_h = c.Sst[r0:r0 + 64, (h // 2) * 128:(h // 2) * 128 + 128]
                Stmp_h = c.Stmp[r0:r0 + 64, 0:128]
                Sbf_h = c.Sbf[r0:r0 + 64, (h // 2) * 128:(h // 2) * 128 + 128]
                a_col = c.cols[r0:r0 + 64, 16 + m * 4 + ch:16 + m * 4 + ch + 1]
                STp = PS[h % 2][:, 0:128]
                Op = PS[2 + h % 2][:, 0:128]
                KVp = PS[4][r0:r0 + 64, 0:128]
                kTp = c.PT[0][:, (h % 2) * 256:(h % 2) * 256 + 64]
                stm = wb[5][:, 1024 + (h % 2) * 128:1024 + (h % 2) * 128 + 128]
                kT_sb = wb[5][:, 1536 + (h % 2) * 64:1536 + (h % 2) * 64 + 64]

                Tn.wait(vp, v_ready)
                if h >= 2:
                    Tn.wait(vp, o_evac.get(h - 2, 0))
                Tn.wait(sp, sq_hist.get((ch, h - 2), sq_hist.get((ch - 1, h + 14), 0)))
                Tn.mm(STp, ks, qs, start=True, stop=True)
                it1 = tp.inc(Tn.last)
                V.wait(tp, it1)
                V.tt(stm, STp, c.smask[:], OP.mult)
                iv1 = vp.inc(V.last)
                Tn.wait(vp, iv1)
                Tn.wait(vp, prev_state_vp.get(h, 0))
                Tn.mm(Op, stm, vs, start=True, stop=False)
                Tn.mm(Op, qs, Sbf_h, start=False, stop=True)
                it2 = tp.inc(Tn.last)
                Tn.trans(kTp, ks, c.sid[r0:r0 + 64, r0:r0 + 64])
                it3 = tp.inc(Tn.last)
                V.wait(tp, it3)
                V.copy(kT_sb, kTp)
                iv2 = vp.inc(V.last)
                Tn.wait(vp, iv2)
                Tn.wait(vp, kv_read_vp)
                Tn.mm(KVp, kT_sb, vs, start=True, stop=True)
                it4 = tp.inc(Tn.last)
                V.wait(tp, it4)
                V.wait(sp, sp_a)
                V.tt(Stmp_h, Sst_h, KVp, OP.add)
                V.ts(Sst_h, Stmp_h, a_col, None, OP.mult)
                V.copy(Sbf_h, Sst_h)
                prev_state_vp[h] = kv_read_vp = vp.inc(V.last)
                V.wait(tp, it2)
                V.copy(fmA[12 + ch][:, h * 128:(h + 1) * 128], Op)
                V.red(c.stat[:, 32 + h:33 + h], Op, OP.add)
                o_evac[h] = vp.inc(V.last)
                S.wait(tp, it2)
                S.act(c.dumb[:, 0:128], Op, AF.Square, accum_out=c.stat[:, 48 + h:49 + h])
                sq_hist[(ch, h)] = sp.inc(S.last)
            V.wait(sp, sp.n)
            V.ts(c.stat[:, 0:16], c.stat[:, 32:48], 1.0 / 128.0, None, OP.mult)
            V.ts(c.cols[:, 48:64], c.stat[:, 48:64], 1.0 / 128.0, None, OP.mult)
            V.tt(c.cols[:, 64:80], c.stat[:, 0:16], c.stat[:, 0:16], OP.mult)
            V.tt(c.cols[:, 80:96], c.cols[:, 48:64], c.cols[:, 64:80], OP.subtract)
            ivs = vp.inc(V.last)
            S.wait(vp, ivs)
            S.act(c.cols[:, 96:112], c.cols[:, 80:96], AF.Ln, bias=LN_EPS)
            S.act(c.stat[:, 16:32], c.cols[:, 96:112], AF.Exp, scale=-0.5)
            S.act(f32a[0][:], fmA[8 + ch][:], AF.Silu)
            isp = sp.inc(S.last)
            V.wait(sp, isp)
            for h in range(16):
                V.ts(fmA[8 + ch][:, h * 128:(h + 1) * 128], fmA[12 + ch][:, h * 128:(h + 1) * 128],
                     c.stat[:, 0 + h:1 + h], c.stat[:, 16 + h:17 + h], OP.subtract, OP.mult)
            for h in range(16):
                V.tt(fmA[12 + ch][:, h * 128:(h + 1) * 128], fmA[8 + ch][:, h * 128:(h + 1) * 128],
                     f32a[0][:, h * 128:(h + 1) * 128], OP.mult)
            vp.inc(V.last)
        c._gla_state_vp = prev_state_vp
        c._gla_kv_vp = kv_read_vp
        c._gla_sq = sq_hist

        z_pairs = []
        for ci in range(4):
            for f in range(16):
                z_pairs.append((fmA[12 + ci][:, f * 128:(f + 1) * 128],
                                fmB[f // 4][:, (f % 4) * 512 + ci * 128:(f % 4) * 512 + ci * 128 + 128]))
        Tn.wait(vp, vp.n)
        _transpose_in(c, z_pairs)
        zfm_ready = vp.n

        for oc in range(4):
            for fg in range(4):
                slot = wb[fg % 2]
                G.wait(tp, ring_rel.get(fg % 2, 0))
                ws = c.wsem[fg % 2]
                iw = ws.inc(G.dma(slot[:], c.gla_vgo_t[2, oc, fg]), 16)
                Tn.wait(ws, iw)
                if fg == 0:
                    Tn.wait(vp, zfm_ready)
                    c.waitb(Tn, [2, 3, 4, 5])
                for fi in range(4):
                    f = fg * 4 + fi
                    for tci in range(4):
                        Tn.mm(PS[2 + tci][:], xfm(f)[:, tci * 128:(tci + 1) * 128],
                              slot[:, fi * 512:(fi + 1) * 512], start=(f == 0), stop=(f == 15))
                ring_rel[fg % 2] = tp.n + 1
                tp.inc(Tn.last)
            itg = tp.n
            V.wait(tp, itg)
            for tci in range(4):
                V.copy(fmA[4 + tci][:, oc * 512:(oc + 1) * 512], PS[2 + tci][:])
            c._wo_evac = vp.inc(V.last)
            for tci in range(4):
                c.relb(2 + tci, vp, c._wo_evac)
        for tci in range(4):
            _ln_write(c, tcs[tci], fmA[4 + tci][:])



FFN0_BLOCKS = None


def _ffn0(c):
    Tn, V, S, G, Y = c.T, c.V, c.S, c.G, c.Y
    tp, vp, sp, ld = c.tp, c.vp, c.sp, c.ld
    PS = c.PS
    fmA, fmB, f32a, wb = c.fmA, c.fmB, c.f32a, c.wb
    ring_rel = {i: tp.n for i in range(5)}

    def xfm(f):
        return fmB[f // 4][:, (f % 4) * 512:(f % 4) * 512 + 512]

    def zsl(jj):
        return fmA[jj // 4][:, (jj % 4) * 512:(jj % 4) * 512 + 512]

    SC1 = 2.0 ** -5.5
    SC2 = 2.0 ** -6.5

    Y.wait(vp, vp.n)
    Y.wait(sp, sp.n)
    Y.wait(tp, tp.n)
    ld.inc(Y.dma(c.cols[:, 0:16], c.su_ug_cols), 16)
    ld.inc(Y.dma(c.cols[:, 16:80], c.wsc_up_cols), 16)
    ld.inc(Y.dma(c.cols[:, 80:144], c.wsc_gate_cols), 16)
    ld.inc(Y.dma(c.stat[:, 0:64], c.su_dp_cols), 16)
    ld.inc(Y.dma(f32a[0][0:1, 0:2048], c.wsc_dp_row), 16)
    V.wait(ld, ld.n)
    V.tt(c.cols[:, 16:80], c.cols[:, 16:80], c.stat[:, 0:64], OP.mult)
    V.ts(c.cols[:, 16:80], c.cols[:, 16:80], SC1, None, OP.mult)
    V.ts(c.cols[:, 80:144], c.cols[:, 80:144], SC1, None, OP.mult)
    V.ts(c.rbf[0:1, 0:2048], f32a[0][0:1, 0:2048], SC2, None, OP.mult)
    ivc = vp.inc(V.last)
    Tn.wait(vp, ivc)
    for oc4 in range(4):
        Tn.mm(PS[2][:], c.soner[0:1, 0:128], c.rbf[0:1, oc4 * 512:(oc4 + 1) * 512], start=True, stop=True)
        itb = tp.inc(Tn.last)
        V.wait(tp, itb)
        V.copy(wb[4][:, oc4 * 512:(oc4 + 1) * 512], PS[2][:])
        ivb = vp.inc(V.last)
        Tn.wait(vp, ivb)

    for sb in range(NSLAB):
        tcs = [sb * 4 + i for i in range(4)]
        pairs, sus = [], []
        for ci in range(4):
            for f in range(16):
                pairs.append((c.res[tcs[ci]][:, f * 128:(f + 1) * 128],
                              fmB[f // 4][:, (f % 4) * 512 + ci * 128:(f % 4) * 512 + ci * 128 + 128]))
                sus.append(c.cols[:, f:f + 1])
        _transpose_in(c, pairs, su_cols=sus)
        Tn.wait(vp, vp.n)
        Tn.wait(sp, sp.n)
        ev = {}
        for f in range(16):
            p = PS[2 + f % 4][:]
            Tn.wait(vp, ev.get(f - 4, 0))
            if f < 4:
                c.waitb(Tn, [2 + f % 4])
            Tn.mm(p, c.sh128[:], xfm(f), start=True, stop=True)
            it = tp.inc(Tn.last)
            V.wait(tp, it)
            V.copy(xfm(f), p)
            ev[f] = vp.inc(V.last)
            c.relb(2 + f % 4, vp, ev[f])
        V.wait(vp, vp.n)
        for s in (1, 2, 4, 8):
            prs = [t0 for t0 in range(16) if not (t0 & s)]
            for t0 in prs:
                V.tt(xfm(t0), xfm(t0), xfm(t0 ^ s), OP.add)
            for t0 in prs:
                V.stt(xfm(t0 ^ s), xfm(t0 ^ s), -2.0, xfm(t0), OP.mult, OP.add)
        xh_ready = vp.inc(V.last)

        for jc in range(4):
            for jg in range(4):
                for pas in range(2):
                    for fg in range(4):
                        slot = wb[fg % 2]
                        G.wait(tp, ring_rel.get(fg % 2, 0))
                        ws = c.wsem[fg % 2]
                        iw = ws.inc(G.dma(slot[:], c.ug_t[pas, jc, jg, fg]), 16)
                        Tn.wait(ws, iw)
                        if fg == 0:
                            Tn.wait(vp, xh_ready)
                            c.waitb(Tn, [2, 3, 4, 5])
                        for fi in range(4):
                            f = fg * 4 + fi
                            for j2 in range(4):
                                Tn.mm(PS[2 + j2][:], slot[:, fi * 512 + j2 * 128:fi * 512 + j2 * 128 + 128],
                                      xfm(f), start=(f == 0), stop=(f == 15))
                        ring_rel[fg % 2] = tp.n + 1
                        tp.inc(Tn.last)
                    itg = tp.n
                    for j2 in range(4):
                        jj = jc * 16 + jg * 4 + j2
                        if pas == 0:
                            V.wait(tp, itg)
                            V.copy(zsl(jj), PS[2 + j2][:])
                        else:
                            S.wait(tp, itg)
                            S.act(f32a[0][:, j2 * 512:(j2 + 1) * 512], PS[2 + j2][:], AF.Silu,
                                  scale=c.cols[:, 80 + jj:81 + jj])
                            isp = sp.inc(S.last)
                            c.relb(2 + j2, sp, isp)
                            V.wait(sp, isp)
                            V.stt(zsl(jj), zsl(jj), c.cols[:, 16 + jj:17 + jj],
                                  f32a[0][:, j2 * 512:(j2 + 1) * 512], OP.mult, OP.mult)
                    iv = vp.inc(V.last)
                    if pas == 0:
                        for j2 in range(4):
                            c.relb(2 + j2, vp, iv)
            Tn.wait(vp, vp.n)
            ev2 = {}
            for jt in range(16):
                jj = jc * 16 + jt
                p = PS[2 + jt % 4][:]
                Tn.wait(vp, ev2.get(jt - 4, 0))
                if jt < 4:
                    c.waitb(Tn, [2 + jt % 4])
                Tn.mm(p, c.sh128[:], zsl(jj), start=True, stop=True)
                it = tp.inc(Tn.last)
                V.wait(tp, it)
                V.copy(zsl(jj), p)
                ev2[jt] = vp.inc(V.last)
                c.relb(2 + jt % 4, vp, ev2[jt])
            V.wait(vp, vp.n)
            for s in (1, 2, 4, 8):
                prs = [t0 for t0 in range(16) if not (t0 & s)]
                for t0 in prs:
                    V.tt(zsl(jc * 16 + t0), zsl(jc * 16 + t0), zsl(jc * 16 + (t0 ^ s)), OP.add)
                for t0 in prs:
                    V.stt(zsl(jc * 16 + (t0 ^ s)), zsl(jc * 16 + (t0 ^ s)), -2.0,
                          zsl(jc * 16 + t0), OP.mult, OP.add)
            vp.inc(V.last)
        for s in (16, 32):
            prs = [t0 for t0 in range(64) if not (t0 & s)]
            for t0 in prs:
                V.tt(zsl(t0), zsl(t0), zsl(t0 ^ s), OP.add)
            for t0 in prs:
                V.stt(zsl(t0 ^ s), zsl(t0 ^ s), -2.0, zsl(t0), OP.mult, OP.add)
        z_ready = vp.inc(V.last)

        for oc in range(4):
            for jg in range(16):
                slot = wb[2 + jg % 2]
                G.wait(tp, ring_rel.get(2 + jg % 2, 0))
                ws = c.wsem[2 + jg % 2]
                iw = ws.inc(G.dma(slot[:], c.dn_t[oc, jg]), 16)
                Tn.wait(ws, iw)
                if jg == 0:
                    Tn.wait(vp, z_ready)
                    c.waitb(Tn, [2, 3, 4, 5])
                for ji in range(4):
                    j = jg * 4 + ji
                    for tci in range(4):
                        Tn.mm(PS[2 + tci][:], zsl(j)[:, tci * 128:(tci + 1) * 128],
                              slot[:, ji * 512:(ji + 1) * 512], start=(j == 0), stop=(j == 63))
                ring_rel[2 + jg % 2] = tp.n + 1
                tp.inc(Tn.last)
            itg = tp.n
            V.wait(tp, itg)
            for tci in range(4):
                V.tt(fmB[tci][:, oc * 512:(oc + 1) * 512], PS[2 + tci][:],
                     wb[4][:, oc * 512:(oc + 1) * 512], OP.mult)
            iv = vp.inc(V.last)
            for tci in range(4):
                c.relb(2 + tci, vp, iv)
        for tci in range(4):
            _ln_write(c, tcs[tci], fmB[tci][:])


def _mla(c):
    Tn, V, S, G, Y = c.T, c.V, c.S, c.G, c.Y
    tp, vp, sp, ld = c.tp, c.vp, c.sp, c.ld
    PS, PT = c.PS, c.PT
    fmA, fmB, f32a, wb = c.fmA, c.fmB, c.f32a, c.wb
    ring_rel = {i: tp.n for i in range(5)}

    Y.wait(vp, vp.n)
    Y.wait(sp, sp.n)
    Y.wait(tp, tp.n)
    ld.inc(Y.dma(c.bcols[:, 16:20], c.kvd_b), 16)
    ld.inc(Y.dma(c.bcols[:, 20:28], c.qd_b), 16)
    ld.inc(Y.dma(c.bcols[:, 28:44], c.kup_b), 16)
    ld.inc(Y.dma(c.bcols[:, 44:60], c.vup_b), 16)
    ld.inc(Y.dma(c.bcols[0:64, 60:76], c.kr_b), 16)
    ld.inc(Y.dma(c.bcols[:, 76:92], c.qup_b), 16)
    ld.inc(Y.dma(c.cols[0:64, 128:144], c.qr_b), 16)
    G.wait(tp, tp.n)
    G.wait(vp, vp.n)
    c.wd.inc(G.dma(c.rbf[0:1, 0:2048], c.outw_b), 16)
    bias_ld, bias_wd = ld.n, c.wd.n

    def xfm(f):
        return fmB[f // 4][:, (f % 4) * 512:(f % 4) * 512 + 512]

    def kvc(m):
        return fmA[m]

    def tin(xs):
        pairs = []
        for ci in range(4):
            tc = xs * 4 + ci
            for f in range(16):
                pairs.append((c.res[tc][:, f * 128:(f + 1) * 128],
                              fmB[f // 4][:, (f % 4) * 512 + ci * 128:(f % 4) * 512 + ci * 128 + 128]))
        _transpose_in(c, pairs)
        Tn.wait(vp, vp.n)
        Tn.wait(sp, sp.n)

    Tn.wait(ld, bias_ld)
    V.wait(ld, bias_ld)
    V.wait(c.wd, bias_wd)
    for xs in range(4):
        tin(xs)
        for fg in range(4):
            slot = wb[fg % 2]
            G.wait(tp, ring_rel.get(fg % 2, 0))
            ws = c.wsem[fg % 2]
            iw = ws.inc(G.dma(slot[:], c.kvd_t[fg]), 16)
            Tn.wait(ws, iw)
            if fg == 0:
                c.waitb(Tn, [2, 3, 4, 5])
            for fi in range(4):
                f = fg * 4 + fi
                for m in range(4):
                    Tn.mm(PS[2 + m][:], slot[:, fi * 512 + m * 128:fi * 512 + m * 128 + 128],
                          xfm(f), start=(f == 0), stop=(f == 15))
            ring_rel[fg % 2] = tp.n + 1
            tp.inc(Tn.last)
        itg = tp.n
        V.wait(tp, itg)
        for m in range(4):
            V.ts(kvc(m)[:, xs * 512:(xs + 1) * 512], PS[2 + m][:], c.bcols[:, 16 + m:17 + m], None, OP.add)
        iv = vp.inc(V.last)
        for m in range(4):
            c.relb(2 + m, vp, iv)

    c._head_prev = {}
    for qsl in range(2):
        for xs2 in range(2):
            xs = qsl * 2 + xs2
            tin(xs)
            for mg in range(2):
                for fg in range(4):
                    slot = wb[fg % 2]
                    G.wait(tp, ring_rel.get(fg % 2, 0))
                    ws = c.wsem[fg % 2]
                    iw = ws.inc(G.dma(slot[:], c.qd_t[mg, fg]), 16)
                    Tn.wait(ws, iw)
                    if fg == 0:
                        c.waitb(Tn, [2, 3, 4, 5])
                    for fi in range(4):
                        f = fg * 4 + fi
                        for m4 in range(4):
                            Tn.mm(PS[2 + m4][:], slot[:, fi * 512 + m4 * 128:fi * 512 + m4 * 128 + 128],
                                  xfm(f), start=(f == 0), stop=(f == 15))
                    ring_rel[fg % 2] = tp.n + 1
                    tp.inc(Tn.last)
                itg = tp.n
                V.wait(tp, itg)
                for m4 in range(4):
                    m = mg * 4 + m4
                    dst = fmA[4 + m // 2][:, (m % 2) * 1024 + xs2 * 512:(m % 2) * 1024 + xs2 * 512 + 512]
                    V.ts(dst, PS[2 + m4][:], c.bcols[:, 20 + m:21 + m], None, OP.add)
                iv = vp.inc(V.last)
                for m4 in range(4):
                    c.relb(2 + m4, vp, iv)

        for h in range(16):
            ws = c.wsem[4]
            G.wait(tp, ring_rel.get(4, 0))
            ws.inc(G.dma(wb[4][:, 0:512], c.kup_t[h]), 16)
            ws.inc(G.dma(wb[4][:, 512:1024], c.vup_t[h]), 16)
            ws.inc(G.dma(wb[4][:, 1024:1280], c.kr_t[h]), 16)
            ws.inc(G.dma(wb[5][:, 0:1024], c.qup_t[h]), 16)
            iw = ws.inc(G.dma(wb[5][:, 1024:1536], c.qr_t[h]), 16)
            Tn.wait(ws, iw)
            ev = {}
            for tkc in range(4):
                p = PS[2 + tkc % 4][:]
                c.waitb(Tn, [2 + tkc % 4])
                for kf in range(4):
                    Tn.mm(p, wb[4][:, kf * 128:(kf + 1) * 128], kvc(kf)[:, tkc * 512:(tkc + 1) * 512],
                          start=(kf == 0), stop=(kf == 3))
                it = tp.inc(Tn.last)
                V.wait(tp, it)
                V.ts(fmB[0][:, tkc * 512:(tkc + 1) * 512], p, c.bcols[:, 28 + h:29 + h], None, OP.add)
                ev[tkc] = vp.inc(V.last)
                c.relb(2 + tkc % 4, vp, ev[tkc])
            for tkc in range(4):
                p = PS[2 + tkc % 4][0:64, :]
                c.waitb(Tn, [2 + tkc % 4])
                for kf in range(4):
                    Tn.mm(p, wb[4][:, 1024 + kf * 64:1024 + (kf + 1) * 64],
                          kvc(kf)[:, tkc * 512:(tkc + 1) * 512], start=(kf == 0), stop=(kf == 3))
                it = tp.inc(Tn.last)
                V.wait(tp, it)
                V.ts(fmB[1][0:64, tkc * 512:(tkc + 1) * 512], p, c.bcols[0:64, 60 + h:61 + h], None, OP.add)
                ev[100 + tkc] = vp.inc(V.last)
                c.relb(2 + tkc % 4, vp, ev[100 + tkc])
            for tk16 in range(16):
                p = PS[2 + tk16 % 4][:, 0:128]
                if tk16 >= 4:
                    Tn.wait(vp, ev[200 + tk16 - 4])
                elif tk16 == 0:
                    c.waitb(Tn, [2, 3, 4, 5])
                    Tn.wait(vp, c._head_prev.get("v", 0))
                for kf in range(4):
                    Tn.mm(p, kvc(kf)[:, tk16 * 128:(tk16 + 1) * 128],
                          wb[4][:, 512 + kf * 128:512 + (kf + 1) * 128], start=(kf == 0), stop=(kf == 3))
                it = tp.inc(Tn.last)
                V.wait(tp, it)
                V.ts(fmB[2][:, tk16 * 128:(tk16 + 1) * 128], p, c.bcols[:, 44 + h:45 + h], None, OP.add)
                ev[200 + tk16] = vp.inc(V.last)
            for b4 in range(4):
                c.relb(2 + b4, vp, ev[200 + 12 + b4])
            for qn in range(2):
                p = PS[2 + qn][:]
                c.waitb(Tn, [2 + qn])
                for qf in range(8):
                    rhs = fmA[4 + qf // 2][:, (qf % 2) * 1024 + qn * 512:(qf % 2) * 1024 + qn * 512 + 512]
                    Tn.mm(p, wb[5][:, qf * 128:(qf + 1) * 128], rhs, start=(qf == 0), stop=(qf == 7))
                it = tp.inc(Tn.last)
                V.wait(tp, it)
                V.ts(fmB[3][:, qn * 512:(qn + 1) * 512], p, c.bcols[:, 76 + h:77 + h], None, OP.add)
                ev[300 + qn] = vp.inc(V.last)
                c.relb(2 + qn, vp, ev[300 + qn])
            for qn in range(2):
                p = PS[2 + qn][0:64, :]
                c.waitb(Tn, [2 + qn])
                for qf in range(8):
                    rhs = fmA[4 + qf // 2][:, (qf % 2) * 1024 + qn * 512:(qf % 2) * 1024 + qn * 512 + 512]
                    Tn.mm(p, wb[5][:, 1024 + qf * 64:1024 + (qf + 1) * 64], rhs, start=(qf == 0), stop=(qf == 7))
                it = tp.inc(Tn.last)
                V.wait(tp, it)
                V.ts(fmB[3][0:64, 1024 + qn * 512:1024 + (qn + 1) * 512], p,
                     c.cols[0:64, 128 + h:129 + h], None, OP.add)
                ev[400 + qn] = vp.inc(V.last)
                c.relb(2 + qn, vp, ev[400 + qn])
            ring_rel[4] = tp.n
            c._head_prev = {"v": ev[200 + 15]}

            Tn.wait(vp, vp.n)
            for kc4 in range(4):
                p = PS[2 + kc4][:]
                Tn.mm(p, fmB[3][:, 0:128], fmB[0][:, kc4 * 512:(kc4 + 1) * 512], start=True, stop=False)
                Tn.mm(p, fmB[3][0:64, 1024:1152], fmB[1][0:64, kc4 * 512:(kc4 + 1) * 512],
                      start=False, stop=True)
            it = tp.inc(Tn.last)
            V.wait(tp, it)
            for kc4 in range(4):
                V.red(c.stat[:, 20 + kc4:21 + kc4], PS[2 + kc4][:], OP.max)
            V.tt(c.stat[:, 20:21], c.stat[:, 20:21], c.stat[:, 21:22], OP.max)
            V.tt(c.stat[:, 22:23], c.stat[:, 22:23], c.stat[:, 23:24], OP.max)
            V.tt(c.stat[:, 20:21], c.stat[:, 20:21], c.stat[:, 22:23], OP.max)
            V.copy(wb[5][:, 1536:1537], c.stat[:, 20:21])
            ivm = vp.inc(V.last)
            Tn.wait(vp, ivm)
            Tn.trans(PT[0][0:1, 0:128], wb[5][:, 1536:1537], c.sid[:])
            it = tp.inc(Tn.last)
            V.wait(tp, it)
            V.red(c.stat[0:1, 24:25], PT[0][0:1, 0:128], OP.max)
            ivm = vp.inc(V.last)
            Tn.wait(vp, ivm)
            Tn.mm(PS[5][:, 0:1], c.sonerf[:], c.stat[0:1, 24:25], start=True, stop=True)
            it = tp.inc(Tn.last)
            V.wait(tp, it)
            V.ts(c.cols[:, 144:145], PS[5][:, 0:1], -MLA_SCALE, None, OP.mult)
            ebias_vp = vp.inc(V.last)
            S.wait(vp, ebias_vp)

            for qn in range(2):
                pexp = {}
                Tn.wait(vp, vp.n)
                c.waitb(Tn, [0, 1, 2, 3, 4, 5])
                for kc in range(16):
                    STp = PS[0 + kc % 2][:]
                    if kc >= 2:
                        Tn.wait(sp, pexp[kc - 2])
                    Tn.mm(STp, fmB[0][:, kc * 128:(kc + 1) * 128], fmB[3][:, qn * 512:(qn + 1) * 512],
                          start=True, stop=False)
                    Tn.mm(STp, fmB[1][0:64, kc * 128:(kc + 1) * 128],
                          fmB[3][0:64, 1024 + qn * 512:1024 + (qn + 1) * 512], start=False, stop=True)
                    it = tp.inc(Tn.last)
                    S.wait(tp, it)
                    pslot = wb[0][:, (kc % 2) * 512:(kc % 2) * 512 + 512]
                    S.act(pslot, STp, AF.Exp, scale=MLA_SCALE, bias=c.cols[:, 144:145])
                    pexp[kc] = sp.inc(S.last)
                    Tn.wait(sp, pexp[kc])
                    Tn.mm(PS[2][:], fmB[2][:, kc * 128:(kc + 1) * 128], pslot, start=(kc == 0), stop=(kc == 15))
                    Tn.mm(PS[3][0:1, :], c.sonec[:], pslot, start=(kc == 0), stop=(kc == 15))
                    tp.inc(Tn.last)
                it = tp.n
                V.wait(tp, it)
                V.recip(c.rows[0:1, 0:512], PS[3][0:1, :])
                ivr = vp.inc(V.last)
                Tn.wait(vp, ivr)
                Tn.mm(PS[4][:], c.sonerf[:], c.rows[0:1, 0:512], start=True, stop=True)
                it = tp.inc(Tn.last)
                V.wait(tp, it)
                V.copy(f32a[0][:, 0:512], PS[4][:])
                V.tt(fmA[8 + h // 2][:, (h % 2) * 1024 + qn * 512:(h % 2) * 1024 + qn * 512 + 512],
                     PS[2][:], f32a[0][:, 0:512], OP.mult)
                iv = vp.inc(V.last)
                c.relb(2, vp, iv)
                c.relb(4, vp, iv)

        c._ctx_ready = vp.n
        ring_rel[0] = ring_rel[1] = tp.n
        for tcg in range(2):
            for oc in range(4):
                for hg in range(4):
                    slot = wb[2 + hg % 2]
                    G.wait(tp, ring_rel.get(2 + hg % 2, 0))
                    ws = c.wsem[2 + hg % 2]
                    iw = ws.inc(G.dma(slot[:], c.outw_t[oc, hg]), 16)
                    Tn.wait(ws, iw)
                    if hg == 0:
                        Tn.wait(vp, c._ctx_ready)
                        c.waitb(Tn, [2, 3, 4, 5])
                    for hi in range(4):
                        h = hg * 4 + hi
                        for tci in range(4):
                            tq = tcg * 4 + tci
                            lhs = fmA[8 + h // 2][:, (h % 2) * 1024 + tq * 128:(h % 2) * 1024 + tq * 128 + 128]
                            Tn.mm(PS[2 + tci][:], lhs, slot[:, hi * 512:(hi + 1) * 512],
                                  start=(h == 0), stop=False)
                    ring_rel[2 + hg % 2] = tp.n + 1
                    tp.inc(Tn.last)
                for tci in range(4):
                    Tn.mm(PS[2 + tci][:], c.soner[0:1, 0:128], c.rbf[0:1, oc * 512:(oc + 1) * 512],
                          start=False, stop=True)
                itg = tp.inc(Tn.last)
                V.wait(tp, itg)
                for tci in range(4):
                    V.copy(fmB[tci][:, oc * 512:(oc + 1) * 512], PS[2 + tci][:])
                iv = vp.inc(V.last)
                for tci in range(4):
                    c.relb(2 + tci, vp, iv)
            for tci in range(4):
                tc = qsl * 8 + tcg * 4 + tci
                _ln_write(c, tc, fmB[tci][:])


def _ffn1(c):
    Tn, V, S, G, Y = c.T, c.V, c.S, c.G, c.Y
    tp, vp, sp, gp, ld = c.tp, c.vp, c.sp, c.gp, c.ld
    PS, PT = c.PS, c.PT
    fmA, fmB, f32a, wb = c.fmA, c.fmB, c.f32a, c.wb
    ring_rel = {i: tp.n for i in range(5)}
    G.wait(tp, tp.n)
    G.wait(vp, vp.n)
    G.wait(sp, sp.n)

    def fsl(i, j, w=512):
        return fmA[i][:, j * w:(j + 1) * w]

    for jc in range(16):
        ws = c.wsem[4]
        G.wait(gp, ring_rel.get("q1", 0))
        for f in range(16):
            ws.inc(G.dma(fsl(f // 4, f % 4), c.w1_t[jc, f]), 16)
        iw = ws.n
        S.wait(c.wsem[4], iw)
        Tn.wait(c.wsem[4], iw)
        mmv = {}
        c.waitb(Tn, [0, 1])
        for f in range(16):
            if f >= 4:
                S.wait(tp, mmv[f - 4])
            S.act(wb[4][:, (f % 4) * 512:(f % 4) * 512 + 512], fsl(f // 4, f % 4), AF.Abs, scale=2.0)
            isp = sp.inc(S.last)
            Tn.wait(sp, isp)
            Tn.mm(PS[0][0:1, :], c.sonec[:], wb[4][:, (f % 4) * 512:(f % 4) * 512 + 512],
                  start=(f == 0), stop=(f == 15))
            mmv[f] = tp.inc(Tn.last)
        it = tp.n
        V.wait(tp, it)
        V.ts(c.rows[0:1, 0:512], PS[0][0:1, :], 1.0 / 4096.0, 1e-5, OP.mult, OP.max)
        ivr = vp.inc(V.last)
        c.relb(0, vp, ivr)
        Tn.wait(vp, ivr)
        Tn.mm(PS[1][:], c.sonerf[:], c.rows[0:1, 0:512], start=True, stop=True)
        it = tp.inc(Tn.last)
        V.wait(tp, it)
        V.copy(f32a[0][:, 0:512], PS[1][:])
        iv = vp.inc(V.last)
        c.relb(1, vp, iv)
        for f in range(16):
            ab = wb[4][:, (f % 4) * 512:(f % 4) * 512 + 512]
            S.wait(vp, vp.n)
            S.act(ab, fsl(f // 4, f % 4), AF.Abs, scale=2.0)
            S.act(wb[5][:, (f % 4) * 512:(f % 4) * 512 + 512], fsl(f // 4, f % 4), AF.Sign)
            isp = sp.inc(S.last)
            V.wait(sp, isp)
            V.tt(ab, ab, f32a[0][:, 0:512], OP.is_ge)
            V.tt(ab, ab, wb[5][:, (f % 4) * 512:(f % 4) * 512 + 512], OP.mult)
            V.tt(fsl(f // 4, f % 4), ab, f32a[0][:, 0:512], OP.mult)
            ivq = vp.inc(V.last)
            Y.wait(vp, ivq)
            gp.inc(Y.dma(c.w1q_hbm[jc, f], fsl(f // 4, f % 4)), 16)
        ring_rel["q1"] = gp.n
    for osr in range(4):
        ws = c.wsem[4]
        G.wait(gp, gp.n)
        for j in range(64):
            ws.inc(G.dma(fsl(j // 4, j % 4), c.w2_t[osr, j]), 16)
        iw = ws.n
        S.wait(c.wsem[4], iw)
        Tn.wait(c.wsem[4], iw)
        mmv = {}
        c.waitb(Tn, [0, 1])
        for j in range(64):
            if j >= 4:
                S.wait(tp, mmv[j - 4])
            S.act(wb[4][:, (j % 4) * 512:(j % 4) * 512 + 512], fsl(j // 4, j % 4), AF.Abs, scale=2.0)
            isp = sp.inc(S.last)
            Tn.wait(sp, isp)
            Tn.mm(PS[0][0:1, :], c.sonec[:], wb[4][:, (j % 4) * 512:(j % 4) * 512 + 512],
                  start=(j == 0), stop=(j == 63))
            mmv[j] = tp.inc(Tn.last)
        it = tp.n
        V.wait(tp, it)
        V.ts(c.rows[0:1, 0:512], PS[0][0:1, :], 1.0 / 16384.0, 1e-5, OP.mult, OP.max)
        ivr = vp.inc(V.last)
        c.relb(0, vp, ivr)
        Tn.wait(vp, ivr)
        Tn.mm(PS[1][:], c.sonerf[:], c.rows[0:1, 0:512], start=True, stop=True)
        it = tp.inc(Tn.last)
        V.wait(tp, it)
        V.copy(f32a[0][:, 0:512], PS[1][:])
        iv = vp.inc(V.last)
        c.relb(1, vp, iv)
        for j in range(64):
            ab = wb[4][:, (j % 4) * 512:(j % 4) * 512 + 512]
            S.wait(vp, vp.n)
            S.act(ab, fsl(j // 4, j % 4), AF.Abs, scale=2.0)
            S.act(wb[5][:, (j % 4) * 512:(j % 4) * 512 + 512], fsl(j // 4, j % 4), AF.Sign)
            isp = sp.inc(S.last)
            V.wait(sp, isp)
            V.tt(ab, ab, f32a[0][:, 0:512], OP.is_ge)
            V.tt(ab, ab, wb[5][:, (j % 4) * 512:(j % 4) * 512 + 512], OP.mult)
            V.tt(fsl(j // 4, j % 4), ab, f32a[0][:, 0:512], OP.mult)
            ivq = vp.inc(V.last)
            Y.wait(vp, ivq)
            gp.inc(Y.dma(c.w2q_hbm[osr, j], fsl(j // 4, j % 4)), 16)
    stash_done = gp.n
    G.wait(gp, stash_done)

    perm = {}
    for tc in range(4):
        for jb in range(64):
            s_id = (tc * 4 + jb // 16) * 16 + (jb % 16)
            d_id = (jb // 4) * 16 + (jb % 4) * 4 + tc
            perm[s_id] = d_id
    cycles = []
    seen = set()
    for s0 in range(256):
        if s0 in seen:
            continue
        cyc = [s0]
        seen.add(s0)
        nxt = perm[s0]
        while nxt != s0:
            cyc.append(nxt)
            seen.add(nxt)
            nxt = perm[nxt]
        cycles.append(cyc)

    def slot_ap(sid):
        return fmA[sid // 16][:, (sid % 16) * 128:(sid % 16) * 128 + 128]

    V.wait(gp, stash_done)
    for sb in range(NSLAB):
        tcs = [sb * 4 + i for i in range(4)]
        pairs = []
        for ci, tc in enumerate(tcs):
            V.wait(tp, tp.n)
            V.red(c.stat[:, 30:31], c.res[tc][:], OP.max, absval=True)
            V.ts(c.stat[:, 31:32], c.stat[:, 30:31], 1e-5, None, OP.max)
            V.recip(c.stat[:, 32:33], c.stat[:, 31:32])
            V.ts(c.cols[:, 32 + ci:33 + ci], c.stat[:, 32:33], 127.0, None, OP.mult)
            V.ts(c.cols[:, 36 + ci:37 + ci], c.stat[:, 31:32], 1.0 / 127.0, None, OP.mult)
            sx = c.cols[:, 32 + ci:33 + ci]
            for half in range(2):
                hs = slice(half * 1024, half * 1024 + 1024)
                v_ = f32a[0][:, 0:1024]
                V.ts(v_, c.res[tc][:, hs], sx, None, OP.mult)
                V.ts(f32a[0][:, 1024:2048], v_, M52, C52_32, OP.add, OP.subtract)
                V.ts(f32a[0][:, 1024:2048], f32a[0][:, 1024:2048], M32, None, OP.subtract)
                V.ts(f32a[1][:, 0:1024], v_, M23, M23, OP.add, OP.subtract)
                V.ts(f32a[1][:, 1024:2048], v_, M32, M32, OP.add, OP.subtract)
                V.tt(f32a[1][:, 0:1024], f32a[1][:, 0:1024], f32a[1][:, 1024:2048], OP.subtract)
                V.tt(fmA[15][:, hs], f32a[0][:, 1024:2048], f32a[1][:, 0:1024], OP.add)
            iv = vp.inc(V.last)
            for f in range(16):
                pairs.append((fmA[15][:, f * 128:(f + 1) * 128],
                              fmB[f // 4][:, (f % 4) * 512 + ci * 128:(f % 4) * 512 + ci * 128 + 128]))
            _transpose_in(c, pairs)
            pairs = []
        xq_ready = vp.n

        for jc in range(16):
            for g2 in range(4):
                slot = wb[g2 % 2]
                G.wait(tp, ring_rel.get(g2 % 2, 0))
                ws = c.wsem[g2 % 2]
                for fi4 in range(4):
                    ws.inc(G.dma(slot[:, fi4 * 512:(fi4 + 1) * 512], c.w1q_hbm[jc, g2 * 4 + fi4]), 16)
                iw = ws.n
                Tn.wait(ws, iw)
                if g2 == 0:
                    Tn.wait(vp, xq_ready)
                    c.waitb(Tn, [2, 3, 4, 5])
                for fi in range(4):
                    f = g2 * 4 + fi
                    for tci in range(4):
                        Tn.mm(PS[2 + tci][:], fmB[f // 4][:, (f % 4) * 512 + tci * 128:(f % 4) * 512 + tci * 128 + 128],
                              slot[:, fi * 512:(fi + 1) * 512], start=(f == 0), stop=(f == 15))
                ring_rel[g2 % 2] = tp.n + 1
                tp.inc(Tn.last)
            itg = tp.n
            S.wait(tp, itg)
            for tci in range(4):
                S.act(fmA[tci * 4 + jc // 4][:, (jc % 4) * 512:(jc % 4) * 512 + 512], PS[2 + tci][:],
                      AF.Gelu_apprx_tanh, scale=c.cols[:, 36 + tci:37 + tci])
            isp = sp.inc(S.last)
            for tci in range(4):
                c.relb(2 + tci, sp, isp)
        h_done = sp.n

        V.wait(sp, h_done)
        for tci in range(4):
            for q in range(4):
                V.red(c.stat[:, 40 + q:41 + q], fmA[tci * 4 + q][:], OP.max, absval=True)
            V.tt(c.stat[:, 40:41], c.stat[:, 40:41], c.stat[:, 41:42], OP.max)
            V.tt(c.stat[:, 42:43], c.stat[:, 42:43], c.stat[:, 43:44], OP.max)
            V.tt(c.stat[:, 40:41], c.stat[:, 40:41], c.stat[:, 42:43], OP.max)
            V.ts(c.stat[:, 44:45], c.stat[:, 40:41], 1e-5, None, OP.max)
            V.recip(c.stat[:, 45:46], c.stat[:, 44:45])
            V.ts(c.cols[:, 40 + tci:41 + tci], c.stat[:, 45:46], 127.0, None, OP.mult)
            V.ts(c.cols[:, 44 + tci:45 + tci], c.stat[:, 44:45], 1.0 / 127.0, None, OP.mult)
            sh = c.cols[:, 40 + tci:41 + tci]
            for q in range(4):
                for half in range(2):
                    hh = slice(half * 1024, half * 1024 + 1024)
                    v_ = f32a[0][:, 0:1024]
                    V.ts(v_, fmA[tci * 4 + q][:, hh], sh, None, OP.mult)
                    V.ts(f32a[0][:, 1024:2048], v_, M52, C52_32, OP.add, OP.subtract)
                    V.ts(f32a[0][:, 1024:2048], f32a[0][:, 1024:2048], M32, None, OP.subtract)
                    V.ts(f32a[1][:, 0:1024], v_, M23, M23, OP.add, OP.subtract)
                    V.ts(f32a[1][:, 1024:2048], v_, M32, M32, OP.add, OP.subtract)
                    V.tt(f32a[1][:, 0:1024], f32a[1][:, 0:1024], f32a[1][:, 1024:2048], OP.subtract)
                    V.tt(fmA[tci * 4 + q][:, hh], f32a[0][:, 1024:2048], f32a[1][:, 0:1024], OP.add)
        nh_done = vp.inc(V.last)

        Tn.wait(vp, nh_done)
        Tn.wait(sp, sp.n)
        lagged = []
        hist = {}
        n = 0
        vp0 = vp.n
        for cyc in cycles:
            for i in range(len(cyc)):
                s_id = cyc[i]
                pt = PT[n % 2][:, ((n // 2) % 4) * 256:((n // 2) % 4) * 256 + 128]
                Tn.wait(vp, hist[n - 2] if n >= 2 else vp0)
                Tn.trans(pt, slot_ap(s_id), c.sid[:])
                it = tp.inc(Tn.last)
                lagged.append((it, pt, perm[s_id]))
                if len(lagged) > 1:
                    it2, pt2, d2 = lagged.pop(0)
                    V.wait(tp, max(it2, it))
                    V.copy(slot_ap(d2), pt2)
                    hist[n - 1] = vp.inc(V.last)
                n += 1
            it2, pt2, d2 = lagged.pop(0)
            V.wait(tp, it2)
            V.copy(slot_ap(d2), pt2)
            hist[n - 1] = vp.inc(V.last)
        hq_ready = vp.n

        for oc in range(4):
            for jg in range(16):
                slot = wb[2 + jg % 2]
                G.wait(tp, ring_rel.get(2 + jg % 2, 0))
                ws = c.wsem[2 + jg % 2]
                for ji4 in range(4):
                    ws.inc(G.dma(slot[:, ji4 * 512:(ji4 + 1) * 512], c.w2q_hbm[oc, jg * 4 + ji4]), 16)
                iw = ws.n
                Tn.wait(ws, iw)
                if jg == 0:
                    Tn.wait(vp, hq_ready)
                    c.waitb(Tn, [2, 3, 4, 5])
                for ji in range(4):
                    j = jg * 4 + ji
                    for tci in range(4):
                        lhs = fmA[j // 4][:, (j % 4) * 512 + tci * 128:(j % 4) * 512 + tci * 128 + 128]
                        Tn.mm(PS[2 + tci][:], lhs, slot[:, ji * 512:(ji + 1) * 512],
                              start=(j == 0), stop=(j == 63))
                ring_rel[2 + jg % 2] = tp.n + 1
                tp.inc(Tn.last)
            itg = tp.n
            V.wait(tp, itg)
            for tci in range(4):
                V.ts(fmB[tci][:, oc * 512:(oc + 1) * 512], PS[2 + tci][:],
                     c.cols[:, 44 + tci:45 + tci], None, OP.mult)
            iv = vp.inc(V.last)
            for tci in range(4):
                c.relb(2 + tci, vp, iv)
        for tci in range(4):
            _ln_write(c, tcs[tci], fmB[tci][:])



def _bf(x):
    return np.ascontiguousarray(x, dtype=ml_dtypes.bfloat16)


def _f32(x):
    return np.ascontiguousarray(x, dtype=np.float32)


def _tile_fm(wT, groups):
    raise NotImplementedError


def _prep_gla_qk(wT):
    out = np.empty((2, 4, 128, 2048), np.float32)
    for mg in range(2):
        for fg in range(4):
            for fi in range(4):
                f = fg * 4 + fi
                blk = wT[f * 128:(f + 1) * 128, mg * 512:(mg + 1) * 512]
                out[mg, fg, :, fi * 512:(fi + 1) * 512] = blk
    return out


def _prep_gla_vgo(wT):
    out = np.empty((4, 4, 128, 2048), np.float32)
    for oc in range(4):
        for fg in range(4):
            for fi in range(4):
                f = fg * 4 + fi
                out[oc, fg, :, fi * 512:(fi + 1) * 512] = wT[f * 128:(f + 1) * 128, oc * 512:(oc + 1) * 512]
    return out


def prep_inputs(inputs):
    g = {}
    g["c_ident"] = np.eye(128, dtype=ml_dtypes.bfloat16)
    g["c_mask"] = _bf(np.tril(np.ones((128, 128), np.float32)).T)
    H = np.array([[1.0]], np.float32)
    while H.shape[0] < 128:
        H = np.block([[H, H], [H, -H]])
    g["c_h128"] = _bf(H)
    g["c_onesrow"] = np.ones((1, 128), ml_dtypes.bfloat16)
    g["c_onescol"] = np.ones((128, 1), ml_dtypes.bfloat16)
    g["c_onesrowf"] = np.ones((1, 128), np.float32)

    wqT = _f32(inputs["L0_attn_Wq"].T)
    wkT = _f32(inputs["L0_attn_Wk"].T)
    g["gla_qk_t"] = np.stack([_prep_gla_qk(wqT), _prep_gla_qk(wkT)])
    wvT = _f32(inputs["L0_attn_Wv"].T)
    wgT = _f32(inputs["L0_attn_Wg"].T)
    woT = _f32(inputs["L0_attn_Wo"].T)
    g["gla_vgo_t"] = np.stack([_prep_gla_vgo(wvT), _prep_gla_vgo(wgT), _prep_gla_vgo(woT)])
    wkg1T = _f32(inputs["L0_attn_Wkg1"].T)
    g["wkg1_t"] = _f32(wkg1T.reshape(16, 128, 16).transpose(1, 0, 2).reshape(128, 256))
    g["wkg2T"] = _f32(inputs["L0_attn_Wkg2"].T)
    g["bkg2_cols"] = _f32(inputs["L0_attn_bkg2"].reshape(8, 128).T)
    g["bg"] = _f32(inputs["L0_attn_bg"].reshape(1, -1))

    wupT = _f32(inputs["L0_ffn_W_up"].T)
    wgateT = _f32(inputs["L0_ffn_W_gate"].T)
    wdownT = _f32(inputs["L0_ffn_W_down"].T)
    ug = np.empty((2, 4, 4, 4, 128, 2048), np.float32)
    for wi, w in enumerate([wupT, wgateT]):
        for jc in range(4):
            for jg in range(4):
                for fg in range(4):
                    for fi in range(4):
                        f = fg * 4 + fi
                        ug[wi, jc, jg, fg, :, fi * 512:(fi + 1) * 512] = \
                            w[f * 128:(f + 1) * 128, jc * 2048 + jg * 512:jc * 2048 + (jg + 1) * 512]
    g["ug_t"] = ug
    dn = np.empty((4, 16, 128, 2048), np.float32)
    for oc in range(4):
        for jg in range(16):
            for ji in range(4):
                j = jg * 4 + ji
                dn[oc, jg, :, ji * 512:(ji + 1) * 512] = \
                    wdownT[j * 128:(j + 1) * 128, oc * 512:(oc + 1) * 512]
    g["dn_t"] = dn
    g["su_ug_cols"] = _f32(inputs["L0_ffn_SU_ug"].reshape(16, 128).T)
    g["su_dp_cols"] = _f32(inputs["L0_ffn_SU_dp"].reshape(64, 128).T)
    wsc = _f32(inputs["L0_ffn_Wscale_ug"])
    g["wsc_up_cols"] = _f32(wsc[:INTER].reshape(64, 128).T)
    g["wsc_gate_cols"] = _f32(wsc[INTER:].reshape(64, 128).T)
    g["wsc_dp_row"] = _f32(inputs["L0_ffn_Wscale_dp"].reshape(1, -1))

    kvdT = _f32(inputs["L1_attn_kv_down_w"].T)
    kvd = np.empty((4, 128, 2048), np.float32)
    for fg in range(4):
        for fi in range(4):
            f = fg * 4 + fi
            kvd[fg, :, fi * 512:(fi + 1) * 512] = kvdT[f * 128:(f + 1) * 128, :]
    g["kvd_t"] = kvd
    qdT = _f32(inputs["L1_attn_query_down_w"].T)
    g["qd_t"] = _prep_gla_qk(qdT)
    kupT = _f32(inputs["L1_attn_key_up_w"].T)
    vupT = _f32(inputs["L1_attn_value_up_w"].T)
    krT = _f32(inputs["L1_attn_key_rope_w"].T)
    kup = np.empty((16, 128, 512), np.float32)
    vup = np.empty((16, 128, 512), np.float32)
    krt = np.empty((16, 128, 256), np.float32)
    for h in range(16):
        for kf in range(4):
            kup[h, :, kf * 128:(kf + 1) * 128] = kupT[kf * 128:(kf + 1) * 128, h * 128:(h + 1) * 128]
            vup[h, :, kf * 128:(kf + 1) * 128] = vupT[kf * 128:(kf + 1) * 128, h * 128:(h + 1) * 128]
            krt[h, :, kf * 64:(kf + 1) * 64] = krT[kf * 128:(kf + 1) * 128, h * 64:(h + 1) * 64]
    g["kup_t"] = kup
    g["vup_t"] = vup
    g["kr_t"] = krt
    qupT = _f32(inputs["L1_attn_query_up_w"].T)
    qup = np.empty((16, 128, 1024), np.float32)
    for h in range(16):
        for qf in range(8):
            qup[h, :, qf * 128:(qf + 1) * 128] = qupT[qf * 128:(qf + 1) * 128, h * 128:(h + 1) * 128]
    g["qup_t"] = qup
    qrT = _f32(inputs["L1_attn_query_rope_w"].T)
    qr = np.empty((16, 128, 512), np.float32)
    for h in range(16):
        for qf in range(8):
            qr[h, :, qf * 64:(qf + 1) * 64] = qrT[qf * 128:(qf + 1) * 128, h * 64:(h + 1) * 64]
    g["qr_t"] = qr
    outwT = _f32(inputs["L1_attn_out_w"].T)
    ow = np.empty((4, 4, 128, 2048), np.float32)
    for oc in range(4):
        for hg in range(4):
            for hi in range(4):
                h = hg * 4 + hi
                ow[oc, hg, :, hi * 512:(hi + 1) * 512] = \
                    outwT[h * 128:(h + 1) * 128, oc * 512:(oc + 1) * 512]
    g["outw_t"] = ow
    for nm, key in [("kvd_b", "L1_attn_kv_down_b"), ("kup_b", "L1_attn_key_up_b"),
                    ("vup_b", "L1_attn_value_up_b"),
                    ("qd_b", "L1_attn_query_down_b"), ("qup_b", "L1_attn_query_up_b")]:
        g[nm] = _f32(inputs[key].reshape(-1, 128).T)
    g["kr_b"] = _f32(inputs["L1_attn_key_rope_b"].reshape(16, 64).T)
    g["qr_b"] = _f32(inputs["L1_attn_query_rope_b"].reshape(16, 64).T)
    g["outw_b"] = _f32(inputs["L1_attn_out_b"].reshape(1, -1))

    w1T = _f32(inputs["L1_ffn_w1"].T)
    w1t = np.empty((16, 16, 128, 512), np.float32)
    for jc in range(16):
        for f in range(16):
            w1t[jc, f] = w1T[f * 128:(f + 1) * 128, jc * 512:(jc + 1) * 512]
    g["w1_t"] = w1t
    w2T = _f32(inputs["L1_ffn_w2"].T)
    w2t = np.empty((4, 64, 128, 512), np.float32)
    for os_ in range(4):
        for j in range(64):
            w2t[os_, j] = w2T[j * 128:(j + 1) * 128, os_ * 512:(os_ + 1) * 512]
    g["w2_t"] = w2t
    return g


def kernel(**inputs):
    g = prep_inputs(inputs)
    x = np.ascontiguousarray(inputs["L0_attn_x"], dtype=np.float32)
    nc = build_program(_UPTO)
    in_maps = []
    for core in range(NCORES):
        m = dict(g)
        m["x_in"] = x[core]
        in_maps.append(m)
    res = run_bass_kernel_spmd(nc, in_maps, core_ids=list(range(NCORES)))
    out = np.stack([np.asarray(res.results[i]["out"], np.float32) for i in range(NCORES)])
    return out


# version 04
# speedup vs baseline: 1.3614x; 1.0373x over previous
import math
from contextlib import ExitStack

import numpy as np
import ml_dtypes

import concourse.bass as bass
import concourse.mybir as mybir
from concourse.bass_utils import run_bass_kernel_spmd

F32 = mybir.dt.float32
BF16 = mybir.dt.bfloat16
AF = mybir.ActivationFunctionType
OP = mybir.AluOpType
AX = mybir.AxisListType

NCORES = 8
T = 2048
D = 2048
NT = 16
SLAB = 512
NSLAB = 4
LN_EPS = 1e-5
GLA_NORM = 16.0
GLA_SCALE_LOG = math.log(128.0 ** (-0.5))
MLA_SCALE = 1.0 / math.sqrt(192.0)
INTER = 8192

M23 = 1.5 * 2.0 ** 23
M52 = 1.5 * 2.0 ** 52
M32 = 1.5 * 2.0 ** 32
C52_32 = M52 - M32

_UPTO = 4



class Eng:

    def __init__(self, e, auto_drain=True):
        self.e = e
        self.auto_drain = auto_drain
        self.dirty = set()
        self.rd = set()
        self.waited = {}

    @staticmethod
    def _names(aps):
        out = []
        for a in aps:
            if a is None or isinstance(a, (int, float)):
                continue
            t = getattr(a, "tensor", None)
            if t is not None:
                out.append(t.name)
        return out

    def _pre(self, outs, ins):
        if not self.auto_drain:
            return
        o = set(self._names(outs))
        i = set(self._names(ins))
        if (i & self.dirty) or (o & self.dirty) or (o & self.rd):
            self.e.drain()
            self.dirty.clear()
            self.rd.clear()
        self.dirty |= o
        self.rd |= i

    def wait(self, sem, val):
        val = int(val)
        if val <= 0 or self.waited.get(sem.name, -1) >= val:
            return
        self.e.wait_ge(sem.h, val)
        self.waited[sem.name] = val

    def _em(self, inst):
        self.last = inst
        return inst

    def dma(self, out, in_):
        self._pre([out], [in_])
        return self._em(self.e.dma_start(out=out, in_=in_))

    def copy(self, out, in_):
        self._pre([out], [in_])
        return self._em(self.e.tensor_copy(out, in_))

    def ts(self, out, in0, s1, s2, op0, op1=None):
        self._pre([out], [in0, s1, s2])
        if op1 is None:
            return self._em(self.e.tensor_scalar(out, in0, s1, None, op0))
        return self._em(self.e.tensor_scalar(out, in0, s1, s2, op0, op1))

    def tt(self, out, in0, in1, op):
        self._pre([out], [in0, in1])
        return self._em(self.e.tensor_tensor(out, in0, in1, op))

    def stt(self, out, in0, s, in1, op0, op1):
        self._pre([out], [in0, s, in1])
        return self._em(self.e.scalar_tensor_tensor(out, in0, s, in1, op0, op1))

    def scan(self, out, d0, d1, init, op0, op1):
        self._pre([out], [d0, d1])
        return self._em(self.e.tensor_tensor_scan(out, d0, d1, init, op0, op1))

    def red(self, out, in_, op, absval=False):
        self._pre([out], [in_])
        return self._em(self.e.tensor_reduce(out, in_, AX.X, op, apply_absolute_value=absval))

    def recip(self, out, in_):
        self._pre([out], [in_])
        return self._em(self.e.reciprocal(out, in_))

    def act(self, out, in_, func, bias=0.0, scale=1.0, accum_out=None):
        outs = [out] + ([accum_out] if accum_out is not None else [])
        self._pre(outs, [in_, bias, scale])
        return self._em(self.e.activation(out, in_, func, bias=bias, scale=scale, accum_out=accum_out))

    def mm(self, out, lhsT, rhs, start, stop):
        self._pre([out], [lhsT, rhs])
        return self._em(self.e.matmul(out, lhsT, rhs, start=start, stop=stop))

    def trans(self, out, in_, ident):
        self._pre([out], [in_, ident])
        return self._em(self.e.transpose(out, in_, ident))

    def memset(self, ap, v):
        self._pre([ap], [])
        return self._em(self.e.memset(ap, v))

    def nop(self):
        return self.e.nop()


class Sem:
    def __init__(self, h, name):
        self.h = h
        self.name = name
        self.n = 0

    def inc(self, inst, by=1):
        inst.then_inc(self.h, by)
        self.n += by
        return self.n


class Ctx:
    pass



def build_program(upto=_UPTO):
    nc = bass.Bass("TRN2", target_bir_lowering=False)
    c = Ctx()
    c.nc = nc

    def din(name, shape, dt=F32):
        return nc.dram_tensor(name, shape, dt, kind="ExternalInput").ap()

    c.x_in = din("x_in", [T, D])
    c.out_hbm = nc.dram_tensor("out", [T, D], F32, kind="ExternalOutput").ap()

    c.ident = din("c_ident", [128, 128], BF16)
    c.maskUT = din("c_mask", [128, 128], BF16)
    c.h128 = din("c_h128", [128, 128], BF16)
    c.onesrow = din("c_onesrow", [1, 128], BF16)
    c.onescol = din("c_onescol", [128, 1], BF16)
    c.onesrowf = din("c_onesrowf", [1, 128])

    c.gla_qk_t = din("gla_qk_t", [2, 2, 4, 128, 2048])
    c.gla_vgo_t = din("gla_vgo_t", [3, 4, 4, 128, 2048])
    c.wkg1_t = din("wkg1_t", [128, 256])
    c.wkg2T = din("wkg2T", [16, 1024])
    c.bkg2_cols = din("bkg2_cols", [128, 8])
    c.bg = din("bg", [1, D])

    c.ug_t = din("ug_t", [2, 4, 4, 4, 128, 2048])
    c.dn_t = din("dn_t", [4, 16, 128, 2048])
    c.su_ug_cols = din("su_ug_cols", [128, 16])
    c.wsc_up_cols = din("wsc_up_cols", [128, 64])
    c.wsc_gate_cols = din("wsc_gate_cols", [128, 64])
    c.su_dp_cols = din("su_dp_cols", [128, 64])
    c.wsc_dp_row = din("wsc_dp_row", [1, D])

    c.kvd_t = din("kvd_t", [4, 128, 2048])
    c.qd_t = din("qd_t", [2, 4, 128, 2048])
    c.kup_t = din("kup_t", [16, 128, 512])
    c.vup_t = din("vup_t", [16, 128, 512])
    c.kr_t = din("kr_t", [16, 128, 256])
    c.qup_t = din("qup_t", [16, 128, 1024])
    c.qr_t = din("qr_t", [16, 128, 512])
    c.outw_t = din("outw_t", [4, 4, 128, 2048])
    c.kvd_b = din("kvd_b", [128, 4])
    c.kup_b = din("kup_b", [128, 16])
    c.vup_b = din("vup_b", [128, 16])
    c.kr_b = din("kr_b", [64, 16])
    c.qd_b = din("qd_b", [128, 8])
    c.qup_b = din("qup_b", [128, 16])
    c.qr_b = din("qr_b", [64, 16])
    c.outw_b = din("outw_b", [1, D])

    c.w1_t = din("w1_t", [16, 16, 128, 512])
    c.w2_t = din("w2_t", [4, 64, 128, 512])
    c.w1q_hbm = nc.dram_tensor("w1q_s", [16, 16, 128, 512], BF16, kind="Internal").ap()
    c.w2q_hbm = nc.dram_tensor("w2q_s", [4, 64, 128, 512], BF16, kind="Internal").ap()

    with ExitStack() as stack:
        e = stack.enter_context
        c.block = e(nc.Block())
        c.T = Eng(nc.tensor, auto_drain=False); c.V = Eng(nc.vector); c.S = Eng(nc.scalar)
        c.G = Eng(nc.gpsimd, auto_drain=False); c.Y = Eng(nc.sync, auto_drain=False)
        for name in ["ld", "wd", "tp", "vp", "sp", "gp", "st", "ini"]:
            setattr(c, name, Sem(e(nc.semaphore(name)), name))
        c.wsem = [Sem(e(nc.semaphore(f"w{i}")), f"w{i}") for i in range(8)]
        c.gsem = [Sem(e(nc.semaphore(f"g{i}")), f"g{i}") for i in range(4)]

        c.res = [e(nc.sbuf_tensor(f"res{i}", [128, D], BF16)) for i in range(NT)]
        c.fmA = [e(nc.sbuf_tensor(f"fmA{i}", [128, D], BF16)) for i in range(16)]
        c.fmB = [e(nc.sbuf_tensor(f"fmB{i}", [128, D], BF16)) for i in range(4)]
        c.f32a = [e(nc.sbuf_tensor(f"f32a{i}", [128, D], F32)) for i in range(2)]
        c.wb = [e(nc.sbuf_tensor(f"wb{i}", [128, D], BF16)) for i in range(6)]
        c.Sst = e(nc.sbuf_tensor("Sst", [128, 1024], F32))
        c.Stmp = e(nc.sbuf_tensor("Stmp", [128, 256], F32))
        c.Sbf = e(nc.sbuf_tensor("Sbf", [128, 1024], BF16))
        c.stat = e(nc.sbuf_tensor("stat", [128, 64], F32))
        c.cols = e(nc.sbuf_tensor("cols", [128, 160], F32))
        c.rows = e(nc.sbuf_tensor("rows", [1, 512], F32))
        c.rbf = e(nc.sbuf_tensor("rbf", [1, 2048], BF16))
        c.bcols = e(nc.sbuf_tensor("bcols", [128, 96], F32))
        c.sid = e(nc.sbuf_tensor("sid", [128, 128], BF16))
        c.smask = e(nc.sbuf_tensor("smask", [128, 128], BF16))
        c.sh128 = e(nc.sbuf_tensor("sh128", [128, 128], BF16))
        c.soner = e(nc.sbuf_tensor("soner", [1, 128], BF16))
        c.sonec = e(nc.sbuf_tensor("sonec", [128, 1], BF16))
        c.sonerf = e(nc.sbuf_tensor("sonerf", [1, 128], F32))
        c.skg2 = e(nc.sbuf_tensor("skg2", [16, 1024], BF16))
        c.outf32 = c.f32a[1]

        c.PS = [e(nc.psum_tensor(f"PS{i}", [128, 512], F32)) for i in range(6)]
        c.PT = [e(nc.psum_tensor(f"PT{i}", [128, 1024], BF16)) for i in range(2)]
        c.dumb = c.f32a[1]
        c.bank_rel = {}

        def _relb(b, sem, val):
            c.bank_rel[b] = (sem, val)
        c.relb = _relb

        def _waitb(E, banks):
            for b in banks:
                if b in c.bank_rel:
                    sem, val = c.bank_rel[b]
                    E.wait(sem, val)
        c.waitb = _waitb

        for i, cv in enumerate([0.0, 1.0, -1.0, 0.5, LN_EPS, GLA_SCALE_LOG]):
            nc.const_aps.register(F32, cv, c.cols[:, 150 + i:151 + i])

        _init(c)
        if upto >= 1:
            _gla(c)
        if upto >= 2:
            _ffn0(c)
        if upto >= 3:
            _mla(c)
        if upto >= 4:
            _ffn1(c)
        _finish(c)

    return nc


def _init(c):
    Tn, V, S, G, Y = c.T, c.V, c.S, c.G, c.Y
    ld, ini = c.ld, c.ini
    ld.inc(Y.dma(c.sid[:], c.ident), 16)
    ld.inc(Y.dma(c.smask[:], c.maskUT), 16)
    ld.inc(Y.dma(c.sh128[:], c.h128), 16)
    ld.inc(Y.dma(c.soner[:], c.onesrow), 16)
    ld.inc(Y.dma(c.sonerf[:], c.onesrowf), 16)
    ld.inc(Y.dma(c.sonec[:], c.onescol), 16)
    for i in range(NT):
        c.wd.inc(G.dma(c.res[i][:], c.x_in[i * 128:(i + 1) * 128, :]), 16)
    for i, cv in enumerate([0.0, 1.0, -1.0, 0.5, LN_EPS, GLA_SCALE_LOG]):
        G.memset(c.cols[:, 150 + i:151 + i], cv)
    c.wd.inc(G.dma(c.skg2[:], c.wkg2T), 16)
    ld.inc(Y.dma(c.bcols[:, 8:16], c.bkg2_cols), 16)
    c.wd.inc(G.dma(c.rbf[:, 0:D], c.bg), 16)
    G.memset(c.Sst[:], 0.0)
    G.memset(c.Sbf[:], 0.0)
    ini.inc(G.last, 1)
    for E in (Tn, V, S, Y):
        E.wait(ini, 1)
        E.wait(ld, ld.n)
        E.wait(c.wd, c.wd.n)
    V.ts(c.bcols[:, 0:8], c.bcols[:, 8:16], -1.0, None, OP.mult)
    c.vp.inc(V.last)
    c.negbkg2 = c.bcols[:, 0:8]
    c.bg_row = c.rbf[:, 0:D]


def _finish(c):
    V, Y = c.V, c.Y
    vp, st = c.vp, c.st
    for tc in range(NT):
        V.wait(st, st.n)
        V.copy(c.outf32[:], c.res[tc][:])
        iv = vp.inc(V.last)
        Y.wait(vp, iv)
        st.inc(Y.dma(c.out_hbm[tc * 128:(tc + 1) * 128, :], c.outf32[:]), 16)
    Y.wait(st, st.n)



def _transpose_in(c, pairs, su_cols=None):
    Tn, V = c.T, c.V
    tp, vp = c.tp, c.vp
    hist = {}
    vp0 = vp.n
    for n, (src, dst) in enumerate(pairs):
        pt = c.PT[n % 2][:, ((n // 2) % 4) * 256:((n // 2) % 4) * 256 + 128]
        Tn.wait(vp, hist[n - 2] if n >= 2 else vp0)
        Tn.trans(pt, src, c.sid[:])
        it = tp.inc(Tn.last)
        V.wait(tp, it)
        if su_cols is not None and su_cols[n] is not None:
            V.ts(dst, pt, su_cols[n], None, OP.mult)
        else:
            V.copy(dst, pt)
        hist[n] = vp.inc(V.last)


def _ln_write(c, tc, src_tm_bf, final=False):
    V, S = c.V, c.S
    vp, sp = c.vp, c.sp
    s = c.f32a[0]
    V.tt(s[:], src_tm_bf, c.res[tc][:], OP.add)
    V.red(c.stat[:, 0:1], s[:], OP.add)
    iv = vp.inc(V.last)
    S.wait(vp, iv)
    for q in range(4):
        S.act(c.dumb[:, 0:512], s[:, q * 512:(q + 1) * 512], AF.Square, accum_out=c.stat[:, 4 + q:5 + q])
    isp = sp.inc(S.last)
    V.wait(sp, isp)
    V.ts(c.stat[:, 1:2], c.stat[:, 0:1], 1.0 / 2048.0, None, OP.mult)
    V.tt(c.stat[:, 8:9], c.stat[:, 4:5], c.stat[:, 5:6], OP.add)
    V.tt(c.stat[:, 9:10], c.stat[:, 6:7], c.stat[:, 7:8], OP.add)
    V.tt(c.stat[:, 10:11], c.stat[:, 8:9], c.stat[:, 9:10], OP.add)
    V.ts(c.stat[:, 11:12], c.stat[:, 10:11], 1.0 / 2048.0, None, OP.mult)
    V.tt(c.stat[:, 12:13], c.stat[:, 1:2], c.stat[:, 1:2], OP.mult)
    V.tt(c.stat[:, 13:14], c.stat[:, 11:12], c.stat[:, 12:13], OP.subtract)
    iv = vp.inc(V.last)
    S.wait(vp, iv)
    S.act(c.stat[:, 14:15], c.stat[:, 13:14], AF.Ln, bias=LN_EPS)
    S.act(c.stat[:, 2:3], c.stat[:, 14:15], AF.Exp, scale=-0.5)
    isp = sp.inc(S.last)
    V.wait(sp, isp)
    dst = c.outf32[:] if final else c.res[tc][:]
    V.ts(dst, s[:], c.stat[:, 1:2], c.stat[:, 2:3], OP.subtract, OP.mult)
    return vp.inc(V.last)



def _gla(c):
    Tn, V, S, G = c.T, c.V, c.S, c.G
    tp, vp, sp, wd = c.tp, c.vp, c.sp, c.wd
    PS = c.PS
    fmA, fmB, f32a, wb = c.fmA, c.fmB, c.f32a, c.wb
    ring_rel = {i: tp.n for i in range(5)}

    def xfm(f):
        return fmB[f // 4][:, (f % 4) * 512:(f % 4) * 512 + 512]

    def gpt(m):
        return f32a[m // 4][:, (m % 4) * 512:(m % 4) * 512 + 512]

    for sb in range(NSLAB):
        tcs = [sb * 4 + i for i in range(4)]

        pairs = []
        for ci in range(4):
            for f in range(16):
                pairs.append((c.res[tcs[ci]][:, f * 128:(f + 1) * 128],
                              fmB[f // 4][:, (f % 4) * 512 + ci * 128:(f % 4) * 512 + ci * 128 + 128]))
        _transpose_in(c, pairs)
        xfm_ready = vp.n

        G.wait(tp, ring_rel.get(4, 0))
        iw4 = c.wsem[4].inc(G.dma(wb[4][:, 0:256], c.wkg1_t), 16)
        Tn.wait(c.wsem[4], iw4)
        Tn.wait(vp, xfm_ready)
        c.waitb(Tn, [0])
        pk = PS[0][0:16, :]
        for f in range(16):
            Tn.mm(pk, wb[4][:, f * 16:(f + 1) * 16], xfm(f), start=(f == 0), stop=(f == 15))
        it = tp.inc(Tn.last)
        ring_rel[4] = it
        V.wait(tp, it)
        kg1 = wb[5][0:16, 0:512]
        V.copy(kg1, pk)
        ivk = vp.inc(V.last)
        Tn.wait(vp, ivk)
        sp_hist = {}
        for m in range(8):
            p = PS[m % 2][:]
            if m >= 2:
                Tn.wait(sp, sp_hist[m - 2])
            Tn.mm(p, c.skg2[:, m * 128:(m + 1) * 128], kg1, start=True, stop=True)
            it = tp.inc(Tn.last)
            S.wait(tp, it)
            S.wait(c.vp, 1)
            S.act(gpt(m), p, AF.Exp, scale=-1.0, bias=c.negbkg2[:, m:m + 1])
            S.act(gpt(m), gpt(m), AF.Ln, bias=1.0)
            sp_hist[m] = sp.inc(S.last)
            c.relb(m % 2, sp, sp_hist[m])
        V.wait(sp, sp.n)
        for m in range(8):
            for ch in range(4):
                seg = gpt(m)[:, ch * 128:(ch + 1) * 128]
                V.scan(seg, seg, seg, 0.0, OP.add, OP.bypass)
        ivG = vp.inc(V.last)
        S.wait(vp, ivG)
        for m in range(8):
            S.act(c.cols[:, 16 + m * 4:16 + m * 4 + 4], gpt(m)[:, 127::128], AF.Exp, scale=-1.0 / GLA_NORM)
        sp_a = sp.inc(S.last)

        def qk(qi, base, escale, ebias):
            for mg in range(2):
                for fg in range(4):
                    slot = wb[fg % 2]
                    G.wait(tp, ring_rel.get(fg % 2, 0))
                    ws = c.wsem[fg % 2]
                    iw = ws.inc(G.dma(slot[:], c.gla_qk_t[qi, mg, fg]), 16)
                    Tn.wait(ws, iw)
                    if fg == 0:
                        Tn.wait(vp, xfm_ready)
                        c.waitb(Tn, [2, 3, 4, 5])
                    for fi in range(4):
                        f = fg * 4 + fi
                        for m4 in range(4):
                            Tn.mm(PS[2 + m4][:], slot[:, fi * 512 + m4 * 128:fi * 512 + m4 * 128 + 128],
                                  xfm(f), start=(f == 0), stop=(f == 15))
                    ring_rel[fg % 2] = tp.n + 1
                    tp.inc(Tn.last)
                itg = tp.n
                for m4 in range(4):
                    m = mg * 4 + m4
                    et = wb[4][:, 256 + (m4 % 2) * 512:256 + (m4 % 2) * 512 + 512]
                    S.wait(vp, ivG)
                    S.wait(vp, c._et_vp.get(m4 % 2, 0))
                    S.act(et, gpt(m), AF.Exp, bias=ebias, scale=escale)
                    isp = sp.inc(S.last)
                    V.wait(tp, itg)
                    V.wait(sp, isp)
                    dst = fmA[base + m // 4][:, (m % 4) * 512:(m % 4) * 512 + 512]
                    V.tt(dst, PS[2 + m4][:], et, OP.mult)
                    c._et_vp[m4 % 2] = vp.inc(V.last)
                    c.relb(2 + m4, vp, c._et_vp[m4 % 2])

        c._et_vp = {}
        qk(0, 0, -1.0 / GLA_NORM, 0.0)
        qk(1, 2, +1.0 / GLA_NORM, GLA_SCALE_LOG)

        def tmproj(wi, base, bias_row):
            for oc in range(4):
                for fg in range(4):
                    slot = wb[2 + fg % 2]
                    G.wait(tp, ring_rel.get(2 + fg % 2, 0))
                    ws = c.wsem[2 + fg % 2]
                    iw = ws.inc(G.dma(slot[:], c.gla_vgo_t[wi, oc, fg]), 16)
                    Tn.wait(ws, iw)
                    if fg == 0:
                        Tn.wait(vp, xfm_ready)
                        c.waitb(Tn, [2, 3, 4, 5])
                    for fi in range(4):
                        f = fg * 4 + fi
                        for tci in range(4):
                            Tn.mm(PS[2 + tci][:], xfm(f)[:, tci * 128:(tci + 1) * 128],
                                  slot[:, fi * 512:(fi + 1) * 512],
                                  start=(f == 0), stop=(f == 15 and bias_row is None))
                    ring_rel[2 + fg % 2] = tp.n + 1
                    tp.inc(Tn.last)
                if bias_row is not None:
                    for tci in range(4):
                        Tn.mm(PS[2 + tci][:], c.soner[0:1, 0:128], bias_row[:, oc * 512:(oc + 1) * 512],
                              start=False, stop=True)
                    itg = tp.inc(Tn.last)
                else:
                    itg = tp.n
                V.wait(tp, itg)
                for tci in range(4):
                    V.copy(fmA[base + tci][:, oc * 512:(oc + 1) * 512], PS[2 + tci][:])
                iv = vp.inc(V.last)
                for tci in range(4):
                    c.relb(2 + tci, vp, iv)

        tmproj(0, 4, None)
        tmproj(1, 8, c.bg_row)

        v_ready = vp.n
        prev_state_vp = getattr(c, "_gla_state_vp", {})
        kv_read_vp = getattr(c, "_gla_kv_vp", 0)
        sq_hist = getattr(c, "_gla_sq", {})
        for ch in range(4):
            o_evac = {}
            for h in range(16):
                m = h // 2
                r0 = (h % 2) * 64
                qs = fmA[m // 4][r0:r0 + 64, (m % 4) * 512 + ch * 128:(m % 4) * 512 + ch * 128 + 128]
                ks = fmA[2 + m // 4][r0:r0 + 64, (m % 4) * 512 + ch * 128:(m % 4) * 512 + ch * 128 + 128]
                vs = fmA[4 + ch][:, h * 128:(h + 1) * 128]
                Sst_h = c.Sst[r0:r0 + 64, (h // 2) * 128:(h // 2) * 128 + 128]
                Stmp_h = c.Stmp[r0:r0 + 64, 0:128]
                Sbf_h = c.Sbf[r0:r0 + 64, (h // 2) * 128:(h // 2) * 128 + 128]
                a_col = c.cols[r0:r0 + 64, 16 + m * 4 + ch:16 + m * 4 + ch + 1]
                STp = PS[h % 2][:, 0:128]
                Op = PS[2 + h % 2][:, 0:128]
                KVp = PS[4][r0:r0 + 64, 0:128]
                kTp = c.PT[0][:, (h % 2) * 256:(h % 2) * 256 + 64]
                stm = wb[5][:, 1024 + (h % 2) * 128:1024 + (h % 2) * 128 + 128]
                kT_sb = wb[5][:, 1536 + (h % 2) * 64:1536 + (h % 2) * 64 + 64]

                Tn.wait(vp, v_ready)
                if h >= 2:
                    Tn.wait(vp, o_evac.get(h - 2, 0))
                Tn.wait(sp, sq_hist.get((ch, h - 2), sq_hist.get((ch - 1, h + 14), 0)))
                Tn.mm(STp, ks, qs, start=True, stop=True)
                it1 = tp.inc(Tn.last)
                V.wait(tp, it1)
                V.tt(stm, STp, c.smask[:], OP.mult)
                iv1 = vp.inc(V.last)
                Tn.wait(vp, iv1)
                Tn.wait(vp, prev_state_vp.get(h, 0))
                Tn.mm(Op, stm, vs, start=True, stop=False)
                Tn.mm(Op, qs, Sbf_h, start=False, stop=True)
                it2 = tp.inc(Tn.last)
                Tn.trans(kTp, ks, c.sid[r0:r0 + 64, r0:r0 + 64])
                it3 = tp.inc(Tn.last)
                V.wait(tp, it3)
                V.copy(kT_sb, kTp)
                iv2 = vp.inc(V.last)
                Tn.wait(vp, iv2)
                Tn.wait(vp, kv_read_vp)
                Tn.mm(KVp, kT_sb, vs, start=True, stop=True)
                it4 = tp.inc(Tn.last)
                V.wait(tp, it4)
                V.wait(sp, sp_a)
                V.tt(Stmp_h, Sst_h, KVp, OP.add)
                V.ts(Sst_h, Stmp_h, a_col, None, OP.mult)
                V.copy(Sbf_h, Sst_h)
                prev_state_vp[h] = kv_read_vp = vp.inc(V.last)
                V.wait(tp, it2)
                V.copy(fmA[12 + ch][:, h * 128:(h + 1) * 128], Op)
                V.red(c.stat[:, 32 + h:33 + h], Op, OP.add)
                o_evac[h] = vp.inc(V.last)
                S.wait(tp, it2)
                S.act(c.dumb[:, 0:128], Op, AF.Square, accum_out=c.stat[:, 48 + h:49 + h])
                sq_hist[(ch, h)] = sp.inc(S.last)
            V.wait(sp, sp.n)
            V.ts(c.stat[:, 0:16], c.stat[:, 32:48], 1.0 / 128.0, None, OP.mult)
            V.ts(c.cols[:, 48:64], c.stat[:, 48:64], 1.0 / 128.0, None, OP.mult)
            V.tt(c.cols[:, 64:80], c.stat[:, 0:16], c.stat[:, 0:16], OP.mult)
            V.tt(c.cols[:, 80:96], c.cols[:, 48:64], c.cols[:, 64:80], OP.subtract)
            ivs = vp.inc(V.last)
            S.wait(vp, ivs)
            S.act(c.cols[:, 96:112], c.cols[:, 80:96], AF.Ln, bias=LN_EPS)
            S.act(c.stat[:, 16:32], c.cols[:, 96:112], AF.Exp, scale=-0.5)
            S.act(f32a[0][:], fmA[8 + ch][:], AF.Silu)
            isp = sp.inc(S.last)
            V.wait(sp, isp)
            for h in range(16):
                V.ts(fmA[8 + ch][:, h * 128:(h + 1) * 128], fmA[12 + ch][:, h * 128:(h + 1) * 128],
                     c.stat[:, 0 + h:1 + h], c.stat[:, 16 + h:17 + h], OP.subtract, OP.mult)
            for h in range(16):
                V.tt(fmA[12 + ch][:, h * 128:(h + 1) * 128], fmA[8 + ch][:, h * 128:(h + 1) * 128],
                     f32a[0][:, h * 128:(h + 1) * 128], OP.mult)
            vp.inc(V.last)
        c._gla_state_vp = prev_state_vp
        c._gla_kv_vp = kv_read_vp
        c._gla_sq = sq_hist

        z_pairs = []
        for ci in range(4):
            for f in range(16):
                z_pairs.append((fmA[12 + ci][:, f * 128:(f + 1) * 128],
                                fmB[f // 4][:, (f % 4) * 512 + ci * 128:(f % 4) * 512 + ci * 128 + 128]))
        Tn.wait(vp, vp.n)
        _transpose_in(c, z_pairs)
        zfm_ready = vp.n

        for oc in range(4):
            for fg in range(4):
                slot = wb[fg % 2]
                G.wait(tp, ring_rel.get(fg % 2, 0))
                ws = c.wsem[fg % 2]
                iw = ws.inc(G.dma(slot[:], c.gla_vgo_t[2, oc, fg]), 16)
                Tn.wait(ws, iw)
                if fg == 0:
                    Tn.wait(vp, zfm_ready)
                    c.waitb(Tn, [2, 3, 4, 5])
                for fi in range(4):
                    f = fg * 4 + fi
                    for tci in range(4):
                        Tn.mm(PS[2 + tci][:], xfm(f)[:, tci * 128:(tci + 1) * 128],
                              slot[:, fi * 512:(fi + 1) * 512], start=(f == 0), stop=(f == 15))
                ring_rel[fg % 2] = tp.n + 1
                tp.inc(Tn.last)
            itg = tp.n
            V.wait(tp, itg)
            for tci in range(4):
                V.copy(fmA[4 + tci][:, oc * 512:(oc + 1) * 512], PS[2 + tci][:])
            c._wo_evac = vp.inc(V.last)
            for tci in range(4):
                c.relb(2 + tci, vp, c._wo_evac)
        for tci in range(4):
            _ln_write(c, tcs[tci], fmA[4 + tci][:])



FFN0_BLOCKS = None


def _ffn0(c):
    Tn, V, S, G, Y = c.T, c.V, c.S, c.G, c.Y
    tp, vp, sp, ld = c.tp, c.vp, c.sp, c.ld
    PS = c.PS
    fmA, fmB, f32a, wb = c.fmA, c.fmB, c.f32a, c.wb
    ring_rel = {i: tp.n for i in range(5)}

    def xfm(f):
        return fmB[f // 4][:, (f % 4) * 512:(f % 4) * 512 + 512]

    def zsl(jj):
        return fmA[jj // 4][:, (jj % 4) * 512:(jj % 4) * 512 + 512]

    SC1 = 2.0 ** -5.5
    SC2 = 2.0 ** -6.5

    Y.wait(vp, vp.n)
    Y.wait(sp, sp.n)
    Y.wait(tp, tp.n)
    ld.inc(Y.dma(c.cols[:, 0:16], c.su_ug_cols), 16)
    ld.inc(Y.dma(c.cols[:, 16:80], c.wsc_up_cols), 16)
    ld.inc(Y.dma(c.cols[:, 80:144], c.wsc_gate_cols), 16)
    ld.inc(Y.dma(c.stat[:, 0:64], c.su_dp_cols), 16)
    ld.inc(Y.dma(f32a[0][0:1, 0:2048], c.wsc_dp_row), 16)
    V.wait(ld, ld.n)
    V.tt(c.cols[:, 16:80], c.cols[:, 16:80], c.stat[:, 0:64], OP.mult)
    V.ts(c.cols[:, 16:80], c.cols[:, 16:80], SC1, None, OP.mult)
    V.ts(c.cols[:, 80:144], c.cols[:, 80:144], SC1, None, OP.mult)
    V.ts(c.rbf[0:1, 0:2048], f32a[0][0:1, 0:2048], SC2, None, OP.mult)
    ivc = vp.inc(V.last)
    Tn.wait(vp, ivc)
    for oc4 in range(4):
        Tn.mm(PS[2][:], c.soner[0:1, 0:128], c.rbf[0:1, oc4 * 512:(oc4 + 1) * 512], start=True, stop=True)
        itb = tp.inc(Tn.last)
        V.wait(tp, itb)
        V.copy(wb[4][:, oc4 * 512:(oc4 + 1) * 512], PS[2][:])
        ivb = vp.inc(V.last)
        Tn.wait(vp, ivb)

    for sb in range(NSLAB):
        tcs = [sb * 4 + i for i in range(4)]
        pairs, sus = [], []
        for ci in range(4):
            for f in range(16):
                pairs.append((c.res[tcs[ci]][:, f * 128:(f + 1) * 128],
                              fmB[f // 4][:, (f % 4) * 512 + ci * 128:(f % 4) * 512 + ci * 128 + 128]))
                sus.append(c.cols[:, f:f + 1])
        _transpose_in(c, pairs, su_cols=sus)
        Tn.wait(vp, vp.n)
        Tn.wait(sp, sp.n)
        ev = {}
        for f in range(16):
            p = PS[2 + f % 4][:]
            Tn.wait(vp, ev.get(f - 4, 0))
            if f < 4:
                c.waitb(Tn, [2 + f % 4])
            Tn.mm(p, c.sh128[:], xfm(f), start=True, stop=True)
            it = tp.inc(Tn.last)
            V.wait(tp, it)
            V.copy(xfm(f), p)
            ev[f] = vp.inc(V.last)
            c.relb(2 + f % 4, vp, ev[f])
        V.wait(vp, vp.n)
        for s in (1, 2, 4, 8):
            prs = [t0 for t0 in range(16) if not (t0 & s)]
            for t0 in prs:
                V.tt(xfm(t0), xfm(t0), xfm(t0 ^ s), OP.add)
            for t0 in prs:
                V.stt(xfm(t0 ^ s), xfm(t0 ^ s), -2.0, xfm(t0), OP.mult, OP.add)
        xh_ready = vp.inc(V.last)

        for jc in range(4):
            for jg in range(4):
                for pas in range(2):
                    for fg in range(4):
                        slot = wb[fg % 2]
                        G.wait(tp, ring_rel.get(fg % 2, 0))
                        ws = c.wsem[fg % 2]
                        iw = ws.inc(G.dma(slot[:], c.ug_t[pas, jc, jg, fg]), 16)
                        Tn.wait(ws, iw)
                        if fg == 0:
                            Tn.wait(vp, xh_ready)
                            c.waitb(Tn, [2, 3, 4, 5])
                        for fi in range(4):
                            f = fg * 4 + fi
                            for j2 in range(4):
                                Tn.mm(PS[2 + j2][:], slot[:, fi * 512 + j2 * 128:fi * 512 + j2 * 128 + 128],
                                      xfm(f), start=(f == 0), stop=(f == 15))
                        ring_rel[fg % 2] = tp.n + 1
                        tp.inc(Tn.last)
                    itg = tp.n
                    for j2 in range(4):
                        jj = jc * 16 + jg * 4 + j2
                        if pas == 0:
                            V.wait(tp, itg)
                            V.copy(zsl(jj), PS[2 + j2][:])
                        else:
                            S.wait(tp, itg)
                            S.act(f32a[0][:, j2 * 512:(j2 + 1) * 512], PS[2 + j2][:], AF.Silu,
                                  scale=c.cols[:, 80 + jj:81 + jj])
                            isp = sp.inc(S.last)
                            c.relb(2 + j2, sp, isp)
                            V.wait(sp, isp)
                            V.stt(zsl(jj), zsl(jj), c.cols[:, 16 + jj:17 + jj],
                                  f32a[0][:, j2 * 512:(j2 + 1) * 512], OP.mult, OP.mult)
                    iv = vp.inc(V.last)
                    if pas == 0:
                        for j2 in range(4):
                            c.relb(2 + j2, vp, iv)
            Tn.wait(vp, vp.n)
            ev2 = {}
            for jt in range(16):
                jj = jc * 16 + jt
                p = PS[2 + jt % 4][:]
                Tn.wait(vp, ev2.get(jt - 4, 0))
                if jt < 4:
                    c.waitb(Tn, [2 + jt % 4])
                Tn.mm(p, c.sh128[:], zsl(jj), start=True, stop=True)
                it = tp.inc(Tn.last)
                V.wait(tp, it)
                V.copy(zsl(jj), p)
                ev2[jt] = vp.inc(V.last)
                c.relb(2 + jt % 4, vp, ev2[jt])
            V.wait(vp, vp.n)
            for s in (1, 2, 4, 8):
                prs = [t0 for t0 in range(16) if not (t0 & s)]
                for t0 in prs:
                    V.tt(zsl(jc * 16 + t0), zsl(jc * 16 + t0), zsl(jc * 16 + (t0 ^ s)), OP.add)
                for t0 in prs:
                    V.stt(zsl(jc * 16 + (t0 ^ s)), zsl(jc * 16 + (t0 ^ s)), -2.0,
                          zsl(jc * 16 + t0), OP.mult, OP.add)
            vp.inc(V.last)
        for s in (16, 32):
            prs = [t0 for t0 in range(64) if not (t0 & s)]
            for t0 in prs:
                V.tt(zsl(t0), zsl(t0), zsl(t0 ^ s), OP.add)
            for t0 in prs:
                V.stt(zsl(t0 ^ s), zsl(t0 ^ s), -2.0, zsl(t0), OP.mult, OP.add)
        z_ready = vp.inc(V.last)

        for oc in range(4):
            for jg in range(16):
                slot = wb[2 + jg % 2]
                G.wait(tp, ring_rel.get(2 + jg % 2, 0))
                ws = c.wsem[2 + jg % 2]
                iw = ws.inc(G.dma(slot[:], c.dn_t[oc, jg]), 16)
                Tn.wait(ws, iw)
                if jg == 0:
                    Tn.wait(vp, z_ready)
                    c.waitb(Tn, [2, 3, 4, 5])
                for ji in range(4):
                    j = jg * 4 + ji
                    for tci in range(4):
                        Tn.mm(PS[2 + tci][:], zsl(j)[:, tci * 128:(tci + 1) * 128],
                              slot[:, ji * 512:(ji + 1) * 512], start=(j == 0), stop=(j == 63))
                ring_rel[2 + jg % 2] = tp.n + 1
                tp.inc(Tn.last)
            itg = tp.n
            V.wait(tp, itg)
            for tci in range(4):
                V.tt(fmB[tci][:, oc * 512:(oc + 1) * 512], PS[2 + tci][:],
                     wb[4][:, oc * 512:(oc + 1) * 512], OP.mult)
            iv = vp.inc(V.last)
            for tci in range(4):
                c.relb(2 + tci, vp, iv)
        for tci in range(4):
            _ln_write(c, tcs[tci], fmB[tci][:])


def _mla(c):
    Tn, V, S, G, Y = c.T, c.V, c.S, c.G, c.Y
    tp, vp, sp, ld = c.tp, c.vp, c.sp, c.ld
    PS, PT = c.PS, c.PT
    fmA, fmB, f32a, wb = c.fmA, c.fmB, c.f32a, c.wb
    ring_rel = {i: tp.n for i in range(5)}

    Y.wait(vp, vp.n)
    Y.wait(sp, sp.n)
    Y.wait(tp, tp.n)
    ld.inc(Y.dma(c.bcols[:, 16:20], c.kvd_b), 16)
    ld.inc(Y.dma(c.bcols[:, 20:28], c.qd_b), 16)
    ld.inc(Y.dma(c.bcols[:, 28:44], c.kup_b), 16)
    ld.inc(Y.dma(c.bcols[:, 44:60], c.vup_b), 16)
    ld.inc(Y.dma(c.bcols[0:64, 60:76], c.kr_b), 16)
    ld.inc(Y.dma(c.bcols[:, 76:92], c.qup_b), 16)
    ld.inc(Y.dma(c.cols[0:64, 128:144], c.qr_b), 16)
    G.wait(tp, tp.n)
    G.wait(vp, vp.n)
    c.wd.inc(G.dma(c.rbf[0:1, 0:2048], c.outw_b), 16)
    bias_ld, bias_wd = ld.n, c.wd.n

    def xfm(f):
        return fmB[f // 4][:, (f % 4) * 512:(f % 4) * 512 + 512]

    def kvc(m):
        return fmA[m]

    def tin(xs):
        pairs = []
        for ci in range(4):
            tc = xs * 4 + ci
            for f in range(16):
                pairs.append((c.res[tc][:, f * 128:(f + 1) * 128],
                              fmB[f // 4][:, (f % 4) * 512 + ci * 128:(f % 4) * 512 + ci * 128 + 128]))
        _transpose_in(c, pairs)
        Tn.wait(vp, vp.n)
        Tn.wait(sp, sp.n)

    Tn.wait(ld, bias_ld)
    V.wait(ld, bias_ld)
    V.wait(c.wd, bias_wd)
    for xs in range(4):
        tin(xs)
        for fg in range(4):
            slot = wb[fg % 2]
            G.wait(tp, ring_rel.get(fg % 2, 0))
            ws = c.wsem[fg % 2]
            iw = ws.inc(G.dma(slot[:], c.kvd_t[fg]), 16)
            Tn.wait(ws, iw)
            if fg == 0:
                c.waitb(Tn, [2, 3, 4, 5])
            for fi in range(4):
                f = fg * 4 + fi
                for m in range(4):
                    Tn.mm(PS[2 + m][:], slot[:, fi * 512 + m * 128:fi * 512 + m * 128 + 128],
                          xfm(f), start=(f == 0), stop=(f == 15))
            ring_rel[fg % 2] = tp.n + 1
            tp.inc(Tn.last)
        itg = tp.n
        V.wait(tp, itg)
        for m in range(4):
            V.ts(kvc(m)[:, xs * 512:(xs + 1) * 512], PS[2 + m][:], c.bcols[:, 16 + m:17 + m], None, OP.add)
        iv = vp.inc(V.last)
        for m in range(4):
            c.relb(2 + m, vp, iv)

    c._head_prev = {}
    for qsl in range(2):
        for xs2 in range(2):
            xs = qsl * 2 + xs2
            tin(xs)
            for mg in range(2):
                for fg in range(4):
                    slot = wb[fg % 2]
                    G.wait(tp, ring_rel.get(fg % 2, 0))
                    ws = c.wsem[fg % 2]
                    iw = ws.inc(G.dma(slot[:], c.qd_t[mg, fg]), 16)
                    Tn.wait(ws, iw)
                    if fg == 0:
                        c.waitb(Tn, [2, 3, 4, 5])
                    for fi in range(4):
                        f = fg * 4 + fi
                        for m4 in range(4):
                            Tn.mm(PS[2 + m4][:], slot[:, fi * 512 + m4 * 128:fi * 512 + m4 * 128 + 128],
                                  xfm(f), start=(f == 0), stop=(f == 15))
                    ring_rel[fg % 2] = tp.n + 1
                    tp.inc(Tn.last)
                itg = tp.n
                V.wait(tp, itg)
                for m4 in range(4):
                    m = mg * 4 + m4
                    dst = fmA[4 + m // 2][:, (m % 2) * 1024 + xs2 * 512:(m % 2) * 1024 + xs2 * 512 + 512]
                    V.ts(dst, PS[2 + m4][:], c.bcols[:, 20 + m:21 + m], None, OP.add)
                iv = vp.inc(V.last)
                for m4 in range(4):
                    c.relb(2 + m4, vp, iv)

        for h in range(16):
            ws = c.wsem[4]
            G.wait(tp, ring_rel.get(4, 0))
            ws.inc(G.dma(wb[4][:, 0:512], c.kup_t[h]), 16)
            ws.inc(G.dma(wb[4][:, 512:1024], c.vup_t[h]), 16)
            ws.inc(G.dma(wb[4][:, 1024:1280], c.kr_t[h]), 16)
            ws.inc(G.dma(wb[5][:, 0:1024], c.qup_t[h]), 16)
            iw = ws.inc(G.dma(wb[5][:, 1024:1536], c.qr_t[h]), 16)
            Tn.wait(ws, iw)
            ev = {}
            for tkc in range(4):
                p = PS[2 + tkc % 4][:]
                c.waitb(Tn, [2 + tkc % 4])
                for kf in range(4):
                    Tn.mm(p, wb[4][:, kf * 128:(kf + 1) * 128], kvc(kf)[:, tkc * 512:(tkc + 1) * 512],
                          start=(kf == 0), stop=(kf == 3))
                it = tp.inc(Tn.last)
                V.wait(tp, it)
                V.ts(fmB[0][:, tkc * 512:(tkc + 1) * 512], p, c.bcols[:, 28 + h:29 + h], None, OP.add)
                ev[tkc] = vp.inc(V.last)
                c.relb(2 + tkc % 4, vp, ev[tkc])
            for tkc in range(4):
                p = PS[2 + tkc % 4][0:64, :]
                c.waitb(Tn, [2 + tkc % 4])
                for kf in range(4):
                    Tn.mm(p, wb[4][:, 1024 + kf * 64:1024 + (kf + 1) * 64],
                          kvc(kf)[:, tkc * 512:(tkc + 1) * 512], start=(kf == 0), stop=(kf == 3))
                it = tp.inc(Tn.last)
                V.wait(tp, it)
                V.ts(fmB[1][0:64, tkc * 512:(tkc + 1) * 512], p, c.bcols[0:64, 60 + h:61 + h], None, OP.add)
                ev[100 + tkc] = vp.inc(V.last)
                c.relb(2 + tkc % 4, vp, ev[100 + tkc])
            for tk16 in range(16):
                p = PS[2 + tk16 % 4][:, 0:128]
                if tk16 >= 4:
                    Tn.wait(vp, ev[200 + tk16 - 4])
                elif tk16 == 0:
                    c.waitb(Tn, [2, 3, 4, 5])
                    Tn.wait(vp, c._head_prev.get("v", 0))
                for kf in range(4):
                    Tn.mm(p, kvc(kf)[:, tk16 * 128:(tk16 + 1) * 128],
                          wb[4][:, 512 + kf * 128:512 + (kf + 1) * 128], start=(kf == 0), stop=(kf == 3))
                it = tp.inc(Tn.last)
                V.wait(tp, it)
                V.ts(fmB[2][:, tk16 * 128:(tk16 + 1) * 128], p, c.bcols[:, 44 + h:45 + h], None, OP.add)
                ev[200 + tk16] = vp.inc(V.last)
            for b4 in range(4):
                c.relb(2 + b4, vp, ev[200 + 12 + b4])
            for qn in range(2):
                p = PS[2 + qn][:]
                c.waitb(Tn, [2 + qn])
                for qf in range(8):
                    rhs = fmA[4 + qf // 2][:, (qf % 2) * 1024 + qn * 512:(qf % 2) * 1024 + qn * 512 + 512]
                    Tn.mm(p, wb[5][:, qf * 128:(qf + 1) * 128], rhs, start=(qf == 0), stop=(qf == 7))
                it = tp.inc(Tn.last)
                V.wait(tp, it)
                V.ts(fmB[3][:, qn * 512:(qn + 1) * 512], p, c.bcols[:, 76 + h:77 + h], None, OP.add)
                ev[300 + qn] = vp.inc(V.last)
                c.relb(2 + qn, vp, ev[300 + qn])
            for qn in range(2):
                p = PS[2 + qn][0:64, :]
                c.waitb(Tn, [2 + qn])
                for qf in range(8):
                    rhs = fmA[4 + qf // 2][:, (qf % 2) * 1024 + qn * 512:(qf % 2) * 1024 + qn * 512 + 512]
                    Tn.mm(p, wb[5][:, 1024 + qf * 64:1024 + (qf + 1) * 64], rhs, start=(qf == 0), stop=(qf == 7))
                it = tp.inc(Tn.last)
                V.wait(tp, it)
                V.ts(fmB[3][0:64, 1024 + qn * 512:1024 + (qn + 1) * 512], p,
                     c.cols[0:64, 128 + h:129 + h], None, OP.add)
                ev[400 + qn] = vp.inc(V.last)
                c.relb(2 + qn, vp, ev[400 + qn])
            ring_rel[4] = tp.n
            c._head_prev = {"v": ev[200 + 15]}

            Tn.wait(vp, vp.n)
            for kc4 in range(4):
                p = PS[2 + kc4][:]
                Tn.mm(p, fmB[3][:, 0:128], fmB[0][:, kc4 * 512:(kc4 + 1) * 512], start=True, stop=False)
                Tn.mm(p, fmB[3][0:64, 1024:1152], fmB[1][0:64, kc4 * 512:(kc4 + 1) * 512],
                      start=False, stop=True)
            it = tp.inc(Tn.last)
            V.wait(tp, it)
            for kc4 in range(4):
                V.red(c.stat[:, 20 + kc4:21 + kc4], PS[2 + kc4][:], OP.max)
            V.tt(c.stat[:, 20:21], c.stat[:, 20:21], c.stat[:, 21:22], OP.max)
            V.tt(c.stat[:, 22:23], c.stat[:, 22:23], c.stat[:, 23:24], OP.max)
            V.tt(c.stat[:, 20:21], c.stat[:, 20:21], c.stat[:, 22:23], OP.max)
            V.copy(wb[5][:, 1536:1537], c.stat[:, 20:21])
            ivm = vp.inc(V.last)
            Tn.wait(vp, ivm)
            Tn.trans(PT[0][0:1, 0:128], wb[5][:, 1536:1537], c.sid[:])
            it = tp.inc(Tn.last)
            V.wait(tp, it)
            V.red(c.stat[0:1, 24:25], PT[0][0:1, 0:128], OP.max)
            ivm = vp.inc(V.last)
            Tn.wait(vp, ivm)
            Tn.mm(PS[5][:, 0:1], c.sonerf[:], c.stat[0:1, 24:25], start=True, stop=True)
            it = tp.inc(Tn.last)
            V.wait(tp, it)
            V.ts(c.cols[:, 144:145], PS[5][:, 0:1], -MLA_SCALE, None, OP.mult)
            ebias_vp = vp.inc(V.last)
            S.wait(vp, ebias_vp)

            for qn in range(2):
                pexp = {}
                Tn.wait(vp, vp.n)
                c.waitb(Tn, [0, 1, 2, 3, 4, 5])
                for kc in range(16):
                    STp = PS[0 + kc % 2][:]
                    if kc >= 2:
                        Tn.wait(sp, pexp[kc - 2])
                    Tn.mm(STp, fmB[0][:, kc * 128:(kc + 1) * 128], fmB[3][:, qn * 512:(qn + 1) * 512],
                          start=True, stop=False)
                    Tn.mm(STp, fmB[1][0:64, kc * 128:(kc + 1) * 128],
                          fmB[3][0:64, 1024 + qn * 512:1024 + (qn + 1) * 512], start=False, stop=True)
                    it = tp.inc(Tn.last)
                    S.wait(tp, it)
                    pslot = wb[0][:, (kc % 2) * 512:(kc % 2) * 512 + 512]
                    S.act(pslot, STp, AF.Exp, scale=MLA_SCALE, bias=c.cols[:, 144:145])
                    pexp[kc] = sp.inc(S.last)
                    Tn.wait(sp, pexp[kc])
                    Tn.mm(PS[2][:], fmB[2][:, kc * 128:(kc + 1) * 128], pslot, start=(kc == 0), stop=(kc == 15))
                    Tn.mm(PS[3][0:1, :], c.sonec[:], pslot, start=(kc == 0), stop=(kc == 15))
                    tp.inc(Tn.last)
                it = tp.n
                V.wait(tp, it)
                V.recip(c.rows[0:1, 0:512], PS[3][0:1, :])
                ivr = vp.inc(V.last)
                Tn.wait(vp, ivr)
                Tn.mm(PS[4][:], c.sonerf[:], c.rows[0:1, 0:512], start=True, stop=True)
                it = tp.inc(Tn.last)
                V.wait(tp, it)
                V.copy(f32a[0][:, 0:512], PS[4][:])
                V.tt(fmA[8 + h // 2][:, (h % 2) * 1024 + qn * 512:(h % 2) * 1024 + qn * 512 + 512],
                     PS[2][:], f32a[0][:, 0:512], OP.mult)
                iv = vp.inc(V.last)
                c.relb(2, vp, iv)
                c.relb(4, vp, iv)

        c._ctx_ready = vp.n
        ring_rel[0] = ring_rel[1] = tp.n
        for tcg in range(2):
            for oc in range(4):
                for hg in range(4):
                    slot = wb[2 + hg % 2]
                    G.wait(tp, ring_rel.get(2 + hg % 2, 0))
                    ws = c.wsem[2 + hg % 2]
                    iw = ws.inc(G.dma(slot[:], c.outw_t[oc, hg]), 16)
                    Tn.wait(ws, iw)
                    if hg == 0:
                        Tn.wait(vp, c._ctx_ready)
                        c.waitb(Tn, [2, 3, 4, 5])
                    for hi in range(4):
                        h = hg * 4 + hi
                        for tci in range(4):
                            tq = tcg * 4 + tci
                            lhs = fmA[8 + h // 2][:, (h % 2) * 1024 + tq * 128:(h % 2) * 1024 + tq * 128 + 128]
                            Tn.mm(PS[2 + tci][:], lhs, slot[:, hi * 512:(hi + 1) * 512],
                                  start=(h == 0), stop=False)
                    ring_rel[2 + hg % 2] = tp.n + 1
                    tp.inc(Tn.last)
                for tci in range(4):
                    Tn.mm(PS[2 + tci][:], c.soner[0:1, 0:128], c.rbf[0:1, oc * 512:(oc + 1) * 512],
                          start=False, stop=True)
                itg = tp.inc(Tn.last)
                V.wait(tp, itg)
                for tci in range(4):
                    V.copy(fmB[tci][:, oc * 512:(oc + 1) * 512], PS[2 + tci][:])
                iv = vp.inc(V.last)
                for tci in range(4):
                    c.relb(2 + tci, vp, iv)
            for tci in range(4):
                tc = qsl * 8 + tcg * 4 + tci
                _ln_write(c, tc, fmB[tci][:])


def _ffn1(c):
    Tn, V, S, G, Y = c.T, c.V, c.S, c.G, c.Y
    tp, vp, sp, gp, ld = c.tp, c.vp, c.sp, c.gp, c.ld
    PS, PT = c.PS, c.PT
    fmA, fmB, f32a, wb = c.fmA, c.fmB, c.f32a, c.wb
    ring_rel = {i: tp.n for i in range(5)}
    G.wait(tp, tp.n)
    G.wait(vp, vp.n)
    G.wait(sp, sp.n)

    def fsl(i, j, w=512):
        return fmA[i][:, j * w:(j + 1) * w]

    def strip_tiles(s):
        if s < 16:
            return [(c.w1_t[s, f], 2048) for f in range(16)]
        return [(c.w2_t[s - 16, j], 8192) for j in range(64)]

    def strip_tiles_q(s):
        if s < 16:
            return [c.w1q_hbm[s, f] for f in range(16)]
        return [c.w2q_hbm[s - 16, j] for j in range(64)]

    stA = {"n": 0}
    stB = {"n": 0}
    mm_rel = {}
    q_rel = {}
    bc_rel = {}
    bc_tp = {}

    def phaseA(s):
        tiles = strip_tiles(s)
        N = tiles[0][1]
        for ti, (src_ap, _) in enumerate(tiles):
            n = stA["n"]
            sk = n % 4
            slot = wb[4 + sk // 2][:, (sk % 2) * 512:(sk % 2) * 512 + 512]
            G.wait(tp, mm_rel.get(n - 4, 0))
            wsk = c.wsem[4 + sk]
            iw = wsk.inc(G.dma(slot, src_ap), 16)
            S.wait(wsk, iw)
            S.act(slot, slot, AF.Abs, scale=2.0)
            isp = sp.inc(S.last)
            Tn.wait(sp, isp)
            Tn.mm(PS[0][0:1, :], c.sonec[:], slot, start=(ti == 0), stop=(ti == len(tiles) - 1))
            mm_rel[n] = tp.inc(Tn.last)
            stA["n"] = n + 1
        V.wait(tp, mm_rel[stA["n"] - 1])
        V.ts(c.rows[0:1, 0:512], PS[0][0:1, :], 1.0 / (2.0 * N), 1e-5, OP.mult, OP.max)
        ivr = vp.inc(V.last)
        c.relb(0, vp, ivr)
        c.waitb(Tn, [1])
        Tn.wait(vp, max(ivr, bc_rel.get(s - 2, 0)))
        Tn.mm(PS[1][:], c.sonerf[:], c.rows[0:1, 0:512], start=True, stop=True)
        itb = tp.inc(Tn.last)
        bc_tp[s] = itb
        V.wait(tp, itb)
        V.copy(f32a[0][:, (s % 2) * 512:(s % 2) * 512 + 512], PS[1][:])
        iv = vp.inc(V.last)
        c.relb(1, vp, iv)

    def phaseB(s):
        tiles = strip_tiles(s)
        qdst = strip_tiles_q(s)
        swb_ = f32a[0][:, (s % 2) * 512:(s % 2) * 512 + 512]
        last = 0
        for ti, (src_ap, _) in enumerate(tiles):
            n = stB["n"]
            k = n % 4
            wslot = wb[0 + k // 2][:, (k % 2) * 512:(k % 2) * 512 + 512]
            aslot = wb[2 + k // 2][:, (k % 2) * 512:(k % 2) * 512 + 512]
            sslot = wb[4 + k // 2][:, 1024 + (k % 2) * 512:1024 + (k % 2) * 512 + 512]
            if n - 4 in q_rel:
                G.wait(*q_rel[n - 4])
            wsk = c.wsem[k]
            iw = wsk.inc(G.dma(wslot, src_ap), 16)
            S.wait(wsk, iw)
            S.wait(tp, bc_tp[s])
            S.act(aslot, wslot, AF.Abs, scale=2.0)
            S.act(sslot, wslot, AF.Sign)
            isp = sp.inc(S.last)
            V.wait(sp, isp)
            V.tt(aslot, aslot, swb_, OP.is_ge)
            V.tt(aslot, aslot, sslot, OP.mult)
            V.tt(wslot, aslot, swb_, OP.mult)
            ivq = vp.inc(V.last)
            Y.wait(vp, ivq)
            gsk = c.gsem[k]
            q_rel[n] = (gsk, gsk.inc(Y.dma(qdst[ti], wslot), 16))
            last = ivq
            stB["n"] = n + 1
        bc_rel[s] = last

    for s in range(20):
        phaseA(s)
        if s > 0:
            phaseB(s - 1)
    phaseB(19)
    for gs in c.gsem:
        G.wait(gs, gs.n)
        V.wait(gs, gs.n)
    stash_done = 0

    perm = {}
    for tc in range(4):
        for jb in range(64):
            s_id = (tc * 4 + jb // 16) * 16 + (jb % 16)
            d_id = (jb // 4) * 16 + (jb % 4) * 4 + tc
            perm[s_id] = d_id
    cycles = []
    seen = set()
    for s0 in range(256):
        if s0 in seen:
            continue
        cyc = [s0]
        seen.add(s0)
        nxt = perm[s0]
        while nxt != s0:
            cyc.append(nxt)
            seen.add(nxt)
            nxt = perm[nxt]
        cycles.append(cyc)

    def slot_ap(sid):
        return fmA[sid // 16][:, (sid % 16) * 128:(sid % 16) * 128 + 128]

    for sb in range(NSLAB):
        tcs = [sb * 4 + i for i in range(4)]
        pairs = []
        for ci, tc in enumerate(tcs):
            V.wait(tp, tp.n)
            V.red(c.stat[:, 30:31], c.res[tc][:], OP.max, absval=True)
            V.ts(c.stat[:, 31:32], c.stat[:, 30:31], 1e-5, None, OP.max)
            V.recip(c.stat[:, 32:33], c.stat[:, 31:32])
            V.ts(c.cols[:, 32 + ci:33 + ci], c.stat[:, 32:33], 127.0, None, OP.mult)
            V.ts(c.cols[:, 36 + ci:37 + ci], c.stat[:, 31:32], 1.0 / 127.0, None, OP.mult)
            sx = c.cols[:, 32 + ci:33 + ci]
            for half in range(2):
                hs = slice(half * 1024, half * 1024 + 1024)
                v_ = f32a[0][:, 0:1024]
                V.ts(v_, c.res[tc][:, hs], sx, None, OP.mult)
                V.ts(f32a[0][:, 1024:2048], v_, M52, C52_32, OP.add, OP.subtract)
                V.ts(f32a[0][:, 1024:2048], f32a[0][:, 1024:2048], M32, None, OP.subtract)
                V.ts(f32a[1][:, 0:1024], v_, M23, M23, OP.add, OP.subtract)
                V.ts(f32a[1][:, 1024:2048], v_, M32, M32, OP.add, OP.subtract)
                V.tt(f32a[1][:, 0:1024], f32a[1][:, 0:1024], f32a[1][:, 1024:2048], OP.subtract)
                V.tt(fmA[15][:, hs], f32a[0][:, 1024:2048], f32a[1][:, 0:1024], OP.add)
            iv = vp.inc(V.last)
            for f in range(16):
                pairs.append((fmA[15][:, f * 128:(f + 1) * 128],
                              fmB[f // 4][:, (f % 4) * 512 + ci * 128:(f % 4) * 512 + ci * 128 + 128]))
            _transpose_in(c, pairs)
            pairs = []
        xq_ready = vp.n

        for jc in range(16):
            for g2 in range(4):
                slot = wb[g2 % 2]
                G.wait(tp, ring_rel.get(g2 % 2, 0))
                ws = c.wsem[g2 % 2]
                for fi4 in range(4):
                    ws.inc(G.dma(slot[:, fi4 * 512:(fi4 + 1) * 512], c.w1q_hbm[jc, g2 * 4 + fi4]), 16)
                iw = ws.n
                Tn.wait(ws, iw)
                if g2 == 0:
                    Tn.wait(vp, xq_ready)
                    c.waitb(Tn, [2, 3, 4, 5])
                for fi in range(4):
                    f = g2 * 4 + fi
                    for tci in range(4):
                        Tn.mm(PS[2 + tci][:], fmB[f // 4][:, (f % 4) * 512 + tci * 128:(f % 4) * 512 + tci * 128 + 128],
                              slot[:, fi * 512:(fi + 1) * 512], start=(f == 0), stop=(f == 15))
                ring_rel[g2 % 2] = tp.n + 1
                tp.inc(Tn.last)
            itg = tp.n
            S.wait(tp, itg)
            for tci in range(4):
                S.act(fmA[tci * 4 + jc // 4][:, (jc % 4) * 512:(jc % 4) * 512 + 512], PS[2 + tci][:],
                      AF.Gelu_apprx_tanh, scale=c.cols[:, 36 + tci:37 + tci])
            isp = sp.inc(S.last)
            for tci in range(4):
                c.relb(2 + tci, sp, isp)
        h_done = sp.n

        V.wait(sp, h_done)
        for tci in range(4):
            for q in range(4):
                V.red(c.stat[:, 40 + q:41 + q], fmA[tci * 4 + q][:], OP.max, absval=True)
            V.tt(c.stat[:, 40:41], c.stat[:, 40:41], c.stat[:, 41:42], OP.max)
            V.tt(c.stat[:, 42:43], c.stat[:, 42:43], c.stat[:, 43:44], OP.max)
            V.tt(c.stat[:, 40:41], c.stat[:, 40:41], c.stat[:, 42:43], OP.max)
            V.ts(c.stat[:, 44:45], c.stat[:, 40:41], 1e-5, None, OP.max)
            V.recip(c.stat[:, 45:46], c.stat[:, 44:45])
            V.ts(c.cols[:, 40 + tci:41 + tci], c.stat[:, 45:46], 127.0, None, OP.mult)
            V.ts(c.cols[:, 44 + tci:45 + tci], c.stat[:, 44:45], 1.0 / 127.0, None, OP.mult)
            sh = c.cols[:, 40 + tci:41 + tci]
            for q in range(4):
                for half in range(2):
                    hh = slice(half * 1024, half * 1024 + 1024)
                    v_ = f32a[0][:, 0:1024]
                    V.ts(v_, fmA[tci * 4 + q][:, hh], sh, None, OP.mult)
                    V.ts(f32a[0][:, 1024:2048], v_, M52, C52_32, OP.add, OP.subtract)
                    V.ts(f32a[0][:, 1024:2048], f32a[0][:, 1024:2048], M32, None, OP.subtract)
                    V.ts(f32a[1][:, 0:1024], v_, M23, M23, OP.add, OP.subtract)
                    V.ts(f32a[1][:, 1024:2048], v_, M32, M32, OP.add, OP.subtract)
                    V.tt(f32a[1][:, 0:1024], f32a[1][:, 0:1024], f32a[1][:, 1024:2048], OP.subtract)
                    V.tt(fmA[tci * 4 + q][:, hh], f32a[0][:, 1024:2048], f32a[1][:, 0:1024], OP.add)
        nh_done = vp.inc(V.last)

        Tn.wait(vp, nh_done)
        Tn.wait(sp, sp.n)
        lagged = []
        hist = {}
        n = 0
        vp0 = vp.n
        for cyc in cycles:
            for i in range(len(cyc)):
                s_id = cyc[i]
                pt = PT[n % 2][:, ((n // 2) % 4) * 256:((n // 2) % 4) * 256 + 128]
                Tn.wait(vp, hist[n - 2] if n >= 2 else vp0)
                Tn.trans(pt, slot_ap(s_id), c.sid[:])
                it = tp.inc(Tn.last)
                lagged.append((it, pt, perm[s_id]))
                if len(lagged) > 1:
                    it2, pt2, d2 = lagged.pop(0)
                    V.wait(tp, max(it2, it))
                    V.copy(slot_ap(d2), pt2)
                    hist[n - 1] = vp.inc(V.last)
                n += 1
            it2, pt2, d2 = lagged.pop(0)
            V.wait(tp, it2)
            V.copy(slot_ap(d2), pt2)
            hist[n - 1] = vp.inc(V.last)
        hq_ready = vp.n

        for oc in range(4):
            for jg in range(16):
                slot = wb[2 + jg % 2]
                G.wait(tp, ring_rel.get(2 + jg % 2, 0))
                ws = c.wsem[2 + jg % 2]
                for ji4 in range(4):
                    ws.inc(G.dma(slot[:, ji4 * 512:(ji4 + 1) * 512], c.w2q_hbm[oc, jg * 4 + ji4]), 16)
                iw = ws.n
                Tn.wait(ws, iw)
                if jg == 0:
                    Tn.wait(vp, hq_ready)
                    c.waitb(Tn, [2, 3, 4, 5])
                for ji in range(4):
                    j = jg * 4 + ji
                    for tci in range(4):
                        lhs = fmA[j // 4][:, (j % 4) * 512 + tci * 128:(j % 4) * 512 + tci * 128 + 128]
                        Tn.mm(PS[2 + tci][:], lhs, slot[:, ji * 512:(ji + 1) * 512],
                              start=(j == 0), stop=(j == 63))
                ring_rel[2 + jg % 2] = tp.n + 1
                tp.inc(Tn.last)
            itg = tp.n
            V.wait(tp, itg)
            for tci in range(4):
                V.ts(fmB[tci][:, oc * 512:(oc + 1) * 512], PS[2 + tci][:],
                     c.cols[:, 44 + tci:45 + tci], None, OP.mult)
            iv = vp.inc(V.last)
            for tci in range(4):
                c.relb(2 + tci, vp, iv)
        for tci in range(4):
            _ln_write(c, tcs[tci], fmB[tci][:])



def _bf(x):
    return np.ascontiguousarray(x, dtype=ml_dtypes.bfloat16)


def _f32(x):
    return np.ascontiguousarray(x, dtype=np.float32)


def _tile_fm(wT, groups):
    raise NotImplementedError


def _prep_gla_qk(wT):
    out = np.empty((2, 4, 128, 2048), np.float32)
    for mg in range(2):
        for fg in range(4):
            for fi in range(4):
                f = fg * 4 + fi
                blk = wT[f * 128:(f + 1) * 128, mg * 512:(mg + 1) * 512]
                out[mg, fg, :, fi * 512:(fi + 1) * 512] = blk
    return out


def _prep_gla_vgo(wT):
    out = np.empty((4, 4, 128, 2048), np.float32)
    for oc in range(4):
        for fg in range(4):
            for fi in range(4):
                f = fg * 4 + fi
                out[oc, fg, :, fi * 512:(fi + 1) * 512] = wT[f * 128:(f + 1) * 128, oc * 512:(oc + 1) * 512]
    return out


def prep_inputs(inputs):
    g = {}
    g["c_ident"] = np.eye(128, dtype=ml_dtypes.bfloat16)
    g["c_mask"] = _bf(np.tril(np.ones((128, 128), np.float32)).T)
    H = np.array([[1.0]], np.float32)
    while H.shape[0] < 128:
        H = np.block([[H, H], [H, -H]])
    g["c_h128"] = _bf(H)
    g["c_onesrow"] = np.ones((1, 128), ml_dtypes.bfloat16)
    g["c_onescol"] = np.ones((128, 1), ml_dtypes.bfloat16)
    g["c_onesrowf"] = np.ones((1, 128), np.float32)

    wqT = _f32(inputs["L0_attn_Wq"].T)
    wkT = _f32(inputs["L0_attn_Wk"].T)
    g["gla_qk_t"] = np.stack([_prep_gla_qk(wqT), _prep_gla_qk(wkT)])
    wvT = _f32(inputs["L0_attn_Wv"].T)
    wgT = _f32(inputs["L0_attn_Wg"].T)
    woT = _f32(inputs["L0_attn_Wo"].T)
    g["gla_vgo_t"] = np.stack([_prep_gla_vgo(wvT), _prep_gla_vgo(wgT), _prep_gla_vgo(woT)])
    wkg1T = _f32(inputs["L0_attn_Wkg1"].T)
    g["wkg1_t"] = _f32(wkg1T.reshape(16, 128, 16).transpose(1, 0, 2).reshape(128, 256))
    g["wkg2T"] = _f32(inputs["L0_attn_Wkg2"].T)
    g["bkg2_cols"] = _f32(inputs["L0_attn_bkg2"].reshape(8, 128).T)
    g["bg"] = _f32(inputs["L0_attn_bg"].reshape(1, -1))

    wupT = _f32(inputs["L0_ffn_W_up"].T)
    wgateT = _f32(inputs["L0_ffn_W_gate"].T)
    wdownT = _f32(inputs["L0_ffn_W_down"].T)
    ug = np.empty((2, 4, 4, 4, 128, 2048), np.float32)
    for wi, w in enumerate([wupT, wgateT]):
        for jc in range(4):
            for jg in range(4):
                for fg in range(4):
                    for fi in range(4):
                        f = fg * 4 + fi
                        ug[wi, jc, jg, fg, :, fi * 512:(fi + 1) * 512] = \
                            w[f * 128:(f + 1) * 128, jc * 2048 + jg * 512:jc * 2048 + (jg + 1) * 512]
    g["ug_t"] = ug
    dn = np.empty((4, 16, 128, 2048), np.float32)
    for oc in range(4):
        for jg in range(16):
            for ji in range(4):
                j = jg * 4 + ji
                dn[oc, jg, :, ji * 512:(ji + 1) * 512] = \
                    wdownT[j * 128:(j + 1) * 128, oc * 512:(oc + 1) * 512]
    g["dn_t"] = dn
    g["su_ug_cols"] = _f32(inputs["L0_ffn_SU_ug"].reshape(16, 128).T)
    g["su_dp_cols"] = _f32(inputs["L0_ffn_SU_dp"].reshape(64, 128).T)
    wsc = _f32(inputs["L0_ffn_Wscale_ug"])
    g["wsc_up_cols"] = _f32(wsc[:INTER].reshape(64, 128).T)
    g["wsc_gate_cols"] = _f32(wsc[INTER:].reshape(64, 128).T)
    g["wsc_dp_row"] = _f32(inputs["L0_ffn_Wscale_dp"].reshape(1, -1))

    kvdT = _f32(inputs["L1_attn_kv_down_w"].T)
    kvd = np.empty((4, 128, 2048), np.float32)
    for fg in range(4):
        for fi in range(4):
            f = fg * 4 + fi
            kvd[fg, :, fi * 512:(fi + 1) * 512] = kvdT[f * 128:(f + 1) * 128, :]
    g["kvd_t"] = kvd
    qdT = _f32(inputs["L1_attn_query_down_w"].T)
    g["qd_t"] = _prep_gla_qk(qdT)
    kupT = _f32(inputs["L1_attn_key_up_w"].T)
    vupT = _f32(inputs["L1_attn_value_up_w"].T)
    krT = _f32(inputs["L1_attn_key_rope_w"].T)
    kup = np.empty((16, 128, 512), np.float32)
    vup = np.empty((16, 128, 512), np.float32)
    krt = np.empty((16, 128, 256), np.float32)
    for h in range(16):
        for kf in range(4):
            kup[h, :, kf * 128:(kf + 1) * 128] = kupT[kf * 128:(kf + 1) * 128, h * 128:(h + 1) * 128]
            vup[h, :, kf * 128:(kf + 1) * 128] = vupT[kf * 128:(kf + 1) * 128, h * 128:(h + 1) * 128]
            krt[h, :, kf * 64:(kf + 1) * 64] = krT[kf * 128:(kf + 1) * 128, h * 64:(h + 1) * 64]
    g["kup_t"] = kup
    g["vup_t"] = vup
    g["kr_t"] = krt
    qupT = _f32(inputs["L1_attn_query_up_w"].T)
    qup = np.empty((16, 128, 1024), np.float32)
    for h in range(16):
        for qf in range(8):
            qup[h, :, qf * 128:(qf + 1) * 128] = qupT[qf * 128:(qf + 1) * 128, h * 128:(h + 1) * 128]
    g["qup_t"] = qup
    qrT = _f32(inputs["L1_attn_query_rope_w"].T)
    qr = np.empty((16, 128, 512), np.float32)
    for h in range(16):
        for qf in range(8):
            qr[h, :, qf * 64:(qf + 1) * 64] = qrT[qf * 128:(qf + 1) * 128, h * 64:(h + 1) * 64]
    g["qr_t"] = qr
    outwT = _f32(inputs["L1_attn_out_w"].T)
    ow = np.empty((4, 4, 128, 2048), np.float32)
    for oc in range(4):
        for hg in range(4):
            for hi in range(4):
                h = hg * 4 + hi
                ow[oc, hg, :, hi * 512:(hi + 1) * 512] = \
                    outwT[h * 128:(h + 1) * 128, oc * 512:(oc + 1) * 512]
    g["outw_t"] = ow
    for nm, key in [("kvd_b", "L1_attn_kv_down_b"), ("kup_b", "L1_attn_key_up_b"),
                    ("vup_b", "L1_attn_value_up_b"),
                    ("qd_b", "L1_attn_query_down_b"), ("qup_b", "L1_attn_query_up_b")]:
        g[nm] = _f32(inputs[key].reshape(-1, 128).T)
    g["kr_b"] = _f32(inputs["L1_attn_key_rope_b"].reshape(16, 64).T)
    g["qr_b"] = _f32(inputs["L1_attn_query_rope_b"].reshape(16, 64).T)
    g["outw_b"] = _f32(inputs["L1_attn_out_b"].reshape(1, -1))

    w1T = _f32(inputs["L1_ffn_w1"].T)
    w1t = np.empty((16, 16, 128, 512), np.float32)
    for jc in range(16):
        for f in range(16):
            w1t[jc, f] = w1T[f * 128:(f + 1) * 128, jc * 512:(jc + 1) * 512]
    g["w1_t"] = w1t
    w2T = _f32(inputs["L1_ffn_w2"].T)
    w2t = np.empty((4, 64, 128, 512), np.float32)
    for os_ in range(4):
        for j in range(64):
            w2t[os_, j] = w2T[j * 128:(j + 1) * 128, os_ * 512:(os_ + 1) * 512]
    g["w2_t"] = w2t
    return g


def kernel(**inputs):
    g = prep_inputs(inputs)
    x = np.ascontiguousarray(inputs["L0_attn_x"], dtype=np.float32)
    nc = build_program(_UPTO)
    in_maps = []
    for core in range(NCORES):
        m = dict(g)
        m["x_in"] = x[core]
        in_maps.append(m)
    res = run_bass_kernel_spmd(nc, in_maps, core_ids=list(range(NCORES)))
    out = np.stack([np.asarray(res.results[i]["out"], np.float32) for i in range(NCORES)])
    return out


# version 05
# speedup vs baseline: 1.3711x; 1.0071x over previous
import math
from contextlib import ExitStack

import numpy as np
import ml_dtypes

import concourse.bass as bass
import concourse.mybir as mybir
from concourse.bass_utils import run_bass_kernel_spmd

F32 = mybir.dt.float32
BF16 = mybir.dt.bfloat16
AF = mybir.ActivationFunctionType
OP = mybir.AluOpType
AX = mybir.AxisListType

NCORES = 8
T = 2048
D = 2048
NT = 16
SLAB = 512
NSLAB = 4
LN_EPS = 1e-5
GLA_NORM = 16.0
GLA_SCALE_LOG = math.log(128.0 ** (-0.5))
MLA_SCALE = 1.0 / math.sqrt(192.0)
INTER = 8192

M23 = 1.5 * 2.0 ** 23
M52 = 1.5 * 2.0 ** 52
M32 = 1.5 * 2.0 ** 32
C52_32 = M52 - M32

_UPTO = 4



class Eng:

    def __init__(self, e, auto_drain=True):
        self.e = e
        self.auto_drain = auto_drain
        self.dirty = set()
        self.rd = set()
        self.waited = {}

    @staticmethod
    def _names(aps):
        out = []
        for a in aps:
            if a is None or isinstance(a, (int, float)):
                continue
            t = getattr(a, "tensor", None)
            if t is not None:
                out.append(t.name)
        return out

    def _pre(self, outs, ins):
        if not self.auto_drain:
            return
        o = set(self._names(outs))
        i = set(self._names(ins))
        if (i & self.dirty) or (o & self.dirty) or (o & self.rd):
            self.e.drain()
            self.dirty.clear()
            self.rd.clear()
        self.dirty |= o
        self.rd |= i

    def wait(self, sem, val):
        val = int(val)
        if val <= 0 or self.waited.get(sem.name, -1) >= val:
            return
        self.e.wait_ge(sem.h, val)
        self.waited[sem.name] = val

    def _em(self, inst):
        self.last = inst
        return inst

    def dma(self, out, in_):
        self._pre([out], [in_])
        return self._em(self.e.dma_start(out=out, in_=in_))

    def copy(self, out, in_):
        self._pre([out], [in_])
        return self._em(self.e.tensor_copy(out, in_))

    def ts(self, out, in0, s1, s2, op0, op1=None):
        self._pre([out], [in0, s1, s2])
        if op1 is None:
            return self._em(self.e.tensor_scalar(out, in0, s1, None, op0))
        return self._em(self.e.tensor_scalar(out, in0, s1, s2, op0, op1))

    def tt(self, out, in0, in1, op):
        self._pre([out], [in0, in1])
        return self._em(self.e.tensor_tensor(out, in0, in1, op))

    def stt(self, out, in0, s, in1, op0, op1):
        self._pre([out], [in0, s, in1])
        return self._em(self.e.scalar_tensor_tensor(out, in0, s, in1, op0, op1))

    def scan(self, out, d0, d1, init, op0, op1):
        self._pre([out], [d0, d1])
        return self._em(self.e.tensor_tensor_scan(out, d0, d1, init, op0, op1))

    def red(self, out, in_, op, absval=False):
        self._pre([out], [in_])
        return self._em(self.e.tensor_reduce(out, in_, AX.X, op, apply_absolute_value=absval))

    def recip(self, out, in_):
        self._pre([out], [in_])
        return self._em(self.e.reciprocal(out, in_))

    def act(self, out, in_, func, bias=0.0, scale=1.0, accum_out=None):
        outs = [out] + ([accum_out] if accum_out is not None else [])
        self._pre(outs, [in_, bias, scale])
        return self._em(self.e.activation(out, in_, func, bias=bias, scale=scale, accum_out=accum_out))

    def mm(self, out, lhsT, rhs, start, stop):
        self._pre([out], [lhsT, rhs])
        return self._em(self.e.matmul(out, lhsT, rhs, start=start, stop=stop))

    def trans(self, out, in_, ident):
        self._pre([out], [in_, ident])
        return self._em(self.e.transpose(out, in_, ident))

    def memset(self, ap, v):
        self._pre([ap], [])
        return self._em(self.e.memset(ap, v))

    def nop(self):
        return self.e.nop()


class Sem:
    def __init__(self, h, name):
        self.h = h
        self.name = name
        self.n = 0

    def inc(self, inst, by=1):
        inst.then_inc(self.h, by)
        self.n += by
        return self.n


class Ctx:
    pass



def build_program(upto=_UPTO):
    nc = bass.Bass("TRN2", target_bir_lowering=False)
    c = Ctx()
    c.nc = nc

    def din(name, shape, dt=F32):
        return nc.dram_tensor(name, shape, dt, kind="ExternalInput").ap()

    c.x_in = din("x_in", [T, D])
    c.out_hbm = nc.dram_tensor("out", [T, D], F32, kind="ExternalOutput").ap()

    c.ident = din("c_ident", [128, 128], BF16)
    c.maskUT = din("c_mask", [128, 128], BF16)
    c.h128 = din("c_h128", [128, 128], BF16)
    c.onesrow = din("c_onesrow", [1, 128], BF16)
    c.onescol = din("c_onescol", [128, 1], BF16)
    c.onesrowf = din("c_onesrowf", [1, 128])

    c.gla_qk_t = din("gla_qk_t", [2, 2, 4, 128, 2048])
    c.gla_vgo_t = din("gla_vgo_t", [3, 4, 4, 128, 2048])
    c.wkg1_t = din("wkg1_t", [128, 256])
    c.wkg2T = din("wkg2T", [16, 1024])
    c.bkg2_cols = din("bkg2_cols", [128, 8])
    c.bg = din("bg", [1, D])

    c.ug_t = din("ug_t", [2, 4, 4, 4, 128, 2048])
    c.dn_t = din("dn_t", [4, 16, 128, 2048])
    c.su_ug_cols = din("su_ug_cols", [128, 16])
    c.wsc_up_cols = din("wsc_up_cols", [128, 64])
    c.wsc_gate_cols = din("wsc_gate_cols", [128, 64])
    c.su_dp_cols = din("su_dp_cols", [128, 64])
    c.wsc_dp_row = din("wsc_dp_row", [1, D])

    c.kvd_t = din("kvd_t", [4, 128, 2048])
    c.qd_t = din("qd_t", [2, 4, 128, 2048])
    c.kup_t = din("kup_t", [16, 128, 512])
    c.vup_t = din("vup_t", [16, 128, 512])
    c.kr_t = din("kr_t", [16, 128, 256])
    c.qup_t = din("qup_t", [16, 128, 1024])
    c.qr_t = din("qr_t", [16, 128, 512])
    c.outw_t = din("outw_t", [4, 4, 128, 2048])
    c.kvd_b = din("kvd_b", [128, 4])
    c.kup_b = din("kup_b", [128, 16])
    c.vup_b = din("vup_b", [128, 16])
    c.kr_b = din("kr_b", [64, 16])
    c.qd_b = din("qd_b", [128, 8])
    c.qup_b = din("qup_b", [128, 16])
    c.qr_b = din("qr_b", [64, 16])
    c.outw_b = din("outw_b", [1, D])

    c.w1_t = din("w1_t", [16, 16, 128, 512])
    c.w2_t = din("w2_t", [4, 64, 128, 512])
    c.w1q_hbm = nc.dram_tensor("w1q_s", [16, 16, 128, 512], BF16, kind="Internal").ap()
    c.w2q_hbm = nc.dram_tensor("w2q_s", [4, 64, 128, 512], BF16, kind="Internal").ap()

    with ExitStack() as stack:
        e = stack.enter_context
        c.block = e(nc.Block())
        c.T = Eng(nc.tensor, auto_drain=False); c.V = Eng(nc.vector); c.S = Eng(nc.scalar)
        c.G = Eng(nc.gpsimd, auto_drain=False); c.Y = Eng(nc.sync, auto_drain=False)
        for name in ["ld", "wd", "tp", "vp", "sp", "gp", "st", "ini"]:
            setattr(c, name, Sem(e(nc.semaphore(name)), name))
        c.wsem = [Sem(e(nc.semaphore(f"w{i}")), f"w{i}") for i in range(8)]
        c.gsem = [Sem(e(nc.semaphore(f"g{i}")), f"g{i}") for i in range(4)]

        c.res = [e(nc.sbuf_tensor(f"res{i}", [128, D], BF16)) for i in range(NT)]
        c.fmA = [e(nc.sbuf_tensor(f"fmA{i}", [128, D], BF16)) for i in range(16)]
        c.fmB = [e(nc.sbuf_tensor(f"fmB{i}", [128, D], BF16)) for i in range(4)]
        c.f32a = [e(nc.sbuf_tensor(f"f32a{i}", [128, D], F32)) for i in range(2)]
        c.wb = [e(nc.sbuf_tensor(f"wb{i}", [128, D], BF16)) for i in range(6)]
        c.Sst = e(nc.sbuf_tensor("Sst", [128, 1024], F32))
        c.Stmp = e(nc.sbuf_tensor("Stmp", [128, 256], F32))
        c.Sbf = e(nc.sbuf_tensor("Sbf", [128, 1024], BF16))
        c.stat = e(nc.sbuf_tensor("stat", [128, 64], F32))
        c.cols = e(nc.sbuf_tensor("cols", [128, 160], F32))
        c.rows = e(nc.sbuf_tensor("rows", [1, 512], F32))
        c.rbf = e(nc.sbuf_tensor("rbf", [1, 2048], BF16))
        c.bcols = e(nc.sbuf_tensor("bcols", [128, 96], F32))
        c.sid = e(nc.sbuf_tensor("sid", [128, 128], BF16))
        c.smask = e(nc.sbuf_tensor("smask", [128, 128], BF16))
        c.sh128 = e(nc.sbuf_tensor("sh128", [128, 128], BF16))
        c.soner = e(nc.sbuf_tensor("soner", [1, 128], BF16))
        c.sonec = e(nc.sbuf_tensor("sonec", [128, 1], BF16))
        c.sonerf = e(nc.sbuf_tensor("sonerf", [1, 128], F32))
        c.skg2 = e(nc.sbuf_tensor("skg2", [16, 1024], BF16))
        c.outf32 = c.f32a[1]

        c.PS = [e(nc.psum_tensor(f"PS{i}", [128, 512], F32)) for i in range(6)]
        c.PT = [e(nc.psum_tensor(f"PT{i}", [128, 1024], BF16)) for i in range(2)]
        c.dumb = c.f32a[1]
        c.bank_rel = {}

        def _relb(b, sem, val):
            c.bank_rel[b] = (sem, val)
        c.relb = _relb

        def _waitb(E, banks):
            for b in banks:
                if b in c.bank_rel:
                    sem, val = c.bank_rel[b]
                    E.wait(sem, val)
        c.waitb = _waitb

        for i, cv in enumerate([0.0, 1.0, -1.0, 0.5, LN_EPS, GLA_SCALE_LOG]):
            nc.const_aps.register(F32, cv, c.cols[:, 150 + i:151 + i])

        _init(c)
        if upto >= 1:
            _gla(c)
        if upto >= 2:
            _ffn0(c)
        if upto >= 3:
            _mla(c)
        if upto >= 4:
            _ffn1(c)
        _finish(c)

    return nc


def _init(c):
    Tn, V, S, G, Y = c.T, c.V, c.S, c.G, c.Y
    ld, ini = c.ld, c.ini
    ld.inc(Y.dma(c.sid[:], c.ident), 16)
    ld.inc(Y.dma(c.smask[:], c.maskUT), 16)
    ld.inc(Y.dma(c.sh128[:], c.h128), 16)
    ld.inc(Y.dma(c.soner[:], c.onesrow), 16)
    ld.inc(Y.dma(c.sonerf[:], c.onesrowf), 16)
    ld.inc(Y.dma(c.sonec[:], c.onescol), 16)
    for i in range(NT):
        c.wd.inc(G.dma(c.res[i][:], c.x_in[i * 128:(i + 1) * 128, :]), 16)
    for i, cv in enumerate([0.0, 1.0, -1.0, 0.5, LN_EPS, GLA_SCALE_LOG]):
        G.memset(c.cols[:, 150 + i:151 + i], cv)
    c.wd.inc(G.dma(c.skg2[:], c.wkg2T), 16)
    ld.inc(Y.dma(c.bcols[:, 8:16], c.bkg2_cols), 16)
    c.wd.inc(G.dma(c.rbf[:, 0:D], c.bg), 16)
    G.memset(c.Sst[:], 0.0)
    G.memset(c.Sbf[:], 0.0)
    ini.inc(G.last, 1)
    for E in (Tn, V, S, Y):
        E.wait(ini, 1)
        E.wait(ld, ld.n)
        E.wait(c.wd, c.wd.n)
    V.ts(c.bcols[:, 0:8], c.bcols[:, 8:16], -1.0, None, OP.mult)
    c.vp.inc(V.last)
    c.negbkg2 = c.bcols[:, 0:8]
    c.bg_row = c.rbf[:, 0:D]


def _finish(c):
    V, Y = c.V, c.Y
    vp, st = c.vp, c.st
    for tc in range(NT):
        V.wait(st, st.n)
        V.copy(c.outf32[:], c.res[tc][:])
        iv = vp.inc(V.last)
        Y.wait(vp, iv)
        st.inc(Y.dma(c.out_hbm[tc * 128:(tc + 1) * 128, :], c.outf32[:]), 16)
    Y.wait(st, st.n)



def _transpose_in(c, pairs, su_cols=None):
    Tn, V = c.T, c.V
    tp, vp = c.tp, c.vp
    hist = {}
    vp0 = vp.n
    for n, (src, dst) in enumerate(pairs):
        pt = c.PT[n % 2][:, ((n // 2) % 4) * 256:((n // 2) % 4) * 256 + 128]
        Tn.wait(vp, hist[n - 2] if n >= 2 else vp0)
        Tn.trans(pt, src, c.sid[:])
        it = tp.inc(Tn.last)
        V.wait(tp, it)
        if su_cols is not None and su_cols[n] is not None:
            V.ts(dst, pt, su_cols[n], None, OP.mult)
        else:
            V.copy(dst, pt)
        hist[n] = vp.inc(V.last)


def _ln_write(c, tc, src_tm_bf, final=False):
    V, S = c.V, c.S
    vp, sp = c.vp, c.sp
    s = c.f32a[0]
    V.tt(s[:], src_tm_bf, c.res[tc][:], OP.add)
    V.red(c.stat[:, 0:1], s[:], OP.add)
    iv = vp.inc(V.last)
    S.wait(vp, iv)
    for q in range(4):
        S.act(c.dumb[:, 0:512], s[:, q * 512:(q + 1) * 512], AF.Square, accum_out=c.stat[:, 4 + q:5 + q])
    isp = sp.inc(S.last)
    V.wait(sp, isp)
    V.ts(c.stat[:, 1:2], c.stat[:, 0:1], 1.0 / 2048.0, None, OP.mult)
    V.tt(c.stat[:, 8:9], c.stat[:, 4:5], c.stat[:, 5:6], OP.add)
    V.tt(c.stat[:, 9:10], c.stat[:, 6:7], c.stat[:, 7:8], OP.add)
    V.tt(c.stat[:, 10:11], c.stat[:, 8:9], c.stat[:, 9:10], OP.add)
    V.ts(c.stat[:, 11:12], c.stat[:, 10:11], 1.0 / 2048.0, None, OP.mult)
    V.tt(c.stat[:, 12:13], c.stat[:, 1:2], c.stat[:, 1:2], OP.mult)
    V.tt(c.stat[:, 13:14], c.stat[:, 11:12], c.stat[:, 12:13], OP.subtract)
    iv = vp.inc(V.last)
    S.wait(vp, iv)
    S.act(c.stat[:, 14:15], c.stat[:, 13:14], AF.Ln, bias=LN_EPS)
    S.act(c.stat[:, 2:3], c.stat[:, 14:15], AF.Exp, scale=-0.5)
    isp = sp.inc(S.last)
    V.wait(sp, isp)
    dst = c.outf32[:] if final else c.res[tc][:]
    V.ts(dst, s[:], c.stat[:, 1:2], c.stat[:, 2:3], OP.subtract, OP.mult)
    return vp.inc(V.last)



def _gla(c):
    Tn, V, S, G = c.T, c.V, c.S, c.G
    tp, vp, sp, wd = c.tp, c.vp, c.sp, c.wd
    PS = c.PS
    fmA, fmB, f32a, wb = c.fmA, c.fmB, c.f32a, c.wb
    ring_rel = {i: tp.n for i in range(5)}

    def xfm(f):
        return fmB[f // 4][:, (f % 4) * 512:(f % 4) * 512 + 512]

    def gpt(m):
        return f32a[m // 4][:, (m % 4) * 512:(m % 4) * 512 + 512]

    for sb in range(NSLAB):
        tcs = [sb * 4 + i for i in range(4)]

        pairs = []
        for ci in range(4):
            for f in range(16):
                pairs.append((c.res[tcs[ci]][:, f * 128:(f + 1) * 128],
                              fmB[f // 4][:, (f % 4) * 512 + ci * 128:(f % 4) * 512 + ci * 128 + 128]))
        _transpose_in(c, pairs)
        xfm_ready = vp.n

        G.wait(tp, ring_rel.get(4, 0))
        iw4 = c.wsem[4].inc(G.dma(wb[4][:, 0:256], c.wkg1_t), 16)
        Tn.wait(c.wsem[4], iw4)
        Tn.wait(vp, xfm_ready)
        c.waitb(Tn, [0])
        pk = PS[0][0:16, :]
        for f in range(16):
            Tn.mm(pk, wb[4][:, f * 16:(f + 1) * 16], xfm(f), start=(f == 0), stop=(f == 15))
        it = tp.inc(Tn.last)
        ring_rel[4] = it
        V.wait(tp, it)
        kg1 = wb[5][0:16, 0:512]
        V.copy(kg1, pk)
        ivk = vp.inc(V.last)
        Tn.wait(vp, ivk)
        sp_hist = {}
        for m in range(8):
            p = PS[m % 2][:]
            if m >= 2:
                Tn.wait(sp, sp_hist[m - 2])
            Tn.mm(p, c.skg2[:, m * 128:(m + 1) * 128], kg1, start=True, stop=True)
            it = tp.inc(Tn.last)
            S.wait(tp, it)
            S.wait(c.vp, 1)
            S.act(gpt(m), p, AF.Exp, scale=-1.0, bias=c.negbkg2[:, m:m + 1])
            S.act(gpt(m), gpt(m), AF.Ln, bias=1.0)
            sp_hist[m] = sp.inc(S.last)
            c.relb(m % 2, sp, sp_hist[m])
        V.wait(sp, sp.n)
        for m in range(8):
            for ch in range(4):
                seg = gpt(m)[:, ch * 128:(ch + 1) * 128]
                V.scan(seg, seg, seg, 0.0, OP.add, OP.bypass)
        ivG = vp.inc(V.last)
        S.wait(vp, ivG)
        for m in range(8):
            S.act(c.cols[:, 16 + m * 4:16 + m * 4 + 4], gpt(m)[:, 127::128], AF.Exp, scale=-1.0 / GLA_NORM)
        sp_a = sp.inc(S.last)

        def qk(qi, base, escale, ebias):
            for mg in range(2):
                for fg in range(4):
                    slot = wb[fg % 2]
                    G.wait(tp, ring_rel.get(fg % 2, 0))
                    ws = c.wsem[fg % 2]
                    iw = ws.inc(G.dma(slot[:], c.gla_qk_t[qi, mg, fg]), 16)
                    Tn.wait(ws, iw)
                    if fg == 0:
                        Tn.wait(vp, xfm_ready)
                        c.waitb(Tn, [2, 3, 4, 5])
                    for fi in range(4):
                        f = fg * 4 + fi
                        for m4 in range(4):
                            Tn.mm(PS[2 + m4][:], slot[:, fi * 512 + m4 * 128:fi * 512 + m4 * 128 + 128],
                                  xfm(f), start=(f == 0), stop=(f == 15))
                    ring_rel[fg % 2] = tp.n + 1
                    tp.inc(Tn.last)
                itg = tp.n
                for m4 in range(4):
                    m = mg * 4 + m4
                    et = wb[4][:, 256 + (m4 % 2) * 512:256 + (m4 % 2) * 512 + 512]
                    S.wait(vp, ivG)
                    S.wait(vp, c._et_vp.get(m4 % 2, 0))
                    S.act(et, gpt(m), AF.Exp, bias=ebias, scale=escale)
                    isp = sp.inc(S.last)
                    V.wait(tp, itg)
                    V.wait(sp, isp)
                    dst = fmA[base + m // 4][:, (m % 4) * 512:(m % 4) * 512 + 512]
                    V.tt(dst, PS[2 + m4][:], et, OP.mult)
                    c._et_vp[m4 % 2] = vp.inc(V.last)
                    c.relb(2 + m4, vp, c._et_vp[m4 % 2])

        c._et_vp = {}
        qk(0, 0, -1.0 / GLA_NORM, 0.0)
        qk(1, 2, +1.0 / GLA_NORM, GLA_SCALE_LOG)

        def tmproj(wi, base, bias_row):
            for oc in range(4):
                for fg in range(4):
                    slot = wb[2 + fg % 2]
                    G.wait(tp, ring_rel.get(2 + fg % 2, 0))
                    ws = c.wsem[2 + fg % 2]
                    iw = ws.inc(G.dma(slot[:], c.gla_vgo_t[wi, oc, fg]), 16)
                    Tn.wait(ws, iw)
                    if fg == 0:
                        Tn.wait(vp, xfm_ready)
                        c.waitb(Tn, [2, 3, 4, 5])
                    for fi in range(4):
                        f = fg * 4 + fi
                        for tci in range(4):
                            Tn.mm(PS[2 + tci][:], xfm(f)[:, tci * 128:(tci + 1) * 128],
                                  slot[:, fi * 512:(fi + 1) * 512],
                                  start=(f == 0), stop=(f == 15 and bias_row is None))
                    ring_rel[2 + fg % 2] = tp.n + 1
                    tp.inc(Tn.last)
                if bias_row is not None:
                    for tci in range(4):
                        Tn.mm(PS[2 + tci][:], c.soner[0:1, 0:128], bias_row[:, oc * 512:(oc + 1) * 512],
                              start=False, stop=True)
                    itg = tp.inc(Tn.last)
                else:
                    itg = tp.n
                V.wait(tp, itg)
                for tci in range(4):
                    V.copy(fmA[base + tci][:, oc * 512:(oc + 1) * 512], PS[2 + tci][:])
                iv = vp.inc(V.last)
                for tci in range(4):
                    c.relb(2 + tci, vp, iv)

        tmproj(0, 4, None)
        tmproj(1, 8, c.bg_row)

        v_ready = vp.n
        prev_state_vp = getattr(c, "_gla_state_vp", {})
        kv_read_vp = getattr(c, "_gla_kv_vp", 0)
        sq_hist = getattr(c, "_gla_sq", {})
        for ch in range(4):
            o_evac = {}
            for h in range(16):
                m = h // 2
                r0 = (h % 2) * 64
                qs = fmA[m // 4][r0:r0 + 64, (m % 4) * 512 + ch * 128:(m % 4) * 512 + ch * 128 + 128]
                ks = fmA[2 + m // 4][r0:r0 + 64, (m % 4) * 512 + ch * 128:(m % 4) * 512 + ch * 128 + 128]
                vs = fmA[4 + ch][:, h * 128:(h + 1) * 128]
                Sst_h = c.Sst[r0:r0 + 64, (h // 2) * 128:(h // 2) * 128 + 128]
                Stmp_h = c.Stmp[r0:r0 + 64, 0:128]
                Sbf_h = c.Sbf[r0:r0 + 64, (h // 2) * 128:(h // 2) * 128 + 128]
                a_col = c.cols[r0:r0 + 64, 16 + m * 4 + ch:16 + m * 4 + ch + 1]
                STp = PS[h % 2][:, 0:128]
                Op = PS[2 + h % 2][:, 0:128]
                KVp = PS[4][r0:r0 + 64, 0:128]
                kTp = c.PT[0][:, (h % 2) * 256:(h % 2) * 256 + 64]
                stm = wb[5][:, 1024 + (h % 2) * 128:1024 + (h % 2) * 128 + 128]
                kT_sb = wb[5][:, 1536 + (h % 2) * 64:1536 + (h % 2) * 64 + 64]

                Tn.wait(vp, v_ready)
                if h >= 2:
                    Tn.wait(vp, o_evac.get(h - 2, 0))
                Tn.wait(sp, sq_hist.get((ch, h - 2), sq_hist.get((ch - 1, h + 14), 0)))
                Tn.mm(STp, ks, qs, start=True, stop=True)
                it1 = tp.inc(Tn.last)
                V.wait(tp, it1)
                V.tt(stm, STp, c.smask[:], OP.mult)
                iv1 = vp.inc(V.last)
                Tn.wait(vp, iv1)
                Tn.wait(vp, prev_state_vp.get(h, 0))
                Tn.mm(Op, stm, vs, start=True, stop=False)
                Tn.mm(Op, qs, Sbf_h, start=False, stop=True)
                it2 = tp.inc(Tn.last)
                Tn.trans(kTp, ks, c.sid[r0:r0 + 64, r0:r0 + 64])
                it3 = tp.inc(Tn.last)
                V.wait(tp, it3)
                V.copy(kT_sb, kTp)
                iv2 = vp.inc(V.last)
                Tn.wait(vp, iv2)
                Tn.wait(vp, kv_read_vp)
                Tn.mm(KVp, kT_sb, vs, start=True, stop=True)
                it4 = tp.inc(Tn.last)
                V.wait(tp, it4)
                V.wait(sp, sp_a)
                V.tt(Stmp_h, Sst_h, KVp, OP.add)
                V.ts(Sst_h, Stmp_h, a_col, None, OP.mult)
                V.copy(Sbf_h, Sst_h)
                prev_state_vp[h] = kv_read_vp = vp.inc(V.last)
                V.wait(tp, it2)
                V.copy(fmA[12 + ch][:, h * 128:(h + 1) * 128], Op)
                V.red(c.stat[:, 32 + h:33 + h], Op, OP.add)
                o_evac[h] = vp.inc(V.last)
                S.wait(tp, it2)
                S.act(c.dumb[:, 0:128], Op, AF.Square, accum_out=c.stat[:, 48 + h:49 + h])
                sq_hist[(ch, h)] = sp.inc(S.last)
            V.wait(sp, sp.n)
            V.ts(c.stat[:, 0:16], c.stat[:, 32:48], 1.0 / 128.0, None, OP.mult)
            V.ts(c.cols[:, 48:64], c.stat[:, 48:64], 1.0 / 128.0, None, OP.mult)
            V.tt(c.cols[:, 64:80], c.stat[:, 0:16], c.stat[:, 0:16], OP.mult)
            V.tt(c.cols[:, 80:96], c.cols[:, 48:64], c.cols[:, 64:80], OP.subtract)
            ivs = vp.inc(V.last)
            S.wait(vp, ivs)
            S.act(c.cols[:, 96:112], c.cols[:, 80:96], AF.Ln, bias=LN_EPS)
            S.act(c.stat[:, 16:32], c.cols[:, 96:112], AF.Exp, scale=-0.5)
            S.act(f32a[0][:], fmA[8 + ch][:], AF.Silu)
            isp = sp.inc(S.last)
            V.wait(sp, isp)
            for h in range(16):
                V.ts(fmA[8 + ch][:, h * 128:(h + 1) * 128], fmA[12 + ch][:, h * 128:(h + 1) * 128],
                     c.stat[:, 0 + h:1 + h], c.stat[:, 16 + h:17 + h], OP.subtract, OP.mult)
            for h in range(16):
                V.tt(fmA[12 + ch][:, h * 128:(h + 1) * 128], fmA[8 + ch][:, h * 128:(h + 1) * 128],
                     f32a[0][:, h * 128:(h + 1) * 128], OP.mult)
            vp.inc(V.last)
        c._gla_state_vp = prev_state_vp
        c._gla_kv_vp = kv_read_vp
        c._gla_sq = sq_hist

        z_pairs = []
        for ci in range(4):
            for f in range(16):
                z_pairs.append((fmA[12 + ci][:, f * 128:(f + 1) * 128],
                                fmB[f // 4][:, (f % 4) * 512 + ci * 128:(f % 4) * 512 + ci * 128 + 128]))
        Tn.wait(vp, vp.n)
        _transpose_in(c, z_pairs)
        zfm_ready = vp.n

        for oc in range(4):
            for fg in range(4):
                slot = wb[fg % 2]
                G.wait(tp, ring_rel.get(fg % 2, 0))
                ws = c.wsem[fg % 2]
                iw = ws.inc(G.dma(slot[:], c.gla_vgo_t[2, oc, fg]), 16)
                Tn.wait(ws, iw)
                if fg == 0:
                    Tn.wait(vp, zfm_ready)
                    c.waitb(Tn, [2, 3, 4, 5])
                for fi in range(4):
                    f = fg * 4 + fi
                    for tci in range(4):
                        Tn.mm(PS[2 + tci][:], xfm(f)[:, tci * 128:(tci + 1) * 128],
                              slot[:, fi * 512:(fi + 1) * 512], start=(f == 0), stop=(f == 15))
                ring_rel[fg % 2] = tp.n + 1
                tp.inc(Tn.last)
            itg = tp.n
            V.wait(tp, itg)
            for tci in range(4):
                V.copy(fmA[4 + tci][:, oc * 512:(oc + 1) * 512], PS[2 + tci][:])
            c._wo_evac = vp.inc(V.last)
            for tci in range(4):
                c.relb(2 + tci, vp, c._wo_evac)
        for tci in range(4):
            _ln_write(c, tcs[tci], fmA[4 + tci][:])



FFN0_BLOCKS = None


def _ffn0(c):
    Tn, V, S, G, Y = c.T, c.V, c.S, c.G, c.Y
    tp, vp, sp, ld = c.tp, c.vp, c.sp, c.ld
    PS = c.PS
    fmA, fmB, f32a, wb = c.fmA, c.fmB, c.f32a, c.wb
    ring_rel = {i: tp.n for i in range(5)}

    def xfm(f):
        return fmB[f // 4][:, (f % 4) * 512:(f % 4) * 512 + 512]

    def zsl(jj):
        return fmA[jj // 4][:, (jj % 4) * 512:(jj % 4) * 512 + 512]

    SC1 = 2.0 ** -5.5
    SC2 = 2.0 ** -6.5

    Y.wait(vp, vp.n)
    Y.wait(sp, sp.n)
    Y.wait(tp, tp.n)
    ld.inc(Y.dma(c.cols[:, 0:16], c.su_ug_cols), 16)
    ld.inc(Y.dma(c.cols[:, 16:80], c.wsc_up_cols), 16)
    ld.inc(Y.dma(c.cols[:, 80:144], c.wsc_gate_cols), 16)
    ld.inc(Y.dma(c.stat[:, 0:64], c.su_dp_cols), 16)
    ld.inc(Y.dma(f32a[0][0:1, 0:2048], c.wsc_dp_row), 16)
    V.wait(ld, ld.n)
    V.tt(c.cols[:, 16:80], c.cols[:, 16:80], c.stat[:, 0:64], OP.mult)
    V.ts(c.cols[:, 16:80], c.cols[:, 16:80], SC1, None, OP.mult)
    V.ts(c.cols[:, 80:144], c.cols[:, 80:144], SC1, None, OP.mult)
    V.ts(c.rbf[0:1, 0:2048], f32a[0][0:1, 0:2048], SC2, None, OP.mult)
    ivc = vp.inc(V.last)
    Tn.wait(vp, ivc)
    for oc4 in range(4):
        Tn.mm(PS[2][:], c.soner[0:1, 0:128], c.rbf[0:1, oc4 * 512:(oc4 + 1) * 512], start=True, stop=True)
        itb = tp.inc(Tn.last)
        V.wait(tp, itb)
        V.copy(wb[4][:, oc4 * 512:(oc4 + 1) * 512], PS[2][:])
        ivb = vp.inc(V.last)
        Tn.wait(vp, ivb)

    for sb in range(NSLAB):
        tcs = [sb * 4 + i for i in range(4)]
        pairs, sus = [], []
        for ci in range(4):
            for f in range(16):
                pairs.append((c.res[tcs[ci]][:, f * 128:(f + 1) * 128],
                              fmB[f // 4][:, (f % 4) * 512 + ci * 128:(f % 4) * 512 + ci * 128 + 128]))
                sus.append(c.cols[:, f:f + 1])
        _transpose_in(c, pairs, su_cols=sus)
        Tn.wait(vp, vp.n)
        Tn.wait(sp, sp.n)
        ev = {}
        for f in range(16):
            p = PS[2 + f % 4][:]
            Tn.wait(vp, ev.get(f - 4, 0))
            if f < 4:
                c.waitb(Tn, [2 + f % 4])
            Tn.mm(p, c.sh128[:], xfm(f), start=True, stop=True)
            it = tp.inc(Tn.last)
            V.wait(tp, it)
            V.copy(xfm(f), p)
            ev[f] = vp.inc(V.last)
            c.relb(2 + f % 4, vp, ev[f])
        V.wait(vp, vp.n)
        for s in (1, 2, 4, 8):
            prs = [t0 for t0 in range(16) if not (t0 & s)]
            for t0 in prs:
                V.tt(xfm(t0), xfm(t0), xfm(t0 ^ s), OP.add)
            for t0 in prs:
                V.stt(xfm(t0 ^ s), xfm(t0 ^ s), -2.0, xfm(t0), OP.mult, OP.add)
        xh_ready = vp.inc(V.last)

        for jc in range(4):
            for jg in range(4):
                for jp in range(2):
                    for fg in range(4):
                        slotU = wb[fg % 2]
                        slotG = wb[2 + fg % 2]
                        G.wait(tp, ring_rel.get(fg % 2, 0))
                        ws = c.wsem[fg % 2]
                        iwU = ws.inc(G.dma(slotU[:], c.ug_t[0, jc, jg, fg]), 16)
                        G.wait(tp, ring_rel.get(2 + fg % 2, 0))
                        ws2 = c.wsem[2 + fg % 2]
                        iwG = ws2.inc(G.dma(slotG[:], c.ug_t[1, jc, jg, fg]), 16)
                        Tn.wait(ws, iwU)
                        Tn.wait(ws2, iwG)
                        if fg == 0:
                            Tn.wait(vp, xh_ready)
                            c.waitb(Tn, [2, 3, 4, 5])
                        for fi in range(4):
                            f = fg * 4 + fi
                            for j2 in range(2):
                                co = fi * 512 + (jp * 2 + j2) * 128
                                Tn.mm(PS[2 + j2][:], slotU[:, co:co + 128], xfm(f),
                                      start=(f == 0), stop=(f == 15))
                                Tn.mm(PS[4 + j2][:], slotG[:, co:co + 128], xfm(f),
                                      start=(f == 0), stop=(f == 15))
                        ring_rel[fg % 2] = tp.n + 1
                        ring_rel[2 + fg % 2] = tp.n + 1
                        tp.inc(Tn.last)
                    itg = tp.n
                    for j2 in range(2):
                        jj = jc * 16 + jg * 4 + jp * 2 + j2
                        S.wait(tp, itg)
                        S.act(f32a[0][:, j2 * 512:(j2 + 1) * 512], PS[4 + j2][:], AF.Silu,
                              scale=c.cols[:, 80 + jj:81 + jj])
                        isp = sp.inc(S.last)
                        c.relb(4 + j2, sp, isp)
                        V.wait(sp, isp)
                        V.stt(zsl(jj), PS[2 + j2][:], c.cols[:, 16 + jj:17 + jj],
                              f32a[0][:, j2 * 512:(j2 + 1) * 512], OP.mult, OP.mult)
                        iv = vp.inc(V.last)
                        c.relb(2 + j2, vp, iv)
            Tn.wait(vp, vp.n)
            ev2 = {}
            for jt in range(16):
                jj = jc * 16 + jt
                p = PS[2 + jt % 4][:]
                Tn.wait(vp, ev2.get(jt - 4, 0))
                if jt < 4:
                    c.waitb(Tn, [2 + jt % 4])
                Tn.mm(p, c.sh128[:], zsl(jj), start=True, stop=True)
                it = tp.inc(Tn.last)
                V.wait(tp, it)
                V.copy(zsl(jj), p)
                ev2[jt] = vp.inc(V.last)
                c.relb(2 + jt % 4, vp, ev2[jt])
            V.wait(vp, vp.n)
            for s in (1, 2, 4, 8):
                prs = [t0 for t0 in range(16) if not (t0 & s)]
                for t0 in prs:
                    V.tt(zsl(jc * 16 + t0), zsl(jc * 16 + t0), zsl(jc * 16 + (t0 ^ s)), OP.add)
                for t0 in prs:
                    V.stt(zsl(jc * 16 + (t0 ^ s)), zsl(jc * 16 + (t0 ^ s)), -2.0,
                          zsl(jc * 16 + t0), OP.mult, OP.add)
            vp.inc(V.last)
        for s in (16, 32):
            prs = [t0 for t0 in range(64) if not (t0 & s)]
            for t0 in prs:
                V.tt(zsl(t0), zsl(t0), zsl(t0 ^ s), OP.add)
            for t0 in prs:
                V.stt(zsl(t0 ^ s), zsl(t0 ^ s), -2.0, zsl(t0), OP.mult, OP.add)
        z_ready = vp.inc(V.last)

        for oc in range(4):
            for jg in range(16):
                slot = wb[2 + jg % 2]
                G.wait(tp, ring_rel.get(2 + jg % 2, 0))
                ws = c.wsem[2 + jg % 2]
                iw = ws.inc(G.dma(slot[:], c.dn_t[oc, jg]), 16)
                Tn.wait(ws, iw)
                if jg == 0:
                    Tn.wait(vp, z_ready)
                    c.waitb(Tn, [2, 3, 4, 5])
                for ji in range(4):
                    j = jg * 4 + ji
                    for tci in range(4):
                        Tn.mm(PS[2 + tci][:], zsl(j)[:, tci * 128:(tci + 1) * 128],
                              slot[:, ji * 512:(ji + 1) * 512], start=(j == 0), stop=(j == 63))
                ring_rel[2 + jg % 2] = tp.n + 1
                tp.inc(Tn.last)
            itg = tp.n
            V.wait(tp, itg)
            for tci in range(4):
                V.tt(fmB[tci][:, oc * 512:(oc + 1) * 512], PS[2 + tci][:],
                     wb[4][:, oc * 512:(oc + 1) * 512], OP.mult)
            iv = vp.inc(V.last)
            for tci in range(4):
                c.relb(2 + tci, vp, iv)
        for tci in range(4):
            _ln_write(c, tcs[tci], fmB[tci][:])


def _mla(c):
    Tn, V, S, G, Y = c.T, c.V, c.S, c.G, c.Y
    tp, vp, sp, ld = c.tp, c.vp, c.sp, c.ld
    PS, PT = c.PS, c.PT
    fmA, fmB, f32a, wb = c.fmA, c.fmB, c.f32a, c.wb
    ring_rel = {i: tp.n for i in range(5)}

    Y.wait(vp, vp.n)
    Y.wait(sp, sp.n)
    Y.wait(tp, tp.n)
    ld.inc(Y.dma(c.bcols[:, 16:20], c.kvd_b), 16)
    ld.inc(Y.dma(c.bcols[:, 20:28], c.qd_b), 16)
    ld.inc(Y.dma(c.bcols[:, 28:44], c.kup_b), 16)
    ld.inc(Y.dma(c.bcols[:, 44:60], c.vup_b), 16)
    ld.inc(Y.dma(c.bcols[0:64, 60:76], c.kr_b), 16)
    ld.inc(Y.dma(c.bcols[:, 76:92], c.qup_b), 16)
    ld.inc(Y.dma(c.cols[0:64, 128:144], c.qr_b), 16)
    G.wait(tp, tp.n)
    G.wait(vp, vp.n)
    c.wd.inc(G.dma(c.rbf[0:1, 0:2048], c.outw_b), 16)
    bias_ld, bias_wd = ld.n, c.wd.n

    def xfm(f):
        return fmB[f // 4][:, (f % 4) * 512:(f % 4) * 512 + 512]

    def kvc(m):
        return fmA[m]

    def tin(xs):
        pairs = []
        for ci in range(4):
            tc = xs * 4 + ci
            for f in range(16):
                pairs.append((c.res[tc][:, f * 128:(f + 1) * 128],
                              fmB[f // 4][:, (f % 4) * 512 + ci * 128:(f % 4) * 512 + ci * 128 + 128]))
        _transpose_in(c, pairs)
        Tn.wait(vp, vp.n)
        Tn.wait(sp, sp.n)

    Tn.wait(ld, bias_ld)
    V.wait(ld, bias_ld)
    V.wait(c.wd, bias_wd)
    for xs in range(4):
        tin(xs)
        for fg in range(4):
            slot = wb[fg % 2]
            G.wait(tp, ring_rel.get(fg % 2, 0))
            ws = c.wsem[fg % 2]
            iw = ws.inc(G.dma(slot[:], c.kvd_t[fg]), 16)
            Tn.wait(ws, iw)
            if fg == 0:
                c.waitb(Tn, [2, 3, 4, 5])
            for fi in range(4):
                f = fg * 4 + fi
                for m in range(4):
                    Tn.mm(PS[2 + m][:], slot[:, fi * 512 + m * 128:fi * 512 + m * 128 + 128],
                          xfm(f), start=(f == 0), stop=(f == 15))
            ring_rel[fg % 2] = tp.n + 1
            tp.inc(Tn.last)
        itg = tp.n
        V.wait(tp, itg)
        for m in range(4):
            V.ts(kvc(m)[:, xs * 512:(xs + 1) * 512], PS[2 + m][:], c.bcols[:, 16 + m:17 + m], None, OP.add)
        iv = vp.inc(V.last)
        for m in range(4):
            c.relb(2 + m, vp, iv)

    c._head_prev = {}
    for qsl in range(2):
        for xs2 in range(2):
            xs = qsl * 2 + xs2
            tin(xs)
            for mg in range(2):
                for fg in range(4):
                    slot = wb[fg % 2]
                    G.wait(tp, ring_rel.get(fg % 2, 0))
                    ws = c.wsem[fg % 2]
                    iw = ws.inc(G.dma(slot[:], c.qd_t[mg, fg]), 16)
                    Tn.wait(ws, iw)
                    if fg == 0:
                        c.waitb(Tn, [2, 3, 4, 5])
                    for fi in range(4):
                        f = fg * 4 + fi
                        for m4 in range(4):
                            Tn.mm(PS[2 + m4][:], slot[:, fi * 512 + m4 * 128:fi * 512 + m4 * 128 + 128],
                                  xfm(f), start=(f == 0), stop=(f == 15))
                    ring_rel[fg % 2] = tp.n + 1
                    tp.inc(Tn.last)
                itg = tp.n
                V.wait(tp, itg)
                for m4 in range(4):
                    m = mg * 4 + m4
                    dst = fmA[4 + m // 2][:, (m % 2) * 1024 + xs2 * 512:(m % 2) * 1024 + xs2 * 512 + 512]
                    V.ts(dst, PS[2 + m4][:], c.bcols[:, 20 + m:21 + m], None, OP.add)
                iv = vp.inc(V.last)
                for m4 in range(4):
                    c.relb(2 + m4, vp, iv)

        for h in range(16):
            ws = c.wsem[4]
            G.wait(tp, ring_rel.get(4, 0))
            ws.inc(G.dma(wb[4][:, 0:512], c.kup_t[h]), 16)
            ws.inc(G.dma(wb[4][:, 512:1024], c.vup_t[h]), 16)
            ws.inc(G.dma(wb[4][:, 1024:1280], c.kr_t[h]), 16)
            ws.inc(G.dma(wb[5][:, 0:1024], c.qup_t[h]), 16)
            iw = ws.inc(G.dma(wb[5][:, 1024:1536], c.qr_t[h]), 16)
            Tn.wait(ws, iw)
            ev = {}
            for tkc in range(4):
                p = PS[2 + tkc % 4][:]
                c.waitb(Tn, [2 + tkc % 4])
                for kf in range(4):
                    Tn.mm(p, wb[4][:, kf * 128:(kf + 1) * 128], kvc(kf)[:, tkc * 512:(tkc + 1) * 512],
                          start=(kf == 0), stop=(kf == 3))
                it = tp.inc(Tn.last)
                V.wait(tp, it)
                V.ts(fmB[0][:, tkc * 512:(tkc + 1) * 512], p, c.bcols[:, 28 + h:29 + h], None, OP.add)
                ev[tkc] = vp.inc(V.last)
                c.relb(2 + tkc % 4, vp, ev[tkc])
            for tkc in range(4):
                p = PS[2 + tkc % 4][0:64, :]
                c.waitb(Tn, [2 + tkc % 4])
                for kf in range(4):
                    Tn.mm(p, wb[4][:, 1024 + kf * 64:1024 + (kf + 1) * 64],
                          kvc(kf)[:, tkc * 512:(tkc + 1) * 512], start=(kf == 0), stop=(kf == 3))
                it = tp.inc(Tn.last)
                V.wait(tp, it)
                V.ts(fmB[1][0:64, tkc * 512:(tkc + 1) * 512], p, c.bcols[0:64, 60 + h:61 + h], None, OP.add)
                ev[100 + tkc] = vp.inc(V.last)
                c.relb(2 + tkc % 4, vp, ev[100 + tkc])
            for tk16 in range(16):
                p = PS[2 + tk16 % 4][:, 0:128]
                if tk16 >= 4:
                    Tn.wait(vp, ev[200 + tk16 - 4])
                elif tk16 == 0:
                    c.waitb(Tn, [2, 3, 4, 5])
                    Tn.wait(vp, c._head_prev.get("v", 0))
                for kf in range(4):
                    Tn.mm(p, kvc(kf)[:, tk16 * 128:(tk16 + 1) * 128],
                          wb[4][:, 512 + kf * 128:512 + (kf + 1) * 128], start=(kf == 0), stop=(kf == 3))
                it = tp.inc(Tn.last)
                V.wait(tp, it)
                V.ts(fmB[2][:, tk16 * 128:(tk16 + 1) * 128], p, c.bcols[:, 44 + h:45 + h], None, OP.add)
                ev[200 + tk16] = vp.inc(V.last)
            for b4 in range(4):
                c.relb(2 + b4, vp, ev[200 + 12 + b4])
            for qn in range(2):
                p = PS[2 + qn][:]
                c.waitb(Tn, [2 + qn])
                for qf in range(8):
                    rhs = fmA[4 + qf // 2][:, (qf % 2) * 1024 + qn * 512:(qf % 2) * 1024 + qn * 512 + 512]
                    Tn.mm(p, wb[5][:, qf * 128:(qf + 1) * 128], rhs, start=(qf == 0), stop=(qf == 7))
                it = tp.inc(Tn.last)
                V.wait(tp, it)
                V.ts(fmB[3][:, qn * 512:(qn + 1) * 512], p, c.bcols[:, 76 + h:77 + h], None, OP.add)
                ev[300 + qn] = vp.inc(V.last)
                c.relb(2 + qn, vp, ev[300 + qn])
            for qn in range(2):
                p = PS[2 + qn][0:64, :]
                c.waitb(Tn, [2 + qn])
                for qf in range(8):
                    rhs = fmA[4 + qf // 2][:, (qf % 2) * 1024 + qn * 512:(qf % 2) * 1024 + qn * 512 + 512]
                    Tn.mm(p, wb[5][:, 1024 + qf * 64:1024 + (qf + 1) * 64], rhs, start=(qf == 0), stop=(qf == 7))
                it = tp.inc(Tn.last)
                V.wait(tp, it)
                V.ts(fmB[3][0:64, 1024 + qn * 512:1024 + (qn + 1) * 512], p,
                     c.cols[0:64, 128 + h:129 + h], None, OP.add)
                ev[400 + qn] = vp.inc(V.last)
                c.relb(2 + qn, vp, ev[400 + qn])
            ring_rel[4] = tp.n
            c._head_prev = {"v": ev[200 + 15]}

            Tn.wait(vp, vp.n)
            for kc4 in range(4):
                p = PS[2 + kc4][:]
                Tn.mm(p, fmB[3][:, 0:128], fmB[0][:, kc4 * 512:(kc4 + 1) * 512], start=True, stop=False)
                Tn.mm(p, fmB[3][0:64, 1024:1152], fmB[1][0:64, kc4 * 512:(kc4 + 1) * 512],
                      start=False, stop=True)
            it = tp.inc(Tn.last)
            V.wait(tp, it)
            for kc4 in range(4):
                V.red(c.stat[:, 20 + kc4:21 + kc4], PS[2 + kc4][:], OP.max)
            V.tt(c.stat[:, 20:21], c.stat[:, 20:21], c.stat[:, 21:22], OP.max)
            V.tt(c.stat[:, 22:23], c.stat[:, 22:23], c.stat[:, 23:24], OP.max)
            V.tt(c.stat[:, 20:21], c.stat[:, 20:21], c.stat[:, 22:23], OP.max)
            V.copy(wb[5][:, 1536:1537], c.stat[:, 20:21])
            ivm = vp.inc(V.last)
            Tn.wait(vp, ivm)
            Tn.trans(PT[0][0:1, 0:128], wb[5][:, 1536:1537], c.sid[:])
            it = tp.inc(Tn.last)
            V.wait(tp, it)
            V.red(c.stat[0:1, 24:25], PT[0][0:1, 0:128], OP.max)
            ivm = vp.inc(V.last)
            Tn.wait(vp, ivm)
            Tn.mm(PS[5][:, 0:1], c.sonerf[:], c.stat[0:1, 24:25], start=True, stop=True)
            it = tp.inc(Tn.last)
            V.wait(tp, it)
            V.ts(c.cols[:, 144:145], PS[5][:, 0:1], -MLA_SCALE, None, OP.mult)
            ebias_vp = vp.inc(V.last)
            S.wait(vp, ebias_vp)

            for qn in range(2):
                pexp = {}
                Tn.wait(vp, vp.n)
                c.waitb(Tn, [0, 1, 2, 3, 4, 5])

                def st_mm(kc):
                    STp = PS[0 + kc % 2][:]
                    if kc >= 2:
                        Tn.wait(sp, pexp[kc - 2])
                    Tn.mm(STp, fmB[0][:, kc * 128:(kc + 1) * 128], fmB[3][:, qn * 512:(qn + 1) * 512],
                          start=True, stop=False)
                    Tn.mm(STp, fmB[1][0:64, kc * 128:(kc + 1) * 128],
                          fmB[3][0:64, 1024 + qn * 512:1024 + (qn + 1) * 512], start=False, stop=True)
                    it = tp.inc(Tn.last)
                    S.wait(tp, it)
                    pslot = wb[0][:, (kc % 2) * 512:(kc % 2) * 512 + 512]
                    S.act(pslot, STp, AF.Exp, scale=MLA_SCALE, bias=c.cols[:, 144:145])
                    pexp[kc] = sp.inc(S.last)

                st_mm(0)
                for kc in range(16):
                    if kc + 1 < 16:
                        st_mm(kc + 1)
                    pslot = wb[0][:, (kc % 2) * 512:(kc % 2) * 512 + 512]
                    Tn.wait(sp, pexp[kc])
                    Tn.mm(PS[2][:], fmB[2][:, kc * 128:(kc + 1) * 128], pslot, start=(kc == 0), stop=(kc == 15))
                    Tn.mm(PS[3][0:1, :], c.sonec[:], pslot, start=(kc == 0), stop=(kc == 15))
                    tp.inc(Tn.last)
                it = tp.n
                V.wait(tp, it)
                V.recip(c.rows[0:1, 0:512], PS[3][0:1, :])
                ivr = vp.inc(V.last)
                Tn.wait(vp, ivr)
                Tn.mm(PS[4][:], c.sonerf[:], c.rows[0:1, 0:512], start=True, stop=True)
                it = tp.inc(Tn.last)
                V.wait(tp, it)
                V.copy(f32a[0][:, 0:512], PS[4][:])
                V.tt(fmA[8 + h // 2][:, (h % 2) * 1024 + qn * 512:(h % 2) * 1024 + qn * 512 + 512],
                     PS[2][:], f32a[0][:, 0:512], OP.mult)
                iv = vp.inc(V.last)
                c.relb(2, vp, iv)
                c.relb(4, vp, iv)

        c._ctx_ready = vp.n
        ring_rel[0] = ring_rel[1] = tp.n
        for tcg in range(2):
            for oc in range(4):
                for hg in range(4):
                    slot = wb[2 + hg % 2]
                    G.wait(tp, ring_rel.get(2 + hg % 2, 0))
                    ws = c.wsem[2 + hg % 2]
                    iw = ws.inc(G.dma(slot[:], c.outw_t[oc, hg]), 16)
                    Tn.wait(ws, iw)
                    if hg == 0:
                        Tn.wait(vp, c._ctx_ready)
                        c.waitb(Tn, [2, 3, 4, 5])
                    for hi in range(4):
                        h = hg * 4 + hi
                        for tci in range(4):
                            tq = tcg * 4 + tci
                            lhs = fmA[8 + h // 2][:, (h % 2) * 1024 + tq * 128:(h % 2) * 1024 + tq * 128 + 128]
                            Tn.mm(PS[2 + tci][:], lhs, slot[:, hi * 512:(hi + 1) * 512],
                                  start=(h == 0), stop=False)
                    ring_rel[2 + hg % 2] = tp.n + 1
                    tp.inc(Tn.last)
                for tci in range(4):
                    Tn.mm(PS[2 + tci][:], c.soner[0:1, 0:128], c.rbf[0:1, oc * 512:(oc + 1) * 512],
                          start=False, stop=True)
                itg = tp.inc(Tn.last)
                V.wait(tp, itg)
                for tci in range(4):
                    V.copy(fmB[tci][:, oc * 512:(oc + 1) * 512], PS[2 + tci][:])
                iv = vp.inc(V.last)
                for tci in range(4):
                    c.relb(2 + tci, vp, iv)
            for tci in range(4):
                tc = qsl * 8 + tcg * 4 + tci
                _ln_write(c, tc, fmB[tci][:])


def _ffn1(c):
    Tn, V, S, G, Y = c.T, c.V, c.S, c.G, c.Y
    tp, vp, sp, gp, ld = c.tp, c.vp, c.sp, c.gp, c.ld
    PS, PT = c.PS, c.PT
    fmA, fmB, f32a, wb = c.fmA, c.fmB, c.f32a, c.wb
    ring_rel = {i: tp.n for i in range(5)}
    G.wait(tp, tp.n)
    G.wait(vp, vp.n)
    G.wait(sp, sp.n)

    def fsl(i, j, w=512):
        return fmA[i][:, j * w:(j + 1) * w]

    def strip_tiles(s):
        if s < 16:
            return [(c.w1_t[s, f], 2048) for f in range(16)]
        return [(c.w2_t[s - 16, j], 8192) for j in range(64)]

    def strip_tiles_q(s):
        if s < 16:
            return [c.w1q_hbm[s, f] for f in range(16)]
        return [c.w2q_hbm[s - 16, j] for j in range(64)]

    stA = {"n": 0}
    stB = {"n": 0}
    mm_rel = {}
    q_rel = {}
    bc_rel = {}
    bc_tp = {}

    def phaseA(s):
        tiles = strip_tiles(s)
        N = tiles[0][1]
        for ti, (src_ap, _) in enumerate(tiles):
            n = stA["n"]
            sk = n % 4
            slot = wb[4 + sk // 2][:, (sk % 2) * 512:(sk % 2) * 512 + 512]
            G.wait(tp, mm_rel.get(n - 4, 0))
            wsk = c.wsem[4 + sk]
            iw = wsk.inc(G.dma(slot, src_ap), 16)
            S.wait(wsk, iw)
            S.act(slot, slot, AF.Abs, scale=2.0)
            isp = sp.inc(S.last)
            Tn.wait(sp, isp)
            Tn.mm(PS[0][0:1, :], c.sonec[:], slot, start=(ti == 0), stop=(ti == len(tiles) - 1))
            mm_rel[n] = tp.inc(Tn.last)
            stA["n"] = n + 1
        V.wait(tp, mm_rel[stA["n"] - 1])
        V.ts(c.rows[0:1, 0:512], PS[0][0:1, :], 1.0 / (2.0 * N), 1e-5, OP.mult, OP.max)
        ivr = vp.inc(V.last)
        c.relb(0, vp, ivr)
        c.waitb(Tn, [1])
        Tn.wait(vp, max(ivr, bc_rel.get(s - 2, 0)))
        Tn.mm(PS[1][:], c.sonerf[:], c.rows[0:1, 0:512], start=True, stop=True)
        itb = tp.inc(Tn.last)
        bc_tp[s] = itb
        V.wait(tp, itb)
        V.copy(f32a[0][:, (s % 2) * 512:(s % 2) * 512 + 512], PS[1][:])
        iv = vp.inc(V.last)
        c.relb(1, vp, iv)

    def phaseB(s):
        tiles = strip_tiles(s)
        qdst = strip_tiles_q(s)
        swb_ = f32a[0][:, (s % 2) * 512:(s % 2) * 512 + 512]
        last = 0
        for ti, (src_ap, _) in enumerate(tiles):
            n = stB["n"]
            k = n % 4
            wslot = wb[0 + k // 2][:, (k % 2) * 512:(k % 2) * 512 + 512]
            aslot = wb[2 + k // 2][:, (k % 2) * 512:(k % 2) * 512 + 512]
            sslot = wb[4 + k // 2][:, 1024 + (k % 2) * 512:1024 + (k % 2) * 512 + 512]
            if n - 4 in q_rel:
                G.wait(*q_rel[n - 4])
            wsk = c.wsem[k]
            iw = wsk.inc(G.dma(wslot, src_ap), 16)
            S.wait(wsk, iw)
            S.wait(tp, bc_tp[s])
            S.act(aslot, wslot, AF.Abs, scale=2.0)
            S.act(sslot, wslot, AF.Sign)
            isp = sp.inc(S.last)
            V.wait(sp, isp)
            V.tt(aslot, aslot, swb_, OP.is_ge)
            V.tt(aslot, aslot, sslot, OP.mult)
            V.tt(wslot, aslot, swb_, OP.mult)
            ivq = vp.inc(V.last)
            Y.wait(vp, ivq)
            gsk = c.gsem[k]
            q_rel[n] = (gsk, gsk.inc(Y.dma(qdst[ti], wslot), 16))
            last = ivq
            stB["n"] = n + 1
        bc_rel[s] = last

    for s in range(20):
        phaseA(s)
        if s > 0:
            phaseB(s - 1)
    phaseB(19)
    for gs in c.gsem:
        G.wait(gs, gs.n)
        V.wait(gs, gs.n)
    stash_done = 0

    perm = {}
    for tc in range(4):
        for jb in range(64):
            s_id = (tc * 4 + jb // 16) * 16 + (jb % 16)
            d_id = (jb // 4) * 16 + (jb % 4) * 4 + tc
            perm[s_id] = d_id
    cycles = []
    seen = set()
    for s0 in range(256):
        if s0 in seen:
            continue
        cyc = [s0]
        seen.add(s0)
        nxt = perm[s0]
        while nxt != s0:
            cyc.append(nxt)
            seen.add(nxt)
            nxt = perm[nxt]
        cycles.append(cyc)

    def slot_ap(sid):
        return fmA[sid // 16][:, (sid % 16) * 128:(sid % 16) * 128 + 128]

    for sb in range(NSLAB):
        tcs = [sb * 4 + i for i in range(4)]
        pairs = []
        for ci, tc in enumerate(tcs):
            V.wait(tp, tp.n)
            V.red(c.stat[:, 30:31], c.res[tc][:], OP.max, absval=True)
            V.ts(c.stat[:, 31:32], c.stat[:, 30:31], 1e-5, None, OP.max)
            V.recip(c.stat[:, 32:33], c.stat[:, 31:32])
            V.ts(c.cols[:, 32 + ci:33 + ci], c.stat[:, 32:33], 127.0, None, OP.mult)
            V.ts(c.cols[:, 36 + ci:37 + ci], c.stat[:, 31:32], 1.0 / 127.0, None, OP.mult)
            sx = c.cols[:, 32 + ci:33 + ci]
            for half in range(2):
                hs = slice(half * 1024, half * 1024 + 1024)
                v_ = f32a[0][:, 0:1024]
                V.ts(v_, c.res[tc][:, hs], sx, None, OP.mult)
                V.ts(f32a[0][:, 1024:2048], v_, M52, C52_32, OP.add, OP.subtract)
                V.ts(f32a[0][:, 1024:2048], f32a[0][:, 1024:2048], M32, None, OP.subtract)
                V.ts(f32a[1][:, 0:1024], v_, M23, M23, OP.add, OP.subtract)
                V.ts(f32a[1][:, 1024:2048], v_, M32, M32, OP.add, OP.subtract)
                V.tt(f32a[1][:, 0:1024], f32a[1][:, 0:1024], f32a[1][:, 1024:2048], OP.subtract)
                V.tt(fmA[15][:, hs], f32a[0][:, 1024:2048], f32a[1][:, 0:1024], OP.add)
            iv = vp.inc(V.last)
            for f in range(16):
                pairs.append((fmA[15][:, f * 128:(f + 1) * 128],
                              fmB[f // 4][:, (f % 4) * 512 + ci * 128:(f % 4) * 512 + ci * 128 + 128]))
            _transpose_in(c, pairs)
            pairs = []
        xq_ready = vp.n

        for jc in range(16):
            for g2 in range(4):
                slot = wb[g2 % 2]
                G.wait(tp, ring_rel.get(g2 % 2, 0))
                ws = c.wsem[g2 % 2]
                for fi4 in range(4):
                    ws.inc(G.dma(slot[:, fi4 * 512:(fi4 + 1) * 512], c.w1q_hbm[jc, g2 * 4 + fi4]), 16)
                iw = ws.n
                Tn.wait(ws, iw)
                if g2 == 0:
                    Tn.wait(vp, xq_ready)
                    c.waitb(Tn, [2, 3, 4, 5])
                for fi in range(4):
                    f = g2 * 4 + fi
                    for tci in range(4):
                        Tn.mm(PS[2 + tci][:], fmB[f // 4][:, (f % 4) * 512 + tci * 128:(f % 4) * 512 + tci * 128 + 128],
                              slot[:, fi * 512:(fi + 1) * 512], start=(f == 0), stop=(f == 15))
                ring_rel[g2 % 2] = tp.n + 1
                tp.inc(Tn.last)
            itg = tp.n
            S.wait(tp, itg)
            for tci in range(4):
                S.act(fmA[tci * 4 + jc // 4][:, (jc % 4) * 512:(jc % 4) * 512 + 512], PS[2 + tci][:],
                      AF.Gelu_apprx_tanh, scale=c.cols[:, 36 + tci:37 + tci])
            isp = sp.inc(S.last)
            for tci in range(4):
                c.relb(2 + tci, sp, isp)
        h_done = sp.n

        V.wait(sp, h_done)
        for tci in range(4):
            for q in range(4):
                V.red(c.stat[:, 40 + q:41 + q], fmA[tci * 4 + q][:], OP.max, absval=True)
            V.tt(c.stat[:, 40:41], c.stat[:, 40:41], c.stat[:, 41:42], OP.max)
            V.tt(c.stat[:, 42:43], c.stat[:, 42:43], c.stat[:, 43:44], OP.max)
            V.tt(c.stat[:, 40:41], c.stat[:, 40:41], c.stat[:, 42:43], OP.max)
            V.ts(c.stat[:, 44:45], c.stat[:, 40:41], 1e-5, None, OP.max)
            V.recip(c.stat[:, 45:46], c.stat[:, 44:45])
            V.ts(c.cols[:, 40 + tci:41 + tci], c.stat[:, 45:46], 127.0, None, OP.mult)
            V.ts(c.cols[:, 44 + tci:45 + tci], c.stat[:, 44:45], 1.0 / 127.0, None, OP.mult)
            sh = c.cols[:, 40 + tci:41 + tci]
            for q in range(4):
                for half in range(2):
                    hh = slice(half * 1024, half * 1024 + 1024)
                    v_ = f32a[0][:, 0:1024]
                    V.ts(v_, fmA[tci * 4 + q][:, hh], sh, None, OP.mult)
                    V.ts(f32a[0][:, 1024:2048], v_, M52, C52_32, OP.add, OP.subtract)
                    V.ts(f32a[0][:, 1024:2048], f32a[0][:, 1024:2048], M32, None, OP.subtract)
                    V.ts(f32a[1][:, 0:1024], v_, M23, M23, OP.add, OP.subtract)
                    V.ts(f32a[1][:, 1024:2048], v_, M32, M32, OP.add, OP.subtract)
                    V.tt(f32a[1][:, 0:1024], f32a[1][:, 0:1024], f32a[1][:, 1024:2048], OP.subtract)
                    V.tt(fmA[tci * 4 + q][:, hh], f32a[0][:, 1024:2048], f32a[1][:, 0:1024], OP.add)
        nh_done = vp.inc(V.last)

        Tn.wait(vp, nh_done)
        Tn.wait(sp, sp.n)
        lagged = []
        hist = {}
        n = 0
        vp0 = vp.n
        for cyc in cycles:
            for i in range(len(cyc)):
                s_id = cyc[i]
                pt = PT[n % 2][:, ((n // 2) % 4) * 256:((n // 2) % 4) * 256 + 128]
                Tn.wait(vp, hist[n - 2] if n >= 2 else vp0)
                Tn.trans(pt, slot_ap(s_id), c.sid[:])
                it = tp.inc(Tn.last)
                lagged.append((it, pt, perm[s_id]))
                if len(lagged) > 1:
                    it2, pt2, d2 = lagged.pop(0)
                    V.wait(tp, max(it2, it))
                    V.copy(slot_ap(d2), pt2)
                    hist[n - 1] = vp.inc(V.last)
                n += 1
            it2, pt2, d2 = lagged.pop(0)
            V.wait(tp, it2)
            V.copy(slot_ap(d2), pt2)
            hist[n - 1] = vp.inc(V.last)
        hq_ready = vp.n

        for oc in range(4):
            for jg in range(16):
                slot = wb[2 + jg % 2]
                G.wait(tp, ring_rel.get(2 + jg % 2, 0))
                ws = c.wsem[2 + jg % 2]
                for ji4 in range(4):
                    ws.inc(G.dma(slot[:, ji4 * 512:(ji4 + 1) * 512], c.w2q_hbm[oc, jg * 4 + ji4]), 16)
                iw = ws.n
                Tn.wait(ws, iw)
                if jg == 0:
                    Tn.wait(vp, hq_ready)
                    c.waitb(Tn, [2, 3, 4, 5])
                for ji in range(4):
                    j = jg * 4 + ji
                    for tci in range(4):
                        lhs = fmA[j // 4][:, (j % 4) * 512 + tci * 128:(j % 4) * 512 + tci * 128 + 128]
                        Tn.mm(PS[2 + tci][:], lhs, slot[:, ji * 512:(ji + 1) * 512],
                              start=(j == 0), stop=(j == 63))
                ring_rel[2 + jg % 2] = tp.n + 1
                tp.inc(Tn.last)
            itg = tp.n
            V.wait(tp, itg)
            for tci in range(4):
                V.ts(fmB[tci][:, oc * 512:(oc + 1) * 512], PS[2 + tci][:],
                     c.cols[:, 44 + tci:45 + tci], None, OP.mult)
            iv = vp.inc(V.last)
            for tci in range(4):
                c.relb(2 + tci, vp, iv)
        for tci in range(4):
            _ln_write(c, tcs[tci], fmB[tci][:])



def _bf(x):
    return np.ascontiguousarray(x, dtype=ml_dtypes.bfloat16)


def _f32(x):
    return np.ascontiguousarray(x, dtype=np.float32)


def _tile_fm(wT, groups):
    raise NotImplementedError


def _prep_gla_qk(wT):
    out = np.empty((2, 4, 128, 2048), np.float32)
    for mg in range(2):
        for fg in range(4):
            for fi in range(4):
                f = fg * 4 + fi
                blk = wT[f * 128:(f + 1) * 128, mg * 512:(mg + 1) * 512]
                out[mg, fg, :, fi * 512:(fi + 1) * 512] = blk
    return out


def _prep_gla_vgo(wT):
    out = np.empty((4, 4, 128, 2048), np.float32)
    for oc in range(4):
        for fg in range(4):
            for fi in range(4):
                f = fg * 4 + fi
                out[oc, fg, :, fi * 512:(fi + 1) * 512] = wT[f * 128:(f + 1) * 128, oc * 512:(oc + 1) * 512]
    return out


def prep_inputs(inputs):
    g = {}
    g["c_ident"] = np.eye(128, dtype=ml_dtypes.bfloat16)
    g["c_mask"] = _bf(np.tril(np.ones((128, 128), np.float32)).T)
    H = np.array([[1.0]], np.float32)
    while H.shape[0] < 128:
        H = np.block([[H, H], [H, -H]])
    g["c_h128"] = _bf(H)
    g["c_onesrow"] = np.ones((1, 128), ml_dtypes.bfloat16)
    g["c_onescol"] = np.ones((128, 1), ml_dtypes.bfloat16)
    g["c_onesrowf"] = np.ones((1, 128), np.float32)

    wqT = _f32(inputs["L0_attn_Wq"].T)
    wkT = _f32(inputs["L0_attn_Wk"].T)
    g["gla_qk_t"] = np.stack([_prep_gla_qk(wqT), _prep_gla_qk(wkT)])
    wvT = _f32(inputs["L0_attn_Wv"].T)
    wgT = _f32(inputs["L0_attn_Wg"].T)
    woT = _f32(inputs["L0_attn_Wo"].T)
    g["gla_vgo_t"] = np.stack([_prep_gla_vgo(wvT), _prep_gla_vgo(wgT), _prep_gla_vgo(woT)])
    wkg1T = _f32(inputs["L0_attn_Wkg1"].T)
    g["wkg1_t"] = _f32(wkg1T.reshape(16, 128, 16).transpose(1, 0, 2).reshape(128, 256))
    g["wkg2T"] = _f32(inputs["L0_attn_Wkg2"].T)
    g["bkg2_cols"] = _f32(inputs["L0_attn_bkg2"].reshape(8, 128).T)
    g["bg"] = _f32(inputs["L0_attn_bg"].reshape(1, -1))

    wupT = _f32(inputs["L0_ffn_W_up"].T)
    wgateT = _f32(inputs["L0_ffn_W_gate"].T)
    wdownT = _f32(inputs["L0_ffn_W_down"].T)
    ug = np.empty((2, 4, 4, 4, 128, 2048), np.float32)
    for wi, w in enumerate([wupT, wgateT]):
        for jc in range(4):
            for jg in range(4):
                for fg in range(4):
                    for fi in range(4):
                        f = fg * 4 + fi
                        ug[wi, jc, jg, fg, :, fi * 512:(fi + 1) * 512] = \
                            w[f * 128:(f + 1) * 128, jc * 2048 + jg * 512:jc * 2048 + (jg + 1) * 512]
    g["ug_t"] = ug
    dn = np.empty((4, 16, 128, 2048), np.float32)
    for oc in range(4):
        for jg in range(16):
            for ji in range(4):
                j = jg * 4 + ji
                dn[oc, jg, :, ji * 512:(ji + 1) * 512] = \
                    wdownT[j * 128:(j + 1) * 128, oc * 512:(oc + 1) * 512]
    g["dn_t"] = dn
    g["su_ug_cols"] = _f32(inputs["L0_ffn_SU_ug"].reshape(16, 128).T)
    g["su_dp_cols"] = _f32(inputs["L0_ffn_SU_dp"].reshape(64, 128).T)
    wsc = _f32(inputs["L0_ffn_Wscale_ug"])
    g["wsc_up_cols"] = _f32(wsc[:INTER].reshape(64, 128).T)
    g["wsc_gate_cols"] = _f32(wsc[INTER:].reshape(64, 128).T)
    g["wsc_dp_row"] = _f32(inputs["L0_ffn_Wscale_dp"].reshape(1, -1))

    kvdT = _f32(inputs["L1_attn_kv_down_w"].T)
    kvd = np.empty((4, 128, 2048), np.float32)
    for fg in range(4):
        for fi in range(4):
            f = fg * 4 + fi
            kvd[fg, :, fi * 512:(fi + 1) * 512] = kvdT[f * 128:(f + 1) * 128, :]
    g["kvd_t"] = kvd
    qdT = _f32(inputs["L1_attn_query_down_w"].T)
    g["qd_t"] = _prep_gla_qk(qdT)
    kupT = _f32(inputs["L1_attn_key_up_w"].T)
    vupT = _f32(inputs["L1_attn_value_up_w"].T)
    krT = _f32(inputs["L1_attn_key_rope_w"].T)
    kup = np.empty((16, 128, 512), np.float32)
    vup = np.empty((16, 128, 512), np.float32)
    krt = np.empty((16, 128, 256), np.float32)
    for h in range(16):
        for kf in range(4):
            kup[h, :, kf * 128:(kf + 1) * 128] = kupT[kf * 128:(kf + 1) * 128, h * 128:(h + 1) * 128]
            vup[h, :, kf * 128:(kf + 1) * 128] = vupT[kf * 128:(kf + 1) * 128, h * 128:(h + 1) * 128]
            krt[h, :, kf * 64:(kf + 1) * 64] = krT[kf * 128:(kf + 1) * 128, h * 64:(h + 1) * 64]
    g["kup_t"] = kup
    g["vup_t"] = vup
    g["kr_t"] = krt
    qupT = _f32(inputs["L1_attn_query_up_w"].T)
    qup = np.empty((16, 128, 1024), np.float32)
    for h in range(16):
        for qf in range(8):
            qup[h, :, qf * 128:(qf + 1) * 128] = qupT[qf * 128:(qf + 1) * 128, h * 128:(h + 1) * 128]
    g["qup_t"] = qup
    qrT = _f32(inputs["L1_attn_query_rope_w"].T)
    qr = np.empty((16, 128, 512), np.float32)
    for h in range(16):
        for qf in range(8):
            qr[h, :, qf * 64:(qf + 1) * 64] = qrT[qf * 128:(qf + 1) * 128, h * 64:(h + 1) * 64]
    g["qr_t"] = qr
    outwT = _f32(inputs["L1_attn_out_w"].T)
    ow = np.empty((4, 4, 128, 2048), np.float32)
    for oc in range(4):
        for hg in range(4):
            for hi in range(4):
                h = hg * 4 + hi
                ow[oc, hg, :, hi * 512:(hi + 1) * 512] = \
                    outwT[h * 128:(h + 1) * 128, oc * 512:(oc + 1) * 512]
    g["outw_t"] = ow
    for nm, key in [("kvd_b", "L1_attn_kv_down_b"), ("kup_b", "L1_attn_key_up_b"),
                    ("vup_b", "L1_attn_value_up_b"),
                    ("qd_b", "L1_attn_query_down_b"), ("qup_b", "L1_attn_query_up_b")]:
        g[nm] = _f32(inputs[key].reshape(-1, 128).T)
    g["kr_b"] = _f32(inputs["L1_attn_key_rope_b"].reshape(16, 64).T)
    g["qr_b"] = _f32(inputs["L1_attn_query_rope_b"].reshape(16, 64).T)
    g["outw_b"] = _f32(inputs["L1_attn_out_b"].reshape(1, -1))

    w1T = _f32(inputs["L1_ffn_w1"].T)
    w1t = np.empty((16, 16, 128, 512), np.float32)
    for jc in range(16):
        for f in range(16):
            w1t[jc, f] = w1T[f * 128:(f + 1) * 128, jc * 512:(jc + 1) * 512]
    g["w1_t"] = w1t
    w2T = _f32(inputs["L1_ffn_w2"].T)
    w2t = np.empty((4, 64, 128, 512), np.float32)
    for os_ in range(4):
        for j in range(64):
            w2t[os_, j] = w2T[j * 128:(j + 1) * 128, os_ * 512:(os_ + 1) * 512]
    g["w2_t"] = w2t
    return g


def kernel(**inputs):
    g = prep_inputs(inputs)
    x = np.ascontiguousarray(inputs["L0_attn_x"], dtype=np.float32)
    nc = build_program(_UPTO)
    in_maps = []
    for core in range(NCORES):
        m = dict(g)
        m["x_in"] = x[core]
        in_maps.append(m)
    res = run_bass_kernel_spmd(nc, in_maps, core_ids=list(range(NCORES)))
    out = np.stack([np.asarray(res.results[i]["out"], np.float32) for i in range(NCORES)])
    return out


# version 06
# speedup vs baseline: 1.3935x; 1.0164x over previous
import math
from contextlib import ExitStack

import numpy as np
import ml_dtypes

import concourse.bass as bass
import concourse.mybir as mybir
from concourse.bass_utils import run_bass_kernel_spmd

F32 = mybir.dt.float32
BF16 = mybir.dt.bfloat16
AF = mybir.ActivationFunctionType
OP = mybir.AluOpType
AX = mybir.AxisListType

NCORES = 8
T = 2048
D = 2048
NT = 16
SLAB = 512
NSLAB = 4
LN_EPS = 1e-5
GLA_NORM = 16.0
GLA_SCALE_LOG = math.log(128.0 ** (-0.5))
MLA_SCALE = 1.0 / math.sqrt(192.0)
INTER = 8192

M23 = 1.5 * 2.0 ** 23
M52 = 1.5 * 2.0 ** 52
M32 = 1.5 * 2.0 ** 32
C52_32 = M52 - M32

_UPTO = 4



class Eng:

    def __init__(self, e, auto_drain=True):
        self.e = e
        self.auto_drain = auto_drain
        self.dirty = set()
        self.rd = set()
        self.waited = {}

    @staticmethod
    def _names(aps):
        out = []
        for a in aps:
            if a is None or isinstance(a, (int, float)):
                continue
            t = getattr(a, "tensor", None)
            if t is not None:
                out.append(t.name)
        return out

    def _pre(self, outs, ins):
        if not self.auto_drain:
            return
        o = set(self._names(outs))
        i = set(self._names(ins))
        if (i & self.dirty) or (o & self.dirty) or (o & self.rd):
            self.e.drain()
            self.dirty.clear()
            self.rd.clear()
        self.dirty |= o
        self.rd |= i

    def wait(self, sem, val):
        val = int(val)
        if val <= 0 or self.waited.get(sem.name, -1) >= val:
            return
        self.e.wait_ge(sem.h, val)
        self.waited[sem.name] = val

    def _em(self, inst):
        self.last = inst
        return inst

    def dma(self, out, in_):
        self._pre([out], [in_])
        return self._em(self.e.dma_start(out=out, in_=in_))

    def copy(self, out, in_):
        self._pre([out], [in_])
        return self._em(self.e.tensor_copy(out, in_))

    def ts(self, out, in0, s1, s2, op0, op1=None):
        self._pre([out], [in0, s1, s2])
        if op1 is None:
            return self._em(self.e.tensor_scalar(out, in0, s1, None, op0))
        return self._em(self.e.tensor_scalar(out, in0, s1, s2, op0, op1))

    def tt(self, out, in0, in1, op):
        self._pre([out], [in0, in1])
        return self._em(self.e.tensor_tensor(out, in0, in1, op))

    def stt(self, out, in0, s, in1, op0, op1):
        self._pre([out], [in0, s, in1])
        return self._em(self.e.scalar_tensor_tensor(out, in0, s, in1, op0, op1))

    def scan(self, out, d0, d1, init, op0, op1):
        self._pre([out], [d0, d1])
        return self._em(self.e.tensor_tensor_scan(out, d0, d1, init, op0, op1))

    def red(self, out, in_, op, absval=False):
        self._pre([out], [in_])
        return self._em(self.e.tensor_reduce(out, in_, AX.X, op, apply_absolute_value=absval))

    def recip(self, out, in_):
        self._pre([out], [in_])
        return self._em(self.e.reciprocal(out, in_))

    def act(self, out, in_, func, bias=0.0, scale=1.0, accum_out=None):
        outs = [out] + ([accum_out] if accum_out is not None else [])
        self._pre(outs, [in_, bias, scale])
        return self._em(self.e.activation(out, in_, func, bias=bias, scale=scale, accum_out=accum_out))

    def mm(self, out, lhsT, rhs, start, stop):
        self._pre([out], [lhsT, rhs])
        return self._em(self.e.matmul(out, lhsT, rhs, start=start, stop=stop))

    def trans(self, out, in_, ident):
        self._pre([out], [in_, ident])
        return self._em(self.e.transpose(out, in_, ident))

    def memset(self, ap, v):
        self._pre([ap], [])
        return self._em(self.e.memset(ap, v))

    def nop(self):
        return self.e.nop()


class Sem:
    def __init__(self, h, name):
        self.h = h
        self.name = name
        self.n = 0

    def inc(self, inst, by=1):
        inst.then_inc(self.h, by)
        self.n += by
        return self.n


class Ctx:
    pass



def build_program(upto=_UPTO):
    nc = bass.Bass("TRN2", target_bir_lowering=False)
    c = Ctx()
    c.nc = nc

    def din(name, shape, dt=F32):
        return nc.dram_tensor(name, shape, dt, kind="ExternalInput").ap()

    c.x_in = din("x_in", [T, D])
    c.out_hbm = nc.dram_tensor("out", [T, D], F32, kind="ExternalOutput").ap()

    c.ident = din("c_ident", [128, 128], BF16)
    c.maskUT = din("c_mask", [128, 128], BF16)
    c.h128 = din("c_h128", [128, 128], BF16)
    c.onesrow = din("c_onesrow", [1, 128], BF16)
    c.onescol = din("c_onescol", [128, 1], BF16)
    c.onesrowf = din("c_onesrowf", [1, 128])

    c.gla_qk_t = din("gla_qk_t", [2, 2, 4, 128, 2048])
    c.gla_vgo_t = din("gla_vgo_t", [3, 4, 4, 128, 2048])
    c.wkg1_t = din("wkg1_t", [128, 256])
    c.wkg2T = din("wkg2T", [16, 1024])
    c.bkg2_cols = din("bkg2_cols", [128, 8])
    c.bg = din("bg", [1, D])

    c.ug_t = din("ug_t", [2, 4, 4, 4, 128, 2048])
    c.dn_t = din("dn_t", [4, 16, 128, 2048])
    c.su_ug_cols = din("su_ug_cols", [128, 16])
    c.wsc_up_cols = din("wsc_up_cols", [128, 64])
    c.wsc_gate_cols = din("wsc_gate_cols", [128, 64])
    c.su_dp_cols = din("su_dp_cols", [128, 64])
    c.wsc_dp_row = din("wsc_dp_row", [1, D])

    c.kvd_t = din("kvd_t", [4, 128, 2048])
    c.qd_t = din("qd_t", [2, 4, 128, 2048])
    c.kup_t = din("kup_t", [16, 128, 512])
    c.vup_t = din("vup_t", [16, 128, 512])
    c.kr_t = din("kr_t", [16, 128, 256])
    c.qup_t = din("qup_t", [16, 128, 1024])
    c.qr_t = din("qr_t", [16, 128, 512])
    c.outw_t = din("outw_t", [4, 4, 128, 2048])
    c.kvd_b = din("kvd_b", [128, 4])
    c.kup_b = din("kup_b", [128, 16])
    c.vup_b = din("vup_b", [128, 16])
    c.kr_b = din("kr_b", [64, 16])
    c.qd_b = din("qd_b", [128, 8])
    c.qup_b = din("qup_b", [128, 16])
    c.qr_b = din("qr_b", [64, 16])
    c.outw_b = din("outw_b", [1, D])

    c.w1_t = din("w1_t", [16, 16, 128, 512])
    c.w2_t = din("w2_t", [4, 64, 128, 512])
    c.w1q_hbm = nc.dram_tensor("w1q_s", [16, 16, 128, 512], BF16, kind="Internal").ap()
    c.w2q_hbm = nc.dram_tensor("w2q_s", [4, 64, 128, 512], BF16, kind="Internal").ap()

    with ExitStack() as stack:
        e = stack.enter_context
        c.block = e(nc.Block())
        c.T = Eng(nc.tensor, auto_drain=False); c.V = Eng(nc.vector); c.S = Eng(nc.scalar)
        c.G = Eng(nc.gpsimd, auto_drain=False); c.Y = Eng(nc.sync, auto_drain=False)
        for name in ["ld", "wd", "tp", "vp", "sp", "gp", "st", "ini"]:
            setattr(c, name, Sem(e(nc.semaphore(name)), name))
        c.wsem = [Sem(e(nc.semaphore(f"w{i}")), f"w{i}") for i in range(8)]
        c.gsem = [Sem(e(nc.semaphore(f"g{i}")), f"g{i}") for i in range(4)]

        c.res = [e(nc.sbuf_tensor(f"res{i}", [128, D], BF16)) for i in range(NT)]
        c.fmA = [e(nc.sbuf_tensor(f"fmA{i}", [128, D], BF16)) for i in range(16)]
        c.fmB = [e(nc.sbuf_tensor(f"fmB{i}", [128, D], BF16)) for i in range(4)]
        c.f32a = [e(nc.sbuf_tensor(f"f32a{i}", [128, D], F32)) for i in range(2)]
        c.wb = [e(nc.sbuf_tensor(f"wb{i}", [128, D], BF16)) for i in range(6)]
        c.Sst = e(nc.sbuf_tensor("Sst", [128, 1024], F32))
        c.Stmp = e(nc.sbuf_tensor("Stmp", [128, 256], F32))
        c.Sbf = e(nc.sbuf_tensor("Sbf", [128, 1024], BF16))
        c.stat = e(nc.sbuf_tensor("stat", [128, 64], F32))
        c.cols = e(nc.sbuf_tensor("cols", [128, 160], F32))
        c.rows = e(nc.sbuf_tensor("rows", [1, 512], F32))
        c.rbf = e(nc.sbuf_tensor("rbf", [1, 2048], BF16))
        c.bcols = e(nc.sbuf_tensor("bcols", [128, 96], F32))
        c.sid = e(nc.sbuf_tensor("sid", [128, 128], BF16))
        c.smask = e(nc.sbuf_tensor("smask", [128, 128], BF16))
        c.sh128 = e(nc.sbuf_tensor("sh128", [128, 128], BF16))
        c.soner = e(nc.sbuf_tensor("soner", [1, 128], BF16))
        c.sonec = e(nc.sbuf_tensor("sonec", [128, 1], BF16))
        c.sonerf = e(nc.sbuf_tensor("sonerf", [1, 128], F32))
        c.skg2 = e(nc.sbuf_tensor("skg2", [16, 1024], BF16))
        c.outf32 = c.f32a[1]

        c.PS = [e(nc.psum_tensor(f"PS{i}", [128, 512], F32)) for i in range(6)]
        c.PT = [e(nc.psum_tensor(f"PT{i}", [128, 1024], BF16)) for i in range(2)]
        c.dumb = c.f32a[1]
        c.bank_rel = {}

        def _relb(b, sem, val):
            c.bank_rel[b] = (sem, val)
        c.relb = _relb

        def _waitb(E, banks):
            for b in banks:
                if b in c.bank_rel:
                    sem, val = c.bank_rel[b]
                    E.wait(sem, val)
        c.waitb = _waitb

        for i, cv in enumerate([0.0, 1.0, -1.0, 0.5, LN_EPS, GLA_SCALE_LOG]):
            nc.const_aps.register(F32, cv, c.cols[:, 150 + i:151 + i])

        _init(c)
        if upto >= 1:
            _gla(c)
        if upto >= 2:
            _ffn0(c)
        if upto >= 3:
            _mla(c)
        if upto >= 4:
            _ffn1(c)
        _finish(c)

    return nc


def _init(c):
    Tn, V, S, G, Y = c.T, c.V, c.S, c.G, c.Y
    ld, ini = c.ld, c.ini
    ld.inc(Y.dma(c.sid[:], c.ident), 16)
    ld.inc(Y.dma(c.smask[:], c.maskUT), 16)
    ld.inc(Y.dma(c.sh128[:], c.h128), 16)
    ld.inc(Y.dma(c.soner[:], c.onesrow), 16)
    ld.inc(Y.dma(c.sonerf[:], c.onesrowf), 16)
    ld.inc(Y.dma(c.sonec[:], c.onescol), 16)
    for i in range(NT):
        c.wd.inc(G.dma(c.res[i][:], c.x_in[i * 128:(i + 1) * 128, :]), 16)
    for i, cv in enumerate([0.0, 1.0, -1.0, 0.5, LN_EPS, GLA_SCALE_LOG]):
        G.memset(c.cols[:, 150 + i:151 + i], cv)
    c.wd.inc(G.dma(c.skg2[:], c.wkg2T), 16)
    ld.inc(Y.dma(c.bcols[:, 8:16], c.bkg2_cols), 16)
    c.wd.inc(G.dma(c.rbf[:, 0:D], c.bg), 16)
    G.memset(c.Sst[:], 0.0)
    G.memset(c.Sbf[:], 0.0)
    ini.inc(G.last, 1)
    for E in (Tn, V, S, Y):
        E.wait(ini, 1)
        E.wait(ld, ld.n)
        E.wait(c.wd, c.wd.n)
    V.ts(c.bcols[:, 0:8], c.bcols[:, 8:16], -1.0, None, OP.mult)
    c.vp.inc(V.last)
    c.negbkg2 = c.bcols[:, 0:8]
    c.bg_row = c.rbf[:, 0:D]


def _finish(c):
    V, Y = c.V, c.Y
    vp, st = c.vp, c.st
    for tc in range(NT):
        V.wait(st, st.n)
        V.copy(c.outf32[:], c.res[tc][:])
        iv = vp.inc(V.last)
        Y.wait(vp, iv)
        st.inc(Y.dma(c.out_hbm[tc * 128:(tc + 1) * 128, :], c.outf32[:]), 16)
    Y.wait(st, st.n)



def _transpose_in(c, pairs, su_cols=None):
    Tn, V = c.T, c.V
    tp, vp = c.tp, c.vp
    hist = {}
    vp0 = vp.n
    for n, (src, dst) in enumerate(pairs):
        pt = c.PT[n % 2][:, ((n // 2) % 4) * 256:((n // 2) % 4) * 256 + 128]
        Tn.wait(vp, hist[n - 2] if n >= 2 else vp0)
        Tn.trans(pt, src, c.sid[:])
        it = tp.inc(Tn.last)
        V.wait(tp, it)
        if su_cols is not None and su_cols[n] is not None:
            V.ts(dst, pt, su_cols[n], None, OP.mult)
        else:
            V.copy(dst, pt)
        hist[n] = vp.inc(V.last)


def _ln_write(c, tc, src_tm_bf, final=False):
    V, S = c.V, c.S
    vp, sp = c.vp, c.sp
    s = c.f32a[0]
    V.tt(s[:], src_tm_bf, c.res[tc][:], OP.add)
    V.red(c.stat[:, 0:1], s[:], OP.add)
    iv = vp.inc(V.last)
    S.wait(vp, iv)
    for q in range(4):
        S.act(c.dumb[:, 0:512], s[:, q * 512:(q + 1) * 512], AF.Square, accum_out=c.stat[:, 4 + q:5 + q])
    isp = sp.inc(S.last)
    V.wait(sp, isp)
    V.ts(c.stat[:, 1:2], c.stat[:, 0:1], 1.0 / 2048.0, None, OP.mult)
    V.tt(c.stat[:, 8:9], c.stat[:, 4:5], c.stat[:, 5:6], OP.add)
    V.tt(c.stat[:, 9:10], c.stat[:, 6:7], c.stat[:, 7:8], OP.add)
    V.tt(c.stat[:, 10:11], c.stat[:, 8:9], c.stat[:, 9:10], OP.add)
    V.ts(c.stat[:, 11:12], c.stat[:, 10:11], 1.0 / 2048.0, None, OP.mult)
    V.tt(c.stat[:, 12:13], c.stat[:, 1:2], c.stat[:, 1:2], OP.mult)
    V.tt(c.stat[:, 13:14], c.stat[:, 11:12], c.stat[:, 12:13], OP.subtract)
    iv = vp.inc(V.last)
    S.wait(vp, iv)
    S.act(c.stat[:, 14:15], c.stat[:, 13:14], AF.Ln, bias=LN_EPS)
    S.act(c.stat[:, 2:3], c.stat[:, 14:15], AF.Exp, scale=-0.5)
    isp = sp.inc(S.last)
    V.wait(sp, isp)
    dst = c.outf32[:] if final else c.res[tc][:]
    V.ts(dst, s[:], c.stat[:, 1:2], c.stat[:, 2:3], OP.subtract, OP.mult)
    return vp.inc(V.last)



def _gla(c):
    Tn, V, S, G = c.T, c.V, c.S, c.G
    tp, vp, sp, wd = c.tp, c.vp, c.sp, c.wd
    PS = c.PS
    fmA, fmB, f32a, wb = c.fmA, c.fmB, c.f32a, c.wb
    ring_rel = {i: tp.n for i in range(5)}

    def xfm(f):
        return fmB[f // 4][:, (f % 4) * 512:(f % 4) * 512 + 512]

    def gpt(m):
        return f32a[m // 4][:, (m % 4) * 512:(m % 4) * 512 + 512]

    for sb in range(NSLAB):
        tcs = [sb * 4 + i for i in range(4)]

        pairs = []
        for ci in range(4):
            for f in range(16):
                pairs.append((c.res[tcs[ci]][:, f * 128:(f + 1) * 128],
                              fmB[f // 4][:, (f % 4) * 512 + ci * 128:(f % 4) * 512 + ci * 128 + 128]))
        _transpose_in(c, pairs)
        xfm_ready = vp.n

        G.wait(tp, ring_rel.get(4, 0))
        iw4 = c.wsem[4].inc(G.dma(wb[4][:, 0:256], c.wkg1_t), 16)
        Tn.wait(c.wsem[4], iw4)
        Tn.wait(vp, xfm_ready)
        c.waitb(Tn, [0])
        pk = PS[0][0:16, :]
        for f in range(16):
            Tn.mm(pk, wb[4][:, f * 16:(f + 1) * 16], xfm(f), start=(f == 0), stop=(f == 15))
        it = tp.inc(Tn.last)
        ring_rel[4] = it
        V.wait(tp, it)
        kg1 = wb[5][0:16, 0:512]
        V.copy(kg1, pk)
        ivk = vp.inc(V.last)
        Tn.wait(vp, ivk)
        sp_hist = {}
        for m in range(8):
            p = PS[m % 2][:]
            if m >= 2:
                Tn.wait(sp, sp_hist[m - 2])
            Tn.mm(p, c.skg2[:, m * 128:(m + 1) * 128], kg1, start=True, stop=True)
            it = tp.inc(Tn.last)
            S.wait(tp, it)
            S.wait(c.vp, 1)
            S.act(gpt(m), p, AF.Exp, scale=-1.0, bias=c.negbkg2[:, m:m + 1])
            S.act(gpt(m), gpt(m), AF.Ln, bias=1.0)
            sp_hist[m] = sp.inc(S.last)
            c.relb(m % 2, sp, sp_hist[m])
        V.wait(sp, sp.n)
        for m in range(8):
            for ch in range(4):
                seg = gpt(m)[:, ch * 128:(ch + 1) * 128]
                V.scan(seg, seg, seg, 0.0, OP.add, OP.bypass)
        ivG = vp.inc(V.last)
        S.wait(vp, ivG)
        for m in range(8):
            S.act(c.cols[:, 16 + m * 4:16 + m * 4 + 4], gpt(m)[:, 127::128], AF.Exp, scale=-1.0 / GLA_NORM)
        sp_a = sp.inc(S.last)

        def qk(qi, base, escale, ebias):
            for mg in range(2):
                for fg in range(4):
                    slot = wb[fg % 2]
                    G.wait(tp, ring_rel.get(fg % 2, 0))
                    ws = c.wsem[fg % 2]
                    iw = ws.inc(G.dma(slot[:], c.gla_qk_t[qi, mg, fg]), 16)
                    Tn.wait(ws, iw)
                    if fg == 0:
                        Tn.wait(vp, xfm_ready)
                        c.waitb(Tn, [2, 3, 4, 5])
                    for fi in range(4):
                        f = fg * 4 + fi
                        for m4 in range(4):
                            Tn.mm(PS[2 + m4][:], slot[:, fi * 512 + m4 * 128:fi * 512 + m4 * 128 + 128],
                                  xfm(f), start=(f == 0), stop=(f == 15))
                    ring_rel[fg % 2] = tp.n + 1
                    tp.inc(Tn.last)
                itg = tp.n
                for m4 in range(4):
                    m = mg * 4 + m4
                    et = wb[4][:, 256 + (m4 % 2) * 512:256 + (m4 % 2) * 512 + 512]
                    S.wait(vp, ivG)
                    S.wait(vp, c._et_vp.get(m4 % 2, 0))
                    S.act(et, gpt(m), AF.Exp, bias=ebias, scale=escale)
                    isp = sp.inc(S.last)
                    V.wait(tp, itg)
                    V.wait(sp, isp)
                    dst = fmA[base + m // 4][:, (m % 4) * 512:(m % 4) * 512 + 512]
                    V.tt(dst, PS[2 + m4][:], et, OP.mult)
                    c._et_vp[m4 % 2] = vp.inc(V.last)
                    c.relb(2 + m4, vp, c._et_vp[m4 % 2])

        c._et_vp = {}
        qk(0, 0, -1.0 / GLA_NORM, 0.0)
        qk(1, 2, +1.0 / GLA_NORM, GLA_SCALE_LOG)

        def tmproj(wi, base, bias_row):
            for oc in range(4):
                for fg in range(4):
                    slot = wb[2 + fg % 2]
                    G.wait(tp, ring_rel.get(2 + fg % 2, 0))
                    ws = c.wsem[2 + fg % 2]
                    iw = ws.inc(G.dma(slot[:], c.gla_vgo_t[wi, oc, fg]), 16)
                    Tn.wait(ws, iw)
                    if fg == 0:
                        Tn.wait(vp, xfm_ready)
                        c.waitb(Tn, [2, 3, 4, 5])
                    for fi in range(4):
                        f = fg * 4 + fi
                        for tci in range(4):
                            Tn.mm(PS[2 + tci][:], xfm(f)[:, tci * 128:(tci + 1) * 128],
                                  slot[:, fi * 512:(fi + 1) * 512],
                                  start=(f == 0), stop=(f == 15 and bias_row is None))
                    ring_rel[2 + fg % 2] = tp.n + 1
                    tp.inc(Tn.last)
                if bias_row is not None:
                    for tci in range(4):
                        Tn.mm(PS[2 + tci][:], c.soner[0:1, 0:128], bias_row[:, oc * 512:(oc + 1) * 512],
                              start=False, stop=True)
                    itg = tp.inc(Tn.last)
                else:
                    itg = tp.n
                V.wait(tp, itg)
                for tci in range(4):
                    V.copy(fmA[base + tci][:, oc * 512:(oc + 1) * 512], PS[2 + tci][:])
                iv = vp.inc(V.last)
                for tci in range(4):
                    c.relb(2 + tci, vp, iv)

        tmproj(0, 4, None)
        tmproj(1, 8, c.bg_row)

        v_ready = vp.n
        prev_state_vp = getattr(c, "_gla_state_vp", {})
        kv_read_vp = getattr(c, "_gla_kv_vp", {})
        sq_hist = getattr(c, "_gla_sq", {})
        for ch in range(4):
            o_evac = {}
            stm_v = {}
            kte_v = {}
            it_st = {}
            it_o = {}
            it_kt = {}
            it_kv = {}

            def refs(h):
                m = h // 2
                r0 = (h % 2) * 64
                return dict(
                    qs=fmA[m // 4][r0:r0 + 64, (m % 4) * 512 + ch * 128:(m % 4) * 512 + ch * 128 + 128],
                    ks=fmA[2 + m // 4][r0:r0 + 64, (m % 4) * 512 + ch * 128:(m % 4) * 512 + ch * 128 + 128],
                    vs=fmA[4 + ch][:, h * 128:(h + 1) * 128],
                    Sst_h=c.Sst[r0:r0 + 64, (h // 2) * 128:(h // 2) * 128 + 128],
                    Stmp_h=c.Stmp[r0:r0 + 64, (h % 2) * 128:(h % 2) * 128 + 128],
                    Sbf_h=c.Sbf[r0:r0 + 64, (h // 2) * 128:(h // 2) * 128 + 128],
                    a_col=c.cols[r0:r0 + 64, 16 + m * 4 + ch:16 + m * 4 + ch + 1],
                    STp=PS[h % 2][:, 0:128],
                    Op=PS[2 + h % 2][:, 0:128],
                    KVp=PS[4 + h % 2][r0:r0 + 64, 0:128],
                    kTp=c.PT[h % 2][:, 0:64],
                    stm=wb[5][:, 1024 + (h % 2) * 128:1024 + (h % 2) * 128 + 128],
                    kT_sb=wb[5][:, 1536 + (h % 2) * 64:1536 + (h % 2) * 64 + 64],
                    r0=r0, m=m)

            Tn.wait(vp, v_ready)
            for hp in range(8):
                h0, h1 = 2 * hp, 2 * hp + 1
                A, B = refs(h0), refs(h1)
                for h, R in ((h0, A), (h1, B)):
                    if h >= 2:
                        Tn.wait(vp, o_evac.get(h - 2, 0))
                        Tn.wait(vp, stm_v.get(h - 2, 0))
                    Tn.wait(sp, sq_hist.get((ch, h - 2), sq_hist.get((ch - 1, h + 14), 0)))
                    Tn.mm(R["STp"], R["ks"], R["qs"], start=True, stop=True)
                    it_st[h] = tp.inc(Tn.last)
                    V.wait(tp, it_st[h])
                    V.tt(R["stm"], R["STp"], c.smask[:], OP.mult)
                    stm_v[h] = vp.inc(V.last)
                for h, R in ((h0, A), (h1, B)):
                    Tn.wait(vp, stm_v[h])
                    Tn.wait(vp, prev_state_vp.get(h, 0))
                    Tn.mm(R["Op"], R["stm"], R["vs"], start=True, stop=False)
                    Tn.mm(R["Op"], R["qs"], R["Sbf_h"], start=False, stop=True)
                    it_o[h] = tp.inc(Tn.last)
                    Tn.wait(vp, kte_v.get(h - 2, 0))
                    Tn.trans(R["kTp"], R["ks"], c.sid[R["r0"]:R["r0"] + 64, R["r0"]:R["r0"] + 64])
                    it_kt[h] = tp.inc(Tn.last)
                    V.wait(tp, it_kt[h])
                    V.copy(R["kT_sb"], R["kTp"])
                    kte_v[h] = vp.inc(V.last)
                for h, R in ((h0, A), (h1, B)):
                    Tn.wait(vp, kte_v[h])
                    Tn.wait(vp, prev_state_vp.get(h - 2 if h >= 2 else h, 0) if False else kv_read_vp.get(h % 2, 0))
                    Tn.mm(R["KVp"], R["kT_sb"], R["vs"], start=True, stop=True)
                    it_kv[h] = tp.inc(Tn.last)
                for h, R in ((h0, A), (h1, B)):
                    V.wait(tp, it_kv[h])
                    V.wait(sp, sp_a)
                    V.tt(R["Stmp_h"], R["Sst_h"], R["KVp"], OP.add)
                    V.ts(R["Sst_h"], R["Stmp_h"], R["a_col"], None, OP.mult)
                    V.copy(R["Sbf_h"], R["Sst_h"])
                    prev_state_vp[h] = vp.inc(V.last)
                    kv_read_vp[h % 2] = prev_state_vp[h]
                    V.wait(tp, it_o[h])
                    V.copy(fmA[12 + ch][:, h * 128:(h + 1) * 128], R["Op"])
                    V.red(c.stat[:, 32 + h:33 + h], R["Op"], OP.add)
                    o_evac[h] = vp.inc(V.last)
                    S.wait(tp, it_o[h])
                    S.act(c.dumb[:, 0:128], R["Op"], AF.Square, accum_out=c.stat[:, 48 + h:49 + h])
                    sq_hist[(ch, h)] = sp.inc(S.last)
            V.wait(sp, sp.n)
            V.ts(c.stat[:, 0:16], c.stat[:, 32:48], 1.0 / 128.0, None, OP.mult)
            V.ts(c.cols[:, 48:64], c.stat[:, 48:64], 1.0 / 128.0, None, OP.mult)
            V.tt(c.cols[:, 64:80], c.stat[:, 0:16], c.stat[:, 0:16], OP.mult)
            V.tt(c.cols[:, 80:96], c.cols[:, 48:64], c.cols[:, 64:80], OP.subtract)
            ivs = vp.inc(V.last)
            S.wait(vp, ivs)
            S.act(c.cols[:, 96:112], c.cols[:, 80:96], AF.Ln, bias=LN_EPS)
            S.act(c.stat[:, 16:32], c.cols[:, 96:112], AF.Exp, scale=-0.5)
            S.act(f32a[0][:], fmA[8 + ch][:], AF.Silu)
            isp = sp.inc(S.last)
            V.wait(sp, isp)
            for h in range(16):
                V.ts(fmA[8 + ch][:, h * 128:(h + 1) * 128], fmA[12 + ch][:, h * 128:(h + 1) * 128],
                     c.stat[:, 0 + h:1 + h], c.stat[:, 16 + h:17 + h], OP.subtract, OP.mult)
            for h in range(16):
                V.tt(fmA[12 + ch][:, h * 128:(h + 1) * 128], fmA[8 + ch][:, h * 128:(h + 1) * 128],
                     f32a[0][:, h * 128:(h + 1) * 128], OP.mult)
            vp.inc(V.last)
        c._gla_state_vp = prev_state_vp
        c._gla_kv_vp = kv_read_vp
        c._gla_sq = sq_hist

        z_pairs = []
        for ci in range(4):
            for f in range(16):
                z_pairs.append((fmA[12 + ci][:, f * 128:(f + 1) * 128],
                                fmB[f // 4][:, (f % 4) * 512 + ci * 128:(f % 4) * 512 + ci * 128 + 128]))
        Tn.wait(vp, vp.n)
        _transpose_in(c, z_pairs)
        zfm_ready = vp.n

        for oc in range(4):
            for fg in range(4):
                slot = wb[fg % 2]
                G.wait(tp, ring_rel.get(fg % 2, 0))
                ws = c.wsem[fg % 2]
                iw = ws.inc(G.dma(slot[:], c.gla_vgo_t[2, oc, fg]), 16)
                Tn.wait(ws, iw)
                if fg == 0:
                    Tn.wait(vp, zfm_ready)
                    c.waitb(Tn, [2, 3, 4, 5])
                for fi in range(4):
                    f = fg * 4 + fi
                    for tci in range(4):
                        Tn.mm(PS[2 + tci][:], xfm(f)[:, tci * 128:(tci + 1) * 128],
                              slot[:, fi * 512:(fi + 1) * 512], start=(f == 0), stop=(f == 15))
                ring_rel[fg % 2] = tp.n + 1
                tp.inc(Tn.last)
            itg = tp.n
            V.wait(tp, itg)
            for tci in range(4):
                V.copy(fmA[4 + tci][:, oc * 512:(oc + 1) * 512], PS[2 + tci][:])
            c._wo_evac = vp.inc(V.last)
            for tci in range(4):
                c.relb(2 + tci, vp, c._wo_evac)
        for tci in range(4):
            _ln_write(c, tcs[tci], fmA[4 + tci][:])



FFN0_BLOCKS = None


def _ffn0(c):
    Tn, V, S, G, Y = c.T, c.V, c.S, c.G, c.Y
    tp, vp, sp, ld = c.tp, c.vp, c.sp, c.ld
    PS = c.PS
    fmA, fmB, f32a, wb = c.fmA, c.fmB, c.f32a, c.wb
    ring_rel = {i: tp.n for i in range(5)}

    def xfm(f):
        return fmB[f // 4][:, (f % 4) * 512:(f % 4) * 512 + 512]

    def zsl(jj):
        return fmA[jj // 4][:, (jj % 4) * 512:(jj % 4) * 512 + 512]

    SC1 = 2.0 ** -5.5
    SC2 = 2.0 ** -6.5

    Y.wait(vp, vp.n)
    Y.wait(sp, sp.n)
    Y.wait(tp, tp.n)
    ld.inc(Y.dma(c.cols[:, 0:16], c.su_ug_cols), 16)
    ld.inc(Y.dma(c.cols[:, 16:80], c.wsc_up_cols), 16)
    ld.inc(Y.dma(c.cols[:, 80:144], c.wsc_gate_cols), 16)
    ld.inc(Y.dma(c.stat[:, 0:64], c.su_dp_cols), 16)
    ld.inc(Y.dma(f32a[0][0:1, 0:2048], c.wsc_dp_row), 16)
    V.wait(ld, ld.n)
    V.tt(c.cols[:, 16:80], c.cols[:, 16:80], c.stat[:, 0:64], OP.mult)
    V.ts(c.cols[:, 16:80], c.cols[:, 16:80], SC1, None, OP.mult)
    V.ts(c.cols[:, 80:144], c.cols[:, 80:144], SC1, None, OP.mult)
    V.ts(c.rbf[0:1, 0:2048], f32a[0][0:1, 0:2048], SC2, None, OP.mult)
    ivc = vp.inc(V.last)
    Tn.wait(vp, ivc)
    for oc4 in range(4):
        Tn.mm(PS[2][:], c.soner[0:1, 0:128], c.rbf[0:1, oc4 * 512:(oc4 + 1) * 512], start=True, stop=True)
        itb = tp.inc(Tn.last)
        V.wait(tp, itb)
        V.copy(wb[4][:, oc4 * 512:(oc4 + 1) * 512], PS[2][:])
        ivb = vp.inc(V.last)
        Tn.wait(vp, ivb)

    for sb in range(NSLAB):
        tcs = [sb * 4 + i for i in range(4)]
        pairs, sus = [], []
        for ci in range(4):
            for f in range(16):
                pairs.append((c.res[tcs[ci]][:, f * 128:(f + 1) * 128],
                              fmB[f // 4][:, (f % 4) * 512 + ci * 128:(f % 4) * 512 + ci * 128 + 128]))
                sus.append(c.cols[:, f:f + 1])
        _transpose_in(c, pairs, su_cols=sus)
        Tn.wait(vp, vp.n)
        Tn.wait(sp, sp.n)
        ev = {}
        for f in range(16):
            p = PS[2 + f % 4][:]
            Tn.wait(vp, ev.get(f - 4, 0))
            if f < 4:
                c.waitb(Tn, [2 + f % 4])
            Tn.mm(p, c.sh128[:], xfm(f), start=True, stop=True)
            it = tp.inc(Tn.last)
            V.wait(tp, it)
            V.copy(xfm(f), p)
            ev[f] = vp.inc(V.last)
            c.relb(2 + f % 4, vp, ev[f])
        V.wait(vp, vp.n)
        for s in (1, 2, 4, 8):
            prs = [t0 for t0 in range(16) if not (t0 & s)]
            for t0 in prs:
                V.tt(xfm(t0), xfm(t0), xfm(t0 ^ s), OP.add)
            for t0 in prs:
                V.stt(xfm(t0 ^ s), xfm(t0 ^ s), -2.0, xfm(t0), OP.mult, OP.add)
        xh_ready = vp.inc(V.last)

        for jc in range(4):
            for jg in range(4):
                for jp in range(2):
                    for fg in range(4):
                        slotU = wb[fg % 2]
                        slotG = wb[2 + fg % 2]
                        G.wait(tp, ring_rel.get(fg % 2, 0))
                        ws = c.wsem[fg % 2]
                        iwU = ws.inc(G.dma(slotU[:], c.ug_t[0, jc, jg, fg]), 16)
                        G.wait(tp, ring_rel.get(2 + fg % 2, 0))
                        ws2 = c.wsem[2 + fg % 2]
                        iwG = ws2.inc(G.dma(slotG[:], c.ug_t[1, jc, jg, fg]), 16)
                        Tn.wait(ws, iwU)
                        Tn.wait(ws2, iwG)
                        if fg == 0:
                            Tn.wait(vp, xh_ready)
                            c.waitb(Tn, [2, 3, 4, 5])
                        for fi in range(4):
                            f = fg * 4 + fi
                            for j2 in range(2):
                                co = fi * 512 + (jp * 2 + j2) * 128
                                Tn.mm(PS[2 + j2][:], slotU[:, co:co + 128], xfm(f),
                                      start=(f == 0), stop=(f == 15))
                                Tn.mm(PS[4 + j2][:], slotG[:, co:co + 128], xfm(f),
                                      start=(f == 0), stop=(f == 15))
                        ring_rel[fg % 2] = tp.n + 1
                        ring_rel[2 + fg % 2] = tp.n + 1
                        tp.inc(Tn.last)
                    itg = tp.n
                    for j2 in range(2):
                        jj = jc * 16 + jg * 4 + jp * 2 + j2
                        S.wait(tp, itg)
                        S.act(f32a[0][:, j2 * 512:(j2 + 1) * 512], PS[4 + j2][:], AF.Silu,
                              scale=c.cols[:, 80 + jj:81 + jj])
                        isp = sp.inc(S.last)
                        c.relb(4 + j2, sp, isp)
                        V.wait(sp, isp)
                        V.stt(zsl(jj), PS[2 + j2][:], c.cols[:, 16 + jj:17 + jj],
                              f32a[0][:, j2 * 512:(j2 + 1) * 512], OP.mult, OP.mult)
                        iv = vp.inc(V.last)
                        c.relb(2 + j2, vp, iv)
            Tn.wait(vp, vp.n)
            ev2 = {}
            for jt in range(16):
                jj = jc * 16 + jt
                p = PS[2 + jt % 4][:]
                Tn.wait(vp, ev2.get(jt - 4, 0))
                if jt < 4:
                    c.waitb(Tn, [2 + jt % 4])
                Tn.mm(p, c.sh128[:], zsl(jj), start=True, stop=True)
                it = tp.inc(Tn.last)
                V.wait(tp, it)
                V.copy(zsl(jj), p)
                ev2[jt] = vp.inc(V.last)
                c.relb(2 + jt % 4, vp, ev2[jt])
            V.wait(vp, vp.n)
            for s in (1, 2, 4, 8):
                prs = [t0 for t0 in range(16) if not (t0 & s)]
                for t0 in prs:
                    V.tt(zsl(jc * 16 + t0), zsl(jc * 16 + t0), zsl(jc * 16 + (t0 ^ s)), OP.add)
                for t0 in prs:
                    V.stt(zsl(jc * 16 + (t0 ^ s)), zsl(jc * 16 + (t0 ^ s)), -2.0,
                          zsl(jc * 16 + t0), OP.mult, OP.add)
            vp.inc(V.last)
        for s in (16, 32):
            prs = [t0 for t0 in range(64) if not (t0 & s)]
            for t0 in prs:
                V.tt(zsl(t0), zsl(t0), zsl(t0 ^ s), OP.add)
            for t0 in prs:
                V.stt(zsl(t0 ^ s), zsl(t0 ^ s), -2.0, zsl(t0), OP.mult, OP.add)
        z_ready = vp.inc(V.last)

        for oc in range(4):
            for jg in range(16):
                slot = wb[2 + jg % 2]
                G.wait(tp, ring_rel.get(2 + jg % 2, 0))
                ws = c.wsem[2 + jg % 2]
                iw = ws.inc(G.dma(slot[:], c.dn_t[oc, jg]), 16)
                Tn.wait(ws, iw)
                if jg == 0:
                    Tn.wait(vp, z_ready)
                    c.waitb(Tn, [2, 3, 4, 5])
                for ji in range(4):
                    j = jg * 4 + ji
                    for tci in range(4):
                        Tn.mm(PS[2 + tci][:], zsl(j)[:, tci * 128:(tci + 1) * 128],
                              slot[:, ji * 512:(ji + 1) * 512], start=(j == 0), stop=(j == 63))
                ring_rel[2 + jg % 2] = tp.n + 1
                tp.inc(Tn.last)
            itg = tp.n
            V.wait(tp, itg)
            for tci in range(4):
                V.tt(fmB[tci][:, oc * 512:(oc + 1) * 512], PS[2 + tci][:],
                     wb[4][:, oc * 512:(oc + 1) * 512], OP.mult)
            iv = vp.inc(V.last)
            for tci in range(4):
                c.relb(2 + tci, vp, iv)
        for tci in range(4):
            _ln_write(c, tcs[tci], fmB[tci][:])


def _mla(c):
    Tn, V, S, G, Y = c.T, c.V, c.S, c.G, c.Y
    tp, vp, sp, ld = c.tp, c.vp, c.sp, c.ld
    PS, PT = c.PS, c.PT
    fmA, fmB, f32a, wb = c.fmA, c.fmB, c.f32a, c.wb
    ring_rel = {i: tp.n for i in range(5)}

    Y.wait(vp, vp.n)
    Y.wait(sp, sp.n)
    Y.wait(tp, tp.n)
    ld.inc(Y.dma(c.bcols[:, 16:20], c.kvd_b), 16)
    ld.inc(Y.dma(c.bcols[:, 20:28], c.qd_b), 16)
    ld.inc(Y.dma(c.bcols[:, 28:44], c.kup_b), 16)
    ld.inc(Y.dma(c.bcols[:, 44:60], c.vup_b), 16)
    ld.inc(Y.dma(c.bcols[0:64, 60:76], c.kr_b), 16)
    ld.inc(Y.dma(c.bcols[:, 76:92], c.qup_b), 16)
    ld.inc(Y.dma(c.cols[0:64, 128:144], c.qr_b), 16)
    G.wait(tp, tp.n)
    G.wait(vp, vp.n)
    c.wd.inc(G.dma(c.rbf[0:1, 0:2048], c.outw_b), 16)
    bias_ld, bias_wd = ld.n, c.wd.n

    def xfm(f):
        return fmB[f // 4][:, (f % 4) * 512:(f % 4) * 512 + 512]

    def kvc(m):
        return fmA[m]

    def tin(xs):
        pairs = []
        for ci in range(4):
            tc = xs * 4 + ci
            for f in range(16):
                pairs.append((c.res[tc][:, f * 128:(f + 1) * 128],
                              fmB[f // 4][:, (f % 4) * 512 + ci * 128:(f % 4) * 512 + ci * 128 + 128]))
        _transpose_in(c, pairs)
        Tn.wait(vp, vp.n)
        Tn.wait(sp, sp.n)

    Tn.wait(ld, bias_ld)
    V.wait(ld, bias_ld)
    V.wait(c.wd, bias_wd)
    for xs in range(4):
        tin(xs)
        for fg in range(4):
            slot = wb[fg % 2]
            G.wait(tp, ring_rel.get(fg % 2, 0))
            ws = c.wsem[fg % 2]
            iw = ws.inc(G.dma(slot[:], c.kvd_t[fg]), 16)
            Tn.wait(ws, iw)
            if fg == 0:
                c.waitb(Tn, [2, 3, 4, 5])
            for fi in range(4):
                f = fg * 4 + fi
                for m in range(4):
                    Tn.mm(PS[2 + m][:], slot[:, fi * 512 + m * 128:fi * 512 + m * 128 + 128],
                          xfm(f), start=(f == 0), stop=(f == 15))
            ring_rel[fg % 2] = tp.n + 1
            tp.inc(Tn.last)
        itg = tp.n
        V.wait(tp, itg)
        for m in range(4):
            V.ts(kvc(m)[:, xs * 512:(xs + 1) * 512], PS[2 + m][:], c.bcols[:, 16 + m:17 + m], None, OP.add)
        iv = vp.inc(V.last)
        for m in range(4):
            c.relb(2 + m, vp, iv)

    c._head_prev = {}
    for qsl in range(2):
        for xs2 in range(2):
            xs = qsl * 2 + xs2
            tin(xs)
            for mg in range(2):
                for fg in range(4):
                    slot = wb[fg % 2]
                    G.wait(tp, ring_rel.get(fg % 2, 0))
                    ws = c.wsem[fg % 2]
                    iw = ws.inc(G.dma(slot[:], c.qd_t[mg, fg]), 16)
                    Tn.wait(ws, iw)
                    if fg == 0:
                        c.waitb(Tn, [2, 3, 4, 5])
                    for fi in range(4):
                        f = fg * 4 + fi
                        for m4 in range(4):
                            Tn.mm(PS[2 + m4][:], slot[:, fi * 512 + m4 * 128:fi * 512 + m4 * 128 + 128],
                                  xfm(f), start=(f == 0), stop=(f == 15))
                    ring_rel[fg % 2] = tp.n + 1
                    tp.inc(Tn.last)
                itg = tp.n
                V.wait(tp, itg)
                for m4 in range(4):
                    m = mg * 4 + m4
                    dst = fmA[4 + m // 2][:, (m % 2) * 1024 + xs2 * 512:(m % 2) * 1024 + xs2 * 512 + 512]
                    V.ts(dst, PS[2 + m4][:], c.bcols[:, 20 + m:21 + m], None, OP.add)
                iv = vp.inc(V.last)
                for m4 in range(4):
                    c.relb(2 + m4, vp, iv)

        for h in range(16):
            ws = c.wsem[4]
            G.wait(tp, ring_rel.get(4, 0))
            ws.inc(G.dma(wb[4][:, 0:512], c.kup_t[h]), 16)
            ws.inc(G.dma(wb[4][:, 512:1024], c.vup_t[h]), 16)
            ws.inc(G.dma(wb[4][:, 1024:1280], c.kr_t[h]), 16)
            ws.inc(G.dma(wb[5][:, 0:1024], c.qup_t[h]), 16)
            iw = ws.inc(G.dma(wb[5][:, 1024:1536], c.qr_t[h]), 16)
            Tn.wait(ws, iw)
            ev = {}
            for tkc in range(4):
                p = PS[2 + tkc % 4][:]
                c.waitb(Tn, [2 + tkc % 4])
                for kf in range(4):
                    Tn.mm(p, wb[4][:, kf * 128:(kf + 1) * 128], kvc(kf)[:, tkc * 512:(tkc + 1) * 512],
                          start=(kf == 0), stop=(kf == 3))
                it = tp.inc(Tn.last)
                V.wait(tp, it)
                V.ts(fmB[0][:, tkc * 512:(tkc + 1) * 512], p, c.bcols[:, 28 + h:29 + h], None, OP.add)
                ev[tkc] = vp.inc(V.last)
                c.relb(2 + tkc % 4, vp, ev[tkc])
            for tkc in range(4):
                p = PS[2 + tkc % 4][0:64, :]
                c.waitb(Tn, [2 + tkc % 4])
                for kf in range(4):
                    Tn.mm(p, wb[4][:, 1024 + kf * 64:1024 + (kf + 1) * 64],
                          kvc(kf)[:, tkc * 512:(tkc + 1) * 512], start=(kf == 0), stop=(kf == 3))
                it = tp.inc(Tn.last)
                V.wait(tp, it)
                V.ts(fmB[1][0:64, tkc * 512:(tkc + 1) * 512], p, c.bcols[0:64, 60 + h:61 + h], None, OP.add)
                ev[100 + tkc] = vp.inc(V.last)
                c.relb(2 + tkc % 4, vp, ev[100 + tkc])
            for tk16 in range(16):
                p = PS[2 + tk16 % 4][:, 0:128]
                if tk16 >= 4:
                    Tn.wait(vp, ev[200 + tk16 - 4])
                elif tk16 == 0:
                    c.waitb(Tn, [2, 3, 4, 5])
                    Tn.wait(vp, c._head_prev.get("v", 0))
                for kf in range(4):
                    Tn.mm(p, kvc(kf)[:, tk16 * 128:(tk16 + 1) * 128],
                          wb[4][:, 512 + kf * 128:512 + (kf + 1) * 128], start=(kf == 0), stop=(kf == 3))
                it = tp.inc(Tn.last)
                V.wait(tp, it)
                V.ts(fmB[2][:, tk16 * 128:(tk16 + 1) * 128], p, c.bcols[:, 44 + h:45 + h], None, OP.add)
                ev[200 + tk16] = vp.inc(V.last)
            for b4 in range(4):
                c.relb(2 + b4, vp, ev[200 + 12 + b4])
            for qn in range(2):
                p = PS[2 + qn][:]
                c.waitb(Tn, [2 + qn])
                for qf in range(8):
                    rhs = fmA[4 + qf // 2][:, (qf % 2) * 1024 + qn * 512:(qf % 2) * 1024 + qn * 512 + 512]
                    Tn.mm(p, wb[5][:, qf * 128:(qf + 1) * 128], rhs, start=(qf == 0), stop=(qf == 7))
                it = tp.inc(Tn.last)
                V.wait(tp, it)
                V.ts(fmB[3][:, qn * 512:(qn + 1) * 512], p, c.bcols[:, 76 + h:77 + h], None, OP.add)
                ev[300 + qn] = vp.inc(V.last)
                c.relb(2 + qn, vp, ev[300 + qn])
            for qn in range(2):
                p = PS[2 + qn][0:64, :]
                c.waitb(Tn, [2 + qn])
                for qf in range(8):
                    rhs = fmA[4 + qf // 2][:, (qf % 2) * 1024 + qn * 512:(qf % 2) * 1024 + qn * 512 + 512]
                    Tn.mm(p, wb[5][:, 1024 + qf * 64:1024 + (qf + 1) * 64], rhs, start=(qf == 0), stop=(qf == 7))
                it = tp.inc(Tn.last)
                V.wait(tp, it)
                V.ts(fmB[3][0:64, 1024 + qn * 512:1024 + (qn + 1) * 512], p,
                     c.cols[0:64, 128 + h:129 + h], None, OP.add)
                ev[400 + qn] = vp.inc(V.last)
                c.relb(2 + qn, vp, ev[400 + qn])
            ring_rel[4] = tp.n
            c._head_prev = {"v": ev[200 + 15]}

            Tn.wait(vp, vp.n)
            for kc4 in range(4):
                p = PS[2 + kc4][:]
                Tn.mm(p, fmB[3][:, 0:128], fmB[0][:, kc4 * 512:(kc4 + 1) * 512], start=True, stop=False)
                Tn.mm(p, fmB[3][0:64, 1024:1152], fmB[1][0:64, kc4 * 512:(kc4 + 1) * 512],
                      start=False, stop=True)
            it = tp.inc(Tn.last)
            V.wait(tp, it)
            for kc4 in range(4):
                V.red(c.stat[:, 20 + kc4:21 + kc4], PS[2 + kc4][:], OP.max)
            V.tt(c.stat[:, 20:21], c.stat[:, 20:21], c.stat[:, 21:22], OP.max)
            V.tt(c.stat[:, 22:23], c.stat[:, 22:23], c.stat[:, 23:24], OP.max)
            V.tt(c.stat[:, 20:21], c.stat[:, 20:21], c.stat[:, 22:23], OP.max)
            V.copy(wb[5][:, 1536:1537], c.stat[:, 20:21])
            ivm = vp.inc(V.last)
            Tn.wait(vp, ivm)
            Tn.trans(PT[0][0:1, 0:128], wb[5][:, 1536:1537], c.sid[:])
            it = tp.inc(Tn.last)
            V.wait(tp, it)
            V.red(c.stat[0:1, 24:25], PT[0][0:1, 0:128], OP.max)
            ivm = vp.inc(V.last)
            Tn.wait(vp, ivm)
            Tn.mm(PS[5][:, 0:1], c.sonerf[:], c.stat[0:1, 24:25], start=True, stop=True)
            it = tp.inc(Tn.last)
            V.wait(tp, it)
            V.ts(c.cols[:, 144:145], PS[5][:, 0:1], -MLA_SCALE, None, OP.mult)
            ebias_vp = vp.inc(V.last)
            S.wait(vp, ebias_vp)

            for qn in range(2):
                pexp = {}
                Tn.wait(vp, vp.n)
                c.waitb(Tn, [0, 1, 2, 3, 4, 5])

                def st_mm(kc):
                    STp = PS[0 + kc % 2][:]
                    if kc >= 2:
                        Tn.wait(sp, pexp[kc - 2])
                    Tn.mm(STp, fmB[0][:, kc * 128:(kc + 1) * 128], fmB[3][:, qn * 512:(qn + 1) * 512],
                          start=True, stop=False)
                    Tn.mm(STp, fmB[1][0:64, kc * 128:(kc + 1) * 128],
                          fmB[3][0:64, 1024 + qn * 512:1024 + (qn + 1) * 512], start=False, stop=True)
                    it = tp.inc(Tn.last)
                    S.wait(tp, it)
                    pslot = wb[0][:, (kc % 2) * 512:(kc % 2) * 512 + 512]
                    S.act(pslot, STp, AF.Exp, scale=MLA_SCALE, bias=c.cols[:, 144:145])
                    pexp[kc] = sp.inc(S.last)

                st_mm(0)
                for kc in range(16):
                    if kc + 1 < 16:
                        st_mm(kc + 1)
                    pslot = wb[0][:, (kc % 2) * 512:(kc % 2) * 512 + 512]
                    Tn.wait(sp, pexp[kc])
                    Tn.mm(PS[2][:], fmB[2][:, kc * 128:(kc + 1) * 128], pslot, start=(kc == 0), stop=(kc == 15))
                    Tn.mm(PS[3][0:1, :], c.sonec[:], pslot, start=(kc == 0), stop=(kc == 15))
                    tp.inc(Tn.last)
                it = tp.n
                V.wait(tp, it)
                V.recip(c.rows[0:1, 0:512], PS[3][0:1, :])
                ivr = vp.inc(V.last)
                Tn.wait(vp, ivr)
                Tn.mm(PS[4][:], c.sonerf[:], c.rows[0:1, 0:512], start=True, stop=True)
                it = tp.inc(Tn.last)
                V.wait(tp, it)
                V.copy(f32a[0][:, 0:512], PS[4][:])
                V.tt(fmA[8 + h // 2][:, (h % 2) * 1024 + qn * 512:(h % 2) * 1024 + qn * 512 + 512],
                     PS[2][:], f32a[0][:, 0:512], OP.mult)
                iv = vp.inc(V.last)
                c.relb(2, vp, iv)
                c.relb(4, vp, iv)

        c._ctx_ready = vp.n
        ring_rel[0] = ring_rel[1] = tp.n
        for tcg in range(2):
            for oc in range(4):
                for hg in range(4):
                    slot = wb[2 + hg % 2]
                    G.wait(tp, ring_rel.get(2 + hg % 2, 0))
                    ws = c.wsem[2 + hg % 2]
                    iw = ws.inc(G.dma(slot[:], c.outw_t[oc, hg]), 16)
                    Tn.wait(ws, iw)
                    if hg == 0:
                        Tn.wait(vp, c._ctx_ready)
                        c.waitb(Tn, [2, 3, 4, 5])
                    for hi in range(4):
                        h = hg * 4 + hi
                        for tci in range(4):
                            tq = tcg * 4 + tci
                            lhs = fmA[8 + h // 2][:, (h % 2) * 1024 + tq * 128:(h % 2) * 1024 + tq * 128 + 128]
                            Tn.mm(PS[2 + tci][:], lhs, slot[:, hi * 512:(hi + 1) * 512],
                                  start=(h == 0), stop=False)
                    ring_rel[2 + hg % 2] = tp.n + 1
                    tp.inc(Tn.last)
                for tci in range(4):
                    Tn.mm(PS[2 + tci][:], c.soner[0:1, 0:128], c.rbf[0:1, oc * 512:(oc + 1) * 512],
                          start=False, stop=True)
                itg = tp.inc(Tn.last)
                V.wait(tp, itg)
                for tci in range(4):
                    V.copy(fmB[tci][:, oc * 512:(oc + 1) * 512], PS[2 + tci][:])
                iv = vp.inc(V.last)
                for tci in range(4):
                    c.relb(2 + tci, vp, iv)
            for tci in range(4):
                tc = qsl * 8 + tcg * 4 + tci
                _ln_write(c, tc, fmB[tci][:])


def _ffn1(c):
    Tn, V, S, G, Y = c.T, c.V, c.S, c.G, c.Y
    tp, vp, sp, gp, ld = c.tp, c.vp, c.sp, c.gp, c.ld
    PS, PT = c.PS, c.PT
    fmA, fmB, f32a, wb = c.fmA, c.fmB, c.f32a, c.wb
    ring_rel = {i: tp.n for i in range(5)}
    G.wait(tp, tp.n)
    G.wait(vp, vp.n)
    G.wait(sp, sp.n)

    def fsl(i, j, w=512):
        return fmA[i][:, j * w:(j + 1) * w]

    def strip_tiles(s):
        if s < 16:
            return [(c.w1_t[s, f], 2048) for f in range(16)]
        return [(c.w2_t[s - 16, j], 8192) for j in range(64)]

    def strip_tiles_q(s):
        if s < 16:
            return [c.w1q_hbm[s, f] for f in range(16)]
        return [c.w2q_hbm[s - 16, j] for j in range(64)]

    stA = {"n": 0}
    stB = {"n": 0}
    mm_rel = {}
    q_rel = {}
    bc_rel = {}
    bc_tp = {}

    def phaseA(s):
        tiles = strip_tiles(s)
        N = tiles[0][1]
        for ti, (src_ap, _) in enumerate(tiles):
            n = stA["n"]
            sk = n % 4
            slot = wb[4 + sk // 2][:, (sk % 2) * 512:(sk % 2) * 512 + 512]
            G.wait(tp, mm_rel.get(n - 4, 0))
            wsk = c.wsem[4 + sk]
            iw = wsk.inc(G.dma(slot, src_ap), 16)
            S.wait(wsk, iw)
            S.act(slot, slot, AF.Abs, scale=2.0)
            isp = sp.inc(S.last)
            Tn.wait(sp, isp)
            Tn.mm(PS[0][0:1, :], c.sonec[:], slot, start=(ti == 0), stop=(ti == len(tiles) - 1))
            mm_rel[n] = tp.inc(Tn.last)
            stA["n"] = n + 1
        V.wait(tp, mm_rel[stA["n"] - 1])
        V.ts(c.rows[0:1, 0:512], PS[0][0:1, :], 1.0 / (2.0 * N), 1e-5, OP.mult, OP.max)
        ivr = vp.inc(V.last)
        c.relb(0, vp, ivr)
        c.waitb(Tn, [1])
        Tn.wait(vp, max(ivr, bc_rel.get(s - 2, 0)))
        Tn.mm(PS[1][:], c.sonerf[:], c.rows[0:1, 0:512], start=True, stop=True)
        itb = tp.inc(Tn.last)
        bc_tp[s] = itb
        V.wait(tp, itb)
        V.copy(f32a[0][:, (s % 2) * 512:(s % 2) * 512 + 512], PS[1][:])
        iv = vp.inc(V.last)
        c.relb(1, vp, iv)

    def phaseB(s):
        tiles = strip_tiles(s)
        qdst = strip_tiles_q(s)
        swb_ = f32a[0][:, (s % 2) * 512:(s % 2) * 512 + 512]
        last = 0
        for ti, (src_ap, _) in enumerate(tiles):
            n = stB["n"]
            k = n % 4
            wslot = wb[0 + k // 2][:, (k % 2) * 512:(k % 2) * 512 + 512]
            aslot = wb[2 + k // 2][:, (k % 2) * 512:(k % 2) * 512 + 512]
            sslot = wb[4 + k // 2][:, 1024 + (k % 2) * 512:1024 + (k % 2) * 512 + 512]
            if n - 4 in q_rel:
                G.wait(*q_rel[n - 4])
            wsk = c.wsem[k]
            iw = wsk.inc(G.dma(wslot, src_ap), 16)
            S.wait(wsk, iw)
            S.wait(tp, bc_tp[s])
            S.act(aslot, wslot, AF.Abs, scale=2.0)
            S.act(sslot, wslot, AF.Sign)
            isp = sp.inc(S.last)
            V.wait(sp, isp)
            V.tt(aslot, aslot, swb_, OP.is_ge)
            V.tt(aslot, aslot, sslot, OP.mult)
            V.tt(wslot, aslot, swb_, OP.mult)
            ivq = vp.inc(V.last)
            Y.wait(vp, ivq)
            gsk = c.gsem[k]
            q_rel[n] = (gsk, gsk.inc(Y.dma(qdst[ti], wslot), 16))
            last = ivq
            stB["n"] = n + 1
        bc_rel[s] = last

    for s in range(20):
        phaseA(s)
        if s > 0:
            phaseB(s - 1)
    phaseB(19)
    for gs in c.gsem:
        G.wait(gs, gs.n)
        V.wait(gs, gs.n)
    stash_done = 0

    perm = {}
    for tc in range(4):
        for jb in range(64):
            s_id = (tc * 4 + jb // 16) * 16 + (jb % 16)
            d_id = (jb // 4) * 16 + (jb % 4) * 4 + tc
            perm[s_id] = d_id
    cycles = []
    seen = set()
    for s0 in range(256):
        if s0 in seen:
            continue
        cyc = [s0]
        seen.add(s0)
        nxt = perm[s0]
        while nxt != s0:
            cyc.append(nxt)
            seen.add(nxt)
            nxt = perm[nxt]
        cycles.append(cyc)

    def slot_ap(sid):
        return fmA[sid // 16][:, (sid % 16) * 128:(sid % 16) * 128 + 128]

    for sb in range(NSLAB):
        tcs = [sb * 4 + i for i in range(4)]
        pairs = []
        for ci, tc in enumerate(tcs):
            V.wait(tp, tp.n)
            V.red(c.stat[:, 30:31], c.res[tc][:], OP.max, absval=True)
            V.ts(c.stat[:, 31:32], c.stat[:, 30:31], 1e-5, None, OP.max)
            V.recip(c.stat[:, 32:33], c.stat[:, 31:32])
            V.ts(c.cols[:, 32 + ci:33 + ci], c.stat[:, 32:33], 127.0, None, OP.mult)
            V.ts(c.cols[:, 36 + ci:37 + ci], c.stat[:, 31:32], 1.0 / 127.0, None, OP.mult)
            sx = c.cols[:, 32 + ci:33 + ci]
            for half in range(2):
                hs = slice(half * 1024, half * 1024 + 1024)
                v_ = f32a[0][:, 0:1024]
                V.ts(v_, c.res[tc][:, hs], sx, None, OP.mult)
                V.ts(f32a[0][:, 1024:2048], v_, M52, C52_32, OP.add, OP.subtract)
                V.ts(f32a[0][:, 1024:2048], f32a[0][:, 1024:2048], M32, None, OP.subtract)
                V.ts(f32a[1][:, 0:1024], v_, M23, M23, OP.add, OP.subtract)
                V.ts(f32a[1][:, 1024:2048], v_, M32, M32, OP.add, OP.subtract)
                V.tt(f32a[1][:, 0:1024], f32a[1][:, 0:1024], f32a[1][:, 1024:2048], OP.subtract)
                V.tt(fmA[15][:, hs], f32a[0][:, 1024:2048], f32a[1][:, 0:1024], OP.add)
            iv = vp.inc(V.last)
            for f in range(16):
                pairs.append((fmA[15][:, f * 128:(f + 1) * 128],
                              fmB[f // 4][:, (f % 4) * 512 + ci * 128:(f % 4) * 512 + ci * 128 + 128]))
            _transpose_in(c, pairs)
            pairs = []
        xq_ready = vp.n

        for jc in range(16):
            for g2 in range(4):
                slot = wb[g2 % 2]
                G.wait(tp, ring_rel.get(g2 % 2, 0))
                ws = c.wsem[g2 % 2]
                for fi4 in range(4):
                    ws.inc(G.dma(slot[:, fi4 * 512:(fi4 + 1) * 512], c.w1q_hbm[jc, g2 * 4 + fi4]), 16)
                iw = ws.n
                Tn.wait(ws, iw)
                if g2 == 0:
                    Tn.wait(vp, xq_ready)
                    c.waitb(Tn, [2, 3, 4, 5])
                for fi in range(4):
                    f = g2 * 4 + fi
                    for tci in range(4):
                        Tn.mm(PS[2 + tci][:], fmB[f // 4][:, (f % 4) * 512 + tci * 128:(f % 4) * 512 + tci * 128 + 128],
                              slot[:, fi * 512:(fi + 1) * 512], start=(f == 0), stop=(f == 15))
                ring_rel[g2 % 2] = tp.n + 1
                tp.inc(Tn.last)
            itg = tp.n
            S.wait(tp, itg)
            for tci in range(4):
                S.act(fmA[tci * 4 + jc // 4][:, (jc % 4) * 512:(jc % 4) * 512 + 512], PS[2 + tci][:],
                      AF.Gelu_apprx_tanh, scale=c.cols[:, 36 + tci:37 + tci])
            isp = sp.inc(S.last)
            for tci in range(4):
                c.relb(2 + tci, sp, isp)
        h_done = sp.n

        V.wait(sp, h_done)
        for tci in range(4):
            for q in range(4):
                V.red(c.stat[:, 40 + q:41 + q], fmA[tci * 4 + q][:], OP.max, absval=True)
            V.tt(c.stat[:, 40:41], c.stat[:, 40:41], c.stat[:, 41:42], OP.max)
            V.tt(c.stat[:, 42:43], c.stat[:, 42:43], c.stat[:, 43:44], OP.max)
            V.tt(c.stat[:, 40:41], c.stat[:, 40:41], c.stat[:, 42:43], OP.max)
            V.ts(c.stat[:, 44:45], c.stat[:, 40:41], 1e-5, None, OP.max)
            V.recip(c.stat[:, 45:46], c.stat[:, 44:45])
            V.ts(c.cols[:, 40 + tci:41 + tci], c.stat[:, 45:46], 127.0, None, OP.mult)
            V.ts(c.cols[:, 44 + tci:45 + tci], c.stat[:, 44:45], 1.0 / 127.0, None, OP.mult)
            sh = c.cols[:, 40 + tci:41 + tci]
            for q in range(4):
                for half in range(2):
                    hh = slice(half * 1024, half * 1024 + 1024)
                    v_ = f32a[0][:, 0:1024]
                    V.ts(v_, fmA[tci * 4 + q][:, hh], sh, None, OP.mult)
                    V.ts(f32a[0][:, 1024:2048], v_, M52, C52_32, OP.add, OP.subtract)
                    V.ts(f32a[0][:, 1024:2048], f32a[0][:, 1024:2048], M32, None, OP.subtract)
                    V.ts(f32a[1][:, 0:1024], v_, M23, M23, OP.add, OP.subtract)
                    V.ts(f32a[1][:, 1024:2048], v_, M32, M32, OP.add, OP.subtract)
                    V.tt(f32a[1][:, 0:1024], f32a[1][:, 0:1024], f32a[1][:, 1024:2048], OP.subtract)
                    V.tt(fmA[tci * 4 + q][:, hh], f32a[0][:, 1024:2048], f32a[1][:, 0:1024], OP.add)
        nh_done = vp.inc(V.last)

        Tn.wait(vp, nh_done)
        Tn.wait(sp, sp.n)
        lagged = []
        hist = {}
        n = 0
        vp0 = vp.n
        for cyc in cycles:
            for i in range(len(cyc)):
                s_id = cyc[i]
                pt = PT[n % 2][:, ((n // 2) % 4) * 256:((n // 2) % 4) * 256 + 128]
                Tn.wait(vp, hist[n - 2] if n >= 2 else vp0)
                Tn.trans(pt, slot_ap(s_id), c.sid[:])
                it = tp.inc(Tn.last)
                lagged.append((it, pt, perm[s_id]))
                if len(lagged) > 1:
                    it2, pt2, d2 = lagged.pop(0)
                    V.wait(tp, max(it2, it))
                    V.copy(slot_ap(d2), pt2)
                    hist[n - 1] = vp.inc(V.last)
                n += 1
            it2, pt2, d2 = lagged.pop(0)
            V.wait(tp, it2)
            V.copy(slot_ap(d2), pt2)
            hist[n - 1] = vp.inc(V.last)
        hq_ready = vp.n

        for oc in range(4):
            for jg in range(16):
                slot = wb[2 + jg % 2]
                G.wait(tp, ring_rel.get(2 + jg % 2, 0))
                ws = c.wsem[2 + jg % 2]
                for ji4 in range(4):
                    ws.inc(G.dma(slot[:, ji4 * 512:(ji4 + 1) * 512], c.w2q_hbm[oc, jg * 4 + ji4]), 16)
                iw = ws.n
                Tn.wait(ws, iw)
                if jg == 0:
                    Tn.wait(vp, hq_ready)
                    c.waitb(Tn, [2, 3, 4, 5])
                for ji in range(4):
                    j = jg * 4 + ji
                    for tci in range(4):
                        lhs = fmA[j // 4][:, (j % 4) * 512 + tci * 128:(j % 4) * 512 + tci * 128 + 128]
                        Tn.mm(PS[2 + tci][:], lhs, slot[:, ji * 512:(ji + 1) * 512],
                              start=(j == 0), stop=(j == 63))
                ring_rel[2 + jg % 2] = tp.n + 1
                tp.inc(Tn.last)
            itg = tp.n
            V.wait(tp, itg)
            for tci in range(4):
                V.ts(fmB[tci][:, oc * 512:(oc + 1) * 512], PS[2 + tci][:],
                     c.cols[:, 44 + tci:45 + tci], None, OP.mult)
            iv = vp.inc(V.last)
            for tci in range(4):
                c.relb(2 + tci, vp, iv)
        for tci in range(4):
            _ln_write(c, tcs[tci], fmB[tci][:])



def _bf(x):
    return np.ascontiguousarray(x, dtype=ml_dtypes.bfloat16)


def _f32(x):
    return np.ascontiguousarray(x, dtype=np.float32)


def _tile_fm(wT, groups):
    raise NotImplementedError


def _prep_gla_qk(wT):
    out = np.empty((2, 4, 128, 2048), np.float32)
    for mg in range(2):
        for fg in range(4):
            for fi in range(4):
                f = fg * 4 + fi
                blk = wT[f * 128:(f + 1) * 128, mg * 512:(mg + 1) * 512]
                out[mg, fg, :, fi * 512:(fi + 1) * 512] = blk
    return out


def _prep_gla_vgo(wT):
    out = np.empty((4, 4, 128, 2048), np.float32)
    for oc in range(4):
        for fg in range(4):
            for fi in range(4):
                f = fg * 4 + fi
                out[oc, fg, :, fi * 512:(fi + 1) * 512] = wT[f * 128:(f + 1) * 128, oc * 512:(oc + 1) * 512]
    return out


def prep_inputs(inputs):
    g = {}
    g["c_ident"] = np.eye(128, dtype=ml_dtypes.bfloat16)
    g["c_mask"] = _bf(np.tril(np.ones((128, 128), np.float32)).T)
    H = np.array([[1.0]], np.float32)
    while H.shape[0] < 128:
        H = np.block([[H, H], [H, -H]])
    g["c_h128"] = _bf(H)
    g["c_onesrow"] = np.ones((1, 128), ml_dtypes.bfloat16)
    g["c_onescol"] = np.ones((128, 1), ml_dtypes.bfloat16)
    g["c_onesrowf"] = np.ones((1, 128), np.float32)

    wqT = _f32(inputs["L0_attn_Wq"].T)
    wkT = _f32(inputs["L0_attn_Wk"].T)
    g["gla_qk_t"] = np.stack([_prep_gla_qk(wqT), _prep_gla_qk(wkT)])
    wvT = _f32(inputs["L0_attn_Wv"].T)
    wgT = _f32(inputs["L0_attn_Wg"].T)
    woT = _f32(inputs["L0_attn_Wo"].T)
    g["gla_vgo_t"] = np.stack([_prep_gla_vgo(wvT), _prep_gla_vgo(wgT), _prep_gla_vgo(woT)])
    wkg1T = _f32(inputs["L0_attn_Wkg1"].T)
    g["wkg1_t"] = _f32(wkg1T.reshape(16, 128, 16).transpose(1, 0, 2).reshape(128, 256))
    g["wkg2T"] = _f32(inputs["L0_attn_Wkg2"].T)
    g["bkg2_cols"] = _f32(inputs["L0_attn_bkg2"].reshape(8, 128).T)
    g["bg"] = _f32(inputs["L0_attn_bg"].reshape(1, -1))

    wupT = _f32(inputs["L0_ffn_W_up"].T)
    wgateT = _f32(inputs["L0_ffn_W_gate"].T)
    wdownT = _f32(inputs["L0_ffn_W_down"].T)
    ug = np.empty((2, 4, 4, 4, 128, 2048), np.float32)
    for wi, w in enumerate([wupT, wgateT]):
        for jc in range(4):
            for jg in range(4):
                for fg in range(4):
                    for fi in range(4):
                        f = fg * 4 + fi
                        ug[wi, jc, jg, fg, :, fi * 512:(fi + 1) * 512] = \
                            w[f * 128:(f + 1) * 128, jc * 2048 + jg * 512:jc * 2048 + (jg + 1) * 512]
    g["ug_t"] = ug
    dn = np.empty((4, 16, 128, 2048), np.float32)
    for oc in range(4):
        for jg in range(16):
            for ji in range(4):
                j = jg * 4 + ji
                dn[oc, jg, :, ji * 512:(ji + 1) * 512] = \
                    wdownT[j * 128:(j + 1) * 128, oc * 512:(oc + 1) * 512]
    g["dn_t"] = dn
    g["su_ug_cols"] = _f32(inputs["L0_ffn_SU_ug"].reshape(16, 128).T)
    g["su_dp_cols"] = _f32(inputs["L0_ffn_SU_dp"].reshape(64, 128).T)
    wsc = _f32(inputs["L0_ffn_Wscale_ug"])
    g["wsc_up_cols"] = _f32(wsc[:INTER].reshape(64, 128).T)
    g["wsc_gate_cols"] = _f32(wsc[INTER:].reshape(64, 128).T)
    g["wsc_dp_row"] = _f32(inputs["L0_ffn_Wscale_dp"].reshape(1, -1))

    kvdT = _f32(inputs["L1_attn_kv_down_w"].T)
    kvd = np.empty((4, 128, 2048), np.float32)
    for fg in range(4):
        for fi in range(4):
            f = fg * 4 + fi
            kvd[fg, :, fi * 512:(fi + 1) * 512] = kvdT[f * 128:(f + 1) * 128, :]
    g["kvd_t"] = kvd
    qdT = _f32(inputs["L1_attn_query_down_w"].T)
    g["qd_t"] = _prep_gla_qk(qdT)
    kupT = _f32(inputs["L1_attn_key_up_w"].T)
    vupT = _f32(inputs["L1_attn_value_up_w"].T)
    krT = _f32(inputs["L1_attn_key_rope_w"].T)
    kup = np.empty((16, 128, 512), np.float32)
    vup = np.empty((16, 128, 512), np.float32)
    krt = np.empty((16, 128, 256), np.float32)
    for h in range(16):
        for kf in range(4):
            kup[h, :, kf * 128:(kf + 1) * 128] = kupT[kf * 128:(kf + 1) * 128, h * 128:(h + 1) * 128]
            vup[h, :, kf * 128:(kf + 1) * 128] = vupT[kf * 128:(kf + 1) * 128, h * 128:(h + 1) * 128]
            krt[h, :, kf * 64:(kf + 1) * 64] = krT[kf * 128:(kf + 1) * 128, h * 64:(h + 1) * 64]
    g["kup_t"] = kup
    g["vup_t"] = vup
    g["kr_t"] = krt
    qupT = _f32(inputs["L1_attn_query_up_w"].T)
    qup = np.empty((16, 128, 1024), np.float32)
    for h in range(16):
        for qf in range(8):
            qup[h, :, qf * 128:(qf + 1) * 128] = qupT[qf * 128:(qf + 1) * 128, h * 128:(h + 1) * 128]
    g["qup_t"] = qup
    qrT = _f32(inputs["L1_attn_query_rope_w"].T)
    qr = np.empty((16, 128, 512), np.float32)
    for h in range(16):
        for qf in range(8):
            qr[h, :, qf * 64:(qf + 1) * 64] = qrT[qf * 128:(qf + 1) * 128, h * 64:(h + 1) * 64]
    g["qr_t"] = qr
    outwT = _f32(inputs["L1_attn_out_w"].T)
    ow = np.empty((4, 4, 128, 2048), np.float32)
    for oc in range(4):
        for hg in range(4):
            for hi in range(4):
                h = hg * 4 + hi
                ow[oc, hg, :, hi * 512:(hi + 1) * 512] = \
                    outwT[h * 128:(h + 1) * 128, oc * 512:(oc + 1) * 512]
    g["outw_t"] = ow
    for nm, key in [("kvd_b", "L1_attn_kv_down_b"), ("kup_b", "L1_attn_key_up_b"),
                    ("vup_b", "L1_attn_value_up_b"),
                    ("qd_b", "L1_attn_query_down_b"), ("qup_b", "L1_attn_query_up_b")]:
        g[nm] = _f32(inputs[key].reshape(-1, 128).T)
    g["kr_b"] = _f32(inputs["L1_attn_key_rope_b"].reshape(16, 64).T)
    g["qr_b"] = _f32(inputs["L1_attn_query_rope_b"].reshape(16, 64).T)
    g["outw_b"] = _f32(inputs["L1_attn_out_b"].reshape(1, -1))

    w1T = _f32(inputs["L1_ffn_w1"].T)
    w1t = np.empty((16, 16, 128, 512), np.float32)
    for jc in range(16):
        for f in range(16):
            w1t[jc, f] = w1T[f * 128:(f + 1) * 128, jc * 512:(jc + 1) * 512]
    g["w1_t"] = w1t
    w2T = _f32(inputs["L1_ffn_w2"].T)
    w2t = np.empty((4, 64, 128, 512), np.float32)
    for os_ in range(4):
        for j in range(64):
            w2t[os_, j] = w2T[j * 128:(j + 1) * 128, os_ * 512:(os_ + 1) * 512]
    g["w2_t"] = w2t
    return g


def kernel(**inputs):
    g = prep_inputs(inputs)
    x = np.ascontiguousarray(inputs["L0_attn_x"], dtype=np.float32)
    nc = build_program(_UPTO)
    in_maps = []
    for core in range(NCORES):
        m = dict(g)
        m["x_in"] = x[core]
        in_maps.append(m)
    res = run_bass_kernel_spmd(nc, in_maps, core_ids=list(range(NCORES)))
    out = np.stack([np.asarray(res.results[i]["out"], np.float32) for i in range(NCORES)])
    return out
